# version 02
# baseline (speedup 1.0000x reference)
import math
import numpy as np
import ml_dtypes
import concourse.bass as bass
import concourse.mybir as mybir
from concourse.bass_utils import run_bass_kernel_spmd

F32 = mybir.dt.float32
BF16 = mybir.dt.bfloat16
AF = mybir.ActivationFunctionType
ALU = mybir.AluOpType
AX = mybir.AxisListType

T = 2048
D = 1024
NCHUNK = T // 128
NDC = D // 128


SHARED_TILES = {"xres", "scr_lnsq", "ident", "identf"}


class Chan:
    __slots__ = ("h", "n")

    def __init__(self, nc, name):
        self.h = nc.alloc_semaphore(name)
        self.n = 0

    def inc(self, inst, k=1):
        inst.then_inc(self.h, k)
        self.n += k
        return self.n


class Ctx:
    def __init__(self, nc, block):
        self.nc = nc
        self.block = block
        self.chans = {}
        self.tiles = {}
        self.consts_np = {}
        self.const_aps = {}
        self._trans_wait = (None, 0)

    def chan(self, name):
        if name not in self.chans:
            self.chans[name] = Chan(self.nc, name)
        return self.chans[name]

    def begin_block(self):
        self.arena_ptr = 0

    def sbuf(self, name, shape, dtype):
        if name not in self.tiles:
            if getattr(self, "arena", None) is not None and name not in SHARED_TILES:
                esz = 2 if dtype == BF16 else 4
                nbytes = shape[1] * esz
                ptr = (self.arena_ptr + 63) & ~63
                assert ptr + nbytes <= self.arena_bytes, \
                    f"arena overflow: {name} needs {nbytes} at {ptr} (cap {self.arena_bytes})"
                self.arena_ptr = ptr + nbytes
                view = self.arena[:, ptr:ptr + nbytes].bitcast(dtype)
                if shape[0] < 128:
                    view = view[0:shape[0], :]
                self.tiles[name] = view
            else:
                self.tiles[name] = self.nc.alloc_sbuf_tensor(name, shape, dtype)
        return self.tiles[name]

    def psum(self, name, shape, dtype=F32):
        if name not in self.tiles:
            n_el = 1024 if dtype == BF16 else 512
            shape = [shape[0], max(shape[1], n_el) if shape[1] <= n_el else shape[1]]
            self.tiles[name] = self.nc.alloc_psum_tensor(name, shape, dtype)
        return self.tiles[name]

    def const(self, name, arr_fn):
        if name not in self.const_aps:
            arr = np.ascontiguousarray(arr_fn() if callable(arr_fn) else arr_fn)
            self.consts_np[name] = arr
            dt = BF16 if arr.dtype == ml_dtypes.bfloat16 else F32
            self.const_aps[name] = self.nc.dram_tensor(
                "c_" + name, list(arr.shape), dt, kind="ExternalInput").ap()
        return self.const_aps[name]


def _ident_bf16():
    return np.eye(128).astype(ml_dtypes.bfloat16)


def _hadamard_bf16():
    h = np.array([[1.0]], dtype=np.float64)
    while h.shape[0] < D:
        h = np.block([[h, h], [h, -h]])
    return (h / np.sqrt(D)).astype(np.float32).astype(ml_dtypes.bfloat16)



def emit_transposes(ctx, pairs):
    ps = [ctx.psum("tpsA", [128, 1024], BF16), ctx.psum("tpsB", [128, 1024], BF16)]
    ident = ctx.sbuf("ident", [128, 128], BF16)
    cht = ctx.chan("tr_t")
    chv = ctx.chan("tr_v")
    wait_ch, wait_val = ctx._trans_wait
    base_t = cht.n
    base_v = chv.n

    def tsec(t):
        t.wait_ge(chv.h, base_v)
        if wait_ch is not None:
            t.wait_ge(wait_ch.h, wait_val)
        for k, (src, dst) in enumerate(pairs):
            if k >= 2:
                t.wait_ge(chv.h, base_v + k - 1)
            cht.inc(t.transpose(ps[k % 2][0:src.shape[-1], 0:src.shape[0]], src, ident[:]))

    def vsec(v):
        for k, (src, dst) in enumerate(pairs):
            v.wait_ge(cht.h, base_t + k + 1)
            chv.inc(v.tensor_copy(dst, ps[k % 2][0:src.shape[-1], 0:src.shape[0]]))

    ctx.block.tensor(tsec)
    ctx.block.vector(vsec)
    return chv, chv.n


def emit_ln(ctx, name, src_cols, dst_cols, eps=1e-5, wait=None, nchunk=NCHUNK, lo_cols=None):
    st = ctx.sbuf(name + "_stat", [128, 5 * NCHUNK], F32)
    sums = st[:, 0 * NCHUNK:0 * NCHUNK + nchunk]
    ss = st[:, 1 * NCHUNK:1 * NCHUNK + nchunk]
    mean = st[:, 2 * NCHUNK:2 * NCHUNK + nchunk]
    var = st[:, 3 * NCHUNK:3 * NCHUNK + nchunk]
    rstd = st[:, 4 * NCHUNK:4 * NCHUNK + nchunk]
    sq = ctx.sbuf("scr_lnsq", [128, 1024], F32)
    chv = ctx.chan(name + "_v")
    chs = ctx.chan(name + "_s")

    def vsec(v):
        if wait is not None:
            v.wait_ge(wait[0].h, wait[1])
        for i in range(nchunk):
            mrk = v.reduce_sum(sums[:, i:i + 1], src_cols(i), axis=AX.X)
        chv.inc(mrk)
        v.wait_ge(chv.h, chv.n)
        chv.inc(v.tensor_scalar(mean[:, 0:nchunk], sums[:, 0:nchunk], 1.0 / D, None, ALU.mult))
        v.wait_ge(chv.h, chv.n)
        v.wait_ge(chs.h, 1)
        chv.inc(v.tensor_scalar(var[:, 0:nchunk], ss[:, 0:nchunk], 1.0 / D, eps, ALU.mult, ALU.add))
        v.wait_ge(chv.h, chv.n)
        chv.inc(v.scalar_tensor_tensor(var[:, 0:nchunk], mean[:, 0:nchunk], -1.0, var[:, 0:nchunk],
                                       ALU.mult, ALU.elemwise_mul) if False else
                v.tensor_tensor(sq[:, 0:nchunk], mean[:, 0:nchunk], mean[:, 0:nchunk], ALU.mult))
        v.wait_ge(chv.h, chv.n)
        chv.inc(v.tensor_sub(var[:, 0:nchunk], var[:, 0:nchunk], sq[:, 0:nchunk]))
        v.wait_ge(chs.h, 2)
        chv.inc(v.reciprocal(rstd[:, 0:nchunk], var[:, 0:nchunk]))
        v.wait_ge(chv.h, chv.n)
        for i in range(nchunk):
            if lo_cols is None:
                chv.inc(v.tensor_scalar(dst_cols(i), src_cols(i), mean[:, i:i + 1], rstd[:, i:i + 1],
                                        ALU.subtract, ALU.mult))
            else:
                v.tensor_scalar(sq[:], src_cols(i), mean[:, i:i + 1], rstd[:, i:i + 1],
                                ALU.subtract, ALU.mult)
                v.tensor_copy(dst_cols(i), sq[:])
                chv.inc(v.tensor_sub(lo_cols(i), sq[:], dst_cols(i)))

    def ssec(s_):
        if wait is not None:
            s_.wait_ge(wait[0].h, wait[1])
        for i in range(nchunk):
            mrk = s_.activation(sq[:], src_cols(i), AF.Square, accum_out=ss[:, i:i + 1])
        chs.inc(mrk)
        s_.wait_ge(chv.h, 5)
        chs.inc(s_.activation(var[:, 0:nchunk], var[:, 0:nchunk], AF.Sqrt))

    ctx.block.vector(vsec)
    ctx.block.scalar(ssec)
    return chv, chv.n


def emit_ln_to_znT(ctx, name, src_cols, out_name, eps=1e-5, wait=None, hilo=False):
    zn = ctx.sbuf(name + "_zn", [128, NCHUNK * 1024], BF16)
    znT = ctx.sbuf(out_name, [128, NDC * T], BF16)
    lo_cols = None
    if hilo:
        zn_lo = ctx.sbuf(name + "_znlo", [128, NCHUNK * 1024], BF16)
        znT_lo = ctx.sbuf(out_name + "_lo", [128, NDC * T], BF16)
        lo_cols = lambda i: zn_lo[:, i * 1024:(i + 1) * 1024]
    ch_ln, v_ln = emit_ln(ctx, name, src_cols,
                          lambda i: zn[:, i * 1024:(i + 1) * 1024], eps=eps, wait=wait,
                          lo_cols=lo_cols)
    pairs = []
    for i in range(NCHUNK):
        for dc in range(NDC):
            pairs.append((zn[:, i * 1024 + dc * 128: i * 1024 + (dc + 1) * 128],
                          znT[:, dc * T + 128 * i: dc * T + 128 * (i + 1)]))
            if hilo:
                pairs.append((zn_lo[:, i * 1024 + dc * 128: i * 1024 + (dc + 1) * 128],
                              znT_lo[:, dc * T + 128 * i: dc * T + 128 * (i + 1)]))
    ctx._trans_wait = (ch_ln, v_ln)
    ch_tr, v_tr = emit_transposes(ctx, pairs)
    return zn, znT, (ch_tr, v_tr)


def emit_ln_T(ctx, name, src_cols, dstT, dstT_lo=None, eps=1e-5, wait=None):
    st = ctx.sbuf(name + "_stat", [128, 5 * NCHUNK], F32)
    sums = st[:, 0:NCHUNK]
    ss = st[:, NCHUNK:2 * NCHUNK]
    mean = st[:, 2 * NCHUNK:3 * NCHUNK]
    var = st[:, 3 * NCHUNK:4 * NCHUNK]
    rstd = st[:, 4 * NCHUNK:5 * NCHUNK]
    sq = ctx.sbuf("scr_lnsq", [128, 1024], F32)
    hilo = dstT_lo is not None
    K = 16 if hilo else 8
    ring = ctx.sbuf(name + "_ring", [128, 2 * 1024], BF16)
    ring_lo = ctx.sbuf(name + "_ringlo", [128, 2 * 1024], BF16) if hilo else None
    ident = ctx.sbuf("ident", [128, 128], BF16)
    ps = [ctx.psum("tpsA", [128, 1024], BF16), ctx.psum("tpsB", [128, 1024], BF16)]
    chv = ctx.chan(name + "_v")
    chs = ctx.chan(name + "_s")
    chw = ctx.chan(name + "_w")
    che = ctx.chan(name + "_e")
    cht = ctx.chan(name + "_t")

    def tdst(i, j):
        dc = j % 8
        tgt = dstT if j < 8 else dstT_lo
        return tgt[:, dc * T + 128 * i: dc * T + 128 * (i + 1)]

    def vsec(v):
        if wait is not None:
            v.wait_ge(wait[0].h, wait[1])
        for i in range(NCHUNK):
            mrk = v.reduce_sum(sums[:, i:i + 1], src_cols(i), axis=AX.X)
        chv.inc(mrk)
        v.wait_ge(chv.h, chv.n)
        chv.inc(v.tensor_scalar(mean[:, :], sums[:, :], 1.0 / D, None, ALU.mult))
        v.wait_ge(chv.h, chv.n)
        v.wait_ge(chs.h, 1)
        chv.inc(v.tensor_scalar(var[:, :], ss[:, :], 1.0 / D, eps, ALU.mult, ALU.add))
        v.wait_ge(chv.h, chv.n)
        chv.inc(v.tensor_tensor(sq[:, 0:NCHUNK], mean[:, :], mean[:, :], ALU.mult))
        v.wait_ge(chv.h, chv.n)
        chv.inc(v.tensor_sub(var[:, :], var[:, :], sq[:, 0:NCHUNK]))
        v.wait_ge(chs.h, 2)
        chv.inc(v.reciprocal(rstd[:, :], var[:, :]))
        v.wait_ge(chv.h, chv.n)
        for i in range(NCHUNK):
            sl = (i % 2) * 1024
            if i >= 2:
                v.wait_ge(cht.h, K * (i - 1))
            if hilo:
                v.tensor_scalar(sq[:], src_cols(i), mean[:, i:i + 1], rstd[:, i:i + 1],
                                ALU.subtract, ALU.mult)
                v.tensor_copy(ring[:, sl:sl + 1024], sq[:])
                chw.inc(v.tensor_sub(ring_lo[:, sl:sl + 1024], sq[:], ring[:, sl:sl + 1024]))
            else:
                chw.inc(v.tensor_scalar(ring[:, sl:sl + 1024], src_cols(i),
                                        mean[:, i:i + 1], rstd[:, i:i + 1],
                                        ALU.subtract, ALU.mult))
            if i >= 1:
                for j in range(K):
                    v.wait_ge(cht.h, K * (i - 1) + j + 1)
                    che.inc(v.tensor_copy(tdst(i - 1, j), ps[(K * (i - 1) + j) % 2][:, 0:128]))
        for j in range(K):
            v.wait_ge(cht.h, K * (NCHUNK - 1) + j + 1)
            che.inc(v.tensor_copy(tdst(NCHUNK - 1, j), ps[(K * (NCHUNK - 1) + j) % 2][:, 0:128]))

    def ssec(s_):
        if wait is not None:
            s_.wait_ge(wait[0].h, wait[1])
        for i in range(NCHUNK):
            mrk = s_.activation(sq[:], src_cols(i), AF.Square, accum_out=ss[:, i:i + 1])
        chs.inc(mrk)
        s_.wait_ge(chv.h, 5)
        chs.inc(s_.activation(var[:, :], var[:, :], AF.Sqrt))

    base_e = che.n
    base_w = chw.n
    base_t = cht.n

    def tsec(t):
        if wait is not None:
            t.wait_ge(wait[0].h, wait[1])
        for i in range(NCHUNK):
            t.wait_ge(chw.h, base_w + i + 1)
            sl = (i % 2) * 1024
            for j in range(K):
                g = K * i + j
                if g >= 2:
                    t.wait_ge(che.h, base_e + g - 1)
                dc = j % 8
                src = (ring if j < 8 else ring_lo)[:, sl + dc * 128: sl + (dc + 1) * 128]
                cht.inc(t.transpose(ps[g % 2][:, 0:128], src, ident[:]))

    ctx.block.vector(vsec)
    ctx.block.scalar(ssec)
    ctx.block.tensor(tsec)
    return che, che.n


def emit_l0_ffn(ctx, waps):
    xres = ctx.tiles["xres"]
    ident = ctx.sbuf("ident", [128, 128], BF16)
    identf = ctx.tiles["identf"]
    NE, NFB, NTP = (0 if MOE_GATE_ONLY else 8), 4, 4

    xnT = ctx.sbuf("moe_xnT", [128, NDC * T], BF16)
    bufB = ctx.sbuf("moe_bufB", [128, 16384], BF16)
    xnT_lo = bufB[:, 0:16384]
    che_ln, v_ln = emit_ln_T(ctx, "moe_ln", lambda i: xres[:, i * 1024:(i + 1) * 1024],
                             xnT, xnT_lo)

    wgs = ctx.sbuf("moe_wg", [128, 64], F32)
    wg_hi = ctx.sbuf("moe_wg_hi", [128, 64], BF16)
    wg_lo = ctx.sbuf("moe_wg_lo", [128, 64], BF16)
    bgr = ctx.sbuf("moe_bg", [1, 8], F32)
    b2a = ctx.sbuf("moe_b2", [8, 1024], F32)
    ones_f = ctx.sbuf("moe_onesf", [1, 128], F32)
    bgb = ctx.sbuf("moe_bgb", [128, 8], F32)
    ps_bg = ctx.psum("psD", [128, 512], F32)
    ch_sw = ctx.chan("moe_swd")
    ch_sv = ctx.chan("moe_swv")
    ch_bgt = ctx.chan("moe_bgt")

    def sdma(s_):
        wg = waps["L0_ffn_Wg"]
        for dc in range(NDC):
            ch_sw.inc(s_.dma_start(out=wgs[:, dc * 8:(dc + 1) * 8],
                                   in_=wg[dc * 128:(dc + 1) * 128, :]), 16)
        ch_sw.inc(s_.dma_start(out=bgr[0:1, :], in_=waps["L0_ffn_bg"][:, :]), 16)
        ch_sw.inc(s_.dma_start(out=b2a[:, :], in_=waps["L0_ffn_b2"][:, :]), 16)
    ctx.block.scalar(sdma)

    def vsw(v):
        v.wait_ge(ch_sw.h, 16 * (NDC + 2))
        v.tensor_copy(wg_hi[:], wgs[:])
        ch_sv.inc(v.tensor_sub(wg_lo[:], wgs[:], wg_hi[:]))
        ch_sv.inc(v.memset(ones_f[:], 1.0))
        v.wait_ge(ch_bgt.h, 1)
        ch_sv.inc(v.tensor_copy(bgb[:, :], ps_bg[:, 0:8]))
        v.wait_ge(ch_sv.h, ch_sv.n)
    ctx.block.vector(vsw)

    def tsw(t):
        t.wait_ge(ch_sw.h, 16 * (NDC + 2))
        t.wait_ge(ch_sv.h, 2)
        ch_bgt.inc(t.matmul(ps_bg[:, 0:8], ones_f[0:1, 0:128], bgr[0:1, :],
                            start=True, stop=True))
    ctx.block.tensor(tsw)

    cw = ctx.sbuf("moe_cw", [128, NCHUNK * 8], F32)
    gl_dbg = ctx.sbuf("moe_gldbg", [128, NCHUNK * 8], F32)
    pr_dbg = ctx.sbuf("moe_prdbg", [128, NCHUNK * 8], F32)
    m8_dbg = ctx.sbuf("moe_m8dbg", [128, NCHUNK * 8], F32)
    cwT = ctx.sbuf("moe_cwT", [8, T], F32)
    gs = ctx.sbuf("moe_gs", [128, 32], F32)
    probs = ctx.sbuf("moe_probs", [128, 8], F32)
    ps_g = ctx.psum("psC", [128, 512], F32)
    ps_d = ctx.psum("psD", [128, 512], F32)
    ch_gt = ctx.chan("moe_gt")
    ch_gv = ctx.chan("moe_gv")
    ch_gs = ctx.chan("moe_gs")
    CWT_DONE = 30 * (NCHUNK - 1) + 21
    GATE_V_TOTAL = CWT_DONE + 2 * NCHUNK

    def gate_t(t):
        t.wait_ge(che_ln.h, v_ln)
        t.wait_ge(ch_sv.h, 1)
        for i in range(NCHUNK):
            if i >= 1:
                t.wait_ge(ch_gv.h, 30 * (i - 1) + 1)
            mm = None
            for dc in range(NDC):
                lh = xnT[:, dc * T + 128 * i: dc * T + 128 * (i + 1)]
                ll = xnT_lo[:, dc * T + 128 * i: dc * T + 128 * (i + 1)]
                t.matmul(ps_g[:, 0:8], lh, wg_hi[:, dc * 8:(dc + 1) * 8],
                         start=(dc == 0), stop=False)
                t.matmul(ps_g[:, 0:8], ll, wg_hi[:, dc * 8:(dc + 1) * 8], start=False, stop=False)
                mm = t.matmul(ps_g[:, 0:8], lh, wg_lo[:, dc * 8:(dc + 1) * 8], start=False,
                              stop=(dc == NDC - 1))
            ch_gt.inc(mm)
            t.wait_ge(ch_gv.h, 30 * i + 20)
            ch_gt.inc(t.transpose(ps_d[0:8, 0:128], cw[:, i * 8:(i + 1) * 8], identf[:]))
        t.wait_ge(ch_gv.h, CWT_DONE)
        for i in range(NCHUNK):
            if i >= 1:
                t.wait_ge(ch_gv.h, CWT_DONE + 2 * i)
            t.matmul(ps_g[:, 0:512], cwT[0:8, 128 * i:128 * (i + 1)], b2a[0:8, 0:512],
                     start=True, stop=True)
            mm = t.matmul(ps_d[:, 0:512], cwT[0:8, 128 * i:128 * (i + 1)], b2a[0:8, 512:1024],
                          start=True, stop=True)
            ch_gt.inc(mm)

    def gate_s(s_):
        for i in range(NCHUNK):
            s_.wait_ge(ch_gv.h, 30 * i + 3)
            ch_gs.inc(s_.activation(probs[:, :], gs[:, 8:16], AF.Exp,
                                    bias=gs[:, 1:2], accum_out=gs[:, 2:3]))

    def gate_v(v):
        v.wait_ge(ch_sv.h, 3)
        for i in range(NCHUNK):
            b = 30 * i
            v.wait_ge(ch_gt.h, 2 * i + 1)
            ch_gv.inc(v.scalar_tensor_tensor(gs[:, 8:16], ps_g[:, 0:8], 1.0, bgb[:, :],
                                             ALU.mult, ALU.add), b + 1 - ch_gv.n)
            v.wait_ge(ch_gv.h, ch_gv.n)
            v.tensor_copy(gl_dbg[:, i * 8:(i + 1) * 8], gs[:, 8:16])
            ch_gv.inc(v.reduce_max(gs[:, 0:1], gs[:, 8:16], axis=AX.X))
            v.wait_ge(ch_gv.h, ch_gv.n)
            ch_gv.inc(v.tensor_scalar(gs[:, 1:2], gs[:, 0:1], -1.0, None, ALU.mult))
            v.wait_ge(ch_gs.h, i + 1)
            ch_gv.inc(v.reciprocal(gs[:, 3:4], gs[:, 2:3]))
            v.wait_ge(ch_gv.h, ch_gv.n)
            ch_gv.inc(v.tensor_scalar(probs[:, :], probs[:, :], gs[:, 3:4], None, ALU.mult))
            v.wait_ge(ch_gv.h, ch_gv.n)
            v.tensor_copy(pr_dbg[:, i * 8:(i + 1) * 8], probs[:, :])
            ch_gv.inc(v.max(gs[:, 8:16], probs[:, :]))
            v.wait_ge(ch_gv.h, ch_gv.n)
            v.tensor_copy(m8_dbg[:, i * 8:(i + 1) * 8], gs[:, 8:16])
            ch_gv.inc(v.tensor_add(gs[:, 4:5], gs[:, 8:9], gs[:, 9:10]))
            v.wait_ge(ch_gv.h, ch_gv.n)
            ch_gv.inc(v.tensor_scalar(gs[:, 4:5], gs[:, 4:5], 1e-8, None, ALU.add))
            v.wait_ge(ch_gv.h, ch_gv.n)
            ch_gv.inc(v.reciprocal(gs[:, 5:6], gs[:, 4:5]))
            v.wait_ge(ch_gv.h, ch_gv.n)
            ch_gv.inc(v.tensor_mul(gs[:, 6:7], gs[:, 8:9], gs[:, 5:6]))
            ch_gv.inc(v.tensor_mul(gs[:, 7:8], gs[:, 9:10], gs[:, 5:6]))
            v.wait_ge(ch_gv.h, ch_gv.n)
            cwc = cw[:, i * 8:(i + 1) * 8]
            ch_gv.inc(v.tensor_scalar(cwc, probs[:, :], gs[:, 8:9], gs[:, 6:7],
                                      ALU.is_equal, ALU.mult))
            ch_gv.inc(v.tensor_scalar(gs[:, 16:24], probs[:, :], gs[:, 9:10], gs[:, 7:8],
                                      ALU.is_equal, ALU.mult))
            v.wait_ge(ch_gv.h, ch_gv.n)
            mrk = v.tensor_add(cwc, cwc, gs[:, 16:24])
            ch_gv.inc(mrk, b + 20 - ch_gv.n)
            v.wait_ge(ch_gt.h, 2 * i + 2)
            ch_gv.inc(v.tensor_copy(cwT[0:8, 128 * i:128 * (i + 1)], ps_d[0:8, 0:128]))
        for i in range(NCHUNK):
            v.wait_ge(ch_gt.h, 2 * NCHUNK + i + 1)
            xc = xres[:, i * 1024:(i + 1) * 1024]
            ch_gv.inc(v.scalar_tensor_tensor(xc[:, 0:512], ps_g[:, 0:512], 1.0, xc[:, 0:512],
                                             ALU.mult, ALU.add))
            ch_gv.inc(v.scalar_tensor_tensor(xc[:, 512:1024], ps_d[:, 0:512], 1.0,
                                             xc[:, 512:1024], ALU.mult, ALU.add))
    ctx.block.tensor(gate_t)
    ctx.block.scalar(gate_s)
    ctx.block.vector(gate_v)

    W1A = ctx.sbuf("moe_W1A", [128, 8192], BF16)
    W2A = ctx.sbuf("moe_W2A", [128, 8192], BF16)
    W1B = bufB[:, 0:8192]
    W2B = bufB[:, 8192:16384]
    b1s = ctx.sbuf("moe_b1s", [32, 128], F32)
    b1T = ctx.sbuf("moe_b1T", [128, 32], F32)
    hfm = [ctx.sbuf("moe_h0", [128, 2048], BF16), ctx.sbuf("moe_h1", [128, 2048], BF16)]
    ps_out = [ctx.psum("psA", [128, 1024], F32), ctx.psum("psB", [128, 1024], F32)]
    ch_wd = ctx.chan("moe_wd")
    ch_et = ctx.chan("moe_et")
    ch_ev = ctx.chan("moe_ev")
    ch_es = ctx.chan("moe_es")
    ch_bt = ctx.chan("moe_bt")
    W1 = waps["L0_ffn_W1"]; W2 = waps["L0_ffn_W2"]; B1 = waps["L0_ffn_b1"]

    def wdma(g):
        for e in range(NE):
            g.wait_ge(ch_bt.h, 2 * e)
            ch_wd.inc(g.dma_start(out=b1s[:, :], in_=B1[e].rearrange("(c p) -> c p", p=128)), 16)
            for fb in range(NFB):
                idx = 4 * e + fb
                w1t = [W1A, W1B][idx % 2]
                w2t = [W2A, W2B][idx % 2]
                if idx == 1:
                    g.wait_ge(ch_gt.h, 2 * NCHUNK)
                if idx >= 2:
                    g.wait_ge(ch_et.h, 80 * (idx - 1))
                for dc in range(NDC):
                    ch_wd.inc(g.dma_start(out=w1t[:, dc * 1024:(dc + 1) * 1024],
                                          in_=W1[e, dc * 128:(dc + 1) * 128,
                                                 fb * 1024:(fb + 1) * 1024]), 16)
                for fc in range(NDC):
                    ch_wd.inc(g.dma_start(out=w2t[:, fc * 1024:(fc + 1) * 1024],
                                          in_=W2[e, fb * 1024 + fc * 128:
                                                 fb * 1024 + (fc + 1) * 128, 0:1024]), 16)
    ctx.block.gpsimd(wdma)

    def exp_t(t):
        t.wait_ge(ch_gv.h, GATE_V_TOTAL)
        dma_n = 0
        for e in range(NE):
            dma_n += 16
            t.wait_ge(ch_wd.h, dma_n)
            if e >= 1:
                t.wait_ge(ch_es.h, 16 * 16 * e)
            ch_bt.inc(t.transpose(ps_d[0:128, 0:32], b1s[0:32, 0:128], identf[0:32, 0:32]))
            t.wait_ge(ch_bt.h, 2 * e + 2)
            for fb in range(NFB):
                idx = 4 * e + fb
                w1t = [W1A, W1B][idx % 2]
                w2t = [W2A, W2B][idx % 2]
                dma_n += 16 * 16
                t.wait_ge(ch_wd.h, dma_n)
                for tp in range(NTP):
                    grp = idx * NTP + tp
                    if grp >= 1:
                        t.wait_ge(ch_es.h, 16 * grp)
                    for fc in range(NDC):
                        for half in range(2):
                            if fc >= 1:
                                t.wait_ge(ch_es.h, 16 * grp + 2 * (fc - 1) + half + 1)
                            pst = [ps_g, ps_d][half]
                            for dc in range(NDC):
                                lhs = w1t[:, dc * 1024 + fc * 128: dc * 1024 + (fc + 1) * 128]
                                rhs = xnT[:, dc * T + tp * 512 + half * 256:
                                          dc * T + tp * 512 + (half + 1) * 256]
                                mm = t.matmul(pst[:, 0:256], lhs, rhs,
                                              start=(dc == 0), stop=(dc == NDC - 1))
                            ch_et.inc(mm)
                    t.wait_ge(ch_es.h, 16 * grp + 16)
                    for tt in range(4):
                        pso = ps_out[tt % 2]
                        otm = 4 * grp + tt
                        if otm >= 2:
                            t.wait_ge(ch_ev.h, otm - 1)
                        hsrc = hfm[tt // 2]
                        for fc in range(NDC):
                            lhs = hsrc[:, fc * 256 + (tt % 2) * 128:
                                       fc * 256 + (tt % 2) * 128 + 128]
                            t.matmul(pso[:, 0:512], lhs, w2t[:, fc * 1024: fc * 1024 + 512],
                                     start=(fc == 0), stop=(fc == NDC - 1))
                            mm = t.matmul(pso[:, 512:1024], lhs,
                                          w2t[:, fc * 1024 + 512:(fc + 1) * 1024],
                                          start=(fc == 0), stop=(fc == NDC - 1))
                        ch_et.inc(mm)

    def exp_s(s_):
        for e in range(NE):
            s_.wait_ge(ch_bt.h, 2 * e + 2)
            for fb in range(NFB):
                idx = 4 * e + fb
                for tp in range(NTP):
                    grp = idx * NTP + tp
                    for fc in range(NDC):
                        for half in range(2):
                            s_.wait_ge(ch_et.h, 20 * grp + 2 * fc + half + 1)
                            pst = [ps_g, ps_d][half]
                            ch_es.inc(s_.activation(
                                hfm[half][:, fc * 256:(fc + 1) * 256], pst[:, 0:256],
                                AF.Gelu_apprx_tanh,
                                bias=b1T[:, fb * 8 + fc: fb * 8 + fc + 1]))

    def exp_v(v):
        for e in range(NE):
            v.wait_ge(ch_bt.h, 2 * e + 1)
            ch_bt.inc(v.tensor_copy(b1T[:, :], ps_d[0:128, 0:32]))
            for fb in range(NFB):
                idx = 4 * e + fb
                for tp in range(NTP):
                    grp = idx * NTP + tp
                    for tt in range(4):
                        otm = 4 * grp + tt
                        v.wait_ge(ch_et.h, 20 * grp + 16 + tt + 1)
                        i_chunk = tp * 4 + tt
                        xc = xres[:, i_chunk * 1024:(i_chunk + 1) * 1024]
                        pso = ps_out[tt % 2]
                        ch_ev.inc(v.scalar_tensor_tensor(
                            xc, pso[:, 0:1024], cw[:, i_chunk * 8 + e: i_chunk * 8 + e + 1],
                            xc, ALU.mult, ALU.add))
    ctx.block.tensor(exp_t)
    ctx.block.scalar(exp_s)
    ctx.block.vector(exp_v)
    if MOE_GATE_ONLY:
        return ch_gv, GATE_V_TOTAL
    return ch_ev, ch_ev.n





def emit_l1_ffn(ctx, w_ap):
    xres = ctx.tiles["xres"]
    zn = ctx.sbuf("l1f_ln_zn", [128, NCHUNK * 1024], BF16)
    zn_lo = ctx.sbuf("l1f_ln_znlo", [128, NCHUNK * 1024], BF16)
    ch_ln, v_ln = emit_ln(ctx, "l1f_ln", lambda i: xres[:, i * 1024:(i + 1) * 1024],
                          lambda i: zn[:, i * 1024:(i + 1) * 1024],
                          lo_cols=lambda i: zn_lo[:, i * 1024:(i + 1) * 1024])

    wld = ctx.sbuf("l1f_xh", [128, 1024], F32)
    wsc = ctx.sbuf("scr_lnsq", [128, 1024], F32)
    tern = ctx.sbuf("l1f_tern", [128, NDC * 1024], BF16)
    ternT = ctx.sbuf("l1f_ternT", [128, NDC * 1024], BF16)
    wstat = ctx.sbuf("l1f_wstat", [128, 4], F32)
    wscols = ctx.sbuf("l1f_wscols", [128, 8], F32)
    wsr8 = ctx.sbuf("l1f_wsr8", [8, 128], F32)
    ws_b = ctx.sbuf("l1f_wsb", [128, 1024], F32)
    e8 = ctx.sbuf("l1f_e8", [8, 1024], F32)
    ident = ctx.sbuf("ident", [128, 128], BF16)
    ps_bc = ctx.psum("psA", [128, 1024], F32)
    ps_ws = ctx.psum("psB", [128, 1024], F32)
    e8np = np.zeros((8, 1024), dtype=np.float32)
    for _oc in range(8):
        e8np[_oc, _oc * 128:(_oc + 1) * 128] = 1.0
    e8c = ctx.const("e8", e8np)
    ch_wd = ctx.chan("l1f_wd")
    ch_wv = ctx.chan("l1f_wv")
    ch_mt = ctx.chan("l1f_mt")
    ch_mv = ctx.chan("l1f_mv")

    def gsec(g):
        for oc in range(NDC):
            if oc >= 1:
                g.wait_ge(ch_wv.h, 5 * (oc - 1) + 5)
            ch_wd.inc(g.dma_start(out=wld[:], in_=w_ap[oc * 128:(oc + 1) * 128, :]), 16)
    ctx.block.gpsimd(gsec)

    def vsec(v):
        for oc in range(NDC):
            v.wait_ge(ch_wd.h, 16 * (oc + 1))
            chv_ = ch_wv
            chv_.inc(v.reduce_sum(wstat[:, 0:1], wld[:], axis=AX.X,
                                  apply_absolute_value=True))
            v.wait_ge(chv_.h, chv_.n)
            chv_.inc(v.tensor_scalar(wscols[:, oc:oc + 1], wstat[:, 0:1], 1.0 / D, 1e-5,
                                     ALU.mult, ALU.max))
            v.wait_ge(chv_.h, chv_.n)
            chv_.inc(v.reciprocal(wstat[:, 2:3], wscols[:, oc:oc + 1]))
            v.wait_ge(chv_.h, chv_.n)
            v.tensor_scalar(wld[:], wld[:], wstat[:, 2:3], None, ALU.mult)
            v.tensor_scalar(wsc[:], wld[:], 0.5, None, ALU.is_gt)
            chv_.inc(v.tensor_scalar(wld[:], wld[:], -0.5, None, ALU.is_lt))
            chv_.inc(v.tensor_sub(tern[:, oc * 1024:(oc + 1) * 1024], wsc[:], wld[:]))
    ctx.block.vector(vsec)

    pairs = []
    for oc in range(NDC):
        for ic in range(NDC):
            pairs.append((tern[:, oc * 1024 + ic * 128: oc * 1024 + (ic + 1) * 128],
                          ternT[:, ic * 1024 + oc * 128: ic * 1024 + (oc + 1) * 128]))
    ctx._trans_wait = (ch_wv, ch_wv.n)
    ch_tT, v_tT = emit_transposes(ctx, pairs)

    ch_e8 = ctx.chan("l1f_e8d")

    def gsec2(g):
        ch_e8.inc(g.dma_start(out=e8[:, :], in_=e8c[:, :]), 16)
    ctx.block.gpsimd(gsec2)

    def tsecW(t):
        t.wait_ge(ch_wv.h, 5 * NDC)
        ch_mt.inc(t.transpose(ps_ws[0:8, 0:128], wscols[:, 0:8], ctx.tiles["identf"][:]))
        t.wait_ge(ch_mv.h, 1)
        t.wait_ge(ch_e8.h, 16)
        for oc in range(NDC):
            ch_mt.inc(t.matmul(ps_bc[:, oc * 128:(oc + 1) * 128],
                               e8[:, oc * 128:(oc + 1) * 128], wsr8[0:8, :],
                               start=True, stop=True))

    def vsecW(v):
        v.wait_ge(ch_mt.h, 1)
        ch_mv.inc(v.tensor_copy(wsr8[:, :], ps_ws[0:8, 0:128]))
        v.wait_ge(ch_mt.h, 9)
        ch_mv.inc(v.tensor_copy(ws_b[:, 0:512], ps_bc[:, 0:512]))
        ch_mv.inc(v.tensor_copy(ws_b[:, 512:1024], ps_bc[:, 512:1024]))
    ctx.block.tensor(tsecW)
    ctx.block.vector(vsecW)

    Hc = ctx.const("hadamard", _hadamard_bf16)
    Hs = ctx.sbuf("l1f_H", [128, NDC * 1024], BF16)
    ch_hd = ctx.chan("l1f_hd")

    def sdma(s):
        for kc in range(NDC):
            ch_hd.inc(s.dma_start(out=Hs[:, kc * 1024:(kc + 1) * 1024],
                                  in_=Hc[kc * 128:(kc + 1) * 128, :]), 16)
    ctx.block.scalar(sdma)

    xh = wld
    q2 = ctx.sbuf("l1f_q2", [128, 1024], BF16)
    q2T = ctx.sbuf("l1f_q2T", [128, 1024], BF16)
    znTt = ctx.sbuf("l1f_znTt", [128, 1024], BF16)
    znTt_lo = ctx.sbuf("l1f_znTt_lo", [128, 1024], BF16)
    qstat = ctx.sbuf("l1f_qstat", [128, 4], F32)
    scl = ctx.sbuf("l1f_scl", [128, NCHUNK], F32)
    ps_h = ctx.psum("psA", [128, 1024], F32)
    ps_o = ctx.psum("psB", [128, 1024], F32)
    ps_t = [ctx.psum("tpsA", [128, 1024], BF16), ctx.psum("tpsB", [128, 1024], BF16)]
    ch_t = ctx.chan("l1f_t")
    ch_v = ctx.chan("l1f_v")

    def tsec(t):
        t.wait_ge(ch_hd.h, 16 * NDC)
        t.wait_ge(ch_ln.h, v_ln)
        t.wait_ge(ch_tT.h, v_tT)
        t.wait_ge(ch_mv.h, 3)
        for i in range(NCHUNK):
            tb = 26 * i
            if i >= 1:
                t.wait_ge(ch_v.h, 100 * i)
            for k in range(16):
                src = zn if k < 8 else zn_lo
                dc = k % 8
                if k >= 2:
                    t.wait_ge(ch_v.h, 100 * i + k - 1)
                ch_t.inc(t.transpose(ps_t[k % 2][:, 0:128],
                                     src[:, i * 1024 + dc * 128: i * 1024 + (dc + 1) * 128],
                                     ident[:]))
            t.wait_ge(ch_v.h, 100 * i + 16)
            for hl in range(2):
                srcT = znTt if hl == 0 else znTt_lo
                for kc in range(NDC):
                    lhs = srcT[:, kc * 128:(kc + 1) * 128]
                    t.matmul(ps_h[:, 0:512], lhs, Hs[:, kc * 1024: kc * 1024 + 512],
                             start=(hl == 0 and kc == 0), stop=(hl == 1 and kc == NDC - 1))
                    mm = t.matmul(ps_h[:, 512:1024], lhs, Hs[:, kc * 1024 + 512:(kc + 1) * 1024],
                                  start=(hl == 0 and kc == 0), stop=(hl == 1 and kc == NDC - 1))
            ch_t.inc(mm)
            t.wait_ge(ch_v.h, 100 * i + 30)
            for dc in range(NDC):
                if dc >= 2:
                    t.wait_ge(ch_v.h, 100 * i + 30 + dc - 1)
                ch_t.inc(t.transpose(ps_t[dc % 2][:, 0:128],
                                     q2[:, dc * 128:(dc + 1) * 128], ident[:]))
            t.wait_ge(ch_v.h, 100 * i + 38)
            for ic in range(NDC):
                lhs = q2T[:, ic * 128:(ic + 1) * 128]
                t.matmul(ps_o[:, 0:512], lhs, ternT[:, ic * 1024: ic * 1024 + 512],
                         start=(ic == 0), stop=(ic == NDC - 1))
                mm = t.matmul(ps_o[:, 512:1024], lhs, ternT[:, ic * 1024 + 512:(ic + 1) * 1024],
                              start=(ic == 0), stop=(ic == NDC - 1))
            ch_t.inc(mm)

    def vsec2(v):
        for i in range(NCHUNK):
            tb = 26 * i
            for k in range(16):
                dstT = znTt if k < 8 else znTt_lo
                dc = k % 8
                v.wait_ge(ch_t.h, tb + k + 1)
                mrk = v.tensor_copy(dstT[:, dc * 128:(dc + 1) * 128], ps_t[k % 2][:, 0:128])
                ch_v.inc(mrk, 100 * i + k + 1 - ch_v.n)
            v.wait_ge(ch_t.h, tb + 17)
            v.tensor_copy(xh[:], ps_h[:, 0:1024])
            ch_v.inc(v.reduce_max(qstat[:, 0:1], xh[:], axis=AX.X, apply_absolute_value=True))
            v.wait_ge(ch_v.h, ch_v.n)
            ch_v.inc(v.tensor_scalar(qstat[:, 1:2], qstat[:, 0:1], 1e-5, None, ALU.max))
            v.wait_ge(ch_v.h, ch_v.n)
            ch_v.inc(v.reciprocal(qstat[:, 2:3], qstat[:, 1:2]))
            v.wait_ge(ch_v.h, ch_v.n)
            ch_v.inc(v.tensor_scalar(qstat[:, 2:3], qstat[:, 2:3], 7.0, None, ALU.mult))
            ch_v.inc(v.tensor_scalar(scl[:, i:i + 1], qstat[:, 1:2], 1.0 / 7.0, None, ALU.mult))
            v.wait_ge(ch_v.h, ch_v.n)
            v.tensor_scalar(xh[:], xh[:], qstat[:, 2:3], 12582912.0, ALU.mult, ALU.add)
            v.tensor_scalar(xh[:], xh[:], 12582912.0, None, ALU.subtract)
            mrk = v.tensor_scalar(q2[:], xh[:], -8.0, 7.0, ALU.max, ALU.min)
            ch_v.inc(mrk, 100 * i + 30 - ch_v.n)
            for dc in range(NDC):
                v.wait_ge(ch_t.h, tb + 18 + dc)
                ch_v.inc(v.tensor_copy(q2T[:, dc * 128:(dc + 1) * 128], ps_t[dc % 2][:, 0:128]))
            v.wait_ge(ch_t.h, tb + 26)
            xc = xres[:, i * 1024:(i + 1) * 1024]
            v.scalar_tensor_tensor(xh[:], ps_o[:, 0:1024], scl[:, i:i + 1], ws_b[:, 0:1024],
                                   ALU.mult, ALU.mult)
            mrk = v.tensor_add(xc, xc, xh[:])
            ch_v.inc(mrk, 100 * (i + 1) - ch_v.n)
    ctx.block.tensor(tsec)
    ctx.block.vector(vsec2)
    return ch_v, ch_v.n





def _l0a_consts():
    H, e = 8, 128
    half = e // 2
    gam = 1.0 - 2.0 ** (-5.0 - np.arange(H, dtype=np.float64))
    j = np.arange(half, dtype=np.float64)
    base = (2 * j + 0.4 * e) / (1.4 * e)
    pos = np.arange(T, dtype=np.float64)
    scale = base[None, :] ** (pos[:, None] / 512.0)
    inv_freq = 1.0 / 10000.0 ** (j / half)
    sinu = pos[:, None] * inv_freq[None, :]
    sn, cs = np.sin(sinu), np.cos(sinu)
    bf = ml_dtypes.bfloat16
    f2b = lambda a: a.astype(np.float32).astype(bf)
    tt_ = np.arange(128)
    diff = tt_[None, :] - tt_[:, None]
    D_ = np.zeros((128, H * 128))
    XI = np.zeros((128, H * 128))
    for h in range(H):
        D_[:, h * 128:(h + 1) * 128] = np.where(diff >= 0, gam[h] ** np.maximum(diff, 0), 0.0)
        XI[:, h * 128:(h + 1) * 128] = (gam[h] ** (tt_ + 1.0))[None, :]
    return dict(
        l0a_cq=f2b((cs * scale).T), l0a_sq=f2b((sn * scale).T),
        l0a_ck=f2b((cs / scale).T), l0a_sk=f2b((sn / scale).T),
        l0a_D=f2b(D_), l0a_XI=f2b(XI),
        l0a_gdec=(gam[None, :] ** (127.0 - tt_[:, None])).astype(np.float32),
        l0a_gc=np.broadcast_to(gam[None, :] ** 128.0, (128, H)).astype(np.float32).copy())


class TrB:

    def __init__(self, ctx):
        self.ctx = ctx
        self.cht = ctx.chan("trb_t")
        self.che = ctx.chan("trb_e")
        self.ps = [ctx.psum("tpsA", [128, 1024], BF16), ctx.psum("tpsB", [128, 1024], BF16)]
        self.ident = ctx.sbuf("ident", [128, 128], BF16)

    def emit(self, pairs, twait=(), vwait=(), self_wait=True):
        assert len(pairs) <= 16
        prev = self.che.n
        t0 = self.cht.n
        ps = self.ps
        ident = self.ident
        cht, che = self.cht, self.che

        def tsec(t):
            t.wait_ge(che.h, prev)
            for ch, val in twait:
                t.wait_ge(ch.h, val)
            for k, (src, dst) in enumerate(pairs):
                slot = ps[k // 8][0:src.shape[-1], (k % 8) * 128:(k % 8) * 128 + src.shape[0]]
                cht.inc(t.transpose(slot, src, ident[:]))

        nb = min(len(pairs), 8)

        def vsec(v):
            for ch, val in vwait:
                v.wait_ge(ch.h, val)
            for k, (src, dst) in enumerate(pairs):
                v.wait_ge(cht.h, t0 + (nb if k < 8 else len(pairs)))
                slot = ps[k // 8][0:src.shape[-1], (k % 8) * 128:(k % 8) * 128 + src.shape[0]]
                che.inc(v.tensor_copy(dst, slot))
            if self_wait:
                v.wait_ge(che.h, che.n)
        self.ctx.block.tensor(tsec)
        self.ctx.block.vector(vsec)
        return self.che.n


def emit_l0_attn(ctx, waps):
    nc = ctx.nc
    y = ctx.tiles["xres"]
    identf = ctx.tiles["identf"]
    H, NST, STW = 8, 8, 256
    EPSI = 1e-6

    for k, v in _l0a_consts().items():
        ctx.const(k, v)
    CAP = ctx.const_aps

    spill = nc.dram_tensor("l0a_spill", [T, D], F32, kind="Internal").ap()
    trb = TrB(ctx)

    u = ctx.sbuf("l0a_u", [128, 16384], BF16)
    qT = u[:, 0:2048]
    kTt = u[:, 2048:4096]
    vtok = u[:, 4096:6144]
    ring = u[:, 4096:6144]
    gtok = u[:, 6144:8192]
    ktok = u[:, 8192:9216]
    Pt = u[:, 9216:10240]
    qxi = u[:, 10240:11264]
    zT = u[:, 11264:13312]
    aT = u[:, 13312:15360]
    xt1 = u[0:64, 15360:16384]
    xt2 = u[64:128, 15360:16384]
    x2T = u[:, 0:16384]
    scr = ctx.sbuf("scr_lnsq", [128, 1024], F32)
    heads = scr
    rowbuf = scr[0:1, 0:1024]
    scrb = ctx.sbuf("l0a_scrb", [128, 1024], BF16)
    sg = scrb
    scrb64 = scrb[0:64, 0:1024]
    St = ctx.sbuf("l0a_S", [128, H * 128], F32)
    Sbf = ctx.sbuf("l0a_Sbf", [128, H * 128], BF16)
    Dm = ctx.sbuf("l0a_Dm", [128, 1024], BF16)
    Xi = ctx.sbuf("l0a_Xi", [128, 1024], BF16)
    gdec = ctx.sbuf("l0a_gdec", [128, 8], F32)
    gc = ctx.sbuf("l0a_gc", [128, 8], F32)
    tabs = ctx.sbuf("l0a_tabs", [128, 4 * 256], BF16)
    st8 = ctx.sbuf("l0a_st8", [128, 48], F32)
    lnst = ctx.sbuf("l0a_lnst", [128, 80], F32)
    bc = ctx.sbuf("l0a_bc", [128, 4 * 1024], BF16)
    bc2 = bc[:, 0:2048]
    onesf = ctx.sbuf("l0a_onesf", [1, 128], F32)
    wq_s = ctx.sbuf("l0a_wq", [128, NDC * 1024], BF16)
    wk_s = ctx.sbuf("l0a_wk", [128, NDC * 1024], BF16)
    wv_s = ctx.sbuf("l0a_wv", [128, NDC * 1024], BF16)
    wg_s = ctx.sbuf("l0a_wg", [128, NDC * 1024], BF16)
    wo_s = ctx.sbuf("l0a_wo", [128, NDC * 1024], BF16)
    w1f = [wq_s, wk_s]
    w2f = [wv_s, wg_s]
    hfm = St[:, :].bitcast(BF16)

    psA = ctx.psum("psA", [128, 1024], F32)
    psB = ctx.psum("psB", [128, 1024], F32)
    psC = ctx.psum("psC", [128, 512], F32)
    psD = ctx.psum("psD", [128, 512], F32)

    ch_sp = ctx.chan("l0a_sp")
    ch_wd = ctx.chan("l0a_wd")
    ch_ld = ctx.chan("l0a_ld")
    ch_t = ctx.chan("l0a_t")
    ch_v = ctx.chan("l0a_v")
    ch_s = ctx.chan("l0a_s")

    SUMS, SS, MEAN, VAR, RSTD, M2 = 0, 16, 32, 48, 64, 0

    def emit_stats(eps, vwait=None):
        v0, s0 = ch_v.n, ch_s.n

        def ssec1(s_):
            if vwait is not None:
                s_.wait_ge(vwait[0].h, vwait[1])
            for i in range(NCHUNK):
                mrk = s_.activation(scrb[:], y[:, i * 1024:(i + 1) * 1024], AF.Square,
                                    accum_out=lnst[:, SS + i:SS + i + 1])
            ch_s.inc(mrk)

        def vsec1(v):
            if vwait is not None:
                v.wait_ge(vwait[0].h, vwait[1])
            for i in range(NCHUNK):
                mrk = v.reduce_sum(lnst[:, SUMS + i:SUMS + i + 1],
                                   y[:, i * 1024:(i + 1) * 1024], axis=AX.X)
            ch_v.inc(mrk)
            v.wait_ge(ch_v.h, v0 + 1)
            ch_v.inc(v.tensor_scalar(lnst[:, MEAN:MEAN + 16], lnst[:, SUMS:SUMS + 16],
                                     1.0 / D, None, ALU.mult))
            v.wait_ge(ch_v.h, v0 + 2)
            ch_v.inc(v.tensor_mul(lnst[:, M2:M2 + 16], lnst[:, MEAN:MEAN + 16],
                                  lnst[:, MEAN:MEAN + 16]))
            v.wait_ge(ch_s.h, s0 + 1)
            ch_v.inc(v.tensor_scalar(lnst[:, VAR:VAR + 16], lnst[:, SS:SS + 16],
                                     1.0 / D, eps, ALU.mult, ALU.add))
            v.wait_ge(ch_v.h, v0 + 4)
            ch_v.inc(v.tensor_sub(lnst[:, VAR:VAR + 16], lnst[:, VAR:VAR + 16],
                                  lnst[:, M2:M2 + 16]))

        def ssec2(s_):
            s_.wait_ge(ch_v.h, v0 + 5)
            ch_s.inc(s_.activation(lnst[:, VAR:VAR + 16], lnst[:, VAR:VAR + 16], AF.Sqrt))

        def vsec2(v):
            v.wait_ge(ch_s.h, s0 + 2)
            ch_v.inc(v.reciprocal(lnst[:, RSTD:RSTD + 16], lnst[:, VAR:VAR + 16]))
            v.wait_ge(ch_v.h, v0 + 6)
        ctx.block.scalar(ssec1)
        ctx.block.vector(vsec1)
        ctx.block.scalar(ssec2)
        ctx.block.vector(vsec2)
        return ch_v.n

    def spill_out(e):
        xr = spill.rearrange("(c p) d -> c p d", p=128)
        for i in range(NCHUNK):
            ch_sp.inc(e.dma_start(out=xr[i], in_=y[:, i * 1024:(i + 1) * 1024]), 16)
    ctx.block.sync(spill_out)
    emit_stats(1e-5)

    def outer_apply(v):
        v.wait_ge(ch_sp.h, 16 * NCHUNK)
        for i in range(NCHUNK):
            mrk = v.tensor_scalar(y[:, i * 1024:(i + 1) * 1024], y[:, i * 1024:(i + 1) * 1024],
                                  lnst[:, MEAN + i:MEAN + i + 1], lnst[:, RSTD + i:RSTD + i + 1],
                                  ALU.subtract, ALU.mult)
        ch_v.inc(mrk)
        v.wait_ge(ch_v.h, ch_v.n)
    ctx.block.vector(outer_apply)
    y_ready_v = ch_v.n

    last_stage1_mm = [0]
    last_ffn_mm = {}

    WQ, WK, WV = waps["L0_attn_Wq"], waps["L0_attn_Wk"], waps["L0_attn_Wv"]
    WG, WO = waps["L0_attn_Wg"], waps["L0_attn_Wo"]
    FW1, FW2 = waps["L0_attn_ffn_w1"], waps["L0_attn_ffn_w2"]

    for li in range(L0A_LAYERS):
        gate_t_val = ch_t.n

        def wdma(g, li=li, gate=gate_t_val):
            if gate:
                g.wait_ge(ch_t.h, gate)
            for wdst, wsrc3 in ((wq_s, WQ), (wk_s, WK), (wv_s, WV)):
                for h in range(8):
                    dst = wdst[:, h * 128: h * 128 + 7 * 1024 + 128]
                    dst = dst.rearrange("p (dc e) -> p dc e", dc=8) if False else \
                        wdst[:, :].rearrange("p (dc he) -> p dc he", dc=8)[:, :, h * 128:(h + 1) * 128]
                    ch_wd.inc(g.dma_start(
                        out=dst,
                        in_=wsrc3[li, h].rearrange("(dc p) e -> p dc e", p=128)), 16)
            for wdst, wsrc2 in ((wg_s, WG), (wo_s, WO)):
                ch_wd.inc(g.dma_start(
                    out=wdst[:, :].rearrange("p (dc m) -> p dc m", dc=8),
                    in_=wsrc2[li].rearrange("(dc p) m -> p dc m", p=128)), 16)
        ctx.block.gpsimd(wdma)
        wd_stage1 = ch_wd.n

        if li == 0:
            def cdma(e):
                ch_ld.inc(e.dma_start(out=Dm[:, :], in_=CAP["l0a_D"][:, :]), 16)
                ch_ld.inc(e.dma_start(out=Xi[:, :], in_=CAP["l0a_XI"][:, :]), 16)
                ch_ld.inc(e.dma_start(out=gdec[:, :], in_=CAP["l0a_gdec"][:, :]), 16)
                ch_ld.inc(e.dma_start(out=gc[:, :], in_=CAP["l0a_gc"][:, :]), 16)
            ctx.block.scalar(cdma)
            CONST_LD = ch_ld.n

        rows = [("L0_attn_ln1_scale", bc, 0), ("L0_attn_ln1_bias", bc, 1),
                ("L0_attn_gn_scale", bc, 2), ("L0_attn_gn_bias", bc, 3)]

        def bc_ones(v):
            ch_v.inc(v.memset(onesf[:], 1.0))
            v.wait_ge(ch_v.h, ch_v.n)
        if li == 0:
            ctx.block.vector(bc_ones)
        def emit_rows(rowlist, li=li):
            for rname, dstt, slot in rowlist:
                r0d = ch_ld.n

                def rdma(e, rname=rname, li=li, tg=ch_t.n, vg=ch_v.n):
                    if tg:
                        e.wait_ge(ch_t.h, tg)
                    if vg:
                        e.wait_ge(ch_v.h, vg)
                    ch_ld.inc(e.dma_start(out=rowbuf[0:1, :], in_=waps[rname][li:li + 1, :]), 16)
                t0m = ch_t.n

                def rmm(t, r0d=r0d, gate=ch_v.n):
                    t.wait_ge(ch_ld.h, r0d + 16)
                    t.wait_ge(ch_v.h, gate)
                    ch_t.inc(t.matmul(psC[:, 0:512], onesf[0:1, 0:128], rowbuf[0:1, 0:512],
                                      start=True, stop=True))
                    ch_t.inc(t.matmul(psD[:, 0:512], onesf[0:1, 0:128], rowbuf[0:1, 512:1024],
                                      start=True, stop=True))

                def rev(v, dstt=dstt, slot=slot, t0m=t0m):
                    v.wait_ge(ch_t.h, t0m + 2)
                    v.tensor_copy(dstt[:, slot * 1024:slot * 1024 + 512], psC[:, 0:512])
                    ch_v.inc(v.tensor_copy(dstt[:, slot * 1024 + 512:(slot + 1) * 1024], psD[:, 0:512]))
                    v.wait_ge(ch_v.h, ch_v.n)
                ctx.block.scalar(rdma)
                ctx.block.tensor(rmm)
                ctx.block.vector(rev)
        emit_rows(rows)
        bc_done_v = ch_v.n

        waitv = (ch_v, y_ready_v) if li == 0 else (ch_v, last_ffn_add[0])
        emit_stats(EPSI, vwait=waitv)
        ln1_stats_v = ch_v.n

        def zeroS(v):
            v.memset(St[:, :], 0.0)
            ch_v.inc(v.memset(Sbf[:, :], 0.0))
            v.wait_ge(ch_v.h, ch_v.n)
        ctx.block.vector(zeroS)

        last_wo_ev = [ch_v.n]
        tab_gate = {0: 0}
        for st in range(L0A_NST):
            def tdma(e, st=st, tg=tab_gate[0]):
                if tg:
                    e.wait_ge(ch_v.h, tg)
                for ti, cap in enumerate((CAP["l0a_cq"], CAP["l0a_sq"], CAP["l0a_ck"], CAP["l0a_sk"])):
                    e.dma_start(out=tabs[0:64, ti * 256:(ti + 1) * 256],
                                in_=cap[:, st * STW:(st + 1) * STW]).then_inc(ch_ld.h, 16)
                    ch_ld.n += 16
                    ch_ld.inc(e.dma_start(out=tabs[64:128, ti * 256:(ti + 1) * 256],
                                          in_=cap[:, st * STW:(st + 1) * STW]), 16)
            ctx.block.scalar(tdma)
            tabs_rdy = ch_ld.n
            CQ = (tabs[0:64, 0:256], tabs[64:128, 0:256])
            SQ = (tabs[0:64, 256:512], tabs[64:128, 256:512])
            CK = (tabs[0:64, 512:768], tabs[64:128, 512:768])
            SK = (tabs[0:64, 768:1024], tabs[64:128, 768:1024])

            def lnap(v, st=st, trg=trb.cht.n, ttg=ch_t.n):
                if trg:
                    v.wait_ge(trb.cht.h, trg)
                if ttg:
                    v.wait_ge(ch_t.h, ttg)
                for cl in range(2):
                    i = st * 2 + cl
                    yc = y[:, i * 1024:(i + 1) * 1024]
                    rg = ring[:, cl * 1024:(cl + 1) * 1024]
                    v.tensor_scalar(scr[:], yc, lnst[:, MEAN + i:MEAN + i + 1],
                                    lnst[:, RSTD + i:RSTD + i + 1], ALU.subtract, ALU.mult)
                    v.tensor_mul(scr[:], scr[:], bc[:, 0:1024])
                    ch_v.inc(v.tensor_add(rg, scr[:], bc[:, 1024:2048]))
                v.wait_ge(ch_v.h, ch_v.n)
            ctx.block.vector(lnap)
            ring_rdy = ch_v.n

            if L0A_ST_PARTS < 2:
                continue
            pairs = []
            for cl in range(2):
                for dc in range(NDC):
                    pairs.append((ring[:, cl * 1024 + dc * 128: cl * 1024 + (dc + 1) * 128],
                                  aT[:, dc * STW + cl * 128: dc * STW + (cl + 1) * 128]))
            aT_rdy = trb.emit(pairs, twait=[(ch_v, ring_rdy)], vwait=[(ch_t, ch_t.n)])

            for which, wsrc, rotdst, CT, STb in (() if L0A_ST_PARTS < 3 else (
                    ("q", wq_s, qT, CQ, SQ), ("k", wk_s, kTt, CK, SK))):
                rawdst = rotdst
                t0m = ch_t.n

                def qmm(t, wsrc=wsrc, aT_rdy=aT_rdy, gate=last_wo_ev[0], wd=wd_stage1):
                    t.wait_ge(trb.che.h, aT_rdy)
                    t.wait_ge(ch_wd.h, wd)
                    t.wait_ge(ch_v.h, gate)
                    for h in range(H):
                        pst = [psA, psB][h // 4]
                        col = (h % 4) * 256
                        for dc in range(NDC):
                            mm = t.matmul(pst[:, col:col + 256],
                                          wsrc[:, dc * 1024 + h * 128:(dc * 1024 + (h + 1) * 128)],
                                          aT[:, dc * STW: dc * STW + 256],
                                          start=(dc == 0), stop=(dc == NDC - 1))
                        ch_t.inc(mm)

                def qev(v, rawdst=rawdst, t0m=t0m, tabs_rdy=tabs_rdy):
                    for h in range(H):
                        v.wait_ge(ch_t.h, t0m + 2 * (h // 2 + 1))
                        pst = [psA, psB][h // 4]
                        col = (h % 4) * 256
                        ch_v.inc(v.tensor_copy(rawdst[:, h * STW:(h + 1) * STW], pst[:, col:col + 256]))
                    v.wait_ge(ch_v.h, ch_v.n)

                def qrot(v, rotdst=rotdst, CT=CT, STb=STb, tabs_rdy=tabs_rdy):
                    v.wait_ge(ch_ld.h, tabs_rdy)
                    for hh in range(2):
                        cols = slice(hh * 1024, (hh + 1) * 1024)
                        r4 = lambda ap: ap.rearrange("p (a b) -> p a b", a=4)
                        bt = lambda ap: ap.rearrange("p (a t) -> p a t", a=1).broadcast_to((64, 4, 256))
                        rlo = r4(rotdst[0:64, cols])
                        rhi = r4(rotdst[64:128, cols])
                        c0, c64 = bt(CT[0]), bt(CT[1])
                        s0, s64 = bt(STb[0]), bt(STb[1])
                        t1 = r4(xt1[0:64, 0:1024])
                        t2 = r4(xt2[0:64, 0:1024])
                        u0 = r4(scrb[0:64, 0:1024])
                        u64 = r4(scrb[64:128, 0:1024])
                        v.tensor_tensor(t1, rlo, c0, ALU.mult)
                        v.tensor_tensor(t2, rlo, s0, ALU.mult)
                        v.tensor_tensor(u0, rhi, s64, ALU.mult)
                        v.tensor_sub(rlo, t1, u0)
                        v.tensor_tensor(u64, rhi, c64, ALU.mult)
                        mrk = v.tensor_add(rhi, u64, t2)
                    ch_v.inc(mrk)
                    v.wait_ge(ch_v.h, ch_v.n)
                ctx.block.tensor(qmm)
                ctx.block.vector(qev)
                ctx.block.vector(qrot)
                last_wo_ev[0] = ch_v.n
            tab_gate[0] = ch_v.n
            qT_rdy = None
            qk_done_v = ch_v.n

            if L0A_ST_PARTS < 4:
                continue
            for cl in range(2):
                t0m = ch_t.n

                def vgmm(t, cl=cl, gate=ch_v.n, wd=wd_stage1):
                    t.wait_ge(ch_wd.h, wd)
                    t.wait_ge(ch_v.h, gate)
                    for dc in range(NDC):
                        lhs = aT[:, dc * STW + cl * 128: dc * STW + (cl + 1) * 128]
                        t.matmul(psA[:, 0:512], lhs, wv_s[:, dc * 1024: dc * 1024 + 512],
                                 start=(dc == 0), stop=(dc == NDC - 1))
                        t.matmul(psA[:, 512:1024], lhs, wv_s[:, dc * 1024 + 512:(dc + 1) * 1024],
                                 start=(dc == 0), stop=(dc == NDC - 1))
                        t.matmul(psB[:, 0:512], lhs, wg_s[:, dc * 1024: dc * 1024 + 512],
                                 start=(dc == 0), stop=(dc == NDC - 1))
                        mm = t.matmul(psB[:, 512:1024], lhs, wg_s[:, dc * 1024 + 512:(dc + 1) * 1024],
                                      start=(dc == 0), stop=(dc == NDC - 1))
                    ch_t.inc(mm)

                def vgev(v, cl=cl, t0m=t0m):
                    v.wait_ge(ch_t.h, t0m + 1)
                    v.tensor_copy(vtok[:, cl * 1024:(cl + 1) * 1024], psA[:, 0:1024])
                    ch_v.inc(v.tensor_copy(gtok[:, cl * 1024:(cl + 1) * 1024], psB[:, 0:1024]))
                    v.wait_ge(ch_v.h, ch_v.n)
                ctx.block.tensor(vgmm)
                ctx.block.vector(vgev)
            vg_done_v = ch_v.n

            for cl in range(2 if L0A_ST_PARTS >= 5 else 0):
                i_chunk = st * 2 + cl
                kpairs = [(kTt[:, h * STW + cl * 128: h * STW + cl * 128 + 128],
                           ktok[:, h * 128:(h + 1) * 128]) for h in range(H)]
                ktok_rdy = trb.emit(kpairs, twait=[(ch_v, vg_done_v)], vwait=[(ch_t, ch_t.n)])

                def kdv(v, rdy=ktok_rdy, cld=CONST_LD):
                    v.wait_ge(ch_ld.h, cld)
                    gb = gdec[:, :].rearrange("p h -> p h 1" if False else "p (h o) -> p h o", o=1).broadcast_to((128, 8, 128))
                    ch_v.inc(v.tensor_tensor(ktok[:].rearrange("p (h e) -> p h e", h=8),
                                             ktok[:].rearrange("p (h e) -> p h e", h=8), gb, ALU.mult))
                    v.wait_ge(ch_v.h, ch_v.n)
                ctx.block.vector(kdv)
                kd_rdy = ch_v.n

                t0m = ch_t.n

                def scmm(t, cl=cl, gate=kd_rdy):
                    t.wait_ge(ch_v.h, gate)
                    for h in range(H):
                        pst = [psC, psD][h // 4]
                        col = (h % 4) * 128
                        mm = t.matmul(pst[:, col:col + 128],
                                      kTt[:, h * STW + cl * 128: h * STW + (cl + 1) * 128],
                                      qT[:, h * STW + cl * 128: h * STW + (cl + 1) * 128],
                                      start=True, stop=True)
                        if h % 4 == 3:
                            ch_t.inc(mm)
                ctx.block.tensor(scmm)
                sc_t = ch_t.n

                def pev(v, cl=cl, t0m=t0m, cld=CONST_LD):
                    v.wait_ge(ch_ld.h, cld)
                    v.wait_ge(ch_t.h, t0m + 1)
                    v.tensor_tensor(Pt[:, 0:512], psC[:, 0:512], Dm[:, 0:512], ALU.mult)
                    v.wait_ge(ch_t.h, t0m + 2)
                    ch_v.inc(v.tensor_tensor(Pt[:, 512:1024], psD[:, 0:512], Dm[:, 512:1024], ALU.mult))
                    qsl = qT[:].rearrange("p (h t) -> p h t", h=8)[:, :, cl * 128:(cl + 1) * 128]
                    ch_v.inc(v.tensor_tensor(qxi[:].rearrange("p (h e) -> p h e", h=8), qsl,
                                             Xi[:].rearrange("p (h e) -> p h e", h=8), ALU.mult))
                    v.wait_ge(ch_v.h, ch_v.n)
                ctx.block.vector(pev)
                p_rdy = ch_v.n

                if L0A_ST_PARTS < 6:
                    continue
                t0m = ch_t.n

                def outmm(t, cl=cl, gate=p_rdy, sgate=None):
                    t.wait_ge(ch_v.h, gate)
                    for h in range(H):
                        t.matmul(psA[:, h * 128:(h + 1) * 128], Pt[:, h * 128:(h + 1) * 128],
                                 vtok[:, cl * 1024 + h * 128: cl * 1024 + (h + 1) * 128],
                                 start=True, stop=False)
                        mm = t.matmul(psA[:, h * 128:(h + 1) * 128], qxi[:, h * 128:(h + 1) * 128],
                                      Sbf[:, h * 128:(h + 1) * 128], start=False, stop=True)
                    ch_t.inc(mm)
                    for h in range(H):
                        mm = t.matmul(psB[:, h * 128:(h + 1) * 128], ktok[:, h * 128:(h + 1) * 128],
                                      vtok[:, cl * 1024 + h * 128: cl * 1024 + (h + 1) * 128],
                                      start=True, stop=True)
                    ch_t.inc(mm)
                ctx.block.tensor(outmm)
                out_t, m_t = ch_t.n - 1, ch_t.n

                def hev(v, out_t=out_t, m_t=m_t):
                    v.wait_ge(ch_t.h, out_t)
                    ch_v.inc(v.tensor_copy(heads[:], psA[:, 0:1024]))
                    gcb = gc[:, :].rearrange("p (h o) -> p h o", o=1).broadcast_to((128, 8, 128))
                    Sr = St[:].rearrange("p (h e) -> p h e", h=8)
                    v.tensor_tensor(Sr, Sr, gcb, ALU.mult)
                    v.wait_ge(ch_t.h, m_t)
                    ch_v.inc(v.tensor_add(St[:], St[:], psB[:, 0:1024]))
                    v.wait_ge(ch_v.h, ch_v.n)
                    ch_v.inc(v.tensor_copy(Sbf[:], St[:]))
                    v.wait_ge(ch_v.h, ch_v.n)
                ctx.block.vector(hev)
                heads_v = ch_v.n
                s_upd_v = ch_v.n

                if L0A_ST_PARTS < 7:
                    continue
                s0m = ch_s.n

                def gns(s_, cl=cl, heads_v=heads_v, vg=vg_done_v):
                    s_.wait_ge(ch_v.h, heads_v)
                    s_.activation(psC[:, 0:512], heads[:, 0:512], AF.Square)
                    ch_s.inc(s_.activation(psD[:, 0:512], heads[:, 512:1024], AF.Square))
                    s_.wait_ge(ch_v.h, vg)
                    ch_s.inc(s_.activation(sg[:], gtok[:, cl * 1024:(cl + 1) * 1024], AF.Silu))
                ctx.block.scalar(gns)

                def gnv(v, s0m=s0m, cl=cl):
                    hr = heads[:].rearrange("p (h e) -> p h e", h=8)
                    ch_v.inc(v.reduce_sum(st8[:, 0:8].rearrange("p (h o) -> p h o", o=1), hr, axis=AX.X))
                    v.wait_ge(ch_s.h, s0m + 1)
                    v.reduce_sum(st8[:, 8:12].rearrange("p (h o) -> p h o", o=1),
                                 psC[:, 0:512].rearrange("p (h e) -> p h e", h=4), axis=AX.X)
                    ch_v.inc(v.reduce_sum(st8[:, 12:16].rearrange("p (h o) -> p h o", o=1),
                                          psD[:, 0:512].rearrange("p (h e) -> p h e", h=4), axis=AX.X))
                    v.wait_ge(ch_v.h, ch_v.n)
                    ch_v.inc(v.tensor_scalar(st8[:, 16:24], st8[:, 0:8], 1.0 / 128, None, ALU.mult))
                    v.wait_ge(ch_v.h, ch_v.n)
                    ch_v.inc(v.tensor_mul(st8[:, 24:32], st8[:, 16:24], st8[:, 16:24]))
                    ch_v.inc(v.tensor_scalar(st8[:, 32:40], st8[:, 8:16], 1.0 / 128, EPSI,
                                             ALU.mult, ALU.add))
                    v.wait_ge(ch_v.h, ch_v.n)
                    ch_v.inc(v.tensor_sub(st8[:, 32:40], st8[:, 32:40], st8[:, 24:32]))
                    v.wait_ge(ch_v.h, ch_v.n)
                ctx.block.vector(gnv)
                gn_v = ch_v.n
                s0m2 = ch_s.n

                def gns2(s_, gn_v=gn_v):
                    s_.wait_ge(ch_v.h, gn_v)
                    ch_s.inc(s_.activation(st8[:, 40:48], st8[:, 32:40], AF.Sqrt))
                ctx.block.scalar(gns2)

                def gnv2(v, s0m2=s0m2, cl=cl):
                    v.wait_ge(ch_s.h, s0m2 + 1)
                    ch_v.inc(v.reciprocal(st8[:, 32:40], st8[:, 40:48]))
                    v.wait_ge(ch_v.h, ch_v.n)
                    hr = heads[:].rearrange("p (h e) -> p h e", h=8)
                    mb = st8[:, 16:24].rearrange("p (h o) -> p h o", o=1).broadcast_to((128, 8, 128))
                    rb = st8[:, 32:40].rearrange("p (h o) -> p h o", o=1).broadcast_to((128, 8, 128))
                    v.tensor_tensor(hr, hr, mb, ALU.subtract)
                    v.tensor_tensor(hr, hr, rb, ALU.mult)
                    v.tensor_mul(heads[:], heads[:], bc[:, 2048:3072])
                    ch_v.inc(v.tensor_add(heads[:], heads[:], bc[:, 3072:4096]))
                    v.wait_ge(ch_s.h, s0m2 + 0 + 1)
                    ch_v.inc(v.tensor_mul(sg[:], sg[:], heads[:]))
                    v.wait_ge(ch_v.h, ch_v.n)
                ctx.block.vector(gnv2)
                z_rdy = ch_v.n

                if L0A_ST_PARTS < 8:
                    continue
                zpairs = [(sg[:, dc * 128:(dc + 1) * 128],
                           zT[:, dc * STW + cl * 128: dc * STW + (cl + 1) * 128])
                          for dc in range(NDC)]
                zT_rdy = trb.emit(zpairs, twait=[(ch_v, z_rdy)], vwait=[(ch_t, ch_t.n)])

                t0m = ch_t.n

                def womm(t, cl=cl, zT_rdy=zT_rdy, hv=heads_v):
                    t.wait_ge(trb.che.h, zT_rdy)
                    t.wait_ge(ch_v.h, hv)
                    for dc in range(NDC):
                        lhs = zT[:, dc * STW + cl * 128: dc * STW + (cl + 1) * 128]
                        t.matmul(psA[:, 0:512], lhs, wo_s[:, dc * 1024: dc * 1024 + 512],
                                 start=(dc == 0), stop=(dc == NDC - 1))
                        mm = t.matmul(psA[:, 512:1024], lhs, wo_s[:, dc * 1024 + 512:(dc + 1) * 1024],
                                      start=(dc == 0), stop=(dc == NDC - 1))
                    ch_t.inc(mm)
                ctx.block.tensor(womm)
                wo_t = ch_t.n

                def woev(v, i_chunk=i_chunk, wo_t=wo_t):
                    v.wait_ge(ch_t.h, wo_t)
                    yc = y[:, i_chunk * 1024:(i_chunk + 1) * 1024]
                    ch_v.inc(v.scalar_tensor_tensor(yc, psA[:, 0:1024], 1.0, yc, ALU.mult, ALU.add))
                    v.wait_ge(ch_v.h, ch_v.n)
                ctx.block.vector(woev)
                last_wo_ev[0] = ch_v.n
            last_stage1_mm[0] = ch_t.n

        if not L0A_FFN:
            last_ffn_add = [ch_v.n]
            continue
        emit_rows([("L0_attn_ln2_scale", bc, 0), ("L0_attn_ln2_bias", bc, 1)])
        emit_stats(EPSI, vwait=(ch_v, last_wo_ev[0]))

        for i in range(2 * L0A_NST):
            def ln2ap(v, i=i, trg=trb.cht.n, ttg=ch_t.n):
                if trg:
                    v.wait_ge(trb.cht.h, trg)
                if ttg:
                    v.wait_ge(ch_t.h, ttg)
                yc = y[:, i * 1024:(i + 1) * 1024]
                v.tensor_scalar(scr[:], yc, lnst[:, MEAN + i:MEAN + i + 1],
                                lnst[:, RSTD + i:RSTD + i + 1], ALU.subtract, ALU.mult)
                v.tensor_mul(scr[:], scr[:], bc2[:, 0:1024])
                ch_v.inc(v.tensor_add(scrb[:, 0:1024], scr[:], bc2[:, 1024:2048]))
                v.wait_ge(ch_v.h, ch_v.n)
            ctx.block.vector(ln2ap)
            rr = ch_v.n
            pairs = [(scrb[:, dc * 128:(dc + 1) * 128],
                      x2T[:, dc * T + i * 128: dc * T + (i + 1) * 128]) for dc in range(NDC)]
            trb.emit(pairs, twait=[(ch_v, rr)], vwait=[(ch_t, ch_t.n)])
        x2T_rdy = trb.che.n

        for fb in range(4):
            bidx = fb % 2
            gate_t_ffn = ch_t.n
            d0 = ch_wd.n

            def fdma(g, li=li, fb=fb, bidx=bidx, gate=gate_t_ffn):
                if gate:
                    g.wait_ge(ch_t.h, gate)
                ch_wd.inc(g.dma_start(out=w1f[bidx][:, :].rearrange("p (dc f) -> p dc f", dc=8),
                                      in_=FW1[li][:, fb * 1024:(fb + 1) * 1024]
                                      .rearrange("(dc p) f -> p dc f", p=128)), 16)
                ch_wd.inc(g.dma_start(out=w2f[bidx][:, :].rearrange("p (fc m) -> p fc m", fc=8),
                                      in_=FW2[li][fb * 1024:(fb + 1) * 1024, :]
                                      .rearrange("(fc p) m -> p fc m", p=128)), 16)
            ctx.block.gpsimd(fdma)
            fdma_n = ch_wd.n
            for st in range(L0A_NST):
                prev_gelu = ch_s.n

                def w1mm(t, st=st, bidx=bidx, fdma_n=fdma_n, x2T_rdy=x2T_rdy,
                         prev_gelu=prev_gelu):
                    t.wait_ge(ch_wd.h, fdma_n)
                    t.wait_ge(trb.che.h, x2T_rdy)
                    t.wait_ge(ch_s.h, prev_gelu)
                    for fcg in range(2):
                        if fcg == 1:
                            t.wait_ge(ch_s.h, prev_gelu + 4)
                        for fc4 in range(4):
                            fc = fcg * 4 + fc4
                            pst = [psC, psD][fc4 // 2]
                            col = (fc4 % 2) * 256
                            for dc in range(NDC):
                                mm = t.matmul(pst[:, col:col + 256],
                                              w1f[bidx][:, dc * 1024 + fc * 128: dc * 1024 + (fc + 1) * 128],
                                              x2T[:, dc * T + st * STW: dc * T + (st + 1) * STW],
                                              start=(dc == 0), stop=(dc == NDC - 1))
                            ch_t.inc(mm)
                t0m = ch_t.n

                def w1ge(s_, t0m=t0m, st=st):
                    for fcg in range(2):
                        for fc4 in range(4):
                            fc = fcg * 4 + fc4
                            s_.wait_ge(ch_t.h, t0m + fcg * 4 + 2 * (fc4 // 2 + 1))
                            pst = [psC, psD][fc4 // 2]
                            col = (fc4 % 2) * 256
                            ch_s.inc(s_.activation(hfm[:, fc * STW:(fc + 1) * STW],
                                                   pst[:, col:col + 256], AF.Gelu_apprx_tanh))
                ctx.block.tensor(w1mm)
                ctx.block.scalar(w1ge)
                gelu_done = ch_s.n

                t0m = ch_t.n

                def w2mm(t, bidx=bidx, gelu_done=gelu_done, gate=last_wo_ev[0]):
                    t.wait_ge(ch_s.h, gelu_done)
                    t.wait_ge(ch_v.h, gate)
                    for tt in range(2):
                        pso = [psA, psB][tt]
                        for fc in range(NDC):
                            lhs = hfm[:, fc * STW + tt * 128: fc * STW + tt * 128 + 128]
                            t.matmul(pso[:, 0:512], lhs, w2f[bidx][:, fc * 1024: fc * 1024 + 512],
                                     start=(fc == 0), stop=(fc == NDC - 1))
                            mm = t.matmul(pso[:, 512:1024], lhs,
                                          w2f[bidx][:, fc * 1024 + 512:(fc + 1) * 1024],
                                          start=(fc == 0), stop=(fc == NDC - 1))
                        ch_t.inc(mm)
                ctx.block.tensor(w2mm)
                w2_t = ch_t.n

                def w2ev(v, st=st, w2_t=w2_t):
                    for tt in range(2):
                        v.wait_ge(ch_t.h, w2_t - 1 + tt + 0 if False else w2_t - 1 + tt)
                        i = st * 2 + tt
                        yc = y[:, i * 1024:(i + 1) * 1024]
                        pso = [psA, psB][tt]
                        ch_v.inc(v.scalar_tensor_tensor(yc, pso[:, 0:1024], 1.0, yc,
                                                        ALU.mult, ALU.add))
                    v.wait_ge(ch_v.h, ch_v.n)
                ctx.block.vector(w2ev)
                last_wo_ev[0] = ch_v.n
            last_ffn_mm[bidx] = ch_t.n
        last_ffn_add = [ch_v.n]

    if L0A_LAYERS == 0:
        last_ffn_add = [ch_v.n]

    def rest_dma(e, gate=last_ffn_add[0]):
        e.wait_ge(ch_v.h, gate)
        xr = spill.rearrange("(c p) d -> c p d", p=128)
        for i in range(NCHUNK):
            if i >= 1:
                e.wait_ge(ch_v.h, REST_BASE[0] + i)
            ch_sp.inc(e.dma_start(out=scr[:, 0:1024], in_=xr[i]), 16)
    REST_BASE = [ch_v.n]
    sp0 = ch_sp.n
    ctx.block.sync(rest_dma)

    def rest_add(v, sp0=sp0):
        for i in range(NCHUNK):
            v.wait_ge(ch_sp.h, sp0 + 16 * (i + 1))
            yc = y[:, i * 1024:(i + 1) * 1024]
            ch_v.inc(v.tensor_add(yc, yc, scr[:, 0:1024]))
        v.wait_ge(ch_v.h, ch_v.n)
    ctx.block.vector(rest_add)
    return ch_v, ch_v.n




def _l1a_consts():
    tt_ = np.arange(128)
    diff = tt_[None, :] - tt_[:, None]
    bf = ml_dtypes.bfloat16
    strict01 = (diff > 0).astype(np.float32)
    masks = np.zeros((4, 128, 512), dtype=np.float32)
    for r in range(4):
        sg = tt_[:, None] + r * 128
        tg = np.arange(512)[None, :]
        masks[r] = (tg >= sg).astype(np.float32)
    PI = [0, 0, 0, 1, 1, 2]
    PJ = [1, 2, 3, 2, 3, 3]
    return dict(l1a_delta=diff.astype(np.float32),
                l1a_strict=strict01.astype(bf),
                l1a_cmask=masks.reshape(4 * 128, 512).astype(bf),
                l1a_ip1=(tt_ + 1.0).astype(np.float32).reshape(128, 1),
                l1a_i127=(127.0 - tt_).astype(np.float32).reshape(128, 1)), PI, PJ


def emit_l1_attn(ctx, waps):
    nc = ctx.nc
    xres = ctx.tiles["xres"]
    identf = ctx.tiles["identf"]
    ident = ctx.sbuf("ident", [128, 128], BF16)
    NH, DH, PD = 16, 64, 4
    SCALE = DH ** -0.5

    consts, PI, PJ = _l1a_consts()
    for k, v in consts.items():
        ctx.const(k, v)
    CAP = ctx.const_aps

    spill = nc.dram_tensor("l1a_spill", [T, D], F32, kind="Internal").ap()
    x3dram = nc.dram_tensor("l1a_x3", [T, D], F32, kind="Internal").ap()
    wpdram = nc.dram_tensor("l1a_wp", [T, 64], BF16, kind="Internal").ap()

    ub = xres[:, :].bitcast(BF16)
    x3nT = ub[:, 0:16384]
    outT = ub[:, 16384:32768]
    shared = ctx.sbuf("l1a_shared", [128, 16640], BF16)
    k_acc = shared[:, 0:16384]
    v_ext = shared
    kvT = ctx.sbuf("l1a_kvT", [128, 16384], BF16)
    qTh = None
    khT = None
    Ptile = ctx.sbuf("l1a_P", [128, 2048], BF16)
    scr = ctx.sbuf("scr_lnsq", [128, 1024], F32)
    scrb = ctx.sbuf("l0a_scrb", [128, 1024], BF16) if "l0a_scrb" in ctx.tiles else ctx.sbuf("l1a_scrb", [128, 1024], BF16)
    ob65 = ctx.sbuf("l1a_ob", [128, 512], F32)
    lnst = ctx.sbuf("l1a_lnst", [128, 80], F32)
    SUMS, SS, MEAN, VAR, RSTD, M2 = 0, 16, 32, 48, 64, 0
    wp = ctx.sbuf("l1a_wpj", [128, 16 * 256], BF16)
    rl = ctx.sbuf("l1a_rl", [128, 16 * 96], BF16)
    jw = ctx.sbuf("l1a_jw", [128, 16 * 96], BF16)
    r36 = ctx.sbuf("l1a_r36", [128, 16 * 36], BF16)
    w36 = ctx.sbuf("l1a_w36", [128, 16 * 36], BF16)
    r36T = ctx.sbuf("l1a_r36T", [64, 2048], BF16)
    w36T = ctx.sbuf("l1a_w36T", [64, 2048], BF16)
    Dg = ctx.sbuf("l1a_Dg", [128, 16 * 128], BF16)
    delta = ctx.sbuf("l1a_delta", [128, 128], F32)
    strict = ctx.sbuf("l1a_strict", [128, 128], BF16)
    cmask = ctx.sbuf("l1a_cmask", [128, 4 * 512], BF16)
    ip1 = ctx.sbuf("l1a_ip1", [128, 1], F32)
    i127 = ctx.sbuf("l1a_i127", [128, 1], F32)
    grow = ctx.sbuf("l1a_grow", [1, 64], F32)
    gsc = ctx.sbuf("l1a_gsc", [128, 256], F32)
    Sg = ctx.sbuf("l1a_Sg", [64, 16], F32)
    SgB = ctx.sbuf("l1a_SgB", [64, 16], BF16)
    x3st = None
    dvecB = ctx.sbuf("l1a_dvecB", [128, 16], BF16)
    ones2 = ctx.sbuf("l1a_ones2", [128, 64], F32)
    x3st = None
    qwh = ctx.sbuf("l1a_qwh", [128, 512], BF16)
    negcrep = ctx.sbuf("l1a_negc", [128, 16], F32)
    ggpre = ctx.sbuf("l1a_ggpre", [128, 16 * 16], F32)
    gg = ctx.sbuf("l1a_gg", [128, 16 * 16], BF16)
    wsm = ctx.sbuf("l1a_wsm", [128, 8 * 256], BF16)
    qTh = wp[0:64, 0:2048]
    khT = wp[0:64, 2048:4096]
    Wst = [ctx.sbuf("l1a_WA", [128, 8 * 512], BF16), ctx.sbuf("l1a_WB", [128, 8 * 512], BF16)]
    qb = ctx.sbuf("l1a_qb", [128, 8], F32)
    kvb = ctx.sbuf("l1a_kvb", [128, 2048], BF16)
    obbc = ctx.sbuf("l1a_obbc", [128, 1024], BF16)
    onest = ctx.sbuf("l1a_onest", [128, 2], BF16)
    onesf = ctx.sbuf("l1a_onesf", [1, 128], F32)
    rowf = scr[0:1, 0:1024]

    psA = ctx.psum("psA", [128, 1024], F32)
    psB = ctx.psum("psB", [128, 1024], F32)
    psC = ctx.psum("psC", [128, 512], F32)
    psD = ctx.psum("psD", [128, 512], F32)
    trb = ctx._trb if hasattr(ctx, "_trb") else TrB(ctx)
    ctx._trb = trb

    ch_sp = ctx.chan("l1a_sp")
    ch_wd = ctx.chan("l1a_wd")
    ch_ld = ctx.chan("l1a_ld")
    ch_t = ctx.chan("l1a_t")
    ch_v = ctx.chan("l1a_v")
    ch_s = ctx.chan("l1a_s")

    def spill_out(e, vg=ctx.chans.get("xin") and 0 or 0):
        xr = spill.rearrange("(c p) d -> c p d", p=128)
        for i in range(NCHUNK):
            ch_sp.inc(e.dma_start(out=xr[i], in_=xres[:, i * 1024:(i + 1) * 1024]), 16)
    ctx.block.sync(spill_out)

    v0, s0 = ch_v.n, ch_s.n

    def ssec1(s_):
        for i in range(NCHUNK):
            mrk = s_.activation(scrb[:], xres[:, i * 1024:(i + 1) * 1024], AF.Square,
                                accum_out=lnst[:, SS + i:SS + i + 1])
        ch_s.inc(mrk)

    def vsec1(v):
        for i in range(NCHUNK):
            mrk = v.reduce_sum(lnst[:, SUMS + i:SUMS + i + 1],
                               xres[:, i * 1024:(i + 1) * 1024], axis=AX.X)
        ch_v.inc(mrk)
        v.wait_ge(ch_v.h, v0 + 1)
        ch_v.inc(v.tensor_scalar(lnst[:, MEAN:MEAN + 16], lnst[:, SUMS:SUMS + 16],
                                 1.0 / D, None, ALU.mult))
        v.wait_ge(ch_v.h, v0 + 2)
        ch_v.inc(v.tensor_mul(lnst[:, M2:M2 + 16], lnst[:, MEAN:MEAN + 16],
                              lnst[:, MEAN:MEAN + 16]))
        v.wait_ge(ch_s.h, s0 + 1)
        ch_v.inc(v.tensor_scalar(lnst[:, VAR:VAR + 16], lnst[:, SS:SS + 16],
                                 1.0 / D, 1e-5, ALU.mult, ALU.add))
        v.wait_ge(ch_v.h, v0 + 4)
        ch_v.inc(v.tensor_sub(lnst[:, VAR:VAR + 16], lnst[:, VAR:VAR + 16],
                              lnst[:, M2:M2 + 16]))

    def ssec2(s_):
        s_.wait_ge(ch_v.h, v0 + 5)
        ch_s.inc(s_.activation(lnst[:, VAR:VAR + 16], lnst[:, VAR:VAR + 16], AF.Sqrt))

    def vsec2(v):
        v.wait_ge(ch_s.h, s0 + 2)
        ch_v.inc(v.reciprocal(lnst[:, RSTD:RSTD + 16], lnst[:, VAR:VAR + 16]))
        v.wait_ge(ch_v.h, ch_v.n)
    ctx.block.scalar(ssec1)
    ctx.block.vector(vsec1)
    ctx.block.scalar(ssec2)
    ctx.block.vector(vsec2)

    for i in range(NCHUNK):
        def lnap(v, i=i, trg=trb.cht.n):
            if trg:
                v.wait_ge(trb.cht.h, trg)
            v.wait_ge(ch_sp.h, 16 * (i + 1))
            yc = xres[:, i * 1024:(i + 1) * 1024]
            ch_v.inc(v.tensor_scalar(scrb[:, 0:1024], yc, lnst[:, MEAN + i:MEAN + i + 1],
                                     lnst[:, RSTD + i:RSTD + i + 1], ALU.subtract, ALU.mult))
            v.wait_ge(ch_v.h, ch_v.n)
        ctx.block.vector(lnap)
        rr = ch_v.n
        pairs = [(scrb[:, dc * 128:(dc + 1) * 128],
                  x3nT[:, i * 1024 + dc * 128: i * 1024 + (dc + 1) * 128]) for dc in range(NDC)]
        trb.emit(pairs, twait=[(ch_v, rr)], vwait=[(ch_t, ch_t.n)])
    x3nT_rdy = trb.che.n

    def cdma(e):
        ch_ld.inc(e.dma_start(out=delta[:, :], in_=CAP["l1a_delta"][:, :]), 16)
        ch_ld.inc(e.dma_start(out=strict[:, :], in_=CAP["l1a_strict"][:, :]), 16)
        for r_ in range(4):
            ch_ld.inc(e.dma_start(out=cmask[:, r_ * 512:(r_ + 1) * 512],
                                  in_=CAP["l1a_cmask"][r_ * 128:(r_ + 1) * 128, :]), 16)
        ch_ld.inc(e.dma_start(out=ip1[:, :], in_=CAP["l1a_ip1"][:, :]), 16)
        ch_ld.inc(e.dma_start(out=i127[:, :], in_=CAP["l1a_i127"][:, :]), 16)
        ch_ld.inc(e.dma_start(out=grow[0:1, 0:16],
                              in_=waps["L1_attn_decay_logits"].rearrange("(a c) -> a c", a=1)), 16)
        ch_ld.inc(e.dma_start(out=grow[0:1, 16:32],
                              in_=waps["L1_attn_mem_scale"].rearrange("(a c) -> a c", a=1)), 16)
        ch_ld.inc(e.dma_start(out=grow[0:1, 32:33],
                              in_=waps["L1_attn_kv_gate"].rearrange("(a c) -> a c", a=1)), 16)
        ch_ld.inc(e.dma_start(out=qb[:, :], in_=waps["L1_attn_qkv_b"][0:1024].rearrange(
            "(hp r) -> r hp", r=128), allow_slow_non_contiguous=True), 16)
    ctx.block.scalar(cdma)

    def cdma_g(gq):
        for wi, wn in enumerate(("L1_attn_w1_write", "L1_attn_w2_write",
                                 "L1_attn_w1_read", "L1_attn_w2_read")):
            ch_ld.inc(gq.dma_start(
                out=wsm[:, :].rearrange("p (dc c) -> p dc c", dc=8)[:, :, wi * 64:(wi + 1) * 64],
                in_=waps[wn].rearrange("(dc p) c -> p dc c", p=128)), 16)
    ctx.block.gpsimd(cdma_g)
    CLD = ch_ld.n

    def mk_row_bcast(rname, roff, dst):
        r0 = ch_ld.n

        def rdma(e, rname=rname, roff=roff, tg=ch_t.n):
            if tg:
                e.wait_ge(ch_t.h, tg)
            ch_ld.inc(e.dma_start(out=rowf[0:1, :], in_=waps[rname][roff:roff + 1024]
                                  .rearrange("(a c) -> a c", a=1)), 16)
        t0 = ch_t.n

        def rmm(t, r0=r0, gate=ch_v.n):
            t.wait_ge(ch_ld.h, r0 + 16)
            t.wait_ge(ch_v.h, gate)
            ch_t.inc(t.matmul(psC[:, 0:512], onesf[0:1, 0:128], rowf[0:1, 0:512],
                              start=True, stop=True))
            ch_t.inc(t.matmul(psD[:, 0:512], onesf[0:1, 0:128], rowf[0:1, 512:1024],
                              start=True, stop=True))

        def rev(v, dst=dst, t0=t0):
            v.wait_ge(ch_t.h, t0 + 2)
            v.tensor_copy(dst[:, 0:512], psC[:, 0:512])
            ch_v.inc(v.tensor_copy(dst[:, 512:1024], psD[:, 0:512]))
            v.wait_ge(ch_v.h, ch_v.n)
        ctx.block.scalar(rdma)
        ctx.block.tensor(rmm)
        ctx.block.vector(rev)

    def ones_v(v):
        ch_v.inc(v.memset(onesf[:], 1.0))
        ch_v.inc(v.memset(onest[:, :], 1.0))
        v.wait_ge(ch_v.h, ch_v.n)
    ctx.block.vector(ones_v)
    mk_row_bcast("L1_attn_qkv_b", 1024, kvb[:, 0:1024])
    mk_row_bcast("L1_attn_qkv_b", 2048, kvb[:, 1024:2048])
    mk_row_bcast("L1_attn_out_b", 0, obbc)

    g0v, g0s = ch_v.n, ch_s.n
    t0g = ch_t.n

    def gram_t(t, gate=ch_v.n):
        t.wait_ge(ch_ld.h, CLD)
        t.wait_ge(ch_v.h, gate)
        ch_t.inc(t.matmul(psC[:, 0:64], onesf[0:1, 0:128], grow[0:1, 0:64],
                          start=True, stop=True))
    ctx.block.tensor(gram_t)

    def gram_v(v, t0g=t0g):
        v.wait_ge(ch_t.h, t0g + 1)
        ch_v.inc(v.tensor_copy(gsc[:, 64:128], psC[:, 0:64]))
        v.wait_ge(ch_v.h, ch_v.n)
    ctx.block.vector(gram_v)
    gv1 = ch_v.n

    def gram_s(s_, gv1=gv1):
        s_.wait_ge(ch_v.h, gv1)
        ch_s.inc(s_.activation(gsc[:, 112:128], gsc[:, 64:80], AF.Sigmoid))
        s_.wait_ge(ch_s.h, ch_s.n)
        ch_s.inc(s_.activation(gsc[:, 128:144], gsc[:, 112:128], AF.Ln))
    ctx.block.scalar(gram_s)
    gs1 = ch_s.n

    def gram_v2(v, gs1=gs1):
        v.wait_ge(ch_s.h, gs1)
        ch_v.inc(v.tensor_scalar(gsc[:, 144:160], gsc[:, 128:144], ip1[:, 0:1], None, ALU.mult))
        ch_v.inc(v.tensor_scalar(gsc[:, 160:176], gsc[:, 128:144], i127[:, 0:1], None, ALU.mult))
        v.wait_ge(ch_v.h, ch_v.n)
    ctx.block.vector(gram_v2)
    gv2 = ch_v.n

    def gram_s2a(s_, gv2=gv2):
        s_.wait_ge(ch_v.h, gv2)
        ch_s.inc(s_.activation(gsc[:, 144:160], gsc[:, 144:160], AF.Exp))
        ch_s.inc(s_.activation(gsc[:, 160:176], gsc[:, 160:176], AF.Exp))
        ch_s.inc(s_.activation(gsc[:, 176:192], gsc[:, 128:144], AF.Exp, scale=128.0))
    ctx.block.scalar(gram_s2a)
    for h in range(NH):
        v0h = ch_v.n

        def dg_v1(v, h=h):
            v.wait_ge(ch_s.h, ch_s.n) if False else None
            ch_v.inc(v.tensor_scalar(scr[:, 0:128], delta[:, :], gsc[:, 128 + h:129 + h],
                                     None, ALU.mult))
            v.wait_ge(ch_v.h, ch_v.n)
        ctx.block.vector(dg_v1)
        vm = ch_v.n
        s0h = ch_s.n

        def dg_s(s_, vm=vm):
            s_.wait_ge(ch_v.h, vm)
            ch_s.inc(s_.activation(scr[:, 0:128], scr[:, 0:128], AF.Exp))
        ctx.block.scalar(dg_s)
        sm = ch_s.n

        def dg_v2(v, h=h, sm=sm):
            v.wait_ge(ch_s.h, sm)
            ch_v.inc(v.tensor_mul(Dg[:, h * 128:(h + 1) * 128], scr[:, 0:128], strict[:, :]))
            v.wait_ge(ch_v.h, ch_v.n)
        ctx.block.vector(dg_v2)


    t0p = ch_t.n
    v0p = ch_v.n

    def wproj_t(t, x3r=x3nT_rdy, gate=ch_v.n, v0p=v0p):
        t.wait_ge(trb.che.h, x3r)
        t.wait_ge(ch_ld.h, CLD)
        t.wait_ge(ch_v.h, gate)
        for i in range(NCHUNK):
            if i >= 2:
                t.wait_ge(ch_v.h, v0p + i - 1)
            for dc in range(NDC):
                mm = t.matmul([psC, psD][i % 2][:, 0:256],
                              x3nT[:, i * 1024 + dc * 128: i * 1024 + (dc + 1) * 128],
                              wsm[:, dc * 256:(dc + 1) * 256],
                              start=(dc == 0), stop=(dc == NDC - 1))
            ch_t.inc(mm)

    def wproj_v(v, t0p=t0p):
        for i in range(NCHUNK):
            v.wait_ge(ch_t.h, t0p + i + 1)
            ch_v.inc(v.tensor_copy(wp[:, i * 256:(i + 1) * 256], [psC, psD][i % 2][:, 0:256]))
        v.wait_ge(ch_v.h, ch_v.n)
    ctx.block.tensor(wproj_t)
    ctx.block.vector(wproj_v)
    wp_done = ch_v.n

    sp0 = ch_sp.n

    def shift_dma(e, wp_done=wp_done, sp0=sp0):
        e.wait_ge(ch_v.h, wp_done)
        wpr = wp[:, :].rearrange("p (c g) -> p c g", c=16)
        ch_sp.inc(e.dma_start(out=wpdram.rearrange("(c p) g -> p c g", p=128),
                              in_=wpr[:, :, 0:64]), 16)
        e.wait_ge(ch_sp.h, ch_sp.n)
        ch_sp.inc(e.dma_start(out=wp[1:128, 0:64], in_=wpdram[0:127, :]), 16)
        for i in range(1, NCHUNK):
            ch_sp.inc(e.dma_start(out=wp[:, i * 256:i * 256 + 64],
                                  in_=wpdram[i * 128 - 1: (i + 1) * 128 - 1, :]), 16)
    ctx.block.sync(shift_dma)
    shift_done = ch_sp.n

    def zrow_v(v, sd=shift_done):
        v.wait_ge(ch_sp.h, sd)
        ch_v.inc(v.memset(wp[0:1, 0:64], 0.0))
        v.wait_ge(ch_v.h, ch_v.n)
    ctx.block.vector(zrow_v)
    wps_done = ch_v.n

    def ext_chunk(i):
        base = i * 256

        def ext_v(v, i=i, base=base, gate=wps_done):
            if i == 0:
                v.wait_ge(ch_v.h, gate)
            for (o1, o2, dstt) in ((0, 64, scrb), (128, 192, scr.bitcast(BF16))):
                p1 = wp[:, base + o1: base + o1 + 64].rearrange("p (h d) -> p h d", h=16)
                p2 = wp[:, base + o2: base + o2 + 64].rearrange("p (h d) -> p h d", h=16)
                Lt = dstt[:, 0:96].rearrange("p (h c) -> p h c", h=16)
                specs = [(0, 1, 0, 3), (1, 2, 3, 2), (2, 3, 5, 1)]
                for (pi, pj0, c0, n) in specs:
                    a = p1[:, :, pi:pi + 1].broadcast_to((128, 16, n))
                    b = p2[:, :, pj0:pj0 + n]
                    v.tensor_tensor(Lt[:, :, c0:c0 + n], a, b, ALU.mult)
                    a2 = p2[:, :, pi:pi + 1].broadcast_to((128, 16, n))
                    b2 = p1[:, :, pj0:pj0 + n]
                    v.tensor_tensor(dstt[:, 96:192].rearrange("p (h c) -> p h c", h=16)[:, :, c0:c0 + n],
                                    a2, b2, ALU.mult)
                ch_v.inc(v.tensor_sub(Lt[:, :, :],
                                      Lt[:, :, :],
                                      dstt[:, 96:192].rearrange("p (h c) -> p h c", h=16)))
                v.wait_ge(ch_v.h, ch_v.n)
            ch_v.inc(v.tensor_mul(scr.bitcast(BF16)[:, 192:288], scrb[:, 0:96], scrb[:, 0:96]))
            ch_v.inc(v.tensor_mul(scr.bitcast(BF16)[:, 288:384],
                                  scr.bitcast(BF16)[:, 0:96], scr.bitcast(BF16)[:, 0:96]))
            v.wait_ge(ch_v.h, ch_v.n)
            ch_v.inc(v.reduce_sum(gsc[:, 0:16].rearrange("p (h o) -> p h o", o=1),
                                  scr.bitcast(BF16)[:, 192:288].rearrange("p (h c) -> p h c", h=16),
                                  axis=AX.X))
            ch_v.inc(v.reduce_sum(gsc[:, 16:32].rearrange("p (h o) -> p h o", o=1),
                                  scr.bitcast(BF16)[:, 288:384].rearrange("p (h c) -> p h c", h=16),
                                  axis=AX.X))
            v.wait_ge(ch_v.h, ch_v.n)
        ctx.block.vector(ext_v)
        vm1 = ch_v.n
        s0e = ch_s.n

        def ext_s(s_, vm1=vm1):
            s_.wait_ge(ch_v.h, vm1)
            ch_s.inc(s_.activation(gsc[:, 0:32], gsc[:, 0:32], AF.Sqrt))
        ctx.block.scalar(ext_s)
        sm1 = ch_s.n

        def ext_v2(v, i=i, sm1=sm1):
            v.wait_ge(ch_s.h, sm1)
            ch_v.inc(v.tensor_scalar(gsc[:, 0:32], gsc[:, 0:32], 1e-12, None, ALU.max))
            v.wait_ge(ch_v.h, ch_v.n)
            ch_v.inc(v.reciprocal(gsc[:, 32:64], gsc[:, 0:32]))
            v.wait_ge(ch_v.h, ch_v.n)
            ch_v.inc(v.tensor_tensor(rl[:, i * 96:(i + 1) * 96].rearrange("p (h c) -> p h c", h=16),
                                     scr.bitcast(BF16)[:, 0:96].rearrange("p (h c) -> p h c", h=16),
                                     gsc[:, 48:64].rearrange("p (h o) -> p h o", o=1)
                                     .broadcast_to((128, 16, 6)), ALU.mult))
            ch_v.inc(v.tensor_tensor(scrb[:, 96:192].rearrange("p (h c) -> p h c", h=16),
                                     scrb[:, 0:96].rearrange("p (h c) -> p h c", h=16),
                                     gsc[:, 32:48].rearrange("p (h o) -> p h o", o=1)
                                     .broadcast_to((128, 16, 6)), ALU.mult))
            v.wait_ge(ch_v.h, ch_v.n)
            wlr = scrb[:, 96:192].rearrange("p (h c) -> p h c", h=16)
            jwr = jw[:, i * 96:(i + 1) * 96].rearrange("p (h c) -> p h c", h=16)
            for c_ in range(6):
                mrk = v.tensor_copy(jwr[:, :, c_:c_ + 1], wlr[:, :, 5 - c_:6 - c_])
            ch_v.inc(mrk)
            v.wait_ge(ch_v.h, ch_v.n)
            ch_v.inc(v.tensor_scalar(jwr[:, :, 1:2], jwr[:, :, 1:2], -1.0, None, ALU.mult))
            ch_v.inc(v.tensor_scalar(jwr[:, :, 4:5], jwr[:, :, 4:5], -1.0, None, ALU.mult))
            v.wait_ge(ch_v.h, ch_v.n)
        ctx.block.vector(ext_v2)
    for i in range(NCHUNK):
        ext_chunk(i)
    ext_done = ch_v.n

    return emit_l1_attn_part2(ctx, waps, locals())


def emit_l1_attn_part2(ctx, waps, g):
    nc = ctx.nc
    xres = ctx.tiles["xres"]
    identf = ctx.tiles["identf"]
    ident = g["ident"]
    (x3nT, outT, shared, k_acc, kvT, qTh, khT, Ptile) = (
        g["x3nT"], g["outT"], g["shared"], g["k_acc"], g["kvT"], g["qTh"], g["khT"], g["Ptile"])
    (scr, scrb, ob65, wp, rl, jw, r36, w36, r36T, w36T, Dg) = (
        g["scr"], g["scrb"], g["ob65"], g["wp"], g["rl"], g["jw"], g["r36"], g["w36"],
        g["r36T"], g["w36T"], g["Dg"])
    (gsc, Sg, ggpre, gg, Wst, qb, kvb, obbc, onest, onesf, cmask) = (
        g["gsc"], g["Sg"], g["ggpre"], g["gg"], g["Wst"], g["qb"], g["kvb"], g["obbc"],
        g["onest"], g["onesf"], g["cmask"])
    (psA, psB, psC, psD, trb) = (g["psA"], g["psB"], g["psC"], g["psD"], g["trb"])
    (ch_sp, ch_wd, ch_ld, ch_t, ch_v, ch_s) = (
        g["ch_sp"], g["ch_wd"], g["ch_ld"], g["ch_t"], g["ch_v"], g["ch_s"])
    (spill, x3dram, x3nT_rdy, ext_done, CLD) = (
        g["spill"], g["x3dram"], g["x3nT_rdy"], g["ext_done"], g["CLD"])
    SgB = g["SgB"]
    dvecB = g["dvecB"]
    ones2 = g["ones2"]
    x3st = g["Ptile"][:, :].bitcast(F32)
    qwh = g["qwh"]
    NH, DH = 16, 64
    SCALE = DH ** -0.5
    QKVW = waps["L1_attn_qkv_w"]
    GEOW = waps["L1_attn_kv_geo_w"]
    OUTW = waps["L1_attn_out_w"]

    def b36_v(v, gate=ext_done):
        v.wait_ge(ch_v.h, gate)
        v.memset(Sg[:, :], 0.0)
        ch_v.inc(v.memset(SgB[:, :], 0.0))
        ch_v.inc(v.tensor_copy(dvecB[:, :], gsc[:, 160:176]))
        ch_v.inc(v.memset(ones2[:, :], 1.0))
        v.wait_ge(ch_v.h, ch_v.n)
    ctx.block.vector(b36_v)
    b36_done = ch_v.n

    for h in range(NH):
        def bh_v(v, h=h, gate=b36_done, trg=trb.cht.n):
            if trg:
                v.wait_ge(trb.cht.h, trg)
            if h == 0:
                v.wait_ge(ch_v.h, gate)
            for (src, dst) in ((rl, r36), (jw, w36)):
                s6 = src[:, :].rearrange("p (c s) -> p c s", s=96)[:, :, h * 6:(h + 1) * 6]
                ina = s6.rearrange("p c (x o) -> p c x o", o=1).broadcast_to((128, 16, 6, 6))
                inb = s6.rearrange("p c (o y) -> p c o y", o=1).broadcast_to((128, 16, 6, 6))
                mrk = v.tensor_tensor(
                    dst[:, :].rearrange("p (c x y) -> p c x y", c=16, x=6), ina, inb, ALU.mult)
            ch_v.inc(mrk)
            v.wait_ge(ch_v.h, ch_v.n)
        ctx.block.vector(bh_v)
        bh_done = ch_v.n
        pairs = []
        for i in range(NCHUNK):
            pairs.append((r36[:, i * 36:(i + 1) * 36], r36T[0:36, i * 128:(i + 1) * 128]))
        m1 = trb.emit(pairs, twait=[(ch_v, bh_done)], vwait=[(ch_t, ch_t.n)])
        pairs = []
        for i in range(NCHUNK):
            pairs.append((w36[:, i * 36:(i + 1) * 36], w36T[0:36, i * 128:(i + 1) * 128]))
        m2 = trb.emit(pairs, vwait=[(ch_t, ch_t.n)])
        tv0, vv0 = ch_t.n, ch_v.n

        def gram_t(t, h=h, m2=m2, tv0=tv0, vv0=vv0):
            t.wait_ge(trb.che.h, m2)
            for i in range(NCHUNK):
                if i >= 1:
                    t.wait_ge(ch_v.h, vv0 + 6 * i)
                t.matmul(psC[:, 0:128], w36T[0:36, i * 128:(i + 1) * 128],
                         r36T[0:36, i * 128:(i + 1) * 128], start=True, stop=True)
                ch_t.inc(t.nop()) if False else None
                ch_t.n += 0
        del gram_t

        def gram_t(t, h=h, m2=m2, tv0=tv0, vv0=vv0):
            t.wait_ge(trb.che.h, m2)
            for i in range(NCHUNK):
                if i >= 1:
                    t.wait_ge(ch_v.h, vv0 + 6 * i)
                ch_t.inc(t.matmul(psC[:, 0:128], w36T[0:36, i * 128:(i + 1) * 128],
                                  r36T[0:36, i * 128:(i + 1) * 128], start=True, stop=True))
                t.wait_ge(ch_v.h, vv0 + 6 * i + 1)
                ch_t.inc(t.matmul(psD[:, 0:1], Ptile[:, 0:128], onest[:, 0:1],
                                  start=True, stop=True))
                ch_t.inc(t.matmul(psD[:, 1:2], r36T[0:36, i * 128:(i + 1) * 128],
                                  SgB[0:36, h:h + 1], start=True, stop=True))
                ch_t.inc(t.matmul(psD[0:36, 4:5], w36[:, i * 36:(i + 1) * 36],
                                  dvecB[:, h:h + 1], start=True, stop=True))

        def gram_v(v, h=h, tv0=tv0, vv0=vv0):
            for i in range(NCHUNK):
                v.wait_ge(ch_t.h, tv0 + 4 * i + 1)
                ch_v.inc(v.tensor_mul(Ptile[:, 0:128], psC[:, 0:128],
                                      Dg[:, h * 128:(h + 1) * 128]))
                v.wait_ge(ch_v.h, ch_v.n)
                v.wait_ge(ch_t.h, tv0 + 4 * i + 4)
                ch_v.inc(v.tensor_copy(gsc[:, 0:8], psD[:, 0:8]))
                v.wait_ge(ch_v.h, ch_v.n)
                ch_v.inc(v.scalar_tensor_tensor(ggpre[:, i * 16 + h: i * 16 + h + 1],
                                                gsc[:, 1:2], gsc[:, 144 + h:145 + h],
                                                gsc[:, 0:1], ALU.mult, ALU.add))
                ch_v.inc(v.tensor_scalar(Sg[0:36, h:h + 1], Sg[0:36, h:h + 1],
                                         gsc[0:36, 176 + h:177 + h], None, ALU.mult))
                v.wait_ge(ch_v.h, ch_v.n)
                ch_v.inc(v.tensor_add(Sg[0:36, h:h + 1], Sg[0:36, h:h + 1],
                                      psD[0:36, 4:5]))
                v.wait_ge(ch_v.h, ch_v.n)
                ch_v.inc(v.tensor_copy(SgB[0:36, h:h + 1], Sg[0:36, h:h + 1]))
                v.wait_ge(ch_v.h, ch_v.n)
        ctx.block.tensor(gram_t)
        ctx.block.vector(gram_v)
    gram_done = ch_v.n

    def gg_v(v, gate=gram_done):
        msb = gsc[:, 80:96].rearrange("p (o h) -> p o h", o=1).broadcast_to((128, 16, 16))
        ch_v.inc(v.tensor_tensor(ggpre[:, :].rearrange("p (c h) -> p c h", c=16),
                                 ggpre[:, :].rearrange("p (c h) -> p c h", c=16),
                                 msb, ALU.mult))
        v.wait_ge(ch_v.h, ch_v.n)
    ctx.block.vector(gg_v)
    ggv = ch_v.n

    def gg_s(s_, ggv=ggv):
        s_.wait_ge(ch_v.h, ggv)
        ch_s.inc(s_.activation(ggpre[:, :], ggpre[:, :], AF.Sigmoid))
    ctx.block.scalar(gg_s)
    ggs = ch_s.n

    def gg_v2(v, ggs=ggs):
        v.wait_ge(ch_s.h, ggs)
        ch_v.inc(v.tensor_scalar(gg[:, :], ggpre[:, :], gsc[:, 96:97], None, ALU.mult))
        ch_v.inc(v.memset(gsc[:, 192:224], 0.0))
        v.wait_ge(ch_v.h, ch_v.n)
    ctx.block.vector(gg_v2)
    gg_done = ch_v.n

    blocks = [("q", 0, "qkv"), ("q", 512, "qkv"), ("k", 1024, "qkv"), ("k", 1536, "qkv"),
              ("gk", 0, "geo"), ("gk", 512, "geo"),
              ("KT", 0, None),
              ("v", 2048, "qkv"), ("v", 2560, "qkv"), ("gv", 1024, "geo"), ("gv", 1536, "geo")]
    wbm = {}
    BEV_MARKS = {}
    bi = 0
    kvT_done = [0]
    for tag, cb, src in blocks:
        if tag == "KT":
            for i in range(NCHUNK):
                pairs = []
                for hh in range(NH):
                    pairs.append((k_acc[:, i * 1024 + hh * 64: i * 1024 + (hh + 1) * 64],
                                  kvT[64 * (hh % 2):64 * (hh % 2) + 64,
                                      (hh // 2) * 2048 + i * 128: (hh // 2) * 2048 + (i + 1) * 128]))
                kvT_done[0] = trb.emit(pairs, twait=[(ch_v, ch_v.n)], vwait=[(ch_t, ch_t.n)])
            for i in range(NCHUNK):
                s0k = ch_s.n

                def kmax_s(s_, i=i, gate=ch_v.n):
                    s_.wait_ge(ch_v.h, gate)
                    ch_s.inc(s_.activation(scrb[:, 0:1024], k_acc[:, i * 1024:(i + 1) * 1024],
                                           AF.Square))
                ctx.block.scalar(kmax_s)
                sm = ch_s.n

                def kmax_v(v, i=i, sm=sm):
                    v.wait_ge(ch_s.h, sm)
                    ch_v.inc(v.reduce_sum(gsc[:, 32:48].rearrange("p (h o) -> p h o", o=1),
                                          scrb[:, 0:1024].rearrange("p (h e) -> p h e", h=16),
                                          axis=AX.X))
                    v.wait_ge(ch_v.h, ch_v.n)
                    ch_v.inc(v.tensor_max(gsc[:, 208:224], gsc[:, 208:224], gsc[:, 32:48]))
                    v.wait_ge(ch_v.h, ch_v.n)
                ctx.block.vector(kmax_v)
            continue
        widx = bi % 2
        ipt = 2 if tag == "q" else 1

        def bdma(gq, cb=cb, src=src, widx=widx, gate=wbm.get(bi - 2, 0)):
            if gate:
                gq.wait_ge(ch_t.h, gate)
            dr = QKVW if src == "qkv" else GEOW
            ch_wd.inc(gq.dma_start(out=Wst[widx][:, :].rearrange("p (dc m) -> p dc m", dc=8),
                                   in_=dr[:, cb:cb + 512].rearrange("(dc p) m -> p dc m", p=128)),
                      16)
        ctx.block.gpsimd(bdma)
        wd_n = ch_wd.n
        t0b = ch_t.n
        vb0 = ch_v.n

        def bmm(t, widx=widx, wd_n=wd_n, x3r=x3nT_rdy, gate=ch_v.n, vb0=vb0, ipt=ipt):
            t.wait_ge(ch_wd.h, wd_n)
            t.wait_ge(trb.che.h, x3r)
            t.wait_ge(ch_v.h, gate)
            for i in range(NCHUNK):
                if i >= 2:
                    t.wait_ge(ch_v.h, vb0 + ipt * (i - 1))
                for dc in range(NDC):
                    mm = t.matmul([psA, psB][i % 2][:, 0:512],
                                  x3nT[:, i * 1024 + dc * 128: i * 1024 + (dc + 1) * 128],
                                  Wst[widx][:, dc * 512:(dc + 1) * 512],
                                  start=(dc == 0), stop=(dc == NDC - 1))
                ch_t.inc(mm)
        ctx.block.tensor(bmm)

        def bev(v, tag=tag, cb=cb, t0b=t0b, bi=bi):
            hw = 0 if cb in (0, 1024, 2048) else 1
            for i in range(NCHUNK):
                v.wait_ge(ch_t.h, t0b + i + 1)
                ps = [psA, psB][i % 2]
                if tag == "q":
                    v.tensor_copy(scrb[:, 0:512], ps[:, 0:512])
                    v.tensor_mul(scr[:, 0:512], scrb[:, 0:512], scrb[:, 0:512])
                    ch_v.inc(v.reduce_sum(gsc[:, 32:40].rearrange("p (h o) -> p h o", o=1),
                                          scr[:, 0:512].rearrange("p (h e) -> p h e", h=8),
                                          axis=AX.X))
                    v.wait_ge(ch_v.h, ch_v.n)
                    qcols = slice(192 + hw * 8, 192 + hw * 8 + 8)
                    ch_v.inc(v.tensor_max(gsc[:, qcols], gsc[:, qcols], gsc[:, 32:40]))
                    v.wait_ge(ch_v.h, ch_v.n)
                elif tag == "k":
                    dst = k_acc[:, i * 1024 + hw * 512: i * 1024 + hw * 512 + 512]
                    ch_v.inc(v.scalar_tensor_tensor(dst, ps[:, 0:512], 1.0,
                                                    kvb[:, hw * 512:(hw + 1) * 512],
                                                    ALU.mult, ALU.add))
                elif tag == "gk":
                    ggb = gg[:, i * 16 + hw * 8: i * 16 + (hw + 1) * 8]
                    ggx = ggb.rearrange("p (h o) -> p h o", o=1).broadcast_to((128, 8, 64))
                    v.tensor_tensor(scrb[:, 0:512].rearrange("p (h e) -> p h e", h=8),
                                    ps[:, 0:512].rearrange("p (h e) -> p h e", h=8), ggx, ALU.mult)
                    dst = k_acc[:, i * 1024 + hw * 512: i * 1024 + hw * 512 + 512]
                    ch_v.inc(v.tensor_add(dst, dst, scrb[:, 0:512]))
                elif tag == "v":
                    dst = shared[:, :].rearrange("p (c x) -> p c x", c=16)[:, i, :] \
                        .rearrange("p (h e) -> p h e", h=16)[:, hw * 8:(hw + 1) * 8, 0:64]
                    vbb = kvb[:, 1024 + hw * 512:1024 + (hw + 1) * 512] \
                        .rearrange("p (h e) -> p h e", h=8)
                    ch_v.inc(v.scalar_tensor_tensor(
                        dst, ps[:, 0:512].rearrange("p (h e) -> p h e", h=8), 1.0,
                        vbb, ALU.mult, ALU.add))
                elif tag == "gv":
                    ggb = gg[:, i * 16 + hw * 8: i * 16 + (hw + 1) * 8]
                    ggx = ggb.rearrange("p (h o) -> p h o", o=1).broadcast_to((128, 8, 64))
                    v.tensor_tensor(scrb[:, 0:512].rearrange("p (h e) -> p h e", h=8),
                                    ps[:, 0:512].rearrange("p (h e) -> p h e", h=8), ggx, ALU.mult)
                    dst = shared[:, :].rearrange("p (c x) -> p c x", c=16)[:, i, :] \
                        .rearrange("p (h e) -> p h e", h=16)[:, hw * 8:(hw + 1) * 8, 0:64]
                    ch_v.inc(v.tensor_add(dst, dst, scrb[:, 0:512]
                                          .rearrange("p (h e) -> p h e", h=8)))
            v.wait_ge(ch_v.h, ch_v.n)
        ctx.block.vector(bev)
        wbm[bi] = ch_t.n
        bi += 1
    def vones(v):
        dst = shared[:, :].rearrange("p (c h e) -> p c h e", c=16, h=16)[:, :, :, 64:65]
        ch_v.inc(v.memset(dst, 1.0))
        v.wait_ge(ch_v.h, ch_v.n)
    ctx.block.vector(vones)
    vext_done = ch_v.n

    t0c = ch_t.n

    def c_t(t, gate=vext_done):
        t.wait_ge(ch_v.h, gate)
        ch_t.inc(t.transpose(psC[0:16, 0:128], gsc[:, 192:208], identf[:]))
        ch_t.inc(t.transpose(psD[0:16, 0:128], gsc[:, 208:224], identf[:]))
        t.wait_ge(ch_v.h, CREP_M[0])
        ch_t.inc(t.transpose(psC[0:1, 0:16], gsc[0:16, 40:41], identf[0:16, 0:16]))
        t.wait_ge(ch_v.h, CREP_M[1])
        ch_t.inc(t.matmul(psD[:, 0:16], onesf[0:1, 0:128], scr[0:1, 0:16],
                          start=True, stop=True))
    CREP_M = {}

    def c_v(v, t0c=t0c):
        v.wait_ge(ch_t.h, t0c + 2)
        v.tensor_copy(gsc[0:16, 32:33].rearrange("p (a b) -> p (a b)") if False else gsc[0:16, 33:34], psC[0:16, 0:1]) if False else None
        ch_v.inc(v.reduce_max(gsc[0:16, 34:35], psC[0:16, 0:128], axis=AX.X))
        ch_v.inc(v.reduce_max(gsc[0:16, 35:36], psD[0:16, 0:128], axis=AX.X))
        v.wait_ge(ch_v.h, ch_v.n)
        ch_v.inc(v.tensor_mul(gsc[0:16, 36:37], gsc[0:16, 34:35], gsc[0:16, 35:36]))
        v.wait_ge(ch_v.h, ch_v.n)
    ctx.block.vector(c_v)
    cv1 = ch_v.n
    s0c = ch_s.n

    def c_s(s_, cv1=cv1):
        s_.wait_ge(ch_v.h, cv1)
        ch_s.inc(s_.activation(gsc[0:16, 37:38], gsc[0:16, 36:37], AF.Sqrt))
    ctx.block.scalar(c_s)
    cs1 = ch_s.n

    def c_v2(v, cs1=cs1):
        v.wait_ge(ch_s.h, cs1)
        ch_v.inc(v.tensor_scalar(gsc[0:16, 40:41], gsc[0:16, 37:38], -SCALE, None, ALU.mult))
        v.wait_ge(ch_v.h, ch_v.n)
        CREP_M[0] = ch_v.n
    ctx.block.vector(c_v2)
    CREP_M[0] = ch_v.n

    def c_v3(v, t0c=t0c):
        v.wait_ge(ch_t.h, t0c + 3)
        ch_v.inc(v.tensor_copy(scr[0:1, 0:16], psC[0:1, 0:16]))
        v.wait_ge(ch_v.h, ch_v.n)
        CREP_M[1] = ch_v.n
    ctx.block.vector(c_v3)
    CREP_M[1] = ch_v.n

    def c_v4(v, t0c=t0c):
        v.wait_ge(ch_t.h, t0c + 4)
        ch_v.inc(v.tensor_copy(gsc[:, 48:64].bitcast(F32)[:, 0:16] if False else negcrep[:, 0:16],
                               psD[:, 0:16]))
        v.wait_ge(ch_v.h, ch_v.n)
    negcrep = g["negcrep"]
    ctx.block.tensor(c_t)
    ctx.block.vector(c_v4)
    crep_done = ch_v.n

    for h in range(NH):
        hp, hl = h // 2, h % 2
        d0 = ch_wd.n

        def qdma(gq, h=h, gate=wbm.get(len(blocks) - 2, 0)):
            if h == 0 and gate:
                gq.wait_ge(ch_t.h, gate)
            if h >= 1:
                gq.wait_ge(ch_t.h, QWH_T[h - 1])
            ch_wd.inc(gq.dma_start(out=qwh[:, :].rearrange("p (dc m) -> p dc m", dc=8),
                                   in_=QKVW[:, h * 64:(h + 1) * 64]
                                   .rearrange("(dc p) m -> p dc m", p=128)), 16)
        ctx.block.gpsimd(qdma)
        qw_n = ch_wd.n
        t0q = ch_t.n

        def qmm(t, h=h, qw_n=qw_n, gate=ch_v.n, kd=kvT_done[0]):
            t.wait_ge(ch_wd.h, qw_n)
            t.wait_ge(trb.che.h, kd)
            t.wait_ge(ch_v.h, gate)
            x3r4 = x3nT.rearrange("p (i x) -> p i x", i=16)
            for tb in range(4):
                for dc in range(NDC):
                    mm = t.matmul(psC[0:64, 0:512], qwh[:, dc * 64:(dc + 1) * 64],
                                  x3r4[:, tb * 4:(tb + 1) * 4, dc * 128:(dc + 1) * 128],
                                  start=(dc == 0), stop=(dc == NDC - 1))
                ch_t.inc(mm)
                t.wait_ge(ch_s.h, QEV_S[(h, tb)])
        QWH_T = {}
        QEV_S = {}
        s0q = ch_s.n
        for tb in range(4):
            QEV_S[(h, tb)] = s0q + tb + 1

        def qev(s_, h=h, hp=hp, hl=hl, t0q=t0q, crd=crep_done):
            s_.wait_ge(ch_v.h, crd)
            for tb in range(4):
                s_.wait_ge(ch_t.h, t0q + tb + 1)
                ch_s.inc(s_.activation(qTh[0:64, tb * 512:(tb + 1) * 512], psC[0:64, 0:512],
                                       AF.Identity, bias=qb[64 * hl:64 * hl + 64, hp:hp + 1]))
        ctx.block.tensor(qmm)
        ctx.block.scalar(qev)
        QWH_T[h] = ch_t.n

        def khcp(v, h=h, hp=hp, hl=hl, gate=ch_s.n):
            v.wait_ge(ch_s.h, gate)
            ch_v.inc(v.tensor_copy(khT[0:64, :], kvT[64 * hl:64 * hl + 64,
                                                     hp * 2048:(hp + 1) * 2048]))
            v.wait_ge(ch_v.h, ch_v.n)
        ctx.block.vector(khcp)
        kh_done = ch_v.n

        outps = [psB[0:65, 0:512], psB[0:65, 512:1024], psC[0:65, 0:512], psD[0:65, 0:512]]
        t0a = ch_t.n
        SC_V = {}

        def att_t(t, h=h, t0a=t0a, kh=kh_done):
            t.wait_ge(ch_v.h, kh)
            for si in range(16):
                for tb in range(si // 4, 4):
                    gidx = (si, tb)
                    if SC_WAIT.get(gidx):
                        t.wait_ge(ch_s.h, SC_WAIT[gidx])
                    ch_t.inc(t.matmul(psA[:, (tb % 2) * 512:(tb % 2) * 512 + 512],
                                      khT[0:64, si * 128:(si + 1) * 128],
                                      qTh[0:64, tb * 512:(tb + 1) * 512],
                                      start=True, stop=True))
                    t.wait_ge(ch_v.h, PV_WAIT[gidx])
                    ch_t.inc(t.matmul(outps[tb],
                                      shared[:, :].rearrange("p (c x) -> p c x", c=16)[:, si, :]
                                      [:, h * 65:h * 65 + 65],
                                      Ptile[:, tb * 512:(tb + 1) * 512],
                                      start=(si == 0), stop=(si == (tb * 4 + 3))))
        SC_WAIT = {}
        PV_WAIT = {}
        s0a = ch_s.n
        v0a = ch_v.n
        cnt_t = 0
        cnt_s = 0
        cnt_v = 0
        for si in range(16):
            for tb in range(si // 4, 4):
                cnt_t += 1
                SC_WAIT[(si, tb)] = (s0a + cnt_s) if cnt_s else 0
                cnt_s += 1
                if tb == si // 4:
                    cnt_v += 1
                PV_WAIT[(si, tb)] = v0a + cnt_v
                cnt_t += 1
        SC_WAIT = {}
        seq = []
        for si in range(16):
            for tb in range(si // 4, 4):
                seq.append((si, tb))
        for n, gidx in enumerate(seq):
            SC_WAIT[gidx] = (s0a + n - 1) if n >= 2 else 0
        PV_WAIT = {}
        nv = 0
        for n, gidx in enumerate(seq):
            si, tb = gidx
            if tb == si // 4:
                nv += 1
                PV_WAIT[gidx] = v0a + nv
            else:
                PV_WAIT[gidx] = 0
        PV_SWAIT = {}
        for n, gidx in enumerate(seq):
            si, tb = gidx
            PV_SWAIT[gidx] = s0a + n + 1
        del att_t

        def att_t(t, h=h, t0a=t0a, kh=kh_done):
            t.wait_ge(ch_v.h, kh)
            for n, (si, tb) in enumerate(seq):
                if SC_WAIT[(si, tb)]:
                    t.wait_ge(ch_s.h, SC_WAIT[(si, tb)])
                ch_t.inc(t.matmul(psA[:, (n % 2) * 512:(n % 2) * 512 + 512],
                                  khT[0:64, si * 128:(si + 1) * 128],
                                  qTh[0:64, tb * 512:(tb + 1) * 512],
                                  start=True, stop=True))
                t.wait_ge(ch_s.h, PV_SWAIT[(si, tb)])
                if PV_WAIT[(si, tb)]:
                    t.wait_ge(ch_v.h, PV_WAIT[(si, tb)])
                ch_t.inc(t.matmul(outps[tb],
                                  shared[:, :].rearrange("p (c x) -> p c x", c=16)[:, si, :]
                                  [:, h * 65:h * 65 + 65],
                                  Ptile[:, tb * 512:(tb + 1) * 512],
                                  start=(si == 0), stop=(si == tb * 4 + 3)))

        def att_s(s_, h=h, t0a=t0a):
            for n, (si, tb) in enumerate(seq):
                s_.wait_ge(ch_t.h, t0a + 2 * n + 1)
                ch_s.inc(s_.activation(Ptile[:, tb * 512:(tb + 1) * 512],
                                       psA[:, (n % 2) * 512:(n % 2) * 512 + 512],
                                       AF.Exp, scale=SCALE,
                                       bias=negcrep[:, h:h + 1]))

        def att_v(v, h=h, s0a=s0a):
            for n, (si, tb) in enumerate(seq):
                if tb == si // 4:
                    v.wait_ge(ch_s.h, s0a + n + 1)
                    ch_v.inc(v.tensor_mul(Ptile[:, tb * 512:(tb + 1) * 512],
                                          Ptile[:, tb * 512:(tb + 1) * 512],
                                          cmask[:, (si % 4) * 512:(si % 4 + 1) * 512]))
                    v.wait_ge(ch_v.h, ch_v.n)
        ctx.block.tensor(att_t)
        ctx.block.scalar(att_s)
        ctx.block.vector(att_v)
        att_done_t = ch_t.n

        for tb in range(4):
            t0n = ch_t.n

            def nrm_v(v, h=h, tb=tb, adt=att_done_t):
                v.wait_ge(ch_t.h, adt)
                ch_v.inc(v.tensor_copy(ob65[0:65, 0:512], outps[tb]))
                v.wait_ge(ch_v.h, ch_v.n)
                ch_v.inc(v.reciprocal(ob65[64:65, 0:512], ob65[64:65, 0:512]))
                v.wait_ge(ch_v.h, ch_v.n)
            ctx.block.vector(nrm_v)
            nv1 = ch_v.n

            def nrm_t(t, nv1=nv1):
                t.wait_ge(ch_v.h, nv1)
                ch_t.inc(t.matmul(psA[0:64, 0:512], ones2[64:65, 0:64], ob65[64:65, 0:512],
                                  start=True, stop=True))
            ctx.block.tensor(nrm_t)
            nt1 = ch_t.n

            def nrm_v2(v, h=h, hp=hp, hl=hl, tb=tb, nt1=nt1):
                v.wait_ge(ch_t.h, nt1)
                ch_v.inc(v.tensor_tensor(
                    outT[64 * hl:64 * hl + 64, hp * 2048 + tb * 512: hp * 2048 + (tb + 1) * 512],
                    ob65[0:64, 0:512], psA[0:64, 0:512], ALU.mult))
                v.wait_ge(ch_v.h, ch_v.n)
            ctx.block.vector(nrm_v2)
    heads_done = ch_v.n

    d0 = ch_wd.n

    def owdma(gq, gate=ch_t.n):
        gq.wait_ge(ch_t.h, gate)
        for mb in range(2):
            ch_wd.inc(gq.dma_start(out=Wst[mb][:, :].rearrange("p (dc m) -> p dc m", dc=8),
                                   in_=OUTW[:, mb * 512:(mb + 1) * 512]
                                   .rearrange("(dc p) m -> p dc m", p=128)), 16)
    ctx.block.gpsimd(owdma)
    ow_n = ch_wd.n
    sp1 = ch_sp.n

    v0w_pre = ch_v.n

    def x2dma(e, hd=heads_done, v0w=v0w_pre):
        e.wait_ge(ch_v.h, hd)
        xr = spill.rearrange("(c p) d -> c p d", p=128)
        for i in range(NCHUNK):
            if i >= 1:
                e.wait_ge(ch_v.h, v0w + 2 * i)
            ch_sp.inc(e.dma_start(out=scr[:, 0:1024], in_=xr[i]), 16)
            e.wait_ge(ch_v.h, v0w + 2 * i + 2)
            ch_sp.inc(e.dma_start(out=x3dram.rearrange("(c p) d -> c p d", p=128)[i],
                                  in_=x3st[:, 0:1024]), 16)
    v0w = ch_v.n
    t0w = ch_t.n

    def ow_t(t, ow_n=ow_n, hd=heads_done, v0w=v0w):
        t.wait_ge(ch_wd.h, ow_n)
        t.wait_ge(ch_v.h, hd)
        for i in range(NCHUNK):
            if i >= 1:
                t.wait_ge(ch_v.h, v0w + 2 * i - 1)
            for hp in range(8):
                lhs = outT[:, hp * 2048 + i * 128: hp * 2048 + (i + 1) * 128]
                t.matmul(psA[:, 0:512], lhs, Wst[0][:, hp * 512:(hp + 1) * 512],
                         start=(hp == 0), stop=(hp == 7))
                mm = t.matmul(psA[:, 512:1024], lhs, Wst[1][:, hp * 512:(hp + 1) * 512],
                              start=(hp == 0), stop=(hp == 7))
            ch_t.inc(mm)

    def ow_v(v, t0w=t0w, sp1=sp1):
        for i in range(NCHUNK):
            v.wait_ge(ch_t.h, t0w + i + 1)
            if i >= 1:
                v.wait_ge(ch_sp.h, sp1 + 32 * i)
            ch_v.inc(v.scalar_tensor_tensor(x3st[:, 0:1024], psA[:, 0:1024], 1.0,
                                            obbc[:, 0:1024], ALU.mult, ALU.add))
            v.wait_ge(ch_v.h, ch_v.n)
            v.wait_ge(ch_sp.h, sp1 + 32 * i + 16)
            ch_v.inc(v.tensor_add(x3st[:, 0:1024], x3st[:, 0:1024], scr[:, 0:1024]))
            v.wait_ge(ch_v.h, ch_v.n)
    ctx.block.sync(x2dma)
    ctx.block.tensor(ow_t)
    ctx.block.vector(ow_v)
    x3_sp = ch_sp.n

    def back_dma(e, x3_sp=x3_sp):
        e.wait_ge(ch_sp.h, x3_sp)
        if L1A_NOBACK:
            return
        xr = x3dram.rearrange("(c p) d -> c p d", p=128)
        for i in range(NCHUNK):
            ch_sp.inc(e.dma_start(out=xres[:, i * 1024:(i + 1) * 1024], in_=xr[i]), 16)
    ctx.block.sync(back_dma)
    bk = ch_sp.n

    def fin_v(v, bk=bk):
        v.wait_ge(ch_sp.h, bk)
        ch_v.inc(v.memset(gsc[:, 0:1], 0.0))
        v.wait_ge(ch_v.h, ch_v.n)
    ctx.block.vector(fin_v)
    return ch_v, ch_v.n



EMITTERS = {}
DEBUG_DUMPS = []
MOE_GATE_ONLY = False
L0A_LAYERS = 2
L0A_NST = 8
L0A_FFN = True
L0A_ST_PARTS = 9
L1A_NOBACK = False


def build_program(blocks, weights_shapes):
    nc = bass.Bass("TRN2", target_bir_lowering=False)
    x_in = nc.dram_tensor("x_in", [T, D], F32, kind="ExternalInput").ap()
    x_out = nc.dram_tensor("x_out", [T, D], F32, kind="ExternalOutput").ap()
    waps = {wn: nc.dram_tensor(wn, list(shp), F32, kind="ExternalInput").ap()
            for wn, shp in weights_shapes.items()}

    with nc.Block() as block:
        ctx = Ctx(nc, block)
        xres = ctx.sbuf("xres", [128, NCHUNK * 1024], F32)
        ident = ctx.sbuf("ident", [128, 128], BF16)
        identf = ctx.sbuf("identf", [128, 128], F32)
        scr0 = ctx.sbuf("scr_lnsq", [128, 1024], F32)
        if len(blocks) > 1:
            ctx.arena_bytes = 137 * 1024
            ctx.arena = nc.alloc_sbuf_tensor("arena", [128, ctx.arena_bytes], mybir.dt.int8)
            ctx.arena_ptr = 0
        else:
            ctx.arena = None
        identc = ctx.const("ident", _ident_bf16)
        identcf = ctx.const("identf", lambda: np.eye(128, dtype=np.float32))
        ch_x = ctx.chan("xin")

        def load_x(e):
            ch_x.inc(e.dma_start(out=ident[:], in_=identc[:, :]), 16)
            ch_x.inc(e.dma_start(out=identf[:], in_=identcf[:, :]), 16)
            xr = x_in.rearrange("(c p) d -> c p d", p=128)
            for i in range(NCHUNK):
                ch_x.inc(e.dma_start(out=xres[:, i * 1024:(i + 1) * 1024], in_=xr[i]), 16)
        block.sync(load_x)

        def gate(e):
            e.wait_ge(ch_x.h, 16 * (NCHUNK + 2))
        block.vector(gate)
        block.scalar(gate)
        block.tensor(gate)
        block.gpsimd(gate)

        done = None
        for bidx_, b in enumerate(blocks):
            if ctx.arena is not None:
                ctx.begin_block()
            if b == "l1f":
                done = emit_l1_ffn(ctx, waps["L1_ffn_weight"])
            elif b == "l0f":
                done = emit_l0_ffn(ctx, waps)
            elif b == "l0a":
                done = emit_l0_attn(ctx, waps)
            elif b == "l1a":
                done = emit_l1_attn(ctx, waps)
            else:
                done = EMITTERS[b](ctx, waps)
            if bidx_ < len(blocks) - 1:
                def _bar(e, dc=done):
                    e.wait_ge(dc[0].h, dc[1])
                block.vector(_bar)
                block.scalar(_bar)
                block.tensor(_bar)
                block.gpsimd(_bar)
                block.sync(_bar)

        ch_out = ctx.chan("xout")
        dump_aps = {}
        for dn in DEBUG_DUMPS:
            th = ctx.tiles[dn]
            shp = list(th.shape)
            dt = th.dtype if hasattr(th, "dtype") else F32
            dump_aps[dn] = nc.dram_tensor("dump_" + dn, shp, dt, kind="ExternalOutput").ap()

        def store_x(e):
            e.wait_ge(done[0].h, done[1])
            xr = x_out.rearrange("(c p) d -> c p d", p=128)
            for i in range(NCHUNK):
                ch_out.inc(e.dma_start(out=xr[i], in_=xres[:, i * 1024:(i + 1) * 1024]), 16)
            for dn, ap in dump_aps.items():
                ch_out.inc(e.dma_start(out=ap[:, :], in_=ctx.tiles[dn][:, :]), 16)
            e.wait_ge(ch_out.h, 16 * (NCHUNK + len(dump_aps)))
        block.sync(store_x)
    return nc, ctx


BLOCK_WEIGHTS = {
    "l0a": ["L0_attn_Wq", "L0_attn_Wk", "L0_attn_Wv", "L0_attn_Wg", "L0_attn_Wo",
            "L0_attn_gn_scale", "L0_attn_gn_bias", "L0_attn_ln1_scale", "L0_attn_ln1_bias",
            "L0_attn_ln2_scale", "L0_attn_ln2_bias", "L0_attn_ffn_w1", "L0_attn_ffn_w2"],
    "l0f": ["L0_ffn_Wg", "L0_ffn_bg", "L0_ffn_W1", "L0_ffn_b1", "L0_ffn_W2", "L0_ffn_b2"],
    "l1a": ["L1_attn_qkv_w", "L1_attn_qkv_b", "L1_attn_w1_write", "L1_attn_w2_write",
            "L1_attn_w1_read", "L1_attn_w2_read", "L1_attn_kv_geo_w", "L1_attn_out_w",
            "L1_attn_out_b", "L1_attn_kv_gate", "L1_attn_mem_scale", "L1_attn_decay_logits"],
    "l1f": ["L1_ffn_weight"],
}


def run_blocks(x_full, inputs, blocks):
    wnames = []
    for b in blocks:
        wnames += BLOCK_WEIGHTS[b]
    shapes = {}
    for w in wnames:
        shp = np.asarray(inputs[w]).shape
        if w == "L0_ffn_bg":
            shp = (1, 8)
        if shp == ():
            shp = (1,)
        shapes[w] = shp
    nc, ctx = build_program(blocks, shapes)
    in_maps = []
    for c in range(8):
        m = {"x_in": np.ascontiguousarray(np.asarray(x_full[c], dtype=np.float32))}
        perm = np.concatenate([np.arange(0, 128, 2), np.arange(1, 128, 2)])
        for w in wnames:
            a = np.ascontiguousarray(np.asarray(inputs[w], dtype=np.float32))
            if a.ndim == 0:
                a = a.reshape(1)
            if w == "L0_ffn_bg":
                a = a.reshape(1, 8)
            if w in ("L0_attn_Wq", "L0_attn_Wk"):
                a = np.ascontiguousarray(a[..., perm])
            m[w] = a
        for cn, arr in ctx.consts_np.items():
            m["c_" + cn] = arr
        in_maps.append(m)
    res = run_bass_kernel_spmd(nc, in_maps, core_ids=list(range(8)))
    if DEBUG_DUMPS:
        global LAST_DUMPS
        LAST_DUMPS = {dn: res.results[0]["dump_" + dn] for dn in DEBUG_DUMPS}
    return np.stack([res.results[c]["x_out"] for c in range(8)], axis=0)


def kernel(**inputs):
    x = np.asarray(inputs["L0_attn_x"], dtype=np.float32)
    return run_blocks(x, inputs, ["l0a", "l0f", "l1a", "l1f"])


# version 03
# speedup vs baseline: 1.1174x; 1.1174x over previous
import math
import numpy as np
import ml_dtypes
import concourse.bass as bass
import concourse.mybir as mybir
from concourse.bass_utils import run_bass_kernel_spmd

F32 = mybir.dt.float32
BF16 = mybir.dt.bfloat16
AF = mybir.ActivationFunctionType
ALU = mybir.AluOpType
AX = mybir.AxisListType

T = 2048
D = 1024
NCHUNK = T // 128
NDC = D // 128


SHARED_TILES = {"xres", "scr_lnsq", "ident", "identf"}


class Chan:
    __slots__ = ("h", "n")

    def __init__(self, nc, name):
        self.h = nc.alloc_semaphore(name)
        self.n = 0

    def inc(self, inst, k=1):
        inst.then_inc(self.h, k)
        self.n += k
        return self.n


class Ctx:
    def __init__(self, nc, block):
        self.nc = nc
        self.block = block
        self.chans = {}
        self.tiles = {}
        self.consts_np = {}
        self.const_aps = {}
        self._trans_wait = (None, 0)

    def chan(self, name):
        if name not in self.chans:
            self.chans[name] = Chan(self.nc, name)
        return self.chans[name]

    def begin_block(self):
        self.arena_ptr = 0

    def sbuf(self, name, shape, dtype):
        if name not in self.tiles:
            if getattr(self, "arena", None) is not None and name not in SHARED_TILES:
                esz = 2 if dtype == BF16 else 4
                nbytes = shape[1] * esz
                ptr = (self.arena_ptr + 63) & ~63
                assert ptr + nbytes <= self.arena_bytes, \
                    f"arena overflow: {name} needs {nbytes} at {ptr} (cap {self.arena_bytes})"
                self.arena_ptr = ptr + nbytes
                view = self.arena[:, ptr:ptr + nbytes].bitcast(dtype)
                if shape[0] < 128:
                    view = view[0:shape[0], :]
                self.tiles[name] = view
            else:
                self.tiles[name] = self.nc.alloc_sbuf_tensor(name, shape, dtype)
        return self.tiles[name]

    def psum(self, name, shape, dtype=F32):
        if name not in self.tiles:
            n_el = 1024 if dtype == BF16 else 512
            shape = [shape[0], max(shape[1], n_el) if shape[1] <= n_el else shape[1]]
            self.tiles[name] = self.nc.alloc_psum_tensor(name, shape, dtype)
        return self.tiles[name]

    def const(self, name, arr_fn):
        if name not in self.const_aps:
            arr = np.ascontiguousarray(arr_fn() if callable(arr_fn) else arr_fn)
            self.consts_np[name] = arr
            dt = BF16 if arr.dtype == ml_dtypes.bfloat16 else F32
            self.const_aps[name] = self.nc.dram_tensor(
                "c_" + name, list(arr.shape), dt, kind="ExternalInput").ap()
        return self.const_aps[name]


def _ident_bf16():
    return np.eye(128).astype(ml_dtypes.bfloat16)


def _hadamard_bf16():
    h = np.array([[1.0]], dtype=np.float64)
    while h.shape[0] < D:
        h = np.block([[h, h], [h, -h]])
    return (h / np.sqrt(D)).astype(np.float32).astype(ml_dtypes.bfloat16)



def emit_transposes(ctx, pairs):
    ps = [ctx.psum("tpsA", [128, 1024], BF16), ctx.psum("tpsB", [128, 1024], BF16)]
    ident = ctx.sbuf("ident", [128, 128], BF16)
    cht = ctx.chan("tr_t")
    chv = ctx.chan("tr_v")
    wait_ch, wait_val = ctx._trans_wait
    base_t = cht.n
    base_v = chv.n

    def tsec(t):
        t.wait_ge(chv.h, base_v)
        if wait_ch is not None:
            t.wait_ge(wait_ch.h, wait_val)
        for k, (src, dst) in enumerate(pairs):
            if k >= 2:
                t.wait_ge(chv.h, base_v + k - 1)
            cht.inc(t.transpose(ps[k % 2][0:src.shape[-1], 0:src.shape[0]], src, ident[:]))

    def vsec(v):
        for k, (src, dst) in enumerate(pairs):
            v.wait_ge(cht.h, base_t + k + 1)
            chv.inc(v.tensor_copy(dst, ps[k % 2][0:src.shape[-1], 0:src.shape[0]]))

    ctx.block.tensor(tsec)
    ctx.block.vector(vsec)
    return chv, chv.n


def emit_ln(ctx, name, src_cols, dst_cols, eps=1e-5, wait=None, nchunk=NCHUNK, lo_cols=None):
    st = ctx.sbuf(name + "_stat", [128, 5 * NCHUNK], F32)
    sums = st[:, 0 * NCHUNK:0 * NCHUNK + nchunk]
    ss = st[:, 1 * NCHUNK:1 * NCHUNK + nchunk]
    mean = st[:, 2 * NCHUNK:2 * NCHUNK + nchunk]
    var = st[:, 3 * NCHUNK:3 * NCHUNK + nchunk]
    rstd = st[:, 4 * NCHUNK:4 * NCHUNK + nchunk]
    sq = ctx.sbuf("scr_lnsq", [128, 1024], F32)
    chv = ctx.chan(name + "_v")
    chs = ctx.chan(name + "_s")

    def vsec(v):
        if wait is not None:
            v.wait_ge(wait[0].h, wait[1])
        for i in range(nchunk):
            mrk = v.reduce_sum(sums[:, i:i + 1], src_cols(i), axis=AX.X)
        chv.inc(mrk)
        v.wait_ge(chv.h, chv.n)
        chv.inc(v.tensor_scalar(mean[:, 0:nchunk], sums[:, 0:nchunk], 1.0 / D, None, ALU.mult))
        v.wait_ge(chv.h, chv.n)
        v.wait_ge(chs.h, 1)
        chv.inc(v.tensor_scalar(var[:, 0:nchunk], ss[:, 0:nchunk], 1.0 / D, eps, ALU.mult, ALU.add))
        v.wait_ge(chv.h, chv.n)
        chv.inc(v.scalar_tensor_tensor(var[:, 0:nchunk], mean[:, 0:nchunk], -1.0, var[:, 0:nchunk],
                                       ALU.mult, ALU.elemwise_mul) if False else
                v.tensor_tensor(sq[:, 0:nchunk], mean[:, 0:nchunk], mean[:, 0:nchunk], ALU.mult))
        v.wait_ge(chv.h, chv.n)
        chv.inc(v.tensor_sub(var[:, 0:nchunk], var[:, 0:nchunk], sq[:, 0:nchunk]))
        v.wait_ge(chs.h, 2)
        chv.inc(v.reciprocal(rstd[:, 0:nchunk], var[:, 0:nchunk]))
        v.wait_ge(chv.h, chv.n)
        for i in range(nchunk):
            if lo_cols is None:
                chv.inc(v.tensor_scalar(dst_cols(i), src_cols(i), mean[:, i:i + 1], rstd[:, i:i + 1],
                                        ALU.subtract, ALU.mult))
            else:
                v.tensor_scalar(sq[:], src_cols(i), mean[:, i:i + 1], rstd[:, i:i + 1],
                                ALU.subtract, ALU.mult)
                v.tensor_copy(dst_cols(i), sq[:])
                chv.inc(v.tensor_sub(lo_cols(i), sq[:], dst_cols(i)))

    def ssec(s_):
        if wait is not None:
            s_.wait_ge(wait[0].h, wait[1])
        for i in range(nchunk):
            mrk = s_.activation(sq[:], src_cols(i), AF.Square, accum_out=ss[:, i:i + 1])
        chs.inc(mrk)
        s_.wait_ge(chv.h, 5)
        chs.inc(s_.activation(var[:, 0:nchunk], var[:, 0:nchunk], AF.Sqrt))

    ctx.block.vector(vsec)
    ctx.block.scalar(ssec)
    return chv, chv.n


def emit_ln_to_znT(ctx, name, src_cols, out_name, eps=1e-5, wait=None, hilo=False):
    zn = ctx.sbuf(name + "_zn", [128, NCHUNK * 1024], BF16)
    znT = ctx.sbuf(out_name, [128, NDC * T], BF16)
    lo_cols = None
    if hilo:
        zn_lo = ctx.sbuf(name + "_znlo", [128, NCHUNK * 1024], BF16)
        znT_lo = ctx.sbuf(out_name + "_lo", [128, NDC * T], BF16)
        lo_cols = lambda i: zn_lo[:, i * 1024:(i + 1) * 1024]
    ch_ln, v_ln = emit_ln(ctx, name, src_cols,
                          lambda i: zn[:, i * 1024:(i + 1) * 1024], eps=eps, wait=wait,
                          lo_cols=lo_cols)
    pairs = []
    for i in range(NCHUNK):
        for dc in range(NDC):
            pairs.append((zn[:, i * 1024 + dc * 128: i * 1024 + (dc + 1) * 128],
                          znT[:, dc * T + 128 * i: dc * T + 128 * (i + 1)]))
            if hilo:
                pairs.append((zn_lo[:, i * 1024 + dc * 128: i * 1024 + (dc + 1) * 128],
                              znT_lo[:, dc * T + 128 * i: dc * T + 128 * (i + 1)]))
    ctx._trans_wait = (ch_ln, v_ln)
    ch_tr, v_tr = emit_transposes(ctx, pairs)
    return zn, znT, (ch_tr, v_tr)


def emit_ln_T(ctx, name, src_cols, dstT, dstT_lo=None, eps=1e-5, wait=None):
    st = ctx.sbuf(name + "_stat", [128, 5 * NCHUNK], F32)
    sums = st[:, 0:NCHUNK]
    ss = st[:, NCHUNK:2 * NCHUNK]
    mean = st[:, 2 * NCHUNK:3 * NCHUNK]
    var = st[:, 3 * NCHUNK:4 * NCHUNK]
    rstd = st[:, 4 * NCHUNK:5 * NCHUNK]
    sq = ctx.sbuf("scr_lnsq", [128, 1024], F32)
    hilo = dstT_lo is not None
    K = 16 if hilo else 8
    ring = ctx.sbuf(name + "_ring", [128, 2 * 1024], BF16)
    ring_lo = ctx.sbuf(name + "_ringlo", [128, 2 * 1024], BF16) if hilo else None
    ident = ctx.sbuf("ident", [128, 128], BF16)
    ps = [ctx.psum("tpsA", [128, 1024], BF16), ctx.psum("tpsB", [128, 1024], BF16)]
    chv = ctx.chan(name + "_v")
    chs = ctx.chan(name + "_s")
    chw = ctx.chan(name + "_w")
    che = ctx.chan(name + "_e")
    cht = ctx.chan(name + "_t")

    def tdst(i, j):
        dc = j % 8
        tgt = dstT if j < 8 else dstT_lo
        return tgt[:, dc * T + 128 * i: dc * T + 128 * (i + 1)]

    def vsec(v):
        if wait is not None:
            v.wait_ge(wait[0].h, wait[1])
        for i in range(NCHUNK):
            mrk = v.reduce_sum(sums[:, i:i + 1], src_cols(i), axis=AX.X)
        chv.inc(mrk)
        v.wait_ge(chv.h, chv.n)
        chv.inc(v.tensor_scalar(mean[:, :], sums[:, :], 1.0 / D, None, ALU.mult))
        v.wait_ge(chv.h, chv.n)
        v.wait_ge(chs.h, 1)
        chv.inc(v.tensor_scalar(var[:, :], ss[:, :], 1.0 / D, eps, ALU.mult, ALU.add))
        v.wait_ge(chv.h, chv.n)
        chv.inc(v.tensor_tensor(sq[:, 0:NCHUNK], mean[:, :], mean[:, :], ALU.mult))
        v.wait_ge(chv.h, chv.n)
        chv.inc(v.tensor_sub(var[:, :], var[:, :], sq[:, 0:NCHUNK]))
        v.wait_ge(chs.h, 2)
        chv.inc(v.reciprocal(rstd[:, :], var[:, :]))
        v.wait_ge(chv.h, chv.n)
        for i in range(NCHUNK):
            sl = (i % 2) * 1024
            if i >= 2:
                v.wait_ge(cht.h, K * (i - 1))
            if hilo:
                v.tensor_scalar(sq[:], src_cols(i), mean[:, i:i + 1], rstd[:, i:i + 1],
                                ALU.subtract, ALU.mult)
                v.tensor_copy(ring[:, sl:sl + 1024], sq[:])
                chw.inc(v.tensor_sub(ring_lo[:, sl:sl + 1024], sq[:], ring[:, sl:sl + 1024]))
            else:
                chw.inc(v.tensor_scalar(ring[:, sl:sl + 1024], src_cols(i),
                                        mean[:, i:i + 1], rstd[:, i:i + 1],
                                        ALU.subtract, ALU.mult))
            if i >= 1:
                for j in range(K):
                    v.wait_ge(cht.h, K * (i - 1) + j + 1)
                    che.inc(v.tensor_copy(tdst(i - 1, j), ps[(K * (i - 1) + j) % 2][:, 0:128]))
        for j in range(K):
            v.wait_ge(cht.h, K * (NCHUNK - 1) + j + 1)
            che.inc(v.tensor_copy(tdst(NCHUNK - 1, j), ps[(K * (NCHUNK - 1) + j) % 2][:, 0:128]))

    def ssec(s_):
        if wait is not None:
            s_.wait_ge(wait[0].h, wait[1])
        for i in range(NCHUNK):
            mrk = s_.activation(sq[:], src_cols(i), AF.Square, accum_out=ss[:, i:i + 1])
        chs.inc(mrk)
        s_.wait_ge(chv.h, 5)
        chs.inc(s_.activation(var[:, :], var[:, :], AF.Sqrt))

    base_e = che.n
    base_w = chw.n
    base_t = cht.n

    def tsec(t):
        if wait is not None:
            t.wait_ge(wait[0].h, wait[1])
        for i in range(NCHUNK):
            t.wait_ge(chw.h, base_w + i + 1)
            sl = (i % 2) * 1024
            for j in range(K):
                g = K * i + j
                if g >= 2:
                    t.wait_ge(che.h, base_e + g - 1)
                dc = j % 8
                src = (ring if j < 8 else ring_lo)[:, sl + dc * 128: sl + (dc + 1) * 128]
                cht.inc(t.transpose(ps[g % 2][:, 0:128], src, ident[:]))

    ctx.block.vector(vsec)
    ctx.block.scalar(ssec)
    ctx.block.tensor(tsec)
    return che, che.n


def emit_l0_ffn(ctx, waps):
    xres = ctx.tiles["xres"]
    ident = ctx.sbuf("ident", [128, 128], BF16)
    identf = ctx.tiles["identf"]
    NE, NFB, NTP = (0 if MOE_GATE_ONLY else 8), 4, 4

    xnT = ctx.sbuf("moe_xnT", [128, NDC * T], BF16)
    bufB = ctx.sbuf("moe_bufB", [128, 16384], BF16)
    xnT_lo = bufB[:, 0:16384]
    che_ln, v_ln = emit_ln_T(ctx, "moe_ln", lambda i: xres[:, i * 1024:(i + 1) * 1024],
                             xnT, xnT_lo)

    wgs = ctx.sbuf("moe_wg", [128, 64], F32)
    wg_hi = ctx.sbuf("moe_wg_hi", [128, 64], BF16)
    wg_lo = ctx.sbuf("moe_wg_lo", [128, 64], BF16)
    bgr = ctx.sbuf("moe_bg", [1, 8], F32)
    b2a = ctx.sbuf("moe_b2", [8, 1024], F32)
    ones_f = ctx.sbuf("moe_onesf", [1, 128], F32)
    bgb = ctx.sbuf("moe_bgb", [128, 8], F32)
    ps_bg = ctx.psum("psD", [128, 512], F32)
    ch_sw = ctx.chan("moe_swd")
    ch_sv = ctx.chan("moe_swv")
    ch_bgt = ctx.chan("moe_bgt")

    def sdma(s_):
        wg = waps["L0_ffn_Wg"]
        for dc in range(NDC):
            ch_sw.inc(s_.dma_start(out=wgs[:, dc * 8:(dc + 1) * 8],
                                   in_=wg[dc * 128:(dc + 1) * 128, :]), 16)
        ch_sw.inc(s_.dma_start(out=bgr[0:1, :], in_=waps["L0_ffn_bg"][:, :]), 16)
        ch_sw.inc(s_.dma_start(out=b2a[:, :], in_=waps["L0_ffn_b2"][:, :]), 16)
    ctx.block.scalar(sdma)

    def vsw(v):
        v.wait_ge(ch_sw.h, 16 * (NDC + 2))
        v.tensor_copy(wg_hi[:], wgs[:])
        ch_sv.inc(v.tensor_sub(wg_lo[:], wgs[:], wg_hi[:]))
        ch_sv.inc(v.memset(ones_f[:], 1.0))
        v.wait_ge(ch_bgt.h, 1)
        ch_sv.inc(v.tensor_copy(bgb[:, :], ps_bg[:, 0:8]))
        v.wait_ge(ch_sv.h, ch_sv.n)
    ctx.block.vector(vsw)

    def tsw(t):
        t.wait_ge(ch_sw.h, 16 * (NDC + 2))
        t.wait_ge(ch_sv.h, 2)
        ch_bgt.inc(t.matmul(ps_bg[:, 0:8], ones_f[0:1, 0:128], bgr[0:1, :],
                            start=True, stop=True))
    ctx.block.tensor(tsw)

    cw = ctx.sbuf("moe_cw", [128, NCHUNK * 8], F32)
    gl_dbg = ctx.sbuf("moe_gldbg", [128, NCHUNK * 8], F32)
    pr_dbg = ctx.sbuf("moe_prdbg", [128, NCHUNK * 8], F32)
    m8_dbg = ctx.sbuf("moe_m8dbg", [128, NCHUNK * 8], F32)
    cwT = ctx.sbuf("moe_cwT", [8, T], F32)
    gs = ctx.sbuf("moe_gs", [128, 32], F32)
    probs = ctx.sbuf("moe_probs", [128, 8], F32)
    ps_g = ctx.psum("psC", [128, 512], F32)
    ps_d = ctx.psum("psD", [128, 512], F32)
    ch_gt = ctx.chan("moe_gt")
    ch_gv = ctx.chan("moe_gv")
    ch_gs = ctx.chan("moe_gs")
    CWT_DONE = 30 * (NCHUNK - 1) + 21
    GATE_V_TOTAL = CWT_DONE + 2 * NCHUNK

    def gate_t(t):
        t.wait_ge(che_ln.h, v_ln)
        t.wait_ge(ch_sv.h, 1)
        for i in range(NCHUNK):
            if i >= 1:
                t.wait_ge(ch_gv.h, 30 * (i - 1) + 1)
            mm = None
            for dc in range(NDC):
                lh = xnT[:, dc * T + 128 * i: dc * T + 128 * (i + 1)]
                ll = xnT_lo[:, dc * T + 128 * i: dc * T + 128 * (i + 1)]
                t.matmul(ps_g[:, 0:8], lh, wg_hi[:, dc * 8:(dc + 1) * 8],
                         start=(dc == 0), stop=False)
                t.matmul(ps_g[:, 0:8], ll, wg_hi[:, dc * 8:(dc + 1) * 8], start=False, stop=False)
                mm = t.matmul(ps_g[:, 0:8], lh, wg_lo[:, dc * 8:(dc + 1) * 8], start=False,
                              stop=(dc == NDC - 1))
            ch_gt.inc(mm)
            t.wait_ge(ch_gv.h, 30 * i + 20)
            ch_gt.inc(t.transpose(ps_d[0:8, 0:128], cw[:, i * 8:(i + 1) * 8], identf[:]))
        t.wait_ge(ch_gv.h, CWT_DONE)
        for i in range(NCHUNK):
            if i >= 1:
                t.wait_ge(ch_gv.h, CWT_DONE + 2 * i)
            t.matmul(ps_g[:, 0:512], cwT[0:8, 128 * i:128 * (i + 1)], b2a[0:8, 0:512],
                     start=True, stop=True)
            mm = t.matmul(ps_d[:, 0:512], cwT[0:8, 128 * i:128 * (i + 1)], b2a[0:8, 512:1024],
                          start=True, stop=True)
            ch_gt.inc(mm)

    def gate_s(s_):
        for i in range(NCHUNK):
            s_.wait_ge(ch_gv.h, 30 * i + 3)
            ch_gs.inc(s_.activation(probs[:, :], gs[:, 8:16], AF.Exp,
                                    bias=gs[:, 1:2], accum_out=gs[:, 2:3]))

    def gate_v(v):
        v.wait_ge(ch_sv.h, 3)
        for i in range(NCHUNK):
            b = 30 * i
            v.wait_ge(ch_gt.h, 2 * i + 1)
            ch_gv.inc(v.scalar_tensor_tensor(gs[:, 8:16], ps_g[:, 0:8], 1.0, bgb[:, :],
                                             ALU.mult, ALU.add), b + 1 - ch_gv.n)
            v.wait_ge(ch_gv.h, ch_gv.n)
            v.tensor_copy(gl_dbg[:, i * 8:(i + 1) * 8], gs[:, 8:16])
            ch_gv.inc(v.reduce_max(gs[:, 0:1], gs[:, 8:16], axis=AX.X))
            v.wait_ge(ch_gv.h, ch_gv.n)
            ch_gv.inc(v.tensor_scalar(gs[:, 1:2], gs[:, 0:1], -1.0, None, ALU.mult))
            v.wait_ge(ch_gs.h, i + 1)
            ch_gv.inc(v.reciprocal(gs[:, 3:4], gs[:, 2:3]))
            v.wait_ge(ch_gv.h, ch_gv.n)
            ch_gv.inc(v.tensor_scalar(probs[:, :], probs[:, :], gs[:, 3:4], None, ALU.mult))
            v.wait_ge(ch_gv.h, ch_gv.n)
            v.tensor_copy(pr_dbg[:, i * 8:(i + 1) * 8], probs[:, :])
            ch_gv.inc(v.max(gs[:, 8:16], probs[:, :]))
            v.wait_ge(ch_gv.h, ch_gv.n)
            v.tensor_copy(m8_dbg[:, i * 8:(i + 1) * 8], gs[:, 8:16])
            ch_gv.inc(v.tensor_add(gs[:, 4:5], gs[:, 8:9], gs[:, 9:10]))
            v.wait_ge(ch_gv.h, ch_gv.n)
            ch_gv.inc(v.tensor_scalar(gs[:, 4:5], gs[:, 4:5], 1e-8, None, ALU.add))
            v.wait_ge(ch_gv.h, ch_gv.n)
            ch_gv.inc(v.reciprocal(gs[:, 5:6], gs[:, 4:5]))
            v.wait_ge(ch_gv.h, ch_gv.n)
            ch_gv.inc(v.tensor_mul(gs[:, 6:7], gs[:, 8:9], gs[:, 5:6]))
            ch_gv.inc(v.tensor_mul(gs[:, 7:8], gs[:, 9:10], gs[:, 5:6]))
            v.wait_ge(ch_gv.h, ch_gv.n)
            cwc = cw[:, i * 8:(i + 1) * 8]
            ch_gv.inc(v.tensor_scalar(cwc, probs[:, :], gs[:, 8:9], gs[:, 6:7],
                                      ALU.is_equal, ALU.mult))
            ch_gv.inc(v.tensor_scalar(gs[:, 16:24], probs[:, :], gs[:, 9:10], gs[:, 7:8],
                                      ALU.is_equal, ALU.mult))
            v.wait_ge(ch_gv.h, ch_gv.n)
            mrk = v.tensor_add(cwc, cwc, gs[:, 16:24])
            ch_gv.inc(mrk, b + 20 - ch_gv.n)
            v.wait_ge(ch_gt.h, 2 * i + 2)
            ch_gv.inc(v.tensor_copy(cwT[0:8, 128 * i:128 * (i + 1)], ps_d[0:8, 0:128]))
        for i in range(NCHUNK):
            v.wait_ge(ch_gt.h, 2 * NCHUNK + i + 1)
            xc = xres[:, i * 1024:(i + 1) * 1024]
            ch_gv.inc(v.scalar_tensor_tensor(xc[:, 0:512], ps_g[:, 0:512], 1.0, xc[:, 0:512],
                                             ALU.mult, ALU.add))
            ch_gv.inc(v.scalar_tensor_tensor(xc[:, 512:1024], ps_d[:, 0:512], 1.0,
                                             xc[:, 512:1024], ALU.mult, ALU.add))
    ctx.block.tensor(gate_t)
    ctx.block.scalar(gate_s)
    ctx.block.vector(gate_v)

    W1A = ctx.sbuf("moe_W1A", [128, 8192], BF16)
    W2A = ctx.sbuf("moe_W2A", [128, 8192], BF16)
    W1B = bufB[:, 0:8192]
    W2B = bufB[:, 8192:16384]
    b1s = ctx.sbuf("moe_b1s", [32, 128], F32)
    b1T = ctx.sbuf("moe_b1T", [128, 32], F32)
    hfm = [ctx.sbuf("moe_h0", [128, 4096], BF16), ctx.sbuf("moe_h1", [128, 4096], BF16)]
    ps_out = [ctx.psum("psA", [128, 1024], F32), ctx.psum("psB", [128, 1024], F32)]
    ch_wd = ctx.chan("moe_wd")
    ch_et = ctx.chan("moe_et")
    ch_ev = ctx.chan("moe_ev")
    ch_es = ctx.chan("moe_es")
    ch_bt = ctx.chan("moe_bt")
    W1 = waps["L0_ffn_W1"]; W2 = waps["L0_ffn_W2"]; B1 = waps["L0_ffn_b1"]

    def wdma(g):
        for e in range(NE):
            g.wait_ge(ch_bt.h, 2 * e)
            ch_wd.inc(g.dma_start(out=b1s[:, :], in_=B1[e].rearrange("(c p) -> c p", p=128)), 16)
            for fb in range(NFB):
                idx = 4 * e + fb
                w1t = [W1A, W1B][idx % 2]
                w2t = [W2A, W2B][idx % 2]
                if idx == 1:
                    g.wait_ge(ch_gt.h, 2 * NCHUNK)
                if idx >= 2:
                    g.wait_ge(ch_et.h, 32 * (idx - 1))
                for dc in range(NDC):
                    ch_wd.inc(g.dma_start(out=w1t[:, dc * 1024:(dc + 1) * 1024],
                                          in_=W1[e, dc * 128:(dc + 1) * 128,
                                                 fb * 1024:(fb + 1) * 1024]), 16)
                for fc in range(NDC):
                    ch_wd.inc(g.dma_start(out=w2t[:, fc * 1024:(fc + 1) * 1024],
                                          in_=W2[e, fb * 1024 + fc * 128:
                                                 fb * 1024 + (fc + 1) * 128, 0:1024]), 16)
    ctx.block.gpsimd(wdma)

    tpsAf = ctx.psum("tpsA", [128, 1024], BF16)[:, :].bitcast(F32)
    tpsBf = ctx.psum("tpsB", [128, 1024], BF16)[:, :].bitcast(F32)
    w1slots = [ps_g[:, 0:256], ps_d[:, 0:256], tpsAf[:, 0:256], tpsBf[:, 0:256]]

    def exp_t(t):
        t.wait_ge(ch_gv.h, GATE_V_TOTAL)
        dma_n = 0
        for e in range(NE):
            dma_n += 16
            t.wait_ge(ch_wd.h, dma_n)
            if e >= 1:
                t.wait_ge(ch_es.h, 32 * 8 * e)
            ch_bt.inc(t.transpose(ps_d[0:128, 0:32], b1s[0:32, 0:128], identf[0:32, 0:32]))
            t.wait_ge(ch_bt.h, 2 * e + 2)
            for fb in range(NFB):
                idx = 4 * e + fb
                w1t = [W1A, W1B][idx % 2]
                w2t = [W2A, W2B][idx % 2]
                dma_n += 16 * 16
                t.wait_ge(ch_wd.h, dma_n)
                for tpp in range(2):
                    grp = idx * 2 + tpp
                    for fc in range(NDC):
                        for dc in range(NDC):
                            for sl in range(4):
                                if dc == 0:
                                    if fc >= 1:
                                        t.wait_ge(ch_es.h, 32 * grp + 4 * (fc - 1) + sl + 1)
                                    elif grp >= 1:
                                        t.wait_ge(ch_es.h, 32 * grp - 4 + sl + 1)
                                tp2, half = sl // 2, sl % 2
                                tok0 = tpp * 1024 + tp2 * 512 + half * 256
                                mm = t.matmul(w1slots[sl],
                                              w1t[:, dc * 1024 + fc * 128:
                                                  dc * 1024 + (fc + 1) * 128],
                                              xnT[:, dc * T + tok0: dc * T + tok0 + 256],
                                              start=(dc == 0), stop=(dc == NDC - 1))
                        ch_et.inc(mm)
                    t.wait_ge(ch_es.h, 32 * grp + 32)
                    for tt in range(8):
                        pso = ps_out[tt % 2]
                        otm = 8 * grp + tt
                        if otm >= 2:
                            t.wait_ge(ch_ev.h, otm - 1)
                        hsrc = hfm[tt // 4]
                        for fc in range(NDC):
                            lhs = hsrc[:, fc * 512 + (tt % 4) * 128:
                                       fc * 512 + (tt % 4) * 128 + 128]
                            t.matmul(pso[:, 0:512], lhs, w2t[:, fc * 1024: fc * 1024 + 512],
                                     start=(fc == 0), stop=(fc == NDC - 1))
                            mm = t.matmul(pso[:, 512:1024], lhs,
                                          w2t[:, fc * 1024 + 512:(fc + 1) * 1024],
                                          start=(fc == 0), stop=(fc == NDC - 1))
                        ch_et.inc(mm)

    def exp_s(s_):
        for e in range(NE):
            s_.wait_ge(ch_bt.h, 2 * e + 2)
            for fb in range(NFB):
                idx = 4 * e + fb
                for tpp in range(2):
                    grp = idx * 2 + tpp
                    for fc in range(NDC):
                        s_.wait_ge(ch_et.h, 16 * grp + fc + 1)
                        for sl in range(4):
                            tp2, half = sl // 2, sl % 2
                            ch_es.inc(s_.activation(
                                hfm[tp2][:, fc * 512 + half * 256: fc * 512 + (half + 1) * 256],
                                w1slots[sl], AF.Gelu_apprx_tanh,
                                bias=b1T[:, fb * 8 + fc: fb * 8 + fc + 1]))

    def exp_v(v):
        for e in range(NE):
            v.wait_ge(ch_bt.h, 2 * e + 1)
            ch_bt.inc(v.tensor_copy(b1T[:, :], ps_d[0:128, 0:32]))
            for fb in range(NFB):
                idx = 4 * e + fb
                for tpp in range(2):
                    grp = idx * 2 + tpp
                    for tt in range(8):
                        otm = 8 * grp + tt
                        v.wait_ge(ch_et.h, 16 * grp + 8 + tt + 1)
                        i_chunk = tpp * 8 + tt
                        xc = xres[:, i_chunk * 1024:(i_chunk + 1) * 1024]
                        pso = ps_out[tt % 2]
                        ch_ev.inc(v.scalar_tensor_tensor(
                            xc, pso[:, 0:1024], cw[:, i_chunk * 8 + e: i_chunk * 8 + e + 1],
                            xc, ALU.mult, ALU.add))
    ctx.block.tensor(exp_t)
    ctx.block.scalar(exp_s)
    ctx.block.vector(exp_v)
    if MOE_GATE_ONLY:
        return ch_gv, GATE_V_TOTAL
    return ch_ev, ch_ev.n





def emit_l1_ffn(ctx, w_ap):
    xres = ctx.tiles["xres"]
    zn = ctx.sbuf("l1f_ln_zn", [128, NCHUNK * 1024], BF16)
    zn_lo = ctx.sbuf("l1f_ln_znlo", [128, NCHUNK * 1024], BF16)
    ch_ln, v_ln = emit_ln(ctx, "l1f_ln", lambda i: xres[:, i * 1024:(i + 1) * 1024],
                          lambda i: zn[:, i * 1024:(i + 1) * 1024],
                          lo_cols=lambda i: zn_lo[:, i * 1024:(i + 1) * 1024])

    wld = ctx.sbuf("l1f_xh", [128, 1024], F32)
    wsc = ctx.sbuf("scr_lnsq", [128, 1024], F32)
    tern = ctx.sbuf("l1f_tern", [128, NDC * 1024], BF16)
    ternT = ctx.sbuf("l1f_ternT", [128, NDC * 1024], BF16)
    wstat = ctx.sbuf("l1f_wstat", [128, 4], F32)
    wscols = ctx.sbuf("l1f_wscols", [128, 8], F32)
    wsr8 = ctx.sbuf("l1f_wsr8", [8, 128], F32)
    ws_b = ctx.sbuf("l1f_wsb", [128, 1024], F32)
    e8 = ctx.sbuf("l1f_e8", [8, 1024], F32)
    ident = ctx.sbuf("ident", [128, 128], BF16)
    ps_bc = ctx.psum("psA", [128, 1024], F32)
    ps_ws = ctx.psum("psB", [128, 1024], F32)
    e8np = np.zeros((8, 1024), dtype=np.float32)
    for _oc in range(8):
        e8np[_oc, _oc * 128:(_oc + 1) * 128] = 1.0
    e8c = ctx.const("e8", e8np)
    ch_wd = ctx.chan("l1f_wd")
    ch_wv = ctx.chan("l1f_wv")
    ch_mt = ctx.chan("l1f_mt")
    ch_mv = ctx.chan("l1f_mv")

    def gsec(g):
        for oc in range(NDC):
            if oc >= 1:
                g.wait_ge(ch_wv.h, 5 * (oc - 1) + 5)
            ch_wd.inc(g.dma_start(out=wld[:], in_=w_ap[oc * 128:(oc + 1) * 128, :]), 16)
    ctx.block.gpsimd(gsec)

    def vsec(v):
        for oc in range(NDC):
            v.wait_ge(ch_wd.h, 16 * (oc + 1))
            chv_ = ch_wv
            chv_.inc(v.reduce_sum(wstat[:, 0:1], wld[:], axis=AX.X,
                                  apply_absolute_value=True))
            v.wait_ge(chv_.h, chv_.n)
            chv_.inc(v.tensor_scalar(wscols[:, oc:oc + 1], wstat[:, 0:1], 1.0 / D, 1e-5,
                                     ALU.mult, ALU.max))
            v.wait_ge(chv_.h, chv_.n)
            chv_.inc(v.reciprocal(wstat[:, 2:3], wscols[:, oc:oc + 1]))
            v.wait_ge(chv_.h, chv_.n)
            v.tensor_scalar(wld[:], wld[:], wstat[:, 2:3], None, ALU.mult)
            v.tensor_scalar(wsc[:], wld[:], 0.5, None, ALU.is_gt)
            chv_.inc(v.tensor_scalar(wld[:], wld[:], -0.5, None, ALU.is_lt))
            chv_.inc(v.tensor_sub(tern[:, oc * 1024:(oc + 1) * 1024], wsc[:], wld[:]))
    ctx.block.vector(vsec)

    pairs = []
    for oc in range(NDC):
        for ic in range(NDC):
            pairs.append((tern[:, oc * 1024 + ic * 128: oc * 1024 + (ic + 1) * 128],
                          ternT[:, ic * 1024 + oc * 128: ic * 1024 + (oc + 1) * 128]))
    ctx._trans_wait = (ch_wv, ch_wv.n)
    ch_tT, v_tT = emit_transposes(ctx, pairs)

    ch_e8 = ctx.chan("l1f_e8d")

    def gsec2(g):
        ch_e8.inc(g.dma_start(out=e8[:, :], in_=e8c[:, :]), 16)
    ctx.block.gpsimd(gsec2)

    def tsecW(t):
        t.wait_ge(ch_wv.h, 5 * NDC)
        ch_mt.inc(t.transpose(ps_ws[0:8, 0:128], wscols[:, 0:8], ctx.tiles["identf"][:]))
        t.wait_ge(ch_mv.h, 1)
        t.wait_ge(ch_e8.h, 16)
        for oc in range(NDC):
            ch_mt.inc(t.matmul(ps_bc[:, oc * 128:(oc + 1) * 128],
                               e8[:, oc * 128:(oc + 1) * 128], wsr8[0:8, :],
                               start=True, stop=True))

    def vsecW(v):
        v.wait_ge(ch_mt.h, 1)
        ch_mv.inc(v.tensor_copy(wsr8[:, :], ps_ws[0:8, 0:128]))
        v.wait_ge(ch_mt.h, 9)
        ch_mv.inc(v.tensor_copy(ws_b[:, 0:512], ps_bc[:, 0:512]))
        ch_mv.inc(v.tensor_copy(ws_b[:, 512:1024], ps_bc[:, 512:1024]))
    ctx.block.tensor(tsecW)
    ctx.block.vector(vsecW)

    Hc = ctx.const("hadamard", _hadamard_bf16)
    Hs = ctx.sbuf("l1f_H", [128, NDC * 1024], BF16)
    ch_hd = ctx.chan("l1f_hd")

    def sdma(s):
        for kc in range(NDC):
            ch_hd.inc(s.dma_start(out=Hs[:, kc * 1024:(kc + 1) * 1024],
                                  in_=Hc[kc * 128:(kc + 1) * 128, :]), 16)
    ctx.block.scalar(sdma)

    xh = wld
    q2 = ctx.sbuf("l1f_q2", [128, 1024], BF16)
    q2T = ctx.sbuf("l1f_q2T", [128, 1024], BF16)
    znTt = ctx.sbuf("l1f_znTt", [128, 1024], BF16)
    znTt_lo = ctx.sbuf("l1f_znTt_lo", [128, 1024], BF16)
    qstat = ctx.sbuf("l1f_qstat", [128, 4], F32)
    scl = ctx.sbuf("l1f_scl", [128, NCHUNK], F32)
    ps_h = ctx.psum("psA", [128, 1024], F32)
    ps_o = ctx.psum("psB", [128, 1024], F32)
    ps_t = [ctx.psum("tpsA", [128, 1024], BF16), ctx.psum("tpsB", [128, 1024], BF16)]
    ch_t = ctx.chan("l1f_t")
    ch_v = ctx.chan("l1f_v")

    def tsec(t):
        t.wait_ge(ch_hd.h, 16 * NDC)
        t.wait_ge(ch_ln.h, v_ln)
        t.wait_ge(ch_tT.h, v_tT)
        t.wait_ge(ch_mv.h, 3)
        for i in range(NCHUNK):
            tb = 26 * i
            if i >= 1:
                t.wait_ge(ch_v.h, 100 * i)
            for k in range(16):
                src = zn if k < 8 else zn_lo
                dc = k % 8
                if k >= 2:
                    t.wait_ge(ch_v.h, 100 * i + k - 1)
                ch_t.inc(t.transpose(ps_t[k % 2][:, 0:128],
                                     src[:, i * 1024 + dc * 128: i * 1024 + (dc + 1) * 128],
                                     ident[:]))
            t.wait_ge(ch_v.h, 100 * i + 16)
            for hl in range(2):
                srcT = znTt if hl == 0 else znTt_lo
                for kc in range(NDC):
                    lhs = srcT[:, kc * 128:(kc + 1) * 128]
                    t.matmul(ps_h[:, 0:512], lhs, Hs[:, kc * 1024: kc * 1024 + 512],
                             start=(hl == 0 and kc == 0), stop=(hl == 1 and kc == NDC - 1))
                    mm = t.matmul(ps_h[:, 512:1024], lhs, Hs[:, kc * 1024 + 512:(kc + 1) * 1024],
                                  start=(hl == 0 and kc == 0), stop=(hl == 1 and kc == NDC - 1))
            ch_t.inc(mm)
            t.wait_ge(ch_v.h, 100 * i + 30)
            for dc in range(NDC):
                if dc >= 2:
                    t.wait_ge(ch_v.h, 100 * i + 30 + dc - 1)
                ch_t.inc(t.transpose(ps_t[dc % 2][:, 0:128],
                                     q2[:, dc * 128:(dc + 1) * 128], ident[:]))
            t.wait_ge(ch_v.h, 100 * i + 38)
            for ic in range(NDC):
                lhs = q2T[:, ic * 128:(ic + 1) * 128]
                t.matmul(ps_o[:, 0:512], lhs, ternT[:, ic * 1024: ic * 1024 + 512],
                         start=(ic == 0), stop=(ic == NDC - 1))
                mm = t.matmul(ps_o[:, 512:1024], lhs, ternT[:, ic * 1024 + 512:(ic + 1) * 1024],
                              start=(ic == 0), stop=(ic == NDC - 1))
            ch_t.inc(mm)

    def vsec2(v):
        for i in range(NCHUNK):
            tb = 26 * i
            for k in range(16):
                dstT = znTt if k < 8 else znTt_lo
                dc = k % 8
                v.wait_ge(ch_t.h, tb + k + 1)
                mrk = v.tensor_copy(dstT[:, dc * 128:(dc + 1) * 128], ps_t[k % 2][:, 0:128])
                ch_v.inc(mrk, 100 * i + k + 1 - ch_v.n)
            v.wait_ge(ch_t.h, tb + 17)
            v.tensor_copy(xh[:], ps_h[:, 0:1024])
            ch_v.inc(v.reduce_max(qstat[:, 0:1], xh[:], axis=AX.X, apply_absolute_value=True))
            v.wait_ge(ch_v.h, ch_v.n)
            ch_v.inc(v.tensor_scalar(qstat[:, 1:2], qstat[:, 0:1], 1e-5, None, ALU.max))
            v.wait_ge(ch_v.h, ch_v.n)
            ch_v.inc(v.reciprocal(qstat[:, 2:3], qstat[:, 1:2]))
            v.wait_ge(ch_v.h, ch_v.n)
            ch_v.inc(v.tensor_scalar(qstat[:, 2:3], qstat[:, 2:3], 7.0, None, ALU.mult))
            ch_v.inc(v.tensor_scalar(scl[:, i:i + 1], qstat[:, 1:2], 1.0 / 7.0, None, ALU.mult))
            v.wait_ge(ch_v.h, ch_v.n)
            v.tensor_scalar(xh[:], xh[:], qstat[:, 2:3], 12582912.0, ALU.mult, ALU.add)
            v.tensor_scalar(xh[:], xh[:], 12582912.0, None, ALU.subtract)
            mrk = v.tensor_scalar(q2[:], xh[:], -8.0, 7.0, ALU.max, ALU.min)
            ch_v.inc(mrk, 100 * i + 30 - ch_v.n)
            for dc in range(NDC):
                v.wait_ge(ch_t.h, tb + 18 + dc)
                ch_v.inc(v.tensor_copy(q2T[:, dc * 128:(dc + 1) * 128], ps_t[dc % 2][:, 0:128]))
            v.wait_ge(ch_t.h, tb + 26)
            xc = xres[:, i * 1024:(i + 1) * 1024]
            v.scalar_tensor_tensor(xh[:], ps_o[:, 0:1024], scl[:, i:i + 1], ws_b[:, 0:1024],
                                   ALU.mult, ALU.mult)
            mrk = v.tensor_add(xc, xc, xh[:])
            ch_v.inc(mrk, 100 * (i + 1) - ch_v.n)
    ctx.block.tensor(tsec)
    ctx.block.vector(vsec2)
    return ch_v, ch_v.n





def _l0a_consts():
    H, e = 8, 128
    half = e // 2
    gam = 1.0 - 2.0 ** (-5.0 - np.arange(H, dtype=np.float64))
    j = np.arange(half, dtype=np.float64)
    base = (2 * j + 0.4 * e) / (1.4 * e)
    pos = np.arange(T, dtype=np.float64)
    scale = base[None, :] ** (pos[:, None] / 512.0)
    inv_freq = 1.0 / 10000.0 ** (j / half)
    sinu = pos[:, None] * inv_freq[None, :]
    sn, cs = np.sin(sinu), np.cos(sinu)
    bf = ml_dtypes.bfloat16
    f2b = lambda a: a.astype(np.float32).astype(bf)
    tt_ = np.arange(128)
    diff = tt_[None, :] - tt_[:, None]
    D_ = np.zeros((128, H * 128))
    XI = np.zeros((128, H * 128))
    for h in range(H):
        D_[:, h * 128:(h + 1) * 128] = np.where(diff >= 0, gam[h] ** np.maximum(diff, 0), 0.0)
        XI[:, h * 128:(h + 1) * 128] = (gam[h] ** (tt_ + 1.0))[None, :]
    return dict(
        l0a_cq=f2b((cs * scale).T), l0a_sq=f2b((sn * scale).T),
        l0a_ck=f2b((cs / scale).T), l0a_sk=f2b((sn / scale).T),
        l0a_D=f2b(D_), l0a_XI=f2b(XI),
        l0a_gdec=(gam[None, :] ** (127.0 - tt_[:, None])).astype(np.float32),
        l0a_gc=np.broadcast_to(gam[None, :] ** 128.0, (128, H)).astype(np.float32).copy())


class TrB:

    def __init__(self, ctx):
        self.ctx = ctx
        self.cht = ctx.chan("trb_t")
        self.che = ctx.chan("trb_e")
        self.ps = [ctx.psum("tpsA", [128, 1024], BF16), ctx.psum("tpsB", [128, 1024], BF16)]
        self.ident = ctx.sbuf("ident", [128, 128], BF16)

    def emit(self, pairs, twait=(), vwait=(), self_wait=True):
        assert len(pairs) <= 16
        prev = self.che.n
        t0 = self.cht.n
        ps = self.ps
        ident = self.ident
        cht, che = self.cht, self.che

        def tsec(t):
            t.wait_ge(che.h, prev)
            for ch, val in twait:
                t.wait_ge(ch.h, val)
            for k, (src, dst) in enumerate(pairs):
                slot = ps[k // 8][0:src.shape[-1], (k % 8) * 128:(k % 8) * 128 + src.shape[0]]
                cht.inc(t.transpose(slot, src, ident[:]))

        nb = min(len(pairs), 8)

        def vsec(v):
            for ch, val in vwait:
                v.wait_ge(ch.h, val)
            for k, (src, dst) in enumerate(pairs):
                v.wait_ge(cht.h, t0 + (nb if k < 8 else len(pairs)))
                slot = ps[k // 8][0:src.shape[-1], (k % 8) * 128:(k % 8) * 128 + src.shape[0]]
                che.inc(v.tensor_copy(dst, slot))
            if self_wait:
                v.wait_ge(che.h, che.n)
        self.ctx.block.tensor(tsec)
        self.ctx.block.vector(vsec)
        return self.che.n


def emit_l0_attn(ctx, waps):
    nc = ctx.nc
    y = ctx.tiles["xres"]
    identf = ctx.tiles["identf"]
    H, NST, STW = 8, 8, 256
    EPSI = 1e-6

    for k, v in _l0a_consts().items():
        ctx.const(k, v)
    CAP = ctx.const_aps

    spill = nc.dram_tensor("l0a_spill", [T, D], F32, kind="Internal").ap()
    trb = TrB(ctx)

    u = ctx.sbuf("l0a_u", [128, 16384], BF16)
    qT = u[:, 0:2048]
    kTt = u[:, 2048:4096]
    vtok = u[:, 4096:6144]
    ring = u[:, 4096:6144]
    gtok = u[:, 6144:8192]
    ktok = u[:, 8192:9216]
    Pt = u[:, 9216:10240]
    qxi = u[:, 10240:11264]
    zT = u[:, 11264:13312]
    aT = u[:, 13312:15360]
    xt1 = u[0:64, 15360:16384]
    xt2 = u[64:128, 15360:16384]
    x2T = u[:, 0:16384]
    scr = ctx.sbuf("scr_lnsq", [128, 1024], F32)
    heads = scr
    rowbuf = scr[0:1, 0:1024]
    scrb = ctx.sbuf("l0a_scrb", [128, 1024], BF16)
    sg = scrb
    scrb64 = scrb[0:64, 0:1024]
    St = ctx.sbuf("l0a_S", [128, H * 128], F32)
    Sbf = ctx.sbuf("l0a_Sbf", [128, H * 128], BF16)
    Dm = ctx.sbuf("l0a_Dm", [128, 1024], BF16)
    Xi = ctx.sbuf("l0a_Xi", [128, 1024], BF16)
    gdec = ctx.sbuf("l0a_gdec", [128, 8], F32)
    gc = ctx.sbuf("l0a_gc", [128, 8], F32)
    tabs = ctx.sbuf("l0a_tabs", [128, 4 * 256], BF16)
    st8 = ctx.sbuf("l0a_st8", [128, 48], F32)
    lnst = ctx.sbuf("l0a_lnst", [128, 80], F32)
    bc = ctx.sbuf("l0a_bc", [128, 4 * 1024], BF16)
    bc2 = bc[:, 0:2048]
    onesf = ctx.sbuf("l0a_onesf", [1, 128], F32)
    wq_s = ctx.sbuf("l0a_wq", [128, NDC * 1024], BF16)
    wk_s = ctx.sbuf("l0a_wk", [128, NDC * 1024], BF16)
    wv_s = ctx.sbuf("l0a_wv", [128, NDC * 1024], BF16)
    wg_s = ctx.sbuf("l0a_wg", [128, NDC * 1024], BF16)
    wo_s = ctx.sbuf("l0a_wo", [128, NDC * 1024], BF16)
    w1f = [wq_s, wk_s]
    w2f = [wv_s, wg_s]
    hfm = St[:, :].bitcast(BF16)

    psA = ctx.psum("psA", [128, 1024], F32)
    psB = ctx.psum("psB", [128, 1024], F32)
    psC = ctx.psum("psC", [128, 512], F32)
    psD = ctx.psum("psD", [128, 512], F32)

    ch_sp = ctx.chan("l0a_sp")
    ch_wd = ctx.chan("l0a_wd")
    ch_ld = ctx.chan("l0a_ld")
    ch_t = ctx.chan("l0a_t")
    ch_v = ctx.chan("l0a_v")
    ch_s = ctx.chan("l0a_s")

    SUMS, SS, MEAN, VAR, RSTD, M2 = 0, 16, 32, 48, 64, 0

    def emit_stats(eps, vwait=None):
        v0, s0 = ch_v.n, ch_s.n

        def ssec1(s_):
            if vwait is not None:
                s_.wait_ge(vwait[0].h, vwait[1])
            for i in range(NCHUNK):
                mrk = s_.activation(scrb[:], y[:, i * 1024:(i + 1) * 1024], AF.Square,
                                    accum_out=lnst[:, SS + i:SS + i + 1])
            ch_s.inc(mrk)

        def vsec1(v):
            if vwait is not None:
                v.wait_ge(vwait[0].h, vwait[1])
            for i in range(NCHUNK):
                mrk = v.reduce_sum(lnst[:, SUMS + i:SUMS + i + 1],
                                   y[:, i * 1024:(i + 1) * 1024], axis=AX.X)
            ch_v.inc(mrk)
            v.wait_ge(ch_v.h, v0 + 1)
            ch_v.inc(v.tensor_scalar(lnst[:, MEAN:MEAN + 16], lnst[:, SUMS:SUMS + 16],
                                     1.0 / D, None, ALU.mult))
            v.wait_ge(ch_v.h, v0 + 2)
            ch_v.inc(v.tensor_mul(lnst[:, M2:M2 + 16], lnst[:, MEAN:MEAN + 16],
                                  lnst[:, MEAN:MEAN + 16]))
            v.wait_ge(ch_s.h, s0 + 1)
            ch_v.inc(v.tensor_scalar(lnst[:, VAR:VAR + 16], lnst[:, SS:SS + 16],
                                     1.0 / D, eps, ALU.mult, ALU.add))
            v.wait_ge(ch_v.h, v0 + 4)
            ch_v.inc(v.tensor_sub(lnst[:, VAR:VAR + 16], lnst[:, VAR:VAR + 16],
                                  lnst[:, M2:M2 + 16]))

        def ssec2(s_):
            s_.wait_ge(ch_v.h, v0 + 5)
            ch_s.inc(s_.activation(lnst[:, VAR:VAR + 16], lnst[:, VAR:VAR + 16], AF.Sqrt))

        def vsec2(v):
            v.wait_ge(ch_s.h, s0 + 2)
            ch_v.inc(v.reciprocal(lnst[:, RSTD:RSTD + 16], lnst[:, VAR:VAR + 16]))
            v.wait_ge(ch_v.h, v0 + 6)
        ctx.block.scalar(ssec1)
        ctx.block.vector(vsec1)
        ctx.block.scalar(ssec2)
        ctx.block.vector(vsec2)
        return ch_v.n

    def spill_out(e):
        xr = spill.rearrange("(c p) d -> c p d", p=128)
        for i in range(NCHUNK):
            ch_sp.inc(e.dma_start(out=xr[i], in_=y[:, i * 1024:(i + 1) * 1024]), 16)
    ctx.block.sync(spill_out)
    emit_stats(1e-5)

    def outer_apply(v):
        v.wait_ge(ch_sp.h, 16 * NCHUNK)
        for i in range(NCHUNK):
            mrk = v.tensor_scalar(y[:, i * 1024:(i + 1) * 1024], y[:, i * 1024:(i + 1) * 1024],
                                  lnst[:, MEAN + i:MEAN + i + 1], lnst[:, RSTD + i:RSTD + i + 1],
                                  ALU.subtract, ALU.mult)
        ch_v.inc(mrk)
        v.wait_ge(ch_v.h, ch_v.n)
    ctx.block.vector(outer_apply)
    y_ready_v = ch_v.n

    last_stage1_mm = [0]
    last_ffn_mm = {}

    WQ, WK, WV = waps["L0_attn_Wq"], waps["L0_attn_Wk"], waps["L0_attn_Wv"]
    WG, WO = waps["L0_attn_Wg"], waps["L0_attn_Wo"]
    FW1, FW2 = waps["L0_attn_ffn_w1"], waps["L0_attn_ffn_w2"]

    for li in range(L0A_LAYERS):
        gate_t_val = ch_t.n

        def wdma(g, li=li, gate=gate_t_val):
            if gate:
                g.wait_ge(ch_t.h, gate)
            for wdst, wsrc3 in ((wq_s, WQ), (wk_s, WK), (wv_s, WV)):
                for h in range(8):
                    dst = wdst[:, h * 128: h * 128 + 7 * 1024 + 128]
                    dst = dst.rearrange("p (dc e) -> p dc e", dc=8) if False else \
                        wdst[:, :].rearrange("p (dc he) -> p dc he", dc=8)[:, :, h * 128:(h + 1) * 128]
                    ch_wd.inc(g.dma_start(
                        out=dst,
                        in_=wsrc3[li, h].rearrange("(dc p) e -> p dc e", p=128)), 16)
            for wdst, wsrc2 in ((wg_s, WG), (wo_s, WO)):
                ch_wd.inc(g.dma_start(
                    out=wdst[:, :].rearrange("p (dc m) -> p dc m", dc=8),
                    in_=wsrc2[li].rearrange("(dc p) m -> p dc m", p=128)), 16)
        ctx.block.gpsimd(wdma)
        wd_stage1 = ch_wd.n

        if li == 0:
            def cdma(e):
                ch_ld.inc(e.dma_start(out=Dm[:, :], in_=CAP["l0a_D"][:, :]), 16)
                ch_ld.inc(e.dma_start(out=Xi[:, :], in_=CAP["l0a_XI"][:, :]), 16)
                ch_ld.inc(e.dma_start(out=gdec[:, :], in_=CAP["l0a_gdec"][:, :]), 16)
                ch_ld.inc(e.dma_start(out=gc[:, :], in_=CAP["l0a_gc"][:, :]), 16)
            ctx.block.scalar(cdma)
            CONST_LD = ch_ld.n

        rows = [("L0_attn_ln1_scale", bc, 0), ("L0_attn_ln1_bias", bc, 1),
                ("L0_attn_gn_scale", bc, 2), ("L0_attn_gn_bias", bc, 3)]

        def bc_ones(v):
            ch_v.inc(v.memset(onesf[:], 1.0))
            v.wait_ge(ch_v.h, ch_v.n)
        if li == 0:
            ctx.block.vector(bc_ones)
        def emit_rows(rowlist, li=li):
            for rname, dstt, slot in rowlist:
                r0d = ch_ld.n

                def rdma(e, rname=rname, li=li, tg=ch_t.n, vg=ch_v.n):
                    if tg:
                        e.wait_ge(ch_t.h, tg)
                    if vg:
                        e.wait_ge(ch_v.h, vg)
                    ch_ld.inc(e.dma_start(out=rowbuf[0:1, :], in_=waps[rname][li:li + 1, :]), 16)
                t0m = ch_t.n

                def rmm(t, r0d=r0d, gate=ch_v.n):
                    t.wait_ge(ch_ld.h, r0d + 16)
                    t.wait_ge(ch_v.h, gate)
                    ch_t.inc(t.matmul(psC[:, 0:512], onesf[0:1, 0:128], rowbuf[0:1, 0:512],
                                      start=True, stop=True))
                    ch_t.inc(t.matmul(psD[:, 0:512], onesf[0:1, 0:128], rowbuf[0:1, 512:1024],
                                      start=True, stop=True))

                def rev(v, dstt=dstt, slot=slot, t0m=t0m):
                    v.wait_ge(ch_t.h, t0m + 2)
                    v.tensor_copy(dstt[:, slot * 1024:slot * 1024 + 512], psC[:, 0:512])
                    ch_v.inc(v.tensor_copy(dstt[:, slot * 1024 + 512:(slot + 1) * 1024], psD[:, 0:512]))
                    v.wait_ge(ch_v.h, ch_v.n)
                ctx.block.scalar(rdma)
                ctx.block.tensor(rmm)
                ctx.block.vector(rev)
        emit_rows(rows)
        bc_done_v = ch_v.n

        waitv = (ch_v, y_ready_v) if li == 0 else (ch_v, last_ffn_add[0])
        emit_stats(EPSI, vwait=waitv)
        ln1_stats_v = ch_v.n

        def zeroS(v):
            v.memset(St[:, :], 0.0)
            ch_v.inc(v.memset(Sbf[:, :], 0.0))
            v.wait_ge(ch_v.h, ch_v.n)
        ctx.block.vector(zeroS)

        last_wo_ev = [ch_v.n]
        tab_gate = {0: 0}
        for st in range(L0A_NST):
            def tdma(e, st=st, tg=tab_gate[0]):
                if tg:
                    e.wait_ge(ch_v.h, tg)
                for ti, cap in enumerate((CAP["l0a_cq"], CAP["l0a_sq"], CAP["l0a_ck"], CAP["l0a_sk"])):
                    e.dma_start(out=tabs[0:64, ti * 256:(ti + 1) * 256],
                                in_=cap[:, st * STW:(st + 1) * STW]).then_inc(ch_ld.h, 16)
                    ch_ld.n += 16
                    ch_ld.inc(e.dma_start(out=tabs[64:128, ti * 256:(ti + 1) * 256],
                                          in_=cap[:, st * STW:(st + 1) * STW]), 16)
            ctx.block.scalar(tdma)
            tabs_rdy = ch_ld.n
            CQ = (tabs[0:64, 0:256], tabs[64:128, 0:256])
            SQ = (tabs[0:64, 256:512], tabs[64:128, 256:512])
            CK = (tabs[0:64, 512:768], tabs[64:128, 512:768])
            SK = (tabs[0:64, 768:1024], tabs[64:128, 768:1024])

            def lnap(v, st=st, trg=trb.cht.n, ttg=ch_t.n):
                if trg:
                    v.wait_ge(trb.cht.h, trg)
                if ttg:
                    v.wait_ge(ch_t.h, ttg)
                for cl in range(2):
                    i = st * 2 + cl
                    yc = y[:, i * 1024:(i + 1) * 1024]
                    rg = ring[:, cl * 1024:(cl + 1) * 1024]
                    v.tensor_scalar(scr[:], yc, lnst[:, MEAN + i:MEAN + i + 1],
                                    lnst[:, RSTD + i:RSTD + i + 1], ALU.subtract, ALU.mult)
                    v.tensor_mul(scr[:], scr[:], bc[:, 0:1024])
                    ch_v.inc(v.tensor_add(rg, scr[:], bc[:, 1024:2048]))
                v.wait_ge(ch_v.h, ch_v.n)
            ctx.block.vector(lnap)
            ring_rdy = ch_v.n

            if L0A_ST_PARTS < 2:
                continue
            pairs = []
            for cl in range(2):
                for dc in range(NDC):
                    pairs.append((ring[:, cl * 1024 + dc * 128: cl * 1024 + (dc + 1) * 128],
                                  aT[:, dc * STW + cl * 128: dc * STW + (cl + 1) * 128]))
            aT_rdy = trb.emit(pairs, twait=[(ch_v, ring_rdy)], vwait=[(ch_t, ch_t.n)])

            for which, wsrc, rotdst, CT, STb in (() if L0A_ST_PARTS < 3 else (
                    ("q", wq_s, qT, CQ, SQ), ("k", wk_s, kTt, CK, SK))):
                rawdst = rotdst
                t0m = ch_t.n

                def qmm(t, wsrc=wsrc, aT_rdy=aT_rdy, gate=last_wo_ev[0], wd=wd_stage1):
                    t.wait_ge(trb.che.h, aT_rdy)
                    t.wait_ge(ch_wd.h, wd)
                    t.wait_ge(ch_v.h, gate)
                    for h in range(H):
                        pst = [psA, psB][h // 4]
                        col = (h % 4) * 256
                        for dc in range(NDC):
                            mm = t.matmul(pst[:, col:col + 256],
                                          wsrc[:, dc * 1024 + h * 128:(dc * 1024 + (h + 1) * 128)],
                                          aT[:, dc * STW: dc * STW + 256],
                                          start=(dc == 0), stop=(dc == NDC - 1))
                        ch_t.inc(mm)

                def qev(v, rawdst=rawdst, t0m=t0m, tabs_rdy=tabs_rdy):
                    for h in range(H):
                        v.wait_ge(ch_t.h, t0m + 2 * (h // 2 + 1))
                        pst = [psA, psB][h // 4]
                        col = (h % 4) * 256
                        ch_v.inc(v.tensor_copy(rawdst[:, h * STW:(h + 1) * STW], pst[:, col:col + 256]))
                    v.wait_ge(ch_v.h, ch_v.n)

                def qrot(v, rotdst=rotdst, CT=CT, STb=STb, tabs_rdy=tabs_rdy):
                    v.wait_ge(ch_ld.h, tabs_rdy)
                    for hh in range(2):
                        cols = slice(hh * 1024, (hh + 1) * 1024)
                        r4 = lambda ap: ap.rearrange("p (a b) -> p a b", a=4)
                        bt = lambda ap: ap.rearrange("p (a t) -> p a t", a=1).broadcast_to((64, 4, 256))
                        rlo = r4(rotdst[0:64, cols])
                        rhi = r4(rotdst[64:128, cols])
                        c0, c64 = bt(CT[0]), bt(CT[1])
                        s0, s64 = bt(STb[0]), bt(STb[1])
                        t1 = r4(xt1[0:64, 0:1024])
                        t2 = r4(xt2[0:64, 0:1024])
                        u0 = r4(scrb[0:64, 0:1024])
                        u64 = r4(scrb[64:128, 0:1024])
                        v.tensor_tensor(t1, rlo, c0, ALU.mult)
                        v.tensor_tensor(t2, rlo, s0, ALU.mult)
                        v.tensor_tensor(u0, rhi, s64, ALU.mult)
                        v.tensor_sub(rlo, t1, u0)
                        v.tensor_tensor(u64, rhi, c64, ALU.mult)
                        mrk = v.tensor_add(rhi, u64, t2)
                    ch_v.inc(mrk)
                    v.wait_ge(ch_v.h, ch_v.n)
                ctx.block.tensor(qmm)
                ctx.block.vector(qev)
                ctx.block.vector(qrot)
                last_wo_ev[0] = ch_v.n
            tab_gate[0] = ch_v.n
            qT_rdy = None
            qk_done_v = ch_v.n

            if L0A_ST_PARTS < 4:
                continue
            for cl in range(2):
                t0m = ch_t.n

                def vgmm(t, cl=cl, gate=ch_v.n, wd=wd_stage1):
                    t.wait_ge(ch_wd.h, wd)
                    t.wait_ge(ch_v.h, gate)
                    for dc in range(NDC):
                        lhs = aT[:, dc * STW + cl * 128: dc * STW + (cl + 1) * 128]
                        t.matmul(psA[:, 0:512], lhs, wv_s[:, dc * 1024: dc * 1024 + 512],
                                 start=(dc == 0), stop=(dc == NDC - 1))
                        t.matmul(psA[:, 512:1024], lhs, wv_s[:, dc * 1024 + 512:(dc + 1) * 1024],
                                 start=(dc == 0), stop=(dc == NDC - 1))
                        t.matmul(psB[:, 0:512], lhs, wg_s[:, dc * 1024: dc * 1024 + 512],
                                 start=(dc == 0), stop=(dc == NDC - 1))
                        mm = t.matmul(psB[:, 512:1024], lhs, wg_s[:, dc * 1024 + 512:(dc + 1) * 1024],
                                      start=(dc == 0), stop=(dc == NDC - 1))
                    ch_t.inc(mm)

                def vgev(v, cl=cl, t0m=t0m):
                    v.wait_ge(ch_t.h, t0m + 1)
                    v.tensor_copy(vtok[:, cl * 1024:(cl + 1) * 1024], psA[:, 0:1024])
                    ch_v.inc(v.tensor_copy(gtok[:, cl * 1024:(cl + 1) * 1024], psB[:, 0:1024]))
                    v.wait_ge(ch_v.h, ch_v.n)
                ctx.block.tensor(vgmm)
                ctx.block.vector(vgev)
            vg_done_v = ch_v.n

            for cl in range(2 if L0A_ST_PARTS >= 5 else 0):
                i_chunk = st * 2 + cl
                kpairs = [(kTt[:, h * STW + cl * 128: h * STW + cl * 128 + 128],
                           ktok[:, h * 128:(h + 1) * 128]) for h in range(H)]
                ktok_rdy = trb.emit(kpairs, twait=[(ch_v, vg_done_v)], vwait=[(ch_t, ch_t.n)])

                def kdv(v, rdy=ktok_rdy, cld=CONST_LD):
                    v.wait_ge(ch_ld.h, cld)
                    gb = gdec[:, :].rearrange("p h -> p h 1" if False else "p (h o) -> p h o", o=1).broadcast_to((128, 8, 128))
                    ch_v.inc(v.tensor_tensor(ktok[:].rearrange("p (h e) -> p h e", h=8),
                                             ktok[:].rearrange("p (h e) -> p h e", h=8), gb, ALU.mult))
                    v.wait_ge(ch_v.h, ch_v.n)
                ctx.block.vector(kdv)
                kd_rdy = ch_v.n

                t0m = ch_t.n

                def scmm(t, cl=cl, gate=kd_rdy):
                    t.wait_ge(ch_v.h, gate)
                    for h in range(H):
                        pst = [psC, psD][h // 4]
                        col = (h % 4) * 128
                        mm = t.matmul(pst[:, col:col + 128],
                                      kTt[:, h * STW + cl * 128: h * STW + (cl + 1) * 128],
                                      qT[:, h * STW + cl * 128: h * STW + (cl + 1) * 128],
                                      start=True, stop=True)
                        if h % 4 == 3:
                            ch_t.inc(mm)
                ctx.block.tensor(scmm)
                sc_t = ch_t.n

                def pev(v, cl=cl, t0m=t0m, cld=CONST_LD):
                    v.wait_ge(ch_ld.h, cld)
                    v.wait_ge(ch_t.h, t0m + 1)
                    v.tensor_tensor(Pt[:, 0:512], psC[:, 0:512], Dm[:, 0:512], ALU.mult)
                    v.wait_ge(ch_t.h, t0m + 2)
                    ch_v.inc(v.tensor_tensor(Pt[:, 512:1024], psD[:, 0:512], Dm[:, 512:1024], ALU.mult))
                    qsl = qT[:].rearrange("p (h t) -> p h t", h=8)[:, :, cl * 128:(cl + 1) * 128]
                    ch_v.inc(v.tensor_tensor(qxi[:].rearrange("p (h e) -> p h e", h=8), qsl,
                                             Xi[:].rearrange("p (h e) -> p h e", h=8), ALU.mult))
                    v.wait_ge(ch_v.h, ch_v.n)
                ctx.block.vector(pev)
                p_rdy = ch_v.n

                if L0A_ST_PARTS < 6:
                    continue
                t0m = ch_t.n

                def outmm(t, cl=cl, gate=p_rdy, sgate=None):
                    t.wait_ge(ch_v.h, gate)
                    for h in range(H):
                        t.matmul(psA[:, h * 128:(h + 1) * 128], Pt[:, h * 128:(h + 1) * 128],
                                 vtok[:, cl * 1024 + h * 128: cl * 1024 + (h + 1) * 128],
                                 start=True, stop=False)
                        mm = t.matmul(psA[:, h * 128:(h + 1) * 128], qxi[:, h * 128:(h + 1) * 128],
                                      Sbf[:, h * 128:(h + 1) * 128], start=False, stop=True)
                    ch_t.inc(mm)
                    for h in range(H):
                        mm = t.matmul(psB[:, h * 128:(h + 1) * 128], ktok[:, h * 128:(h + 1) * 128],
                                      vtok[:, cl * 1024 + h * 128: cl * 1024 + (h + 1) * 128],
                                      start=True, stop=True)
                    ch_t.inc(mm)
                ctx.block.tensor(outmm)
                out_t, m_t = ch_t.n - 1, ch_t.n

                def hev(v, out_t=out_t, m_t=m_t):
                    v.wait_ge(ch_t.h, out_t)
                    ch_v.inc(v.tensor_copy(heads[:], psA[:, 0:1024]))
                    gcb = gc[:, :].rearrange("p (h o) -> p h o", o=1).broadcast_to((128, 8, 128))
                    Sr = St[:].rearrange("p (h e) -> p h e", h=8)
                    v.tensor_tensor(Sr, Sr, gcb, ALU.mult)
                    v.wait_ge(ch_t.h, m_t)
                    ch_v.inc(v.tensor_add(St[:], St[:], psB[:, 0:1024]))
                    v.wait_ge(ch_v.h, ch_v.n)
                    ch_v.inc(v.tensor_copy(Sbf[:], St[:]))
                    v.wait_ge(ch_v.h, ch_v.n)
                ctx.block.vector(hev)
                heads_v = ch_v.n
                s_upd_v = ch_v.n

                if L0A_ST_PARTS < 7:
                    continue
                s0m = ch_s.n

                def gns(s_, cl=cl, heads_v=heads_v, vg=vg_done_v):
                    s_.wait_ge(ch_v.h, heads_v)
                    s_.activation(psC[:, 0:512], heads[:, 0:512], AF.Square)
                    ch_s.inc(s_.activation(psD[:, 0:512], heads[:, 512:1024], AF.Square))
                    s_.wait_ge(ch_v.h, vg)
                    ch_s.inc(s_.activation(sg[:], gtok[:, cl * 1024:(cl + 1) * 1024], AF.Silu))
                ctx.block.scalar(gns)

                def gnv(v, s0m=s0m, cl=cl):
                    hr = heads[:].rearrange("p (h e) -> p h e", h=8)
                    ch_v.inc(v.reduce_sum(st8[:, 0:8].rearrange("p (h o) -> p h o", o=1), hr, axis=AX.X))
                    v.wait_ge(ch_s.h, s0m + 1)
                    v.reduce_sum(st8[:, 8:12].rearrange("p (h o) -> p h o", o=1),
                                 psC[:, 0:512].rearrange("p (h e) -> p h e", h=4), axis=AX.X)
                    ch_v.inc(v.reduce_sum(st8[:, 12:16].rearrange("p (h o) -> p h o", o=1),
                                          psD[:, 0:512].rearrange("p (h e) -> p h e", h=4), axis=AX.X))
                    v.wait_ge(ch_v.h, ch_v.n)
                    ch_v.inc(v.tensor_scalar(st8[:, 16:24], st8[:, 0:8], 1.0 / 128, None, ALU.mult))
                    v.wait_ge(ch_v.h, ch_v.n)
                    ch_v.inc(v.tensor_mul(st8[:, 24:32], st8[:, 16:24], st8[:, 16:24]))
                    ch_v.inc(v.tensor_scalar(st8[:, 32:40], st8[:, 8:16], 1.0 / 128, EPSI,
                                             ALU.mult, ALU.add))
                    v.wait_ge(ch_v.h, ch_v.n)
                    ch_v.inc(v.tensor_sub(st8[:, 32:40], st8[:, 32:40], st8[:, 24:32]))
                    v.wait_ge(ch_v.h, ch_v.n)
                ctx.block.vector(gnv)
                gn_v = ch_v.n
                s0m2 = ch_s.n

                def gns2(s_, gn_v=gn_v):
                    s_.wait_ge(ch_v.h, gn_v)
                    ch_s.inc(s_.activation(st8[:, 40:48], st8[:, 32:40], AF.Sqrt))
                ctx.block.scalar(gns2)

                def gnv2(v, s0m2=s0m2, cl=cl):
                    v.wait_ge(ch_s.h, s0m2 + 1)
                    ch_v.inc(v.reciprocal(st8[:, 32:40], st8[:, 40:48]))
                    v.wait_ge(ch_v.h, ch_v.n)
                    hr = heads[:].rearrange("p (h e) -> p h e", h=8)
                    mb = st8[:, 16:24].rearrange("p (h o) -> p h o", o=1).broadcast_to((128, 8, 128))
                    rb = st8[:, 32:40].rearrange("p (h o) -> p h o", o=1).broadcast_to((128, 8, 128))
                    v.tensor_tensor(hr, hr, mb, ALU.subtract)
                    v.tensor_tensor(hr, hr, rb, ALU.mult)
                    v.tensor_mul(heads[:], heads[:], bc[:, 2048:3072])
                    ch_v.inc(v.tensor_add(heads[:], heads[:], bc[:, 3072:4096]))
                    v.wait_ge(ch_s.h, s0m2 + 0 + 1)
                    ch_v.inc(v.tensor_mul(sg[:], sg[:], heads[:]))
                    v.wait_ge(ch_v.h, ch_v.n)
                ctx.block.vector(gnv2)
                z_rdy = ch_v.n

                if L0A_ST_PARTS < 8:
                    continue
                zpairs = [(sg[:, dc * 128:(dc + 1) * 128],
                           zT[:, dc * STW + cl * 128: dc * STW + (cl + 1) * 128])
                          for dc in range(NDC)]
                zT_rdy = trb.emit(zpairs, twait=[(ch_v, z_rdy)], vwait=[(ch_t, ch_t.n)])

                t0m = ch_t.n

                def womm(t, cl=cl, zT_rdy=zT_rdy, hv=heads_v):
                    t.wait_ge(trb.che.h, zT_rdy)
                    t.wait_ge(ch_v.h, hv)
                    for dc in range(NDC):
                        lhs = zT[:, dc * STW + cl * 128: dc * STW + (cl + 1) * 128]
                        t.matmul(psA[:, 0:512], lhs, wo_s[:, dc * 1024: dc * 1024 + 512],
                                 start=(dc == 0), stop=(dc == NDC - 1))
                        mm = t.matmul(psA[:, 512:1024], lhs, wo_s[:, dc * 1024 + 512:(dc + 1) * 1024],
                                      start=(dc == 0), stop=(dc == NDC - 1))
                    ch_t.inc(mm)
                ctx.block.tensor(womm)
                wo_t = ch_t.n

                def woev(v, i_chunk=i_chunk, wo_t=wo_t):
                    v.wait_ge(ch_t.h, wo_t)
                    yc = y[:, i_chunk * 1024:(i_chunk + 1) * 1024]
                    ch_v.inc(v.scalar_tensor_tensor(yc, psA[:, 0:1024], 1.0, yc, ALU.mult, ALU.add))
                    v.wait_ge(ch_v.h, ch_v.n)
                ctx.block.vector(woev)
                last_wo_ev[0] = ch_v.n
            last_stage1_mm[0] = ch_t.n

        if not L0A_FFN:
            last_ffn_add = [ch_v.n]
            continue
        emit_rows([("L0_attn_ln2_scale", bc, 0), ("L0_attn_ln2_bias", bc, 1)])
        emit_stats(EPSI, vwait=(ch_v, last_wo_ev[0]))

        for i in range(2 * L0A_NST):
            def ln2ap(v, i=i, trg=trb.cht.n, ttg=ch_t.n):
                if trg:
                    v.wait_ge(trb.cht.h, trg)
                if ttg:
                    v.wait_ge(ch_t.h, ttg)
                yc = y[:, i * 1024:(i + 1) * 1024]
                v.tensor_scalar(scr[:], yc, lnst[:, MEAN + i:MEAN + i + 1],
                                lnst[:, RSTD + i:RSTD + i + 1], ALU.subtract, ALU.mult)
                v.tensor_mul(scr[:], scr[:], bc2[:, 0:1024])
                ch_v.inc(v.tensor_add(scrb[:, 0:1024], scr[:], bc2[:, 1024:2048]))
                v.wait_ge(ch_v.h, ch_v.n)
            ctx.block.vector(ln2ap)
            rr = ch_v.n
            pairs = [(scrb[:, dc * 128:(dc + 1) * 128],
                      x2T[:, dc * T + i * 128: dc * T + (i + 1) * 128]) for dc in range(NDC)]
            trb.emit(pairs, twait=[(ch_v, rr)], vwait=[(ch_t, ch_t.n)])
        x2T_rdy = trb.che.n

        for fb in range(4):
            bidx = fb % 2
            gate_t_ffn = ch_t.n
            d0 = ch_wd.n

            def fdma(g, li=li, fb=fb, bidx=bidx, gate=gate_t_ffn):
                if gate:
                    g.wait_ge(ch_t.h, gate)
                ch_wd.inc(g.dma_start(out=w1f[bidx][:, :].rearrange("p (dc f) -> p dc f", dc=8),
                                      in_=FW1[li][:, fb * 1024:(fb + 1) * 1024]
                                      .rearrange("(dc p) f -> p dc f", p=128)), 16)
                ch_wd.inc(g.dma_start(out=w2f[bidx][:, :].rearrange("p (fc m) -> p fc m", fc=8),
                                      in_=FW2[li][fb * 1024:(fb + 1) * 1024, :]
                                      .rearrange("(fc p) m -> p fc m", p=128)), 16)
            ctx.block.gpsimd(fdma)
            fdma_n = ch_wd.n
            for st in range(L0A_NST):
                prev_gelu = ch_s.n

                def w1mm(t, st=st, bidx=bidx, fdma_n=fdma_n, x2T_rdy=x2T_rdy,
                         prev_gelu=prev_gelu):
                    t.wait_ge(ch_wd.h, fdma_n)
                    t.wait_ge(trb.che.h, x2T_rdy)
                    t.wait_ge(ch_s.h, prev_gelu)
                    for fcg in range(2):
                        if fcg == 1:
                            t.wait_ge(ch_s.h, prev_gelu + 4)
                        for fc4 in range(4):
                            fc = fcg * 4 + fc4
                            pst = [psC, psD][fc4 // 2]
                            col = (fc4 % 2) * 256
                            for dc in range(NDC):
                                mm = t.matmul(pst[:, col:col + 256],
                                              w1f[bidx][:, dc * 1024 + fc * 128: dc * 1024 + (fc + 1) * 128],
                                              x2T[:, dc * T + st * STW: dc * T + (st + 1) * STW],
                                              start=(dc == 0), stop=(dc == NDC - 1))
                            ch_t.inc(mm)
                t0m = ch_t.n

                def w1ge(s_, t0m=t0m, st=st):
                    for fcg in range(2):
                        for fc4 in range(4):
                            fc = fcg * 4 + fc4
                            s_.wait_ge(ch_t.h, t0m + fcg * 4 + 2 * (fc4 // 2 + 1))
                            pst = [psC, psD][fc4 // 2]
                            col = (fc4 % 2) * 256
                            ch_s.inc(s_.activation(hfm[:, fc * STW:(fc + 1) * STW],
                                                   pst[:, col:col + 256], AF.Gelu_apprx_tanh))
                ctx.block.tensor(w1mm)
                ctx.block.scalar(w1ge)
                gelu_done = ch_s.n

                t0m = ch_t.n

                def w2mm(t, bidx=bidx, gelu_done=gelu_done, gate=last_wo_ev[0]):
                    t.wait_ge(ch_s.h, gelu_done)
                    t.wait_ge(ch_v.h, gate)
                    for tt in range(2):
                        pso = [psA, psB][tt]
                        for fc in range(NDC):
                            lhs = hfm[:, fc * STW + tt * 128: fc * STW + tt * 128 + 128]
                            t.matmul(pso[:, 0:512], lhs, w2f[bidx][:, fc * 1024: fc * 1024 + 512],
                                     start=(fc == 0), stop=(fc == NDC - 1))
                            mm = t.matmul(pso[:, 512:1024], lhs,
                                          w2f[bidx][:, fc * 1024 + 512:(fc + 1) * 1024],
                                          start=(fc == 0), stop=(fc == NDC - 1))
                        ch_t.inc(mm)
                ctx.block.tensor(w2mm)
                w2_t = ch_t.n

                def w2ev(v, st=st, w2_t=w2_t):
                    for tt in range(2):
                        v.wait_ge(ch_t.h, w2_t - 1 + tt + 0 if False else w2_t - 1 + tt)
                        i = st * 2 + tt
                        yc = y[:, i * 1024:(i + 1) * 1024]
                        pso = [psA, psB][tt]
                        ch_v.inc(v.scalar_tensor_tensor(yc, pso[:, 0:1024], 1.0, yc,
                                                        ALU.mult, ALU.add))
                    v.wait_ge(ch_v.h, ch_v.n)
                ctx.block.vector(w2ev)
                last_wo_ev[0] = ch_v.n
            last_ffn_mm[bidx] = ch_t.n
        last_ffn_add = [ch_v.n]

    if L0A_LAYERS == 0:
        last_ffn_add = [ch_v.n]

    def rest_dma(e, gate=last_ffn_add[0]):
        e.wait_ge(ch_v.h, gate)
        xr = spill.rearrange("(c p) d -> c p d", p=128)
        for i in range(NCHUNK):
            if i >= 1:
                e.wait_ge(ch_v.h, REST_BASE[0] + i)
            ch_sp.inc(e.dma_start(out=scr[:, 0:1024], in_=xr[i]), 16)
    REST_BASE = [ch_v.n]
    sp0 = ch_sp.n
    ctx.block.sync(rest_dma)

    def rest_add(v, sp0=sp0):
        for i in range(NCHUNK):
            v.wait_ge(ch_sp.h, sp0 + 16 * (i + 1))
            yc = y[:, i * 1024:(i + 1) * 1024]
            ch_v.inc(v.tensor_add(yc, yc, scr[:, 0:1024]))
        v.wait_ge(ch_v.h, ch_v.n)
    ctx.block.vector(rest_add)
    return ch_v, ch_v.n




def _l1a_consts():
    tt_ = np.arange(128)
    diff = tt_[None, :] - tt_[:, None]
    bf = ml_dtypes.bfloat16
    strict01 = (diff > 0).astype(np.float32)
    masks = np.zeros((4, 128, 512), dtype=np.float32)
    for r in range(4):
        sg = tt_[:, None] + r * 128
        tg = np.arange(512)[None, :]
        masks[r] = (tg >= sg).astype(np.float32)
    PI = [0, 0, 0, 1, 1, 2]
    PJ = [1, 2, 3, 2, 3, 3]
    return dict(l1a_delta=diff.astype(np.float32),
                l1a_strict=strict01.astype(bf),
                l1a_cmask=masks.reshape(4 * 128, 512).astype(bf),
                l1a_ip1=(tt_ + 1.0).astype(np.float32).reshape(128, 1),
                l1a_i127=(127.0 - tt_).astype(np.float32).reshape(128, 1)), PI, PJ


def emit_l1_attn(ctx, waps):
    nc = ctx.nc
    xres = ctx.tiles["xres"]
    identf = ctx.tiles["identf"]
    ident = ctx.sbuf("ident", [128, 128], BF16)
    NH, DH, PD = 16, 64, 4
    SCALE = DH ** -0.5

    consts, PI, PJ = _l1a_consts()
    for k, v in consts.items():
        ctx.const(k, v)
    CAP = ctx.const_aps

    spill = nc.dram_tensor("l1a_spill", [T, D], F32, kind="Internal").ap()
    x3dram = nc.dram_tensor("l1a_x3", [T, D], F32, kind="Internal").ap()
    wpdram = nc.dram_tensor("l1a_wp", [T, 64], BF16, kind="Internal").ap()

    ub = xres[:, :].bitcast(BF16)
    x3nT = ub[:, 0:16384]
    outT = ub[:, 16384:32768]
    shared = ctx.sbuf("l1a_shared", [128, 16640], BF16)
    k_acc = shared[:, 0:16384]
    v_ext = shared
    kvT = ctx.sbuf("l1a_kvT", [128, 16384], BF16)
    qTh = None
    khT = None
    Ptile = ctx.sbuf("l1a_P", [128, 2048], BF16)
    scr = ctx.sbuf("scr_lnsq", [128, 1024], F32)
    scrb = ctx.sbuf("l0a_scrb", [128, 1024], BF16) if "l0a_scrb" in ctx.tiles else ctx.sbuf("l1a_scrb", [128, 1024], BF16)
    ob65 = ctx.sbuf("l1a_ob", [128, 512], F32)
    lnst = ctx.sbuf("l1a_lnst", [128, 80], F32)
    SUMS, SS, MEAN, VAR, RSTD, M2 = 0, 16, 32, 48, 64, 0
    wp = ctx.sbuf("l1a_wpj", [128, 16 * 256], BF16)
    rl = ctx.sbuf("l1a_rl", [128, 16 * 96], BF16)
    jw = ctx.sbuf("l1a_jw", [128, 16 * 96], BF16)
    r36 = ctx.sbuf("l1a_r36", [128, 16 * 36], BF16)
    w36 = ctx.sbuf("l1a_w36", [128, 16 * 36], BF16)
    r36T = ctx.sbuf("l1a_r36T", [64, 2048], BF16)
    w36T = ctx.sbuf("l1a_w36T", [64, 2048], BF16)
    Dg = ctx.sbuf("l1a_Dg", [128, 16 * 128], BF16)
    delta = ctx.sbuf("l1a_delta", [128, 128], F32)
    strict = ctx.sbuf("l1a_strict", [128, 128], BF16)
    cmask = ctx.sbuf("l1a_cmask", [128, 4 * 512], BF16)
    ip1 = ctx.sbuf("l1a_ip1", [128, 1], F32)
    i127 = ctx.sbuf("l1a_i127", [128, 1], F32)
    grow = ctx.sbuf("l1a_grow", [1, 64], F32)
    gsc = ctx.sbuf("l1a_gsc", [128, 256], F32)
    Sg = ctx.sbuf("l1a_Sg", [64, 16], F32)
    SgB = ctx.sbuf("l1a_SgB", [64, 16], BF16)
    x3st = None
    dvecB = ctx.sbuf("l1a_dvecB", [128, 16], BF16)
    ones2 = ctx.sbuf("l1a_ones2", [128, 64], F32)
    x3st = None
    qwh = ctx.sbuf("l1a_qwh", [128, 512], BF16)
    negcrep = ctx.sbuf("l1a_negc", [128, 16], F32)
    ggpre = ctx.sbuf("l1a_ggpre", [128, 16 * 16], F32)
    gg = ctx.sbuf("l1a_gg", [128, 16 * 16], BF16)
    wsm = ctx.sbuf("l1a_wsm", [128, 8 * 256], BF16)
    qTh = wp[0:64, 0:2048]
    khT = wp[0:64, 2048:4096]
    Wst = [ctx.sbuf("l1a_WA", [128, 8 * 512], BF16), ctx.sbuf("l1a_WB", [128, 8 * 512], BF16)]
    qb = ctx.sbuf("l1a_qb", [128, 8], F32)
    kvb = ctx.sbuf("l1a_kvb", [128, 2048], BF16)
    obbc = ctx.sbuf("l1a_obbc", [128, 1024], BF16)
    onest = ctx.sbuf("l1a_onest", [128, 2], BF16)
    onesf = ctx.sbuf("l1a_onesf", [1, 128], F32)
    rowf = scr[0:1, 0:1024]

    psA = ctx.psum("psA", [128, 1024], F32)
    psB = ctx.psum("psB", [128, 1024], F32)
    psC = ctx.psum("psC", [128, 512], F32)
    psD = ctx.psum("psD", [128, 512], F32)
    trb = ctx._trb if hasattr(ctx, "_trb") else TrB(ctx)
    ctx._trb = trb

    ch_sp = ctx.chan("l1a_sp")
    ch_wd = ctx.chan("l1a_wd")
    ch_ld = ctx.chan("l1a_ld")
    ch_t = ctx.chan("l1a_t")
    ch_v = ctx.chan("l1a_v")
    ch_s = ctx.chan("l1a_s")

    def spill_out(e, vg=ctx.chans.get("xin") and 0 or 0):
        xr = spill.rearrange("(c p) d -> c p d", p=128)
        for i in range(NCHUNK):
            ch_sp.inc(e.dma_start(out=xr[i], in_=xres[:, i * 1024:(i + 1) * 1024]), 16)
    ctx.block.sync(spill_out)

    v0, s0 = ch_v.n, ch_s.n

    def ssec1(s_):
        for i in range(NCHUNK):
            mrk = s_.activation(scrb[:], xres[:, i * 1024:(i + 1) * 1024], AF.Square,
                                accum_out=lnst[:, SS + i:SS + i + 1])
        ch_s.inc(mrk)

    def vsec1(v):
        for i in range(NCHUNK):
            mrk = v.reduce_sum(lnst[:, SUMS + i:SUMS + i + 1],
                               xres[:, i * 1024:(i + 1) * 1024], axis=AX.X)
        ch_v.inc(mrk)
        v.wait_ge(ch_v.h, v0 + 1)
        ch_v.inc(v.tensor_scalar(lnst[:, MEAN:MEAN + 16], lnst[:, SUMS:SUMS + 16],
                                 1.0 / D, None, ALU.mult))
        v.wait_ge(ch_v.h, v0 + 2)
        ch_v.inc(v.tensor_mul(lnst[:, M2:M2 + 16], lnst[:, MEAN:MEAN + 16],
                              lnst[:, MEAN:MEAN + 16]))
        v.wait_ge(ch_s.h, s0 + 1)
        ch_v.inc(v.tensor_scalar(lnst[:, VAR:VAR + 16], lnst[:, SS:SS + 16],
                                 1.0 / D, 1e-5, ALU.mult, ALU.add))
        v.wait_ge(ch_v.h, v0 + 4)
        ch_v.inc(v.tensor_sub(lnst[:, VAR:VAR + 16], lnst[:, VAR:VAR + 16],
                              lnst[:, M2:M2 + 16]))

    def ssec2(s_):
        s_.wait_ge(ch_v.h, v0 + 5)
        ch_s.inc(s_.activation(lnst[:, VAR:VAR + 16], lnst[:, VAR:VAR + 16], AF.Sqrt))

    def vsec2(v):
        v.wait_ge(ch_s.h, s0 + 2)
        ch_v.inc(v.reciprocal(lnst[:, RSTD:RSTD + 16], lnst[:, VAR:VAR + 16]))
        v.wait_ge(ch_v.h, ch_v.n)
    ctx.block.scalar(ssec1)
    ctx.block.vector(vsec1)
    ctx.block.scalar(ssec2)
    ctx.block.vector(vsec2)

    for i in range(NCHUNK):
        def lnap(v, i=i, trg=trb.cht.n):
            if trg:
                v.wait_ge(trb.cht.h, trg)
            v.wait_ge(ch_sp.h, 16 * (i + 1))
            yc = xres[:, i * 1024:(i + 1) * 1024]
            ch_v.inc(v.tensor_scalar(scrb[:, 0:1024], yc, lnst[:, MEAN + i:MEAN + i + 1],
                                     lnst[:, RSTD + i:RSTD + i + 1], ALU.subtract, ALU.mult))
            v.wait_ge(ch_v.h, ch_v.n)
        ctx.block.vector(lnap)
        rr = ch_v.n
        pairs = [(scrb[:, dc * 128:(dc + 1) * 128],
                  x3nT[:, i * 1024 + dc * 128: i * 1024 + (dc + 1) * 128]) for dc in range(NDC)]
        trb.emit(pairs, twait=[(ch_v, rr)], vwait=[(ch_t, ch_t.n)])
    x3nT_rdy = trb.che.n

    def cdma(e):
        ch_ld.inc(e.dma_start(out=delta[:, :], in_=CAP["l1a_delta"][:, :]), 16)
        ch_ld.inc(e.dma_start(out=strict[:, :], in_=CAP["l1a_strict"][:, :]), 16)
        for r_ in range(4):
            ch_ld.inc(e.dma_start(out=cmask[:, r_ * 512:(r_ + 1) * 512],
                                  in_=CAP["l1a_cmask"][r_ * 128:(r_ + 1) * 128, :]), 16)
        ch_ld.inc(e.dma_start(out=ip1[:, :], in_=CAP["l1a_ip1"][:, :]), 16)
        ch_ld.inc(e.dma_start(out=i127[:, :], in_=CAP["l1a_i127"][:, :]), 16)
        ch_ld.inc(e.dma_start(out=grow[0:1, 0:16],
                              in_=waps["L1_attn_decay_logits"].rearrange("(a c) -> a c", a=1)), 16)
        ch_ld.inc(e.dma_start(out=grow[0:1, 16:32],
                              in_=waps["L1_attn_mem_scale"].rearrange("(a c) -> a c", a=1)), 16)
        ch_ld.inc(e.dma_start(out=grow[0:1, 32:33],
                              in_=waps["L1_attn_kv_gate"].rearrange("(a c) -> a c", a=1)), 16)
        ch_ld.inc(e.dma_start(out=qb[:, :], in_=waps["L1_attn_qkv_b"][0:1024].rearrange(
            "(hp r) -> r hp", r=128), allow_slow_non_contiguous=True), 16)
    ctx.block.scalar(cdma)

    def cdma_g(gq):
        for wi, wn in enumerate(("L1_attn_w1_write", "L1_attn_w2_write",
                                 "L1_attn_w1_read", "L1_attn_w2_read")):
            ch_ld.inc(gq.dma_start(
                out=wsm[:, :].rearrange("p (dc c) -> p dc c", dc=8)[:, :, wi * 64:(wi + 1) * 64],
                in_=waps[wn].rearrange("(dc p) c -> p dc c", p=128)), 16)
    ctx.block.gpsimd(cdma_g)
    CLD = ch_ld.n

    def mk_row_bcast(rname, roff, dst):
        r0 = ch_ld.n

        def rdma(e, rname=rname, roff=roff, tg=ch_t.n):
            if tg:
                e.wait_ge(ch_t.h, tg)
            ch_ld.inc(e.dma_start(out=rowf[0:1, :], in_=waps[rname][roff:roff + 1024]
                                  .rearrange("(a c) -> a c", a=1)), 16)
        t0 = ch_t.n

        def rmm(t, r0=r0, gate=ch_v.n):
            t.wait_ge(ch_ld.h, r0 + 16)
            t.wait_ge(ch_v.h, gate)
            ch_t.inc(t.matmul(psC[:, 0:512], onesf[0:1, 0:128], rowf[0:1, 0:512],
                              start=True, stop=True))
            ch_t.inc(t.matmul(psD[:, 0:512], onesf[0:1, 0:128], rowf[0:1, 512:1024],
                              start=True, stop=True))

        def rev(v, dst=dst, t0=t0):
            v.wait_ge(ch_t.h, t0 + 2)
            v.tensor_copy(dst[:, 0:512], psC[:, 0:512])
            ch_v.inc(v.tensor_copy(dst[:, 512:1024], psD[:, 0:512]))
            v.wait_ge(ch_v.h, ch_v.n)
        ctx.block.scalar(rdma)
        ctx.block.tensor(rmm)
        ctx.block.vector(rev)

    def ones_v(v):
        ch_v.inc(v.memset(onesf[:], 1.0))
        ch_v.inc(v.memset(onest[:, :], 1.0))
        v.wait_ge(ch_v.h, ch_v.n)
    ctx.block.vector(ones_v)
    mk_row_bcast("L1_attn_qkv_b", 1024, kvb[:, 0:1024])
    mk_row_bcast("L1_attn_qkv_b", 2048, kvb[:, 1024:2048])
    mk_row_bcast("L1_attn_out_b", 0, obbc)

    g0v, g0s = ch_v.n, ch_s.n
    t0g = ch_t.n

    def gram_t(t, gate=ch_v.n):
        t.wait_ge(ch_ld.h, CLD)
        t.wait_ge(ch_v.h, gate)
        ch_t.inc(t.matmul(psC[:, 0:64], onesf[0:1, 0:128], grow[0:1, 0:64],
                          start=True, stop=True))
    ctx.block.tensor(gram_t)

    def gram_v(v, t0g=t0g):
        v.wait_ge(ch_t.h, t0g + 1)
        ch_v.inc(v.tensor_copy(gsc[:, 64:128], psC[:, 0:64]))
        v.wait_ge(ch_v.h, ch_v.n)
    ctx.block.vector(gram_v)
    gv1 = ch_v.n

    def gram_s(s_, gv1=gv1):
        s_.wait_ge(ch_v.h, gv1)
        ch_s.inc(s_.activation(gsc[:, 112:128], gsc[:, 64:80], AF.Sigmoid))
        s_.wait_ge(ch_s.h, ch_s.n)
        ch_s.inc(s_.activation(gsc[:, 128:144], gsc[:, 112:128], AF.Ln))
    ctx.block.scalar(gram_s)
    gs1 = ch_s.n

    def gram_v2(v, gs1=gs1):
        v.wait_ge(ch_s.h, gs1)
        ch_v.inc(v.tensor_scalar(gsc[:, 144:160], gsc[:, 128:144], ip1[:, 0:1], None, ALU.mult))
        ch_v.inc(v.tensor_scalar(gsc[:, 160:176], gsc[:, 128:144], i127[:, 0:1], None, ALU.mult))
        v.wait_ge(ch_v.h, ch_v.n)
    ctx.block.vector(gram_v2)
    gv2 = ch_v.n

    def gram_s2a(s_, gv2=gv2):
        s_.wait_ge(ch_v.h, gv2)
        ch_s.inc(s_.activation(gsc[:, 144:160], gsc[:, 144:160], AF.Exp))
        ch_s.inc(s_.activation(gsc[:, 160:176], gsc[:, 160:176], AF.Exp))
        ch_s.inc(s_.activation(gsc[:, 176:192], gsc[:, 128:144], AF.Exp, scale=128.0))
    ctx.block.scalar(gram_s2a)
    for h in range(NH):
        v0h = ch_v.n

        def dg_v1(v, h=h):
            v.wait_ge(ch_s.h, ch_s.n) if False else None
            ch_v.inc(v.tensor_scalar(scr[:, 0:128], delta[:, :], gsc[:, 128 + h:129 + h],
                                     None, ALU.mult))
            v.wait_ge(ch_v.h, ch_v.n)
        ctx.block.vector(dg_v1)
        vm = ch_v.n
        s0h = ch_s.n

        def dg_s(s_, vm=vm):
            s_.wait_ge(ch_v.h, vm)
            ch_s.inc(s_.activation(scr[:, 0:128], scr[:, 0:128], AF.Exp))
        ctx.block.scalar(dg_s)
        sm = ch_s.n

        def dg_v2(v, h=h, sm=sm):
            v.wait_ge(ch_s.h, sm)
            ch_v.inc(v.tensor_mul(Dg[:, h * 128:(h + 1) * 128], scr[:, 0:128], strict[:, :]))
            v.wait_ge(ch_v.h, ch_v.n)
        ctx.block.vector(dg_v2)


    t0p = ch_t.n
    v0p = ch_v.n

    def wproj_t(t, x3r=x3nT_rdy, gate=ch_v.n, v0p=v0p):
        t.wait_ge(trb.che.h, x3r)
        t.wait_ge(ch_ld.h, CLD)
        t.wait_ge(ch_v.h, gate)
        for i in range(NCHUNK):
            if i >= 2:
                t.wait_ge(ch_v.h, v0p + i - 1)
            for dc in range(NDC):
                mm = t.matmul([psC, psD][i % 2][:, 0:256],
                              x3nT[:, i * 1024 + dc * 128: i * 1024 + (dc + 1) * 128],
                              wsm[:, dc * 256:(dc + 1) * 256],
                              start=(dc == 0), stop=(dc == NDC - 1))
            ch_t.inc(mm)

    def wproj_v(v, t0p=t0p):
        for i in range(NCHUNK):
            v.wait_ge(ch_t.h, t0p + i + 1)
            ch_v.inc(v.tensor_copy(wp[:, i * 256:(i + 1) * 256], [psC, psD][i % 2][:, 0:256]))
        v.wait_ge(ch_v.h, ch_v.n)
    ctx.block.tensor(wproj_t)
    ctx.block.vector(wproj_v)
    wp_done = ch_v.n

    sp0 = ch_sp.n

    def shift_dma(e, wp_done=wp_done, sp0=sp0):
        e.wait_ge(ch_v.h, wp_done)
        wpr = wp[:, :].rearrange("p (c g) -> p c g", c=16)
        ch_sp.inc(e.dma_start(out=wpdram.rearrange("(c p) g -> p c g", p=128),
                              in_=wpr[:, :, 0:64]), 16)
        e.wait_ge(ch_sp.h, ch_sp.n)
        ch_sp.inc(e.dma_start(out=wp[1:128, 0:64], in_=wpdram[0:127, :]), 16)
        for i in range(1, NCHUNK):
            ch_sp.inc(e.dma_start(out=wp[:, i * 256:i * 256 + 64],
                                  in_=wpdram[i * 128 - 1: (i + 1) * 128 - 1, :]), 16)
    ctx.block.sync(shift_dma)
    shift_done = ch_sp.n

    def zrow_v(v, sd=shift_done):
        v.wait_ge(ch_sp.h, sd)
        ch_v.inc(v.memset(wp[0:1, 0:64], 0.0))
        v.wait_ge(ch_v.h, ch_v.n)
    ctx.block.vector(zrow_v)
    wps_done = ch_v.n

    def ext_chunk(i):
        base = i * 256

        def ext_v(v, i=i, base=base, gate=wps_done):
            if i == 0:
                v.wait_ge(ch_v.h, gate)
            for (o1, o2, dstt) in ((0, 64, scrb), (128, 192, scr.bitcast(BF16))):
                p1 = wp[:, base + o1: base + o1 + 64].rearrange("p (h d) -> p h d", h=16)
                p2 = wp[:, base + o2: base + o2 + 64].rearrange("p (h d) -> p h d", h=16)
                Lt = dstt[:, 0:96].rearrange("p (h c) -> p h c", h=16)
                specs = [(0, 1, 0, 3), (1, 2, 3, 2), (2, 3, 5, 1)]
                for (pi, pj0, c0, n) in specs:
                    a = p1[:, :, pi:pi + 1].broadcast_to((128, 16, n))
                    b = p2[:, :, pj0:pj0 + n]
                    v.tensor_tensor(Lt[:, :, c0:c0 + n], a, b, ALU.mult)
                    a2 = p2[:, :, pi:pi + 1].broadcast_to((128, 16, n))
                    b2 = p1[:, :, pj0:pj0 + n]
                    v.tensor_tensor(dstt[:, 96:192].rearrange("p (h c) -> p h c", h=16)[:, :, c0:c0 + n],
                                    a2, b2, ALU.mult)
                ch_v.inc(v.tensor_sub(Lt[:, :, :],
                                      Lt[:, :, :],
                                      dstt[:, 96:192].rearrange("p (h c) -> p h c", h=16)))
                v.wait_ge(ch_v.h, ch_v.n)
            ch_v.inc(v.tensor_mul(scr.bitcast(BF16)[:, 192:288], scrb[:, 0:96], scrb[:, 0:96]))
            ch_v.inc(v.tensor_mul(scr.bitcast(BF16)[:, 288:384],
                                  scr.bitcast(BF16)[:, 0:96], scr.bitcast(BF16)[:, 0:96]))
            v.wait_ge(ch_v.h, ch_v.n)
            ch_v.inc(v.reduce_sum(gsc[:, 0:16].rearrange("p (h o) -> p h o", o=1),
                                  scr.bitcast(BF16)[:, 192:288].rearrange("p (h c) -> p h c", h=16),
                                  axis=AX.X))
            ch_v.inc(v.reduce_sum(gsc[:, 16:32].rearrange("p (h o) -> p h o", o=1),
                                  scr.bitcast(BF16)[:, 288:384].rearrange("p (h c) -> p h c", h=16),
                                  axis=AX.X))
            v.wait_ge(ch_v.h, ch_v.n)
        ctx.block.vector(ext_v)
        vm1 = ch_v.n
        s0e = ch_s.n

        def ext_s(s_, vm1=vm1):
            s_.wait_ge(ch_v.h, vm1)
            ch_s.inc(s_.activation(gsc[:, 0:32], gsc[:, 0:32], AF.Sqrt))
        ctx.block.scalar(ext_s)
        sm1 = ch_s.n

        def ext_v2(v, i=i, sm1=sm1):
            v.wait_ge(ch_s.h, sm1)
            ch_v.inc(v.tensor_scalar(gsc[:, 0:32], gsc[:, 0:32], 1e-12, None, ALU.max))
            v.wait_ge(ch_v.h, ch_v.n)
            ch_v.inc(v.reciprocal(gsc[:, 32:64], gsc[:, 0:32]))
            v.wait_ge(ch_v.h, ch_v.n)
            ch_v.inc(v.tensor_tensor(rl[:, i * 96:(i + 1) * 96].rearrange("p (h c) -> p h c", h=16),
                                     scr.bitcast(BF16)[:, 0:96].rearrange("p (h c) -> p h c", h=16),
                                     gsc[:, 48:64].rearrange("p (h o) -> p h o", o=1)
                                     .broadcast_to((128, 16, 6)), ALU.mult))
            ch_v.inc(v.tensor_tensor(scrb[:, 96:192].rearrange("p (h c) -> p h c", h=16),
                                     scrb[:, 0:96].rearrange("p (h c) -> p h c", h=16),
                                     gsc[:, 32:48].rearrange("p (h o) -> p h o", o=1)
                                     .broadcast_to((128, 16, 6)), ALU.mult))
            v.wait_ge(ch_v.h, ch_v.n)
            wlr = scrb[:, 96:192].rearrange("p (h c) -> p h c", h=16)
            jwr = jw[:, i * 96:(i + 1) * 96].rearrange("p (h c) -> p h c", h=16)
            for c_ in range(6):
                mrk = v.tensor_copy(jwr[:, :, c_:c_ + 1], wlr[:, :, 5 - c_:6 - c_])
            ch_v.inc(mrk)
            v.wait_ge(ch_v.h, ch_v.n)
            ch_v.inc(v.tensor_scalar(jwr[:, :, 1:2], jwr[:, :, 1:2], -1.0, None, ALU.mult))
            ch_v.inc(v.tensor_scalar(jwr[:, :, 4:5], jwr[:, :, 4:5], -1.0, None, ALU.mult))
            v.wait_ge(ch_v.h, ch_v.n)
        ctx.block.vector(ext_v2)
    for i in range(NCHUNK):
        ext_chunk(i)
    ext_done = ch_v.n

    return emit_l1_attn_part2(ctx, waps, locals())


def emit_l1_attn_part2(ctx, waps, g):
    nc = ctx.nc
    xres = ctx.tiles["xres"]
    identf = ctx.tiles["identf"]
    ident = g["ident"]
    (x3nT, outT, shared, k_acc, kvT, qTh, khT, Ptile) = (
        g["x3nT"], g["outT"], g["shared"], g["k_acc"], g["kvT"], g["qTh"], g["khT"], g["Ptile"])
    (scr, scrb, ob65, wp, rl, jw, r36, w36, r36T, w36T, Dg) = (
        g["scr"], g["scrb"], g["ob65"], g["wp"], g["rl"], g["jw"], g["r36"], g["w36"],
        g["r36T"], g["w36T"], g["Dg"])
    (gsc, Sg, ggpre, gg, Wst, qb, kvb, obbc, onest, onesf, cmask) = (
        g["gsc"], g["Sg"], g["ggpre"], g["gg"], g["Wst"], g["qb"], g["kvb"], g["obbc"],
        g["onest"], g["onesf"], g["cmask"])
    (psA, psB, psC, psD, trb) = (g["psA"], g["psB"], g["psC"], g["psD"], g["trb"])
    (ch_sp, ch_wd, ch_ld, ch_t, ch_v, ch_s) = (
        g["ch_sp"], g["ch_wd"], g["ch_ld"], g["ch_t"], g["ch_v"], g["ch_s"])
    (spill, x3dram, x3nT_rdy, ext_done, CLD) = (
        g["spill"], g["x3dram"], g["x3nT_rdy"], g["ext_done"], g["CLD"])
    SgB = g["SgB"]
    dvecB = g["dvecB"]
    ones2 = g["ones2"]
    x3st = g["Ptile"][:, :].bitcast(F32)
    qwh = g["qwh"]
    NH, DH = 16, 64
    SCALE = DH ** -0.5
    QKVW = waps["L1_attn_qkv_w"]
    GEOW = waps["L1_attn_kv_geo_w"]
    OUTW = waps["L1_attn_out_w"]

    def b36_v(v, gate=ext_done):
        v.wait_ge(ch_v.h, gate)
        v.memset(Sg[:, :], 0.0)
        ch_v.inc(v.memset(SgB[:, :], 0.0))
        ch_v.inc(v.tensor_copy(dvecB[:, :], gsc[:, 160:176]))
        ch_v.inc(v.memset(ones2[:, :], 1.0))
        v.wait_ge(ch_v.h, ch_v.n)
    ctx.block.vector(b36_v)
    b36_done = ch_v.n

    for h in range(NH):
        def bh_v(v, h=h, gate=b36_done, trg=trb.cht.n):
            if trg:
                v.wait_ge(trb.cht.h, trg)
            if h == 0:
                v.wait_ge(ch_v.h, gate)
            for (src, dst) in ((rl, r36), (jw, w36)):
                s6 = src[:, :].rearrange("p (c s) -> p c s", s=96)[:, :, h * 6:(h + 1) * 6]
                ina = s6.rearrange("p c (x o) -> p c x o", o=1).broadcast_to((128, 16, 6, 6))
                inb = s6.rearrange("p c (o y) -> p c o y", o=1).broadcast_to((128, 16, 6, 6))
                mrk = v.tensor_tensor(
                    dst[:, :].rearrange("p (c x y) -> p c x y", c=16, x=6), ina, inb, ALU.mult)
            ch_v.inc(mrk)
            v.wait_ge(ch_v.h, ch_v.n)
        ctx.block.vector(bh_v)
        bh_done = ch_v.n
        pairs = []
        for i in range(NCHUNK):
            pairs.append((r36[:, i * 36:(i + 1) * 36], r36T[0:36, i * 128:(i + 1) * 128]))
        m1 = trb.emit(pairs, twait=[(ch_v, bh_done)], vwait=[(ch_t, ch_t.n)])
        pairs = []
        for i in range(NCHUNK):
            pairs.append((w36[:, i * 36:(i + 1) * 36], w36T[0:36, i * 128:(i + 1) * 128]))
        m2 = trb.emit(pairs, vwait=[(ch_t, ch_t.n)])
        tv0, vv0 = ch_t.n, ch_v.n

        def gram_t(t, h=h, m2=m2, tv0=tv0, vv0=vv0):
            t.wait_ge(trb.che.h, m2)
            for i in range(NCHUNK):
                if i >= 1:
                    t.wait_ge(ch_v.h, vv0 + 6 * i)
                t.matmul(psC[:, 0:128], w36T[0:36, i * 128:(i + 1) * 128],
                         r36T[0:36, i * 128:(i + 1) * 128], start=True, stop=True)
                ch_t.inc(t.nop()) if False else None
                ch_t.n += 0
        del gram_t

        def gram_t(t, h=h, m2=m2, tv0=tv0, vv0=vv0):
            t.wait_ge(trb.che.h, m2)
            for i in range(NCHUNK):
                if i >= 1:
                    t.wait_ge(ch_v.h, vv0 + 6 * i)
                ch_t.inc(t.matmul(psC[:, 0:128], w36T[0:36, i * 128:(i + 1) * 128],
                                  r36T[0:36, i * 128:(i + 1) * 128], start=True, stop=True))
                t.wait_ge(ch_v.h, vv0 + 6 * i + 1)
                ch_t.inc(t.matmul(psD[:, 0:1], Ptile[:, 0:128], onest[:, 0:1],
                                  start=True, stop=True))
                ch_t.inc(t.matmul(psD[:, 1:2], r36T[0:36, i * 128:(i + 1) * 128],
                                  SgB[0:36, h:h + 1], start=True, stop=True))
                ch_t.inc(t.matmul(psD[0:36, 4:5], w36[:, i * 36:(i + 1) * 36],
                                  dvecB[:, h:h + 1], start=True, stop=True))

        def gram_v(v, h=h, tv0=tv0, vv0=vv0):
            for i in range(NCHUNK):
                v.wait_ge(ch_t.h, tv0 + 4 * i + 1)
                ch_v.inc(v.tensor_mul(Ptile[:, 0:128], psC[:, 0:128],
                                      Dg[:, h * 128:(h + 1) * 128]))
                v.wait_ge(ch_v.h, ch_v.n)
                v.wait_ge(ch_t.h, tv0 + 4 * i + 4)
                ch_v.inc(v.tensor_copy(gsc[:, 0:8], psD[:, 0:8]))
                v.wait_ge(ch_v.h, ch_v.n)
                ch_v.inc(v.scalar_tensor_tensor(ggpre[:, i * 16 + h: i * 16 + h + 1],
                                                gsc[:, 1:2], gsc[:, 144 + h:145 + h],
                                                gsc[:, 0:1], ALU.mult, ALU.add))
                ch_v.inc(v.tensor_scalar(Sg[0:36, h:h + 1], Sg[0:36, h:h + 1],
                                         gsc[0:36, 176 + h:177 + h], None, ALU.mult))
                v.wait_ge(ch_v.h, ch_v.n)
                ch_v.inc(v.tensor_add(Sg[0:36, h:h + 1], Sg[0:36, h:h + 1],
                                      psD[0:36, 4:5]))
                v.wait_ge(ch_v.h, ch_v.n)
                ch_v.inc(v.tensor_copy(SgB[0:36, h:h + 1], Sg[0:36, h:h + 1]))
                v.wait_ge(ch_v.h, ch_v.n)
        ctx.block.tensor(gram_t)
        ctx.block.vector(gram_v)
    gram_done = ch_v.n

    def gg_v(v, gate=gram_done):
        msb = gsc[:, 80:96].rearrange("p (o h) -> p o h", o=1).broadcast_to((128, 16, 16))
        ch_v.inc(v.tensor_tensor(ggpre[:, :].rearrange("p (c h) -> p c h", c=16),
                                 ggpre[:, :].rearrange("p (c h) -> p c h", c=16),
                                 msb, ALU.mult))
        v.wait_ge(ch_v.h, ch_v.n)
    ctx.block.vector(gg_v)
    ggv = ch_v.n

    def gg_s(s_, ggv=ggv):
        s_.wait_ge(ch_v.h, ggv)
        ch_s.inc(s_.activation(ggpre[:, :], ggpre[:, :], AF.Sigmoid))
    ctx.block.scalar(gg_s)
    ggs = ch_s.n

    def gg_v2(v, ggs=ggs):
        v.wait_ge(ch_s.h, ggs)
        ch_v.inc(v.tensor_scalar(gg[:, :], ggpre[:, :], gsc[:, 96:97], None, ALU.mult))
        ch_v.inc(v.memset(gsc[:, 192:224], 0.0))
        v.wait_ge(ch_v.h, ch_v.n)
    ctx.block.vector(gg_v2)
    gg_done = ch_v.n

    blocks = [("q", 0, "qkv"), ("q", 512, "qkv"), ("k", 1024, "qkv"), ("k", 1536, "qkv"),
              ("gk", 0, "geo"), ("gk", 512, "geo"),
              ("KT", 0, None),
              ("v", 2048, "qkv"), ("v", 2560, "qkv"), ("gv", 1024, "geo"), ("gv", 1536, "geo")]
    wbm = {}
    BEV_MARKS = {}
    bi = 0
    kvT_done = [0]
    for tag, cb, src in blocks:
        if tag == "KT":
            for i in range(NCHUNK):
                pairs = []
                for hh in range(NH):
                    pairs.append((k_acc[:, i * 1024 + hh * 64: i * 1024 + (hh + 1) * 64],
                                  kvT[64 * (hh % 2):64 * (hh % 2) + 64,
                                      (hh // 2) * 2048 + i * 128: (hh // 2) * 2048 + (i + 1) * 128]))
                kvT_done[0] = trb.emit(pairs, twait=[(ch_v, ch_v.n)], vwait=[(ch_t, ch_t.n)])
            for i in range(NCHUNK):
                s0k = ch_s.n

                def kmax_s(s_, i=i, gate=ch_v.n):
                    s_.wait_ge(ch_v.h, gate)
                    ch_s.inc(s_.activation(scrb[:, 0:1024], k_acc[:, i * 1024:(i + 1) * 1024],
                                           AF.Square))
                ctx.block.scalar(kmax_s)
                sm = ch_s.n

                def kmax_v(v, i=i, sm=sm):
                    v.wait_ge(ch_s.h, sm)
                    ch_v.inc(v.reduce_sum(gsc[:, 32:48].rearrange("p (h o) -> p h o", o=1),
                                          scrb[:, 0:1024].rearrange("p (h e) -> p h e", h=16),
                                          axis=AX.X))
                    v.wait_ge(ch_v.h, ch_v.n)
                    ch_v.inc(v.tensor_max(gsc[:, 208:224], gsc[:, 208:224], gsc[:, 32:48]))
                    v.wait_ge(ch_v.h, ch_v.n)
                ctx.block.vector(kmax_v)
            continue
        widx = bi % 2
        ipt = 2 if tag == "q" else 1

        def bdma(gq, cb=cb, src=src, widx=widx, gate=wbm.get(bi - 2, 0)):
            if gate:
                gq.wait_ge(ch_t.h, gate)
            dr = QKVW if src == "qkv" else GEOW
            ch_wd.inc(gq.dma_start(out=Wst[widx][:, :].rearrange("p (dc m) -> p dc m", dc=8),
                                   in_=dr[:, cb:cb + 512].rearrange("(dc p) m -> p dc m", p=128)),
                      16)
        ctx.block.gpsimd(bdma)
        wd_n = ch_wd.n
        t0b = ch_t.n
        vb0 = ch_v.n

        def bmm(t, widx=widx, wd_n=wd_n, x3r=x3nT_rdy, gate=ch_v.n, vb0=vb0, ipt=ipt):
            t.wait_ge(ch_wd.h, wd_n)
            t.wait_ge(trb.che.h, x3r)
            t.wait_ge(ch_v.h, gate)
            for i in range(NCHUNK):
                if i >= 2:
                    t.wait_ge(ch_v.h, vb0 + ipt * (i - 1))
                for dc in range(NDC):
                    mm = t.matmul([psA, psB][i % 2][:, 0:512],
                                  x3nT[:, i * 1024 + dc * 128: i * 1024 + (dc + 1) * 128],
                                  Wst[widx][:, dc * 512:(dc + 1) * 512],
                                  start=(dc == 0), stop=(dc == NDC - 1))
                ch_t.inc(mm)
        ctx.block.tensor(bmm)

        def bev(v, tag=tag, cb=cb, t0b=t0b, bi=bi):
            hw = 0 if cb in (0, 1024, 2048) else 1
            for i in range(NCHUNK):
                v.wait_ge(ch_t.h, t0b + i + 1)
                ps = [psA, psB][i % 2]
                if tag == "q":
                    v.tensor_copy(scrb[:, 0:512], ps[:, 0:512])
                    v.tensor_mul(scr[:, 0:512], scrb[:, 0:512], scrb[:, 0:512])
                    ch_v.inc(v.reduce_sum(gsc[:, 32:40].rearrange("p (h o) -> p h o", o=1),
                                          scr[:, 0:512].rearrange("p (h e) -> p h e", h=8),
                                          axis=AX.X))
                    v.wait_ge(ch_v.h, ch_v.n)
                    qcols = slice(192 + hw * 8, 192 + hw * 8 + 8)
                    ch_v.inc(v.tensor_max(gsc[:, qcols], gsc[:, qcols], gsc[:, 32:40]))
                    v.wait_ge(ch_v.h, ch_v.n)
                elif tag == "k":
                    dst = k_acc[:, i * 1024 + hw * 512: i * 1024 + hw * 512 + 512]
                    ch_v.inc(v.scalar_tensor_tensor(dst, ps[:, 0:512], 1.0,
                                                    kvb[:, hw * 512:(hw + 1) * 512],
                                                    ALU.mult, ALU.add))
                elif tag == "gk":
                    ggb = gg[:, i * 16 + hw * 8: i * 16 + (hw + 1) * 8]
                    ggx = ggb.rearrange("p (h o) -> p h o", o=1).broadcast_to((128, 8, 64))
                    v.tensor_tensor(scrb[:, 0:512].rearrange("p (h e) -> p h e", h=8),
                                    ps[:, 0:512].rearrange("p (h e) -> p h e", h=8), ggx, ALU.mult)
                    dst = k_acc[:, i * 1024 + hw * 512: i * 1024 + hw * 512 + 512]
                    ch_v.inc(v.tensor_add(dst, dst, scrb[:, 0:512]))
                elif tag == "v":
                    dst = shared[:, :].rearrange("p (c x) -> p c x", c=16)[:, i, :] \
                        .rearrange("p (h e) -> p h e", h=16)[:, hw * 8:(hw + 1) * 8, 0:64]
                    vbb = kvb[:, 1024 + hw * 512:1024 + (hw + 1) * 512] \
                        .rearrange("p (h e) -> p h e", h=8)
                    ch_v.inc(v.scalar_tensor_tensor(
                        dst, ps[:, 0:512].rearrange("p (h e) -> p h e", h=8), 1.0,
                        vbb, ALU.mult, ALU.add))
                elif tag == "gv":
                    ggb = gg[:, i * 16 + hw * 8: i * 16 + (hw + 1) * 8]
                    ggx = ggb.rearrange("p (h o) -> p h o", o=1).broadcast_to((128, 8, 64))
                    v.tensor_tensor(scrb[:, 0:512].rearrange("p (h e) -> p h e", h=8),
                                    ps[:, 0:512].rearrange("p (h e) -> p h e", h=8), ggx, ALU.mult)
                    dst = shared[:, :].rearrange("p (c x) -> p c x", c=16)[:, i, :] \
                        .rearrange("p (h e) -> p h e", h=16)[:, hw * 8:(hw + 1) * 8, 0:64]
                    ch_v.inc(v.tensor_add(dst, dst, scrb[:, 0:512]
                                          .rearrange("p (h e) -> p h e", h=8)))
            v.wait_ge(ch_v.h, ch_v.n)
        ctx.block.vector(bev)
        wbm[bi] = ch_t.n
        bi += 1
    def vones(v):
        dst = shared[:, :].rearrange("p (c h e) -> p c h e", c=16, h=16)[:, :, :, 64:65]
        ch_v.inc(v.memset(dst, 1.0))
        v.wait_ge(ch_v.h, ch_v.n)
    ctx.block.vector(vones)
    vext_done = ch_v.n

    t0c = ch_t.n

    def c_t(t, gate=vext_done):
        t.wait_ge(ch_v.h, gate)
        ch_t.inc(t.transpose(psC[0:16, 0:128], gsc[:, 192:208], identf[:]))
        ch_t.inc(t.transpose(psD[0:16, 0:128], gsc[:, 208:224], identf[:]))
        t.wait_ge(ch_v.h, CREP_M[0])
        ch_t.inc(t.transpose(psC[0:1, 0:16], gsc[0:16, 40:41], identf[0:16, 0:16]))
        t.wait_ge(ch_v.h, CREP_M[1])
        ch_t.inc(t.matmul(psD[:, 0:16], onesf[0:1, 0:128], scr[0:1, 0:16],
                          start=True, stop=True))
    CREP_M = {}

    def c_v(v, t0c=t0c):
        v.wait_ge(ch_t.h, t0c + 2)
        v.tensor_copy(gsc[0:16, 32:33].rearrange("p (a b) -> p (a b)") if False else gsc[0:16, 33:34], psC[0:16, 0:1]) if False else None
        ch_v.inc(v.reduce_max(gsc[0:16, 34:35], psC[0:16, 0:128], axis=AX.X))
        ch_v.inc(v.reduce_max(gsc[0:16, 35:36], psD[0:16, 0:128], axis=AX.X))
        v.wait_ge(ch_v.h, ch_v.n)
        ch_v.inc(v.tensor_mul(gsc[0:16, 36:37], gsc[0:16, 34:35], gsc[0:16, 35:36]))
        v.wait_ge(ch_v.h, ch_v.n)
    ctx.block.vector(c_v)
    cv1 = ch_v.n
    s0c = ch_s.n

    def c_s(s_, cv1=cv1):
        s_.wait_ge(ch_v.h, cv1)
        ch_s.inc(s_.activation(gsc[0:16, 37:38], gsc[0:16, 36:37], AF.Sqrt))
    ctx.block.scalar(c_s)
    cs1 = ch_s.n

    def c_v2(v, cs1=cs1):
        v.wait_ge(ch_s.h, cs1)
        ch_v.inc(v.tensor_scalar(gsc[0:16, 40:41], gsc[0:16, 37:38], -SCALE, None, ALU.mult))
        v.wait_ge(ch_v.h, ch_v.n)
        CREP_M[0] = ch_v.n
    ctx.block.vector(c_v2)
    CREP_M[0] = ch_v.n

    def c_v3(v, t0c=t0c):
        v.wait_ge(ch_t.h, t0c + 3)
        ch_v.inc(v.tensor_copy(scr[0:1, 0:16], psC[0:1, 0:16]))
        v.wait_ge(ch_v.h, ch_v.n)
        CREP_M[1] = ch_v.n
    ctx.block.vector(c_v3)
    CREP_M[1] = ch_v.n

    def c_v4(v, t0c=t0c):
        v.wait_ge(ch_t.h, t0c + 4)
        ch_v.inc(v.tensor_copy(gsc[:, 48:64].bitcast(F32)[:, 0:16] if False else negcrep[:, 0:16],
                               psD[:, 0:16]))
        v.wait_ge(ch_v.h, ch_v.n)
    negcrep = g["negcrep"]
    ctx.block.tensor(c_t)
    ctx.block.vector(c_v4)
    crep_done = ch_v.n

    for h in range(NH):
        hp, hl = h // 2, h % 2
        d0 = ch_wd.n

        def qdma(gq, h=h, gate=wbm.get(len(blocks) - 2, 0)):
            if h == 0 and gate:
                gq.wait_ge(ch_t.h, gate)
            if h >= 1:
                gq.wait_ge(ch_t.h, QWH_T[h - 1])
            ch_wd.inc(gq.dma_start(out=qwh[:, :].rearrange("p (dc m) -> p dc m", dc=8),
                                   in_=QKVW[:, h * 64:(h + 1) * 64]
                                   .rearrange("(dc p) m -> p dc m", p=128)), 16)
        ctx.block.gpsimd(qdma)
        qw_n = ch_wd.n
        t0q = ch_t.n

        def qmm(t, h=h, qw_n=qw_n, gate=ch_v.n, kd=kvT_done[0]):
            t.wait_ge(ch_wd.h, qw_n)
            t.wait_ge(trb.che.h, kd)
            t.wait_ge(ch_v.h, gate)
            x3r4 = x3nT.rearrange("p (i x) -> p i x", i=16)
            for tb in range(4):
                for dc in range(NDC):
                    mm = t.matmul(psC[0:64, 0:512], qwh[:, dc * 64:(dc + 1) * 64],
                                  x3r4[:, tb * 4:(tb + 1) * 4, dc * 128:(dc + 1) * 128],
                                  start=(dc == 0), stop=(dc == NDC - 1))
                ch_t.inc(mm)
                t.wait_ge(ch_s.h, QEV_S[(h, tb)])
        QWH_T = {}
        QEV_S = {}
        s0q = ch_s.n
        for tb in range(4):
            QEV_S[(h, tb)] = s0q + tb + 1

        def qev(s_, h=h, hp=hp, hl=hl, t0q=t0q, crd=crep_done):
            s_.wait_ge(ch_v.h, crd)
            for tb in range(4):
                s_.wait_ge(ch_t.h, t0q + tb + 1)
                ch_s.inc(s_.activation(qTh[0:64, tb * 512:(tb + 1) * 512], psC[0:64, 0:512],
                                       AF.Identity, bias=qb[64 * hl:64 * hl + 64, hp:hp + 1]))
        ctx.block.tensor(qmm)
        ctx.block.scalar(qev)
        QWH_T[h] = ch_t.n

        def khcp(v, h=h, hp=hp, hl=hl, gate=ch_s.n):
            v.wait_ge(ch_s.h, gate)
            ch_v.inc(v.tensor_copy(khT[0:64, :], kvT[64 * hl:64 * hl + 64,
                                                     hp * 2048:(hp + 1) * 2048]))
            v.wait_ge(ch_v.h, ch_v.n)
        ctx.block.vector(khcp)
        kh_done = ch_v.n

        outps = [psB[0:65, 0:512], psB[0:65, 512:1024], psC[0:65, 0:512], psD[0:65, 0:512]]
        t0a = ch_t.n
        SC_V = {}

        def att_t(t, h=h, t0a=t0a, kh=kh_done):
            t.wait_ge(ch_v.h, kh)
            for si in range(16):
                for tb in range(si // 4, 4):
                    gidx = (si, tb)
                    if SC_WAIT.get(gidx):
                        t.wait_ge(ch_s.h, SC_WAIT[gidx])
                    ch_t.inc(t.matmul(psA[:, (tb % 2) * 512:(tb % 2) * 512 + 512],
                                      khT[0:64, si * 128:(si + 1) * 128],
                                      qTh[0:64, tb * 512:(tb + 1) * 512],
                                      start=True, stop=True))
                    t.wait_ge(ch_v.h, PV_WAIT[gidx])
                    ch_t.inc(t.matmul(outps[tb],
                                      shared[:, :].rearrange("p (c x) -> p c x", c=16)[:, si, :]
                                      [:, h * 65:h * 65 + 65],
                                      Ptile[:, tb * 512:(tb + 1) * 512],
                                      start=(si == 0), stop=(si == (tb * 4 + 3))))
        SC_WAIT = {}
        PV_WAIT = {}
        s0a = ch_s.n
        v0a = ch_v.n
        cnt_t = 0
        cnt_s = 0
        cnt_v = 0
        for si in range(16):
            for tb in range(si // 4, 4):
                cnt_t += 1
                SC_WAIT[(si, tb)] = (s0a + cnt_s) if cnt_s else 0
                cnt_s += 1
                if tb == si // 4:
                    cnt_v += 1
                PV_WAIT[(si, tb)] = v0a + cnt_v
                cnt_t += 1
        SC_WAIT = {}
        seq = []
        for si in range(16):
            for tb in range(si // 4, 4):
                seq.append((si, tb))
        for n, gidx in enumerate(seq):
            SC_WAIT[gidx] = (s0a + n - 1) if n >= 2 else 0
        PV_WAIT = {}
        nv = 0
        for n, gidx in enumerate(seq):
            si, tb = gidx
            if tb == si // 4:
                nv += 1
                PV_WAIT[gidx] = v0a + nv
            else:
                PV_WAIT[gidx] = 0
        PV_SWAIT = {}
        for n, gidx in enumerate(seq):
            si, tb = gidx
            PV_SWAIT[gidx] = s0a + n + 1
        del att_t

        def att_t(t, h=h, t0a=t0a, kh=kh_done):
            t.wait_ge(ch_v.h, kh)
            for n, (si, tb) in enumerate(seq):
                if SC_WAIT[(si, tb)]:
                    t.wait_ge(ch_s.h, SC_WAIT[(si, tb)])
                ch_t.inc(t.matmul(psA[:, (n % 2) * 512:(n % 2) * 512 + 512],
                                  khT[0:64, si * 128:(si + 1) * 128],
                                  qTh[0:64, tb * 512:(tb + 1) * 512],
                                  start=True, stop=True))
                t.wait_ge(ch_s.h, PV_SWAIT[(si, tb)])
                if PV_WAIT[(si, tb)]:
                    t.wait_ge(ch_v.h, PV_WAIT[(si, tb)])
                ch_t.inc(t.matmul(outps[tb],
                                  shared[:, :].rearrange("p (c x) -> p c x", c=16)[:, si, :]
                                  [:, h * 65:h * 65 + 65],
                                  Ptile[:, tb * 512:(tb + 1) * 512],
                                  start=(si == 0), stop=(si == tb * 4 + 3)))

        def att_s(s_, h=h, t0a=t0a):
            for n, (si, tb) in enumerate(seq):
                s_.wait_ge(ch_t.h, t0a + 2 * n + 1)
                ch_s.inc(s_.activation(Ptile[:, tb * 512:(tb + 1) * 512],
                                       psA[:, (n % 2) * 512:(n % 2) * 512 + 512],
                                       AF.Exp, scale=SCALE,
                                       bias=negcrep[:, h:h + 1]))

        def att_v(v, h=h, s0a=s0a):
            for n, (si, tb) in enumerate(seq):
                if tb == si // 4:
                    v.wait_ge(ch_s.h, s0a + n + 1)
                    ch_v.inc(v.tensor_mul(Ptile[:, tb * 512:(tb + 1) * 512],
                                          Ptile[:, tb * 512:(tb + 1) * 512],
                                          cmask[:, (si % 4) * 512:(si % 4 + 1) * 512]))
                    v.wait_ge(ch_v.h, ch_v.n)
        ctx.block.tensor(att_t)
        ctx.block.scalar(att_s)
        ctx.block.vector(att_v)
        att_done_t = ch_t.n

        for tb in range(4):
            t0n = ch_t.n

            def nrm_v(v, h=h, tb=tb, adt=att_done_t):
                v.wait_ge(ch_t.h, adt)
                ch_v.inc(v.tensor_copy(ob65[0:65, 0:512], outps[tb]))
                v.wait_ge(ch_v.h, ch_v.n)
                ch_v.inc(v.reciprocal(ob65[64:65, 0:512], ob65[64:65, 0:512]))
                v.wait_ge(ch_v.h, ch_v.n)
            ctx.block.vector(nrm_v)
            nv1 = ch_v.n

            def nrm_t(t, nv1=nv1):
                t.wait_ge(ch_v.h, nv1)
                ch_t.inc(t.matmul(psA[0:64, 0:512], ones2[64:65, 0:64], ob65[64:65, 0:512],
                                  start=True, stop=True))
            ctx.block.tensor(nrm_t)
            nt1 = ch_t.n

            def nrm_v2(v, h=h, hp=hp, hl=hl, tb=tb, nt1=nt1):
                v.wait_ge(ch_t.h, nt1)
                ch_v.inc(v.tensor_tensor(
                    outT[64 * hl:64 * hl + 64, hp * 2048 + tb * 512: hp * 2048 + (tb + 1) * 512],
                    ob65[0:64, 0:512], psA[0:64, 0:512], ALU.mult))
                v.wait_ge(ch_v.h, ch_v.n)
            ctx.block.vector(nrm_v2)
    heads_done = ch_v.n

    d0 = ch_wd.n

    def owdma(gq, gate=ch_t.n):
        gq.wait_ge(ch_t.h, gate)
        for mb in range(2):
            ch_wd.inc(gq.dma_start(out=Wst[mb][:, :].rearrange("p (dc m) -> p dc m", dc=8),
                                   in_=OUTW[:, mb * 512:(mb + 1) * 512]
                                   .rearrange("(dc p) m -> p dc m", p=128)), 16)
    ctx.block.gpsimd(owdma)
    ow_n = ch_wd.n
    sp1 = ch_sp.n

    v0w_pre = ch_v.n

    def x2dma(e, hd=heads_done, v0w=v0w_pre):
        e.wait_ge(ch_v.h, hd)
        xr = spill.rearrange("(c p) d -> c p d", p=128)
        for i in range(NCHUNK):
            if i >= 1:
                e.wait_ge(ch_v.h, v0w + 2 * i)
            ch_sp.inc(e.dma_start(out=scr[:, 0:1024], in_=xr[i]), 16)
            e.wait_ge(ch_v.h, v0w + 2 * i + 2)
            ch_sp.inc(e.dma_start(out=x3dram.rearrange("(c p) d -> c p d", p=128)[i],
                                  in_=x3st[:, 0:1024]), 16)
    v0w = ch_v.n
    t0w = ch_t.n

    def ow_t(t, ow_n=ow_n, hd=heads_done, v0w=v0w):
        t.wait_ge(ch_wd.h, ow_n)
        t.wait_ge(ch_v.h, hd)
        for i in range(NCHUNK):
            if i >= 1:
                t.wait_ge(ch_v.h, v0w + 2 * i - 1)
            for hp in range(8):
                lhs = outT[:, hp * 2048 + i * 128: hp * 2048 + (i + 1) * 128]
                t.matmul(psA[:, 0:512], lhs, Wst[0][:, hp * 512:(hp + 1) * 512],
                         start=(hp == 0), stop=(hp == 7))
                mm = t.matmul(psA[:, 512:1024], lhs, Wst[1][:, hp * 512:(hp + 1) * 512],
                              start=(hp == 0), stop=(hp == 7))
            ch_t.inc(mm)

    def ow_v(v, t0w=t0w, sp1=sp1):
        for i in range(NCHUNK):
            v.wait_ge(ch_t.h, t0w + i + 1)
            if i >= 1:
                v.wait_ge(ch_sp.h, sp1 + 32 * i)
            ch_v.inc(v.scalar_tensor_tensor(x3st[:, 0:1024], psA[:, 0:1024], 1.0,
                                            obbc[:, 0:1024], ALU.mult, ALU.add))
            v.wait_ge(ch_v.h, ch_v.n)
            v.wait_ge(ch_sp.h, sp1 + 32 * i + 16)
            ch_v.inc(v.tensor_add(x3st[:, 0:1024], x3st[:, 0:1024], scr[:, 0:1024]))
            v.wait_ge(ch_v.h, ch_v.n)
    ctx.block.sync(x2dma)
    ctx.block.tensor(ow_t)
    ctx.block.vector(ow_v)
    x3_sp = ch_sp.n

    def back_dma(e, x3_sp=x3_sp):
        e.wait_ge(ch_sp.h, x3_sp)
        if L1A_NOBACK:
            return
        xr = x3dram.rearrange("(c p) d -> c p d", p=128)
        for i in range(NCHUNK):
            ch_sp.inc(e.dma_start(out=xres[:, i * 1024:(i + 1) * 1024], in_=xr[i]), 16)
    ctx.block.sync(back_dma)
    bk = ch_sp.n

    def fin_v(v, bk=bk):
        v.wait_ge(ch_sp.h, bk)
        ch_v.inc(v.memset(gsc[:, 0:1], 0.0))
        v.wait_ge(ch_v.h, ch_v.n)
    ctx.block.vector(fin_v)
    return ch_v, ch_v.n



EMITTERS = {}
DEBUG_DUMPS = []
MOE_GATE_ONLY = False
L0A_LAYERS = 2
L0A_NST = 8
L0A_FFN = True
L0A_ST_PARTS = 9
L1A_NOBACK = False


def build_program(blocks, weights_shapes):
    nc = bass.Bass("TRN2", target_bir_lowering=False)
    x_in = nc.dram_tensor("x_in", [T, D], F32, kind="ExternalInput").ap()
    x_out = nc.dram_tensor("x_out", [T, D], F32, kind="ExternalOutput").ap()
    waps = {wn: nc.dram_tensor(wn, list(shp), F32, kind="ExternalInput").ap()
            for wn, shp in weights_shapes.items()}

    with nc.Block() as block:
        ctx = Ctx(nc, block)
        xres = ctx.sbuf("xres", [128, NCHUNK * 1024], F32)
        ident = ctx.sbuf("ident", [128, 128], BF16)
        identf = ctx.sbuf("identf", [128, 128], F32)
        scr0 = ctx.sbuf("scr_lnsq", [128, 1024], F32)
        if len(blocks) > 1:
            ctx.arena_bytes = 137 * 1024
            ctx.arena = nc.alloc_sbuf_tensor("arena", [128, ctx.arena_bytes], mybir.dt.int8)
            ctx.arena_ptr = 0
        else:
            ctx.arena = None
        identc = ctx.const("ident", _ident_bf16)
        identcf = ctx.const("identf", lambda: np.eye(128, dtype=np.float32))
        ch_x = ctx.chan("xin")

        def load_x(e):
            ch_x.inc(e.dma_start(out=ident[:], in_=identc[:, :]), 16)
            ch_x.inc(e.dma_start(out=identf[:], in_=identcf[:, :]), 16)
            xr = x_in.rearrange("(c p) d -> c p d", p=128)
            for i in range(NCHUNK):
                ch_x.inc(e.dma_start(out=xres[:, i * 1024:(i + 1) * 1024], in_=xr[i]), 16)
        block.sync(load_x)

        def gate(e):
            e.wait_ge(ch_x.h, 16 * (NCHUNK + 2))
        block.vector(gate)
        block.scalar(gate)
        block.tensor(gate)
        block.gpsimd(gate)

        done = None
        for bidx_, b in enumerate(blocks):
            if ctx.arena is not None:
                ctx.begin_block()
            if b == "l1f":
                done = emit_l1_ffn(ctx, waps["L1_ffn_weight"])
            elif b == "l0f":
                done = emit_l0_ffn(ctx, waps)
            elif b == "l0a":
                done = emit_l0_attn(ctx, waps)
            elif b == "l1a":
                done = emit_l1_attn(ctx, waps)
            else:
                done = EMITTERS[b](ctx, waps)
            if bidx_ < len(blocks) - 1:
                def _bar(e, dc=done):
                    e.wait_ge(dc[0].h, dc[1])
                block.vector(_bar)
                block.scalar(_bar)
                block.tensor(_bar)
                block.gpsimd(_bar)
                block.sync(_bar)

        ch_out = ctx.chan("xout")
        dump_aps = {}
        for dn in DEBUG_DUMPS:
            th = ctx.tiles[dn]
            shp = list(th.shape)
            dt = th.dtype if hasattr(th, "dtype") else F32
            dump_aps[dn] = nc.dram_tensor("dump_" + dn, shp, dt, kind="ExternalOutput").ap()

        def store_x(e):
            e.wait_ge(done[0].h, done[1])
            xr = x_out.rearrange("(c p) d -> c p d", p=128)
            for i in range(NCHUNK):
                ch_out.inc(e.dma_start(out=xr[i], in_=xres[:, i * 1024:(i + 1) * 1024]), 16)
            for dn, ap in dump_aps.items():
                ch_out.inc(e.dma_start(out=ap[:, :], in_=ctx.tiles[dn][:, :]), 16)
            e.wait_ge(ch_out.h, 16 * (NCHUNK + len(dump_aps)))
        block.sync(store_x)
    return nc, ctx


BLOCK_WEIGHTS = {
    "l0a": ["L0_attn_Wq", "L0_attn_Wk", "L0_attn_Wv", "L0_attn_Wg", "L0_attn_Wo",
            "L0_attn_gn_scale", "L0_attn_gn_bias", "L0_attn_ln1_scale", "L0_attn_ln1_bias",
            "L0_attn_ln2_scale", "L0_attn_ln2_bias", "L0_attn_ffn_w1", "L0_attn_ffn_w2"],
    "l0f": ["L0_ffn_Wg", "L0_ffn_bg", "L0_ffn_W1", "L0_ffn_b1", "L0_ffn_W2", "L0_ffn_b2"],
    "l1a": ["L1_attn_qkv_w", "L1_attn_qkv_b", "L1_attn_w1_write", "L1_attn_w2_write",
            "L1_attn_w1_read", "L1_attn_w2_read", "L1_attn_kv_geo_w", "L1_attn_out_w",
            "L1_attn_out_b", "L1_attn_kv_gate", "L1_attn_mem_scale", "L1_attn_decay_logits"],
    "l1f": ["L1_ffn_weight"],
}


def run_blocks(x_full, inputs, blocks):
    wnames = []
    for b in blocks:
        wnames += BLOCK_WEIGHTS[b]
    shapes = {}
    for w in wnames:
        shp = np.asarray(inputs[w]).shape
        if w == "L0_ffn_bg":
            shp = (1, 8)
        if shp == ():
            shp = (1,)
        shapes[w] = shp
    nc, ctx = build_program(blocks, shapes)
    in_maps = []
    for c in range(8):
        m = {"x_in": np.ascontiguousarray(np.asarray(x_full[c], dtype=np.float32))}
        perm = np.concatenate([np.arange(0, 128, 2), np.arange(1, 128, 2)])
        for w in wnames:
            a = np.ascontiguousarray(np.asarray(inputs[w], dtype=np.float32))
            if a.ndim == 0:
                a = a.reshape(1)
            if w == "L0_ffn_bg":
                a = a.reshape(1, 8)
            if w in ("L0_attn_Wq", "L0_attn_Wk"):
                a = np.ascontiguousarray(a[..., perm])
            m[w] = a
        for cn, arr in ctx.consts_np.items():
            m["c_" + cn] = arr
        in_maps.append(m)
    res = run_bass_kernel_spmd(nc, in_maps, core_ids=list(range(8)))
    if DEBUG_DUMPS:
        global LAST_DUMPS
        LAST_DUMPS = {dn: res.results[0]["dump_" + dn] for dn in DEBUG_DUMPS}
    return np.stack([res.results[c]["x_out"] for c in range(8)], axis=0)


def kernel(**inputs):
    x = np.asarray(inputs["L0_attn_x"], dtype=np.float32)
    return run_blocks(x, inputs, ["l0a", "l0f", "l1a", "l1f"])


# version 04 (final)
# speedup vs baseline: 1.2097x; 1.0825x over previous
import math
import numpy as np
import ml_dtypes
import concourse.bass as bass
import concourse.mybir as mybir
from concourse.bass_utils import run_bass_kernel_spmd

F32 = mybir.dt.float32
BF16 = mybir.dt.bfloat16
AF = mybir.ActivationFunctionType
ALU = mybir.AluOpType
AX = mybir.AxisListType

T = 2048
D = 1024
NCHUNK = T // 128
NDC = D // 128


SHARED_TILES = {"xres", "scr_lnsq", "ident", "identf"}


class Chan:
    __slots__ = ("h", "n")

    def __init__(self, nc, name):
        self.h = nc.alloc_semaphore(name)
        self.n = 0

    def inc(self, inst, k=1):
        inst.then_inc(self.h, k)
        self.n += k
        return self.n


class Ctx:
    def __init__(self, nc, block):
        self.nc = nc
        self.block = block
        self.chans = {}
        self.tiles = {}
        self.consts_np = {}
        self.const_aps = {}
        self._trans_wait = (None, 0)

    def chan(self, name):
        if name not in self.chans:
            self.chans[name] = Chan(self.nc, name)
        return self.chans[name]

    def begin_block(self):
        self.arena_ptr = 0

    def sbuf(self, name, shape, dtype):
        if name not in self.tiles:
            if getattr(self, "arena", None) is not None and name not in SHARED_TILES:
                esz = 2 if dtype == BF16 else 4
                nbytes = shape[1] * esz
                ptr = (self.arena_ptr + 63) & ~63
                assert ptr + nbytes <= self.arena_bytes, \
                    f"arena overflow: {name} needs {nbytes} at {ptr} (cap {self.arena_bytes})"
                self.arena_ptr = ptr + nbytes
                view = self.arena[:, ptr:ptr + nbytes].bitcast(dtype)
                if shape[0] < 128:
                    view = view[0:shape[0], :]
                self.tiles[name] = view
            else:
                self.tiles[name] = self.nc.alloc_sbuf_tensor(name, shape, dtype)
        return self.tiles[name]

    def psum(self, name, shape, dtype=F32):
        if name not in self.tiles:
            n_el = 1024 if dtype == BF16 else 512
            shape = [shape[0], max(shape[1], n_el) if shape[1] <= n_el else shape[1]]
            self.tiles[name] = self.nc.alloc_psum_tensor(name, shape, dtype)
        return self.tiles[name]

    def const(self, name, arr_fn):
        if name not in self.const_aps:
            arr = np.ascontiguousarray(arr_fn() if callable(arr_fn) else arr_fn)
            self.consts_np[name] = arr
            dt = BF16 if arr.dtype == ml_dtypes.bfloat16 else F32
            self.const_aps[name] = self.nc.dram_tensor(
                "c_" + name, list(arr.shape), dt, kind="ExternalInput").ap()
        return self.const_aps[name]


def _ident_bf16():
    return np.eye(128).astype(ml_dtypes.bfloat16)


def _hadamard_bf16():
    h = np.array([[1.0]], dtype=np.float64)
    while h.shape[0] < D:
        h = np.block([[h, h], [h, -h]])
    return (h / np.sqrt(D)).astype(np.float32).astype(ml_dtypes.bfloat16)



def emit_transposes(ctx, pairs):
    ps = [ctx.psum("tpsA", [128, 1024], BF16), ctx.psum("tpsB", [128, 1024], BF16)]
    ident = ctx.sbuf("ident", [128, 128], BF16)
    cht = ctx.chan("tr_t")
    chv = ctx.chan("tr_v")
    wait_ch, wait_val = ctx._trans_wait
    base_t = cht.n
    base_v = chv.n

    def tsec(t):
        t.wait_ge(chv.h, base_v)
        if wait_ch is not None:
            t.wait_ge(wait_ch.h, wait_val)
        for k, (src, dst) in enumerate(pairs):
            if k >= 2:
                t.wait_ge(chv.h, base_v + k - 1)
            cht.inc(t.transpose(ps[k % 2][0:src.shape[-1], 0:src.shape[0]], src, ident[:]))

    def vsec(v):
        for k, (src, dst) in enumerate(pairs):
            v.wait_ge(cht.h, base_t + k + 1)
            chv.inc(v.tensor_copy(dst, ps[k % 2][0:src.shape[-1], 0:src.shape[0]]))

    ctx.block.tensor(tsec)
    ctx.block.vector(vsec)
    return chv, chv.n


def emit_ln(ctx, name, src_cols, dst_cols, eps=1e-5, wait=None, nchunk=NCHUNK, lo_cols=None):
    st = ctx.sbuf(name + "_stat", [128, 5 * NCHUNK], F32)
    sums = st[:, 0 * NCHUNK:0 * NCHUNK + nchunk]
    ss = st[:, 1 * NCHUNK:1 * NCHUNK + nchunk]
    mean = st[:, 2 * NCHUNK:2 * NCHUNK + nchunk]
    var = st[:, 3 * NCHUNK:3 * NCHUNK + nchunk]
    rstd = st[:, 4 * NCHUNK:4 * NCHUNK + nchunk]
    sq = ctx.sbuf("scr_lnsq", [128, 1024], F32)
    chv = ctx.chan(name + "_v")
    chs = ctx.chan(name + "_s")

    def vsec(v):
        if wait is not None:
            v.wait_ge(wait[0].h, wait[1])
        for i in range(nchunk):
            mrk = v.reduce_sum(sums[:, i:i + 1], src_cols(i), axis=AX.X)
        chv.inc(mrk)
        v.wait_ge(chv.h, chv.n)
        chv.inc(v.tensor_scalar(mean[:, 0:nchunk], sums[:, 0:nchunk], 1.0 / D, None, ALU.mult))
        v.wait_ge(chv.h, chv.n)
        v.wait_ge(chs.h, 1)
        chv.inc(v.tensor_scalar(var[:, 0:nchunk], ss[:, 0:nchunk], 1.0 / D, eps, ALU.mult, ALU.add))
        v.wait_ge(chv.h, chv.n)
        chv.inc(v.scalar_tensor_tensor(var[:, 0:nchunk], mean[:, 0:nchunk], -1.0, var[:, 0:nchunk],
                                       ALU.mult, ALU.elemwise_mul) if False else
                v.tensor_tensor(sq[:, 0:nchunk], mean[:, 0:nchunk], mean[:, 0:nchunk], ALU.mult))
        v.wait_ge(chv.h, chv.n)
        chv.inc(v.tensor_sub(var[:, 0:nchunk], var[:, 0:nchunk], sq[:, 0:nchunk]))
        v.wait_ge(chs.h, 2)
        chv.inc(v.reciprocal(rstd[:, 0:nchunk], var[:, 0:nchunk]))
        v.wait_ge(chv.h, chv.n)
        for i in range(nchunk):
            if lo_cols is None:
                chv.inc(v.tensor_scalar(dst_cols(i), src_cols(i), mean[:, i:i + 1], rstd[:, i:i + 1],
                                        ALU.subtract, ALU.mult))
            else:
                v.tensor_scalar(sq[:], src_cols(i), mean[:, i:i + 1], rstd[:, i:i + 1],
                                ALU.subtract, ALU.mult)
                v.tensor_copy(dst_cols(i), sq[:])
                chv.inc(v.tensor_sub(lo_cols(i), sq[:], dst_cols(i)))

    def ssec(s_):
        if wait is not None:
            s_.wait_ge(wait[0].h, wait[1])
        for i in range(nchunk):
            mrk = s_.activation(sq[:], src_cols(i), AF.Square, accum_out=ss[:, i:i + 1])
        chs.inc(mrk)
        s_.wait_ge(chv.h, 5)
        chs.inc(s_.activation(var[:, 0:nchunk], var[:, 0:nchunk], AF.Sqrt))

    ctx.block.vector(vsec)
    ctx.block.scalar(ssec)
    return chv, chv.n


def emit_ln_to_znT(ctx, name, src_cols, out_name, eps=1e-5, wait=None, hilo=False):
    zn = ctx.sbuf(name + "_zn", [128, NCHUNK * 1024], BF16)
    znT = ctx.sbuf(out_name, [128, NDC * T], BF16)
    lo_cols = None
    if hilo:
        zn_lo = ctx.sbuf(name + "_znlo", [128, NCHUNK * 1024], BF16)
        znT_lo = ctx.sbuf(out_name + "_lo", [128, NDC * T], BF16)
        lo_cols = lambda i: zn_lo[:, i * 1024:(i + 1) * 1024]
    ch_ln, v_ln = emit_ln(ctx, name, src_cols,
                          lambda i: zn[:, i * 1024:(i + 1) * 1024], eps=eps, wait=wait,
                          lo_cols=lo_cols)
    pairs = []
    for i in range(NCHUNK):
        for dc in range(NDC):
            pairs.append((zn[:, i * 1024 + dc * 128: i * 1024 + (dc + 1) * 128],
                          znT[:, dc * T + 128 * i: dc * T + 128 * (i + 1)]))
            if hilo:
                pairs.append((zn_lo[:, i * 1024 + dc * 128: i * 1024 + (dc + 1) * 128],
                              znT_lo[:, dc * T + 128 * i: dc * T + 128 * (i + 1)]))
    ctx._trans_wait = (ch_ln, v_ln)
    ch_tr, v_tr = emit_transposes(ctx, pairs)
    return zn, znT, (ch_tr, v_tr)


def emit_ln_T(ctx, name, src_cols, dstT, dstT_lo=None, eps=1e-5, wait=None):
    st = ctx.sbuf(name + "_stat", [128, 5 * NCHUNK], F32)
    sums = st[:, 0:NCHUNK]
    ss = st[:, NCHUNK:2 * NCHUNK]
    mean = st[:, 2 * NCHUNK:3 * NCHUNK]
    var = st[:, 3 * NCHUNK:4 * NCHUNK]
    rstd = st[:, 4 * NCHUNK:5 * NCHUNK]
    sq = ctx.sbuf("scr_lnsq", [128, 1024], F32)
    hilo = dstT_lo is not None
    K = 16 if hilo else 8
    ring = ctx.sbuf(name + "_ring", [128, 2 * 1024], BF16)
    ring_lo = ctx.sbuf(name + "_ringlo", [128, 2 * 1024], BF16) if hilo else None
    ident = ctx.sbuf("ident", [128, 128], BF16)
    ps = [ctx.psum("tpsA", [128, 1024], BF16), ctx.psum("tpsB", [128, 1024], BF16)]
    chv = ctx.chan(name + "_v")
    chs = ctx.chan(name + "_s")
    chw = ctx.chan(name + "_w")
    che = ctx.chan(name + "_e")
    cht = ctx.chan(name + "_t")

    def tdst(i, j):
        dc = j % 8
        tgt = dstT if j < 8 else dstT_lo
        return tgt[:, dc * T + 128 * i: dc * T + 128 * (i + 1)]

    def vsec(v):
        if wait is not None:
            v.wait_ge(wait[0].h, wait[1])
        for i in range(NCHUNK):
            mrk = v.reduce_sum(sums[:, i:i + 1], src_cols(i), axis=AX.X)
        chv.inc(mrk)
        v.wait_ge(chv.h, chv.n)
        chv.inc(v.tensor_scalar(mean[:, :], sums[:, :], 1.0 / D, None, ALU.mult))
        v.wait_ge(chv.h, chv.n)
        v.wait_ge(chs.h, 1)
        chv.inc(v.tensor_scalar(var[:, :], ss[:, :], 1.0 / D, eps, ALU.mult, ALU.add))
        v.wait_ge(chv.h, chv.n)
        chv.inc(v.tensor_tensor(sq[:, 0:NCHUNK], mean[:, :], mean[:, :], ALU.mult))
        v.wait_ge(chv.h, chv.n)
        chv.inc(v.tensor_sub(var[:, :], var[:, :], sq[:, 0:NCHUNK]))
        v.wait_ge(chs.h, 2)
        chv.inc(v.reciprocal(rstd[:, :], var[:, :]))
        v.wait_ge(chv.h, chv.n)
        for i in range(NCHUNK):
            sl = (i % 2) * 1024
            if i >= 2:
                v.wait_ge(cht.h, K * (i - 1))
            if hilo:
                v.tensor_scalar(sq[:], src_cols(i), mean[:, i:i + 1], rstd[:, i:i + 1],
                                ALU.subtract, ALU.mult)
                v.tensor_copy(ring[:, sl:sl + 1024], sq[:])
                chw.inc(v.tensor_sub(ring_lo[:, sl:sl + 1024], sq[:], ring[:, sl:sl + 1024]))
            else:
                chw.inc(v.tensor_scalar(ring[:, sl:sl + 1024], src_cols(i),
                                        mean[:, i:i + 1], rstd[:, i:i + 1],
                                        ALU.subtract, ALU.mult))
            if i >= 1:
                for j in range(K):
                    v.wait_ge(cht.h, K * (i - 1) + j + 1)
                    che.inc(v.tensor_copy(tdst(i - 1, j), ps[(K * (i - 1) + j) % 2][:, 0:128]))
        for j in range(K):
            v.wait_ge(cht.h, K * (NCHUNK - 1) + j + 1)
            che.inc(v.tensor_copy(tdst(NCHUNK - 1, j), ps[(K * (NCHUNK - 1) + j) % 2][:, 0:128]))

    def ssec(s_):
        if wait is not None:
            s_.wait_ge(wait[0].h, wait[1])
        for i in range(NCHUNK):
            mrk = s_.activation(sq[:], src_cols(i), AF.Square, accum_out=ss[:, i:i + 1])
        chs.inc(mrk)
        s_.wait_ge(chv.h, 5)
        chs.inc(s_.activation(var[:, :], var[:, :], AF.Sqrt))

    base_e = che.n
    base_w = chw.n
    base_t = cht.n

    def tsec(t):
        if wait is not None:
            t.wait_ge(wait[0].h, wait[1])
        for i in range(NCHUNK):
            t.wait_ge(chw.h, base_w + i + 1)
            sl = (i % 2) * 1024
            for j in range(K):
                g = K * i + j
                if g >= 2:
                    t.wait_ge(che.h, base_e + g - 1)
                dc = j % 8
                src = (ring if j < 8 else ring_lo)[:, sl + dc * 128: sl + (dc + 1) * 128]
                cht.inc(t.transpose(ps[g % 2][:, 0:128], src, ident[:]))

    ctx.block.vector(vsec)
    ctx.block.scalar(ssec)
    ctx.block.tensor(tsec)
    return che, che.n


def emit_l0_ffn(ctx, waps):
    xres = ctx.tiles["xres"]
    ident = ctx.sbuf("ident", [128, 128], BF16)
    identf = ctx.tiles["identf"]
    NE, NFB, NTP = (0 if MOE_GATE_ONLY else 8), 4, 4

    xnT = ctx.sbuf("moe_xnT", [128, NDC * T], BF16)
    bufB = ctx.sbuf("moe_bufB", [128, 16384], BF16)
    xnT_lo = bufB[:, 0:16384]
    che_ln, v_ln = emit_ln_T(ctx, "moe_ln", lambda i: xres[:, i * 1024:(i + 1) * 1024],
                             xnT, xnT_lo)

    wgs = ctx.sbuf("moe_wg", [128, 64], F32)
    wg_hi = ctx.sbuf("moe_wg_hi", [128, 64], BF16)
    wg_lo = ctx.sbuf("moe_wg_lo", [128, 64], BF16)
    bgr = ctx.sbuf("moe_bg", [1, 8], F32)
    b2a = ctx.sbuf("moe_b2", [8, 1024], F32)
    ones_f = ctx.sbuf("moe_onesf", [1, 128], F32)
    bgb = ctx.sbuf("moe_bgb", [128, 8], F32)
    ps_bg = ctx.psum("psD", [128, 512], F32)
    ch_sw = ctx.chan("moe_swd")
    ch_sv = ctx.chan("moe_swv")
    ch_bgt = ctx.chan("moe_bgt")

    def sdma(s_):
        wg = waps["L0_ffn_Wg"]
        for dc in range(NDC):
            ch_sw.inc(s_.dma_start(out=wgs[:, dc * 8:(dc + 1) * 8],
                                   in_=wg[dc * 128:(dc + 1) * 128, :]), 16)
        ch_sw.inc(s_.dma_start(out=bgr[0:1, :], in_=waps["L0_ffn_bg"][:, :]), 16)
        ch_sw.inc(s_.dma_start(out=b2a[:, :], in_=waps["L0_ffn_b2"][:, :]), 16)
    ctx.block.scalar(sdma)

    def vsw(v):
        v.wait_ge(ch_sw.h, 16 * (NDC + 2))
        v.tensor_copy(wg_hi[:], wgs[:])
        ch_sv.inc(v.tensor_sub(wg_lo[:], wgs[:], wg_hi[:]))
        ch_sv.inc(v.memset(ones_f[:], 1.0))
        v.wait_ge(ch_bgt.h, 1)
        ch_sv.inc(v.tensor_copy(bgb[:, :], ps_bg[:, 0:8]))
        v.wait_ge(ch_sv.h, ch_sv.n)
    ctx.block.vector(vsw)

    def tsw(t):
        t.wait_ge(ch_sw.h, 16 * (NDC + 2))
        t.wait_ge(ch_sv.h, 2)
        ch_bgt.inc(t.matmul(ps_bg[:, 0:8], ones_f[0:1, 0:128], bgr[0:1, :],
                            start=True, stop=True))
    ctx.block.tensor(tsw)

    cw = ctx.sbuf("moe_cw", [128, NCHUNK * 8], F32)
    gl_dbg = ctx.sbuf("moe_gldbg", [128, NCHUNK * 8], F32)
    pr_dbg = ctx.sbuf("moe_prdbg", [128, NCHUNK * 8], F32)
    m8_dbg = ctx.sbuf("moe_m8dbg", [128, NCHUNK * 8], F32)
    cwT = ctx.sbuf("moe_cwT", [8, T], F32)
    gs = ctx.sbuf("moe_gs", [128, 32], F32)
    probs = ctx.sbuf("moe_probs", [128, 8], F32)
    ps_g = ctx.psum("psC", [128, 512], F32)
    ps_d = ctx.psum("psD", [128, 512], F32)
    ch_gt = ctx.chan("moe_gt")
    ch_gv = ctx.chan("moe_gv")
    ch_gs = ctx.chan("moe_gs")
    CWT_DONE = 30 * (NCHUNK - 1) + 21
    GATE_V_TOTAL = CWT_DONE + 2 * NCHUNK

    def gate_t(t):
        t.wait_ge(che_ln.h, v_ln)
        t.wait_ge(ch_sv.h, 1)
        for i in range(NCHUNK):
            if i >= 1:
                t.wait_ge(ch_gv.h, 30 * (i - 1) + 1)
            mm = None
            for dc in range(NDC):
                lh = xnT[:, dc * T + 128 * i: dc * T + 128 * (i + 1)]
                ll = xnT_lo[:, dc * T + 128 * i: dc * T + 128 * (i + 1)]
                t.matmul(ps_g[:, 0:8], lh, wg_hi[:, dc * 8:(dc + 1) * 8],
                         start=(dc == 0), stop=False)
                t.matmul(ps_g[:, 0:8], ll, wg_hi[:, dc * 8:(dc + 1) * 8], start=False, stop=False)
                mm = t.matmul(ps_g[:, 0:8], lh, wg_lo[:, dc * 8:(dc + 1) * 8], start=False,
                              stop=(dc == NDC - 1))
            ch_gt.inc(mm)
            t.wait_ge(ch_gv.h, 30 * i + 20)
            ch_gt.inc(t.transpose(ps_d[0:8, 0:128], cw[:, i * 8:(i + 1) * 8], identf[:]))
        t.wait_ge(ch_gv.h, CWT_DONE)
        for i in range(NCHUNK):
            if i >= 1:
                t.wait_ge(ch_gv.h, CWT_DONE + 2 * i)
            t.matmul(ps_g[:, 0:512], cwT[0:8, 128 * i:128 * (i + 1)], b2a[0:8, 0:512],
                     start=True, stop=True)
            mm = t.matmul(ps_d[:, 0:512], cwT[0:8, 128 * i:128 * (i + 1)], b2a[0:8, 512:1024],
                          start=True, stop=True)
            ch_gt.inc(mm)

    def gate_s(s_):
        for i in range(NCHUNK):
            s_.wait_ge(ch_gv.h, 30 * i + 3)
            ch_gs.inc(s_.activation(probs[:, :], gs[:, 8:16], AF.Exp,
                                    bias=gs[:, 1:2], accum_out=gs[:, 2:3]))

    def gate_v(v):
        v.wait_ge(ch_sv.h, 3)
        for i in range(NCHUNK):
            b = 30 * i
            v.wait_ge(ch_gt.h, 2 * i + 1)
            ch_gv.inc(v.scalar_tensor_tensor(gs[:, 8:16], ps_g[:, 0:8], 1.0, bgb[:, :],
                                             ALU.mult, ALU.add), b + 1 - ch_gv.n)
            v.wait_ge(ch_gv.h, ch_gv.n)
            v.tensor_copy(gl_dbg[:, i * 8:(i + 1) * 8], gs[:, 8:16])
            ch_gv.inc(v.reduce_max(gs[:, 0:1], gs[:, 8:16], axis=AX.X))
            v.wait_ge(ch_gv.h, ch_gv.n)
            ch_gv.inc(v.tensor_scalar(gs[:, 1:2], gs[:, 0:1], -1.0, None, ALU.mult))
            v.wait_ge(ch_gs.h, i + 1)
            ch_gv.inc(v.reciprocal(gs[:, 3:4], gs[:, 2:3]))
            v.wait_ge(ch_gv.h, ch_gv.n)
            ch_gv.inc(v.tensor_scalar(probs[:, :], probs[:, :], gs[:, 3:4], None, ALU.mult))
            v.wait_ge(ch_gv.h, ch_gv.n)
            v.tensor_copy(pr_dbg[:, i * 8:(i + 1) * 8], probs[:, :])
            ch_gv.inc(v.max(gs[:, 8:16], probs[:, :]))
            v.wait_ge(ch_gv.h, ch_gv.n)
            v.tensor_copy(m8_dbg[:, i * 8:(i + 1) * 8], gs[:, 8:16])
            ch_gv.inc(v.tensor_add(gs[:, 4:5], gs[:, 8:9], gs[:, 9:10]))
            v.wait_ge(ch_gv.h, ch_gv.n)
            ch_gv.inc(v.tensor_scalar(gs[:, 4:5], gs[:, 4:5], 1e-8, None, ALU.add))
            v.wait_ge(ch_gv.h, ch_gv.n)
            ch_gv.inc(v.reciprocal(gs[:, 5:6], gs[:, 4:5]))
            v.wait_ge(ch_gv.h, ch_gv.n)
            ch_gv.inc(v.tensor_mul(gs[:, 6:7], gs[:, 8:9], gs[:, 5:6]))
            ch_gv.inc(v.tensor_mul(gs[:, 7:8], gs[:, 9:10], gs[:, 5:6]))
            v.wait_ge(ch_gv.h, ch_gv.n)
            cwc = cw[:, i * 8:(i + 1) * 8]
            ch_gv.inc(v.tensor_scalar(cwc, probs[:, :], gs[:, 8:9], gs[:, 6:7],
                                      ALU.is_equal, ALU.mult))
            ch_gv.inc(v.tensor_scalar(gs[:, 16:24], probs[:, :], gs[:, 9:10], gs[:, 7:8],
                                      ALU.is_equal, ALU.mult))
            v.wait_ge(ch_gv.h, ch_gv.n)
            mrk = v.tensor_add(cwc, cwc, gs[:, 16:24])
            ch_gv.inc(mrk, b + 20 - ch_gv.n)
            v.wait_ge(ch_gt.h, 2 * i + 2)
            ch_gv.inc(v.tensor_copy(cwT[0:8, 128 * i:128 * (i + 1)], ps_d[0:8, 0:128]))
        for i in range(NCHUNK):
            v.wait_ge(ch_gt.h, 2 * NCHUNK + i + 1)
            xc = xres[:, i * 1024:(i + 1) * 1024]
            ch_gv.inc(v.scalar_tensor_tensor(xc[:, 0:512], ps_g[:, 0:512], 1.0, xc[:, 0:512],
                                             ALU.mult, ALU.add))
            ch_gv.inc(v.scalar_tensor_tensor(xc[:, 512:1024], ps_d[:, 0:512], 1.0,
                                             xc[:, 512:1024], ALU.mult, ALU.add))
    ctx.block.tensor(gate_t)
    ctx.block.scalar(gate_s)
    ctx.block.vector(gate_v)

    W1A = ctx.sbuf("moe_W1A", [128, 8192], BF16)
    W2A = ctx.sbuf("moe_W2A", [128, 8192], BF16)
    W1B = bufB[:, 0:8192]
    W2B = bufB[:, 8192:16384]
    b1s = ctx.sbuf("moe_b1s", [32, 128], F32)
    b1T = ctx.sbuf("moe_b1T", [128, 32], F32)
    hfm = [ctx.sbuf("moe_h0", [128, 2048], BF16), ctx.sbuf("moe_h1", [128, 2048], BF16)]
    ps_out = [ctx.psum("psA", [128, 1024], F32), ctx.psum("psB", [128, 1024], F32)]
    ch_wd = ctx.chan("moe_wd")
    ch_et = ctx.chan("moe_et")
    ch_ev = ctx.chan("moe_ev")
    ch_es = ctx.chan("moe_es")
    ch_bt = ctx.chan("moe_bt")
    W1 = waps["L0_ffn_W1"]; W2 = waps["L0_ffn_W2"]; B1 = waps["L0_ffn_b1"]

    def wdma(g):
        for e in range(NE):
            g.wait_ge(ch_bt.h, 2 * e)
            ch_wd.inc(g.dma_start(out=b1s[:, :], in_=B1[e].rearrange("(c p) -> c p", p=128)), 16)
            for fb in range(NFB):
                idx = 4 * e + fb
                w1t = [W1A, W1B][idx % 2]
                w2t = [W2A, W2B][idx % 2]
                if idx == 1:
                    g.wait_ge(ch_gt.h, 2 * NCHUNK)
                if idx >= 2:
                    g.wait_ge(ch_et.h, 80 * (idx - 1))
                for dc in range(NDC):
                    ch_wd.inc(g.dma_start(out=w1t[:, dc * 1024:(dc + 1) * 1024],
                                          in_=W1[e, dc * 128:(dc + 1) * 128,
                                                 fb * 1024:(fb + 1) * 1024]), 16)
                for fc in range(NDC):
                    ch_wd.inc(g.dma_start(out=w2t[:, fc * 1024:(fc + 1) * 1024],
                                          in_=W2[e, fb * 1024 + fc * 128:
                                                 fb * 1024 + (fc + 1) * 128, 0:1024]), 16)
    ctx.block.gpsimd(wdma)

    def exp_t(t):
        t.wait_ge(ch_gv.h, GATE_V_TOTAL)
        dma_n = 0
        for e in range(NE):
            dma_n += 16
            t.wait_ge(ch_wd.h, dma_n)
            if e >= 1:
                t.wait_ge(ch_es.h, 16 * 16 * e)
            ch_bt.inc(t.transpose(ps_d[0:128, 0:32], b1s[0:32, 0:128], identf[0:32, 0:32]))
            t.wait_ge(ch_bt.h, 2 * e + 2)
            for fb in range(NFB):
                idx = 4 * e + fb
                w1t = [W1A, W1B][idx % 2]
                w2t = [W2A, W2B][idx % 2]
                dma_n += 16 * 16
                t.wait_ge(ch_wd.h, dma_n)
                for tp in range(NTP):
                    grp = idx * NTP + tp
                    if grp >= 1:
                        t.wait_ge(ch_es.h, 16 * grp)
                    for fc in range(NDC):
                        for half in range(2):
                            if fc >= 1:
                                t.wait_ge(ch_es.h, 16 * grp + 2 * (fc - 1) + half + 1)
                            pst = [ps_g, ps_d][half]
                            for dc in range(NDC):
                                lhs = w1t[:, dc * 1024 + fc * 128: dc * 1024 + (fc + 1) * 128]
                                rhs = xnT[:, dc * T + tp * 512 + half * 256:
                                          dc * T + tp * 512 + (half + 1) * 256]
                                mm = t.matmul(pst[:, 0:256], lhs, rhs,
                                              start=(dc == 0), stop=(dc == NDC - 1))
                            ch_et.inc(mm)
                    t.wait_ge(ch_es.h, 16 * grp + 16)
                    for tt in range(4):
                        pso = ps_out[tt % 2]
                        otm = 4 * grp + tt
                        if otm >= 2:
                            t.wait_ge(ch_ev.h, otm - 1)
                        hsrc = hfm[tt // 2]
                        for fc in range(NDC):
                            lhs = hsrc[:, fc * 256 + (tt % 2) * 128:
                                       fc * 256 + (tt % 2) * 128 + 128]
                            t.matmul(pso[:, 0:512], lhs, w2t[:, fc * 1024: fc * 1024 + 512],
                                     start=(fc == 0), stop=(fc == NDC - 1))
                            mm = t.matmul(pso[:, 512:1024], lhs,
                                          w2t[:, fc * 1024 + 512:(fc + 1) * 1024],
                                          start=(fc == 0), stop=(fc == NDC - 1))
                        ch_et.inc(mm)

    def exp_s(s_):
        for e in range(NE):
            s_.wait_ge(ch_bt.h, 2 * e + 2)
            for fb in range(NFB):
                idx = 4 * e + fb
                for tp in range(NTP):
                    grp = idx * NTP + tp
                    for fc in range(NDC):
                        for half in range(2):
                            s_.wait_ge(ch_et.h, 20 * grp + 2 * fc + half + 1)
                            pst = [ps_g, ps_d][half]
                            ch_es.inc(s_.activation(
                                hfm[half][:, fc * 256:(fc + 1) * 256], pst[:, 0:256],
                                AF.Gelu_apprx_tanh,
                                bias=b1T[:, fb * 8 + fc: fb * 8 + fc + 1]))

    def exp_v(v):
        for e in range(NE):
            v.wait_ge(ch_bt.h, 2 * e + 1)
            ch_bt.inc(v.tensor_copy(b1T[:, :], ps_d[0:128, 0:32]))
            for fb in range(NFB):
                idx = 4 * e + fb
                for tp in range(NTP):
                    grp = idx * NTP + tp
                    for tt in range(4):
                        otm = 4 * grp + tt
                        v.wait_ge(ch_et.h, 20 * grp + 16 + tt + 1)
                        i_chunk = tp * 4 + tt
                        xc = xres[:, i_chunk * 1024:(i_chunk + 1) * 1024]
                        pso = ps_out[tt % 2]
                        ch_ev.inc(v.scalar_tensor_tensor(
                            xc, pso[:, 0:1024], cw[:, i_chunk * 8 + e: i_chunk * 8 + e + 1],
                            xc, ALU.mult, ALU.add))
    ctx.block.tensor(exp_t)
    ctx.block.scalar(exp_s)
    ctx.block.vector(exp_v)
    if MOE_GATE_ONLY:
        return ch_gv, GATE_V_TOTAL
    return ch_ev, ch_ev.n





def emit_l1_ffn(ctx, w_ap):
    xres = ctx.tiles["xres"]
    zn = ctx.sbuf("l1f_ln_zn", [128, NCHUNK * 1024], BF16)
    zn_lo = ctx.sbuf("l1f_ln_znlo", [128, NCHUNK * 1024], BF16)
    ch_ln, v_ln = emit_ln(ctx, "l1f_ln", lambda i: xres[:, i * 1024:(i + 1) * 1024],
                          lambda i: zn[:, i * 1024:(i + 1) * 1024],
                          lo_cols=lambda i: zn_lo[:, i * 1024:(i + 1) * 1024])

    wld = ctx.sbuf("l1f_xh", [128, 1024], F32)
    wsc = ctx.sbuf("scr_lnsq", [128, 1024], F32)
    tern = ctx.sbuf("l1f_tern", [128, NDC * 1024], BF16)
    ternT = ctx.sbuf("l1f_ternT", [128, NDC * 1024], BF16)
    wstat = ctx.sbuf("l1f_wstat", [128, 4], F32)
    wscols = ctx.sbuf("l1f_wscols", [128, 8], F32)
    wsr8 = ctx.sbuf("l1f_wsr8", [8, 128], F32)
    ws_b = ctx.sbuf("l1f_wsb", [128, 1024], F32)
    e8 = ctx.sbuf("l1f_e8", [8, 1024], F32)
    ident = ctx.sbuf("ident", [128, 128], BF16)
    ps_bc = ctx.psum("psA", [128, 1024], F32)
    ps_ws = ctx.psum("psB", [128, 1024], F32)
    e8np = np.zeros((8, 1024), dtype=np.float32)
    for _oc in range(8):
        e8np[_oc, _oc * 128:(_oc + 1) * 128] = 1.0
    e8c = ctx.const("e8", e8np)
    ch_wd = ctx.chan("l1f_wd")
    ch_wv = ctx.chan("l1f_wv")
    ch_mt = ctx.chan("l1f_mt")
    ch_mv = ctx.chan("l1f_mv")

    def gsec(g):
        for oc in range(NDC):
            if oc >= 1:
                g.wait_ge(ch_wv.h, 5 * (oc - 1) + 5)
            ch_wd.inc(g.dma_start(out=wld[:], in_=w_ap[oc * 128:(oc + 1) * 128, :]), 16)
    ctx.block.gpsimd(gsec)

    def vsec(v):
        for oc in range(NDC):
            v.wait_ge(ch_wd.h, 16 * (oc + 1))
            chv_ = ch_wv
            chv_.inc(v.reduce_sum(wstat[:, 0:1], wld[:], axis=AX.X,
                                  apply_absolute_value=True))
            v.wait_ge(chv_.h, chv_.n)
            chv_.inc(v.tensor_scalar(wscols[:, oc:oc + 1], wstat[:, 0:1], 1.0 / D, 1e-5,
                                     ALU.mult, ALU.max))
            v.wait_ge(chv_.h, chv_.n)
            chv_.inc(v.reciprocal(wstat[:, 2:3], wscols[:, oc:oc + 1]))
            v.wait_ge(chv_.h, chv_.n)
            v.tensor_scalar(wld[:], wld[:], wstat[:, 2:3], None, ALU.mult)
            v.tensor_scalar(wsc[:], wld[:], 0.5, None, ALU.is_gt)
            chv_.inc(v.tensor_scalar(wld[:], wld[:], -0.5, None, ALU.is_lt))
            chv_.inc(v.tensor_sub(tern[:, oc * 1024:(oc + 1) * 1024], wsc[:], wld[:]))
    ctx.block.vector(vsec)

    pairs = []
    for oc in range(NDC):
        for ic in range(NDC):
            pairs.append((tern[:, oc * 1024 + ic * 128: oc * 1024 + (ic + 1) * 128],
                          ternT[:, ic * 1024 + oc * 128: ic * 1024 + (oc + 1) * 128]))
    ctx._trans_wait = (ch_wv, ch_wv.n)
    ch_tT, v_tT = emit_transposes(ctx, pairs)

    ch_e8 = ctx.chan("l1f_e8d")

    def gsec2(g):
        ch_e8.inc(g.dma_start(out=e8[:, :], in_=e8c[:, :]), 16)
    ctx.block.gpsimd(gsec2)

    def tsecW(t):
        t.wait_ge(ch_wv.h, 5 * NDC)
        ch_mt.inc(t.transpose(ps_ws[0:8, 0:128], wscols[:, 0:8], ctx.tiles["identf"][:]))
        t.wait_ge(ch_mv.h, 1)
        t.wait_ge(ch_e8.h, 16)
        for oc in range(NDC):
            ch_mt.inc(t.matmul(ps_bc[:, oc * 128:(oc + 1) * 128],
                               e8[:, oc * 128:(oc + 1) * 128], wsr8[0:8, :],
                               start=True, stop=True))

    def vsecW(v):
        v.wait_ge(ch_mt.h, 1)
        ch_mv.inc(v.tensor_copy(wsr8[:, :], ps_ws[0:8, 0:128]))
        v.wait_ge(ch_mt.h, 9)
        ch_mv.inc(v.tensor_copy(ws_b[:, 0:512], ps_bc[:, 0:512]))
        ch_mv.inc(v.tensor_copy(ws_b[:, 512:1024], ps_bc[:, 512:1024]))
    ctx.block.tensor(tsecW)
    ctx.block.vector(vsecW)

    Hc = ctx.const("hadamard", _hadamard_bf16)
    Hs = ctx.sbuf("l1f_H", [128, NDC * 1024], BF16)
    ch_hd = ctx.chan("l1f_hd")

    def sdma(s):
        for kc in range(NDC):
            ch_hd.inc(s.dma_start(out=Hs[:, kc * 1024:(kc + 1) * 1024],
                                  in_=Hc[kc * 128:(kc + 1) * 128, :]), 16)
    ctx.block.scalar(sdma)

    xh = wld
    q2 = ctx.sbuf("l1f_q2", [128, 1024], BF16)
    q2T = ctx.sbuf("l1f_q2T", [128, 1024], BF16)
    znTt = ctx.sbuf("l1f_znTt", [128, 1024], BF16)
    znTt_lo = ctx.sbuf("l1f_znTt_lo", [128, 1024], BF16)
    qstat = ctx.sbuf("l1f_qstat", [128, 4], F32)
    scl = ctx.sbuf("l1f_scl", [128, NCHUNK], F32)
    ps_h = ctx.psum("psA", [128, 1024], F32)
    ps_o = ctx.psum("psB", [128, 1024], F32)
    ps_t = [ctx.psum("tpsA", [128, 1024], BF16), ctx.psum("tpsB", [128, 1024], BF16)]
    ch_t = ctx.chan("l1f_t")
    ch_v = ctx.chan("l1f_v")

    def tsec(t):
        t.wait_ge(ch_hd.h, 16 * NDC)
        t.wait_ge(ch_ln.h, v_ln)
        t.wait_ge(ch_tT.h, v_tT)
        t.wait_ge(ch_mv.h, 3)
        for i in range(NCHUNK):
            tb = 26 * i
            if i >= 1:
                t.wait_ge(ch_v.h, 100 * i)
            for k in range(16):
                src = zn if k < 8 else zn_lo
                dc = k % 8
                if k >= 2:
                    t.wait_ge(ch_v.h, 100 * i + k - 1)
                ch_t.inc(t.transpose(ps_t[k % 2][:, 0:128],
                                     src[:, i * 1024 + dc * 128: i * 1024 + (dc + 1) * 128],
                                     ident[:]))
            t.wait_ge(ch_v.h, 100 * i + 16)
            for hl in range(2):
                srcT = znTt if hl == 0 else znTt_lo
                for kc in range(NDC):
                    lhs = srcT[:, kc * 128:(kc + 1) * 128]
                    t.matmul(ps_h[:, 0:512], lhs, Hs[:, kc * 1024: kc * 1024 + 512],
                             start=(hl == 0 and kc == 0), stop=(hl == 1 and kc == NDC - 1))
                    mm = t.matmul(ps_h[:, 512:1024], lhs, Hs[:, kc * 1024 + 512:(kc + 1) * 1024],
                                  start=(hl == 0 and kc == 0), stop=(hl == 1 and kc == NDC - 1))
            ch_t.inc(mm)
            t.wait_ge(ch_v.h, 100 * i + 30)
            for dc in range(NDC):
                if dc >= 2:
                    t.wait_ge(ch_v.h, 100 * i + 30 + dc - 1)
                ch_t.inc(t.transpose(ps_t[dc % 2][:, 0:128],
                                     q2[:, dc * 128:(dc + 1) * 128], ident[:]))
            t.wait_ge(ch_v.h, 100 * i + 38)
            for ic in range(NDC):
                lhs = q2T[:, ic * 128:(ic + 1) * 128]
                t.matmul(ps_o[:, 0:512], lhs, ternT[:, ic * 1024: ic * 1024 + 512],
                         start=(ic == 0), stop=(ic == NDC - 1))
                mm = t.matmul(ps_o[:, 512:1024], lhs, ternT[:, ic * 1024 + 512:(ic + 1) * 1024],
                              start=(ic == 0), stop=(ic == NDC - 1))
            ch_t.inc(mm)

    def vsec2(v):
        for i in range(NCHUNK):
            tb = 26 * i
            for k in range(16):
                dstT = znTt if k < 8 else znTt_lo
                dc = k % 8
                v.wait_ge(ch_t.h, tb + k + 1)
                mrk = v.tensor_copy(dstT[:, dc * 128:(dc + 1) * 128], ps_t[k % 2][:, 0:128])
                ch_v.inc(mrk, 100 * i + k + 1 - ch_v.n)
            v.wait_ge(ch_t.h, tb + 17)
            v.tensor_copy(xh[:], ps_h[:, 0:1024])
            ch_v.inc(v.reduce_max(qstat[:, 0:1], xh[:], axis=AX.X, apply_absolute_value=True))
            v.wait_ge(ch_v.h, ch_v.n)
            ch_v.inc(v.tensor_scalar(qstat[:, 1:2], qstat[:, 0:1], 1e-5, None, ALU.max))
            v.wait_ge(ch_v.h, ch_v.n)
            ch_v.inc(v.reciprocal(qstat[:, 2:3], qstat[:, 1:2]))
            v.wait_ge(ch_v.h, ch_v.n)
            ch_v.inc(v.tensor_scalar(qstat[:, 2:3], qstat[:, 2:3], 7.0, None, ALU.mult))
            ch_v.inc(v.tensor_scalar(scl[:, i:i + 1], qstat[:, 1:2], 1.0 / 7.0, None, ALU.mult))
            v.wait_ge(ch_v.h, ch_v.n)
            v.tensor_scalar(xh[:], xh[:], qstat[:, 2:3], 12582912.0, ALU.mult, ALU.add)
            v.tensor_scalar(xh[:], xh[:], 12582912.0, None, ALU.subtract)
            mrk = v.tensor_scalar(q2[:], xh[:], -8.0, 7.0, ALU.max, ALU.min)
            ch_v.inc(mrk, 100 * i + 30 - ch_v.n)
            for dc in range(NDC):
                v.wait_ge(ch_t.h, tb + 18 + dc)
                ch_v.inc(v.tensor_copy(q2T[:, dc * 128:(dc + 1) * 128], ps_t[dc % 2][:, 0:128]))
            v.wait_ge(ch_t.h, tb + 26)
            xc = xres[:, i * 1024:(i + 1) * 1024]
            v.scalar_tensor_tensor(xh[:], ps_o[:, 0:1024], scl[:, i:i + 1], ws_b[:, 0:1024],
                                   ALU.mult, ALU.mult)
            mrk = v.tensor_add(xc, xc, xh[:])
            ch_v.inc(mrk, 100 * (i + 1) - ch_v.n)
    ctx.block.tensor(tsec)
    ctx.block.vector(vsec2)
    return ch_v, ch_v.n





def _l0a_consts():
    H, e = 8, 128
    half = e // 2
    gam = 1.0 - 2.0 ** (-5.0 - np.arange(H, dtype=np.float64))
    j = np.arange(half, dtype=np.float64)
    base = (2 * j + 0.4 * e) / (1.4 * e)
    pos = np.arange(T, dtype=np.float64)
    scale = base[None, :] ** (pos[:, None] / 512.0)
    inv_freq = 1.0 / 10000.0 ** (j / half)
    sinu = pos[:, None] * inv_freq[None, :]
    sn, cs = np.sin(sinu), np.cos(sinu)
    bf = ml_dtypes.bfloat16
    f2b = lambda a: a.astype(np.float32).astype(bf)
    tt_ = np.arange(128)
    diff = tt_[None, :] - tt_[:, None]
    D_ = np.zeros((128, H * 128))
    XI = np.zeros((128, H * 128))
    for h in range(H):
        D_[:, h * 128:(h + 1) * 128] = np.where(diff >= 0, gam[h] ** np.maximum(diff, 0), 0.0)
        XI[:, h * 128:(h + 1) * 128] = (gam[h] ** (tt_ + 1.0))[None, :]
    return dict(
        l0a_cq=f2b((cs * scale).T), l0a_sq=f2b((sn * scale).T),
        l0a_ck=f2b((cs / scale).T), l0a_sk=f2b((sn / scale).T),
        l0a_D=f2b(D_), l0a_XI=f2b(XI),
        l0a_gdec=(gam[None, :] ** (127.0 - tt_[:, None])).astype(np.float32),
        l0a_gc=np.broadcast_to(gam[None, :] ** 128.0, (128, H)).astype(np.float32).copy())


class TrB:

    def __init__(self, ctx):
        self.ctx = ctx
        self.cht = ctx.chan("trb_t")
        self.che = ctx.chan("trb_e")
        self.ps = [ctx.psum("tpsA", [128, 1024], BF16), ctx.psum("tpsB", [128, 1024], BF16)]
        self.ident = ctx.sbuf("ident", [128, 128], BF16)

    def emit(self, pairs, twait=(), vwait=(), self_wait=True):
        assert len(pairs) <= 16
        prev = self.che.n
        t0 = self.cht.n
        ps = self.ps
        ident = self.ident
        cht, che = self.cht, self.che

        def tsec(t):
            t.wait_ge(che.h, prev)
            for ch, val in twait:
                t.wait_ge(ch.h, val)
            for k, (src, dst) in enumerate(pairs):
                slot = ps[k // 8][0:src.shape[-1], (k % 8) * 128:(k % 8) * 128 + src.shape[0]]
                cht.inc(t.transpose(slot, src, ident[:]))

        nb = min(len(pairs), 8)

        def vsec(v):
            for ch, val in vwait:
                v.wait_ge(ch.h, val)
            for k, (src, dst) in enumerate(pairs):
                v.wait_ge(cht.h, t0 + (nb if k < 8 else len(pairs)))
                slot = ps[k // 8][0:src.shape[-1], (k % 8) * 128:(k % 8) * 128 + src.shape[0]]
                che.inc(v.tensor_copy(dst, slot))
            if self_wait:
                v.wait_ge(che.h, che.n)
        self.ctx.block.tensor(tsec)
        self.ctx.block.vector(vsec)
        return self.che.n


def emit_l0_attn(ctx, waps):
    nc = ctx.nc
    y = ctx.tiles["xres"]
    identf = ctx.tiles["identf"]
    H, NST, STW = 8, 8, 256
    EPSI = 1e-6

    for k, v in _l0a_consts().items():
        ctx.const(k, v)
    CAP = ctx.const_aps

    spill = nc.dram_tensor("l0a_spill", [T, D], F32, kind="Internal").ap()
    trb = TrB(ctx)

    u = ctx.sbuf("l0a_u", [128, 16384], BF16)
    qT = u[:, 0:2048]
    kTt = u[:, 2048:4096]
    vtok = u[:, 4096:6144]
    ring = u[:, 4096:6144]
    gtok = u[:, 6144:8192]
    ktok = u[:, 8192:9216]
    Pt = u[:, 9216:10240]
    qxi = u[:, 10240:11264]
    zT = u[:, 11264:13312]
    aT = u[:, 13312:15360]
    xt1 = u[0:64, 15360:16384]
    xt2 = u[64:128, 15360:16384]
    x2T = u[:, 0:16384]
    scr = ctx.sbuf("scr_lnsq", [128, 1024], F32)
    heads = scr
    rowbuf = scr[0:1, 0:1024]
    scrb = ctx.sbuf("l0a_scrb", [128, 1024], BF16)
    sg = scrb
    scrb64 = scrb[0:64, 0:1024]
    St = ctx.sbuf("l0a_S", [128, H * 128], F32)
    Sbf = ctx.sbuf("l0a_Sbf", [128, H * 128], BF16)
    Dm = ctx.sbuf("l0a_Dm", [128, 1024], BF16)
    Xi = ctx.sbuf("l0a_Xi", [128, 1024], BF16)
    gdec = ctx.sbuf("l0a_gdec", [128, 8], F32)
    gc = ctx.sbuf("l0a_gc", [128, 8], F32)
    tabs = ctx.sbuf("l0a_tabs", [128, 4 * 256], BF16)
    st8 = ctx.sbuf("l0a_st8", [128, 48], F32)
    lnst = ctx.sbuf("l0a_lnst", [128, 80], F32)
    bc = ctx.sbuf("l0a_bc", [128, 4 * 1024], BF16)
    bc2 = bc[:, 0:2048]
    onesf = ctx.sbuf("l0a_onesf", [1, 128], F32)
    wq_s = ctx.sbuf("l0a_wq", [128, NDC * 1024], BF16)
    wk_s = ctx.sbuf("l0a_wk", [128, NDC * 1024], BF16)
    wv_s = ctx.sbuf("l0a_wv", [128, NDC * 1024], BF16)
    wg_s = ctx.sbuf("l0a_wg", [128, NDC * 1024], BF16)
    wo_s = ctx.sbuf("l0a_wo", [128, NDC * 1024], BF16)
    w1f = [wq_s, wk_s]
    w2f = [wv_s, wg_s]
    hfm = St[:, :].bitcast(BF16)

    psA = ctx.psum("psA", [128, 1024], F32)
    psB = ctx.psum("psB", [128, 1024], F32)
    psC = ctx.psum("psC", [128, 512], F32)
    psD = ctx.psum("psD", [128, 512], F32)

    ch_sp = ctx.chan("l0a_sp")
    ch_wd = ctx.chan("l0a_wd")
    ch_ld = ctx.chan("l0a_ld")
    ch_t = ctx.chan("l0a_t")
    ch_v = ctx.chan("l0a_v")
    ch_s = ctx.chan("l0a_s")

    SUMS, SS, MEAN, VAR, RSTD, M2 = 0, 16, 32, 48, 64, 0

    def emit_stats(eps, vwait=None):
        v0, s0 = ch_v.n, ch_s.n

        def ssec1(s_):
            if vwait is not None:
                s_.wait_ge(vwait[0].h, vwait[1])
            for i in range(NCHUNK):
                mrk = s_.activation(scrb[:], y[:, i * 1024:(i + 1) * 1024], AF.Square,
                                    accum_out=lnst[:, SS + i:SS + i + 1])
            ch_s.inc(mrk)

        def vsec1(v):
            if vwait is not None:
                v.wait_ge(vwait[0].h, vwait[1])
            for i in range(NCHUNK):
                mrk = v.reduce_sum(lnst[:, SUMS + i:SUMS + i + 1],
                                   y[:, i * 1024:(i + 1) * 1024], axis=AX.X)
            ch_v.inc(mrk)
            v.wait_ge(ch_v.h, v0 + 1)
            ch_v.inc(v.tensor_scalar(lnst[:, MEAN:MEAN + 16], lnst[:, SUMS:SUMS + 16],
                                     1.0 / D, None, ALU.mult))
            v.wait_ge(ch_v.h, v0 + 2)
            ch_v.inc(v.tensor_mul(lnst[:, M2:M2 + 16], lnst[:, MEAN:MEAN + 16],
                                  lnst[:, MEAN:MEAN + 16]))
            v.wait_ge(ch_s.h, s0 + 1)
            ch_v.inc(v.tensor_scalar(lnst[:, VAR:VAR + 16], lnst[:, SS:SS + 16],
                                     1.0 / D, eps, ALU.mult, ALU.add))
            v.wait_ge(ch_v.h, v0 + 4)
            ch_v.inc(v.tensor_sub(lnst[:, VAR:VAR + 16], lnst[:, VAR:VAR + 16],
                                  lnst[:, M2:M2 + 16]))

        def ssec2(s_):
            s_.wait_ge(ch_v.h, v0 + 5)
            ch_s.inc(s_.activation(lnst[:, VAR:VAR + 16], lnst[:, VAR:VAR + 16], AF.Sqrt))

        def vsec2(v):
            v.wait_ge(ch_s.h, s0 + 2)
            ch_v.inc(v.reciprocal(lnst[:, RSTD:RSTD + 16], lnst[:, VAR:VAR + 16]))
            v.wait_ge(ch_v.h, v0 + 6)
        ctx.block.scalar(ssec1)
        ctx.block.vector(vsec1)
        ctx.block.scalar(ssec2)
        ctx.block.vector(vsec2)
        return ch_v.n

    def spill_out(e):
        xr = spill.rearrange("(c p) d -> c p d", p=128)
        for i in range(NCHUNK):
            ch_sp.inc(e.dma_start(out=xr[i], in_=y[:, i * 1024:(i + 1) * 1024]), 16)
    ctx.block.sync(spill_out)
    emit_stats(1e-5)

    def outer_apply(v):
        v.wait_ge(ch_sp.h, 16 * NCHUNK)
        for i in range(NCHUNK):
            mrk = v.tensor_scalar(y[:, i * 1024:(i + 1) * 1024], y[:, i * 1024:(i + 1) * 1024],
                                  lnst[:, MEAN + i:MEAN + i + 1], lnst[:, RSTD + i:RSTD + i + 1],
                                  ALU.subtract, ALU.mult)
        ch_v.inc(mrk)
        v.wait_ge(ch_v.h, ch_v.n)
    ctx.block.vector(outer_apply)
    y_ready_v = ch_v.n

    last_stage1_mm = [0]
    last_ffn_mm = {}

    WQ, WK, WV = waps["L0_attn_Wq"], waps["L0_attn_Wk"], waps["L0_attn_Wv"]
    WG, WO = waps["L0_attn_Wg"], waps["L0_attn_Wo"]
    FW1, FW2 = waps["L0_attn_ffn_w1"], waps["L0_attn_ffn_w2"]

    for li in range(L0A_LAYERS):
        gate_t_val = ch_t.n

        def wdma(g, li=li, gate=gate_t_val):
            if gate:
                g.wait_ge(ch_t.h, gate)
            for wdst, wsrc3 in ((wq_s, WQ), (wk_s, WK), (wv_s, WV)):
                for h in range(8):
                    dst = wdst[:, h * 128: h * 128 + 7 * 1024 + 128]
                    dst = dst.rearrange("p (dc e) -> p dc e", dc=8) if False else \
                        wdst[:, :].rearrange("p (dc he) -> p dc he", dc=8)[:, :, h * 128:(h + 1) * 128]
                    ch_wd.inc(g.dma_start(
                        out=dst,
                        in_=wsrc3[li, h].rearrange("(dc p) e -> p dc e", p=128)), 16)
            for wdst, wsrc2 in ((wg_s, WG), (wo_s, WO)):
                ch_wd.inc(g.dma_start(
                    out=wdst[:, :].rearrange("p (dc m) -> p dc m", dc=8),
                    in_=wsrc2[li].rearrange("(dc p) m -> p dc m", p=128)), 16)
        ctx.block.gpsimd(wdma)
        wd_stage1 = ch_wd.n

        if li == 0:
            def cdma(e):
                ch_ld.inc(e.dma_start(out=Dm[:, :], in_=CAP["l0a_D"][:, :]), 16)
                ch_ld.inc(e.dma_start(out=Xi[:, :], in_=CAP["l0a_XI"][:, :]), 16)
                ch_ld.inc(e.dma_start(out=gdec[:, :], in_=CAP["l0a_gdec"][:, :]), 16)
                ch_ld.inc(e.dma_start(out=gc[:, :], in_=CAP["l0a_gc"][:, :]), 16)
            ctx.block.scalar(cdma)
            CONST_LD = ch_ld.n

        rows = [("L0_attn_ln1_scale", bc, 0), ("L0_attn_ln1_bias", bc, 1),
                ("L0_attn_gn_scale", bc, 2), ("L0_attn_gn_bias", bc, 3)]

        def bc_ones(v):
            ch_v.inc(v.memset(onesf[:], 1.0))
            v.wait_ge(ch_v.h, ch_v.n)
        if li == 0:
            ctx.block.vector(bc_ones)
        def emit_rows(rowlist, li=li):
            for rname, dstt, slot in rowlist:
                r0d = ch_ld.n

                def rdma(e, rname=rname, li=li, tg=ch_t.n, vg=ch_v.n):
                    if tg:
                        e.wait_ge(ch_t.h, tg)
                    if vg:
                        e.wait_ge(ch_v.h, vg)
                    ch_ld.inc(e.dma_start(out=rowbuf[0:1, :], in_=waps[rname][li:li + 1, :]), 16)
                t0m = ch_t.n

                def rmm(t, r0d=r0d, gate=ch_v.n):
                    t.wait_ge(ch_ld.h, r0d + 16)
                    t.wait_ge(ch_v.h, gate)
                    ch_t.inc(t.matmul(psC[:, 0:512], onesf[0:1, 0:128], rowbuf[0:1, 0:512],
                                      start=True, stop=True))
                    ch_t.inc(t.matmul(psD[:, 0:512], onesf[0:1, 0:128], rowbuf[0:1, 512:1024],
                                      start=True, stop=True))

                def rev(v, dstt=dstt, slot=slot, t0m=t0m):
                    v.wait_ge(ch_t.h, t0m + 2)
                    v.tensor_copy(dstt[:, slot * 1024:slot * 1024 + 512], psC[:, 0:512])
                    ch_v.inc(v.tensor_copy(dstt[:, slot * 1024 + 512:(slot + 1) * 1024], psD[:, 0:512]))
                    v.wait_ge(ch_v.h, ch_v.n)
                ctx.block.scalar(rdma)
                ctx.block.tensor(rmm)
                ctx.block.vector(rev)
        emit_rows(rows)
        bc_done_v = ch_v.n

        waitv = (ch_v, y_ready_v) if li == 0 else (ch_v, last_ffn_add[0])
        emit_stats(EPSI, vwait=waitv)
        ln1_stats_v = ch_v.n

        def zeroS(v):
            v.memset(St[:, :], 0.0)
            ch_v.inc(v.memset(Sbf[:, :], 0.0))
            v.wait_ge(ch_v.h, ch_v.n)
        ctx.block.vector(zeroS)

        last_wo_ev = [ch_v.n]
        tab_gate = {0: 0}
        for st in range(L0A_NST):
            def tdma(e, st=st, tg=tab_gate[0]):
                if tg:
                    e.wait_ge(ch_v.h, tg)
                for ti, cap in enumerate((CAP["l0a_cq"], CAP["l0a_sq"], CAP["l0a_ck"], CAP["l0a_sk"])):
                    e.dma_start(out=tabs[0:64, ti * 256:(ti + 1) * 256],
                                in_=cap[:, st * STW:(st + 1) * STW]).then_inc(ch_ld.h, 16)
                    ch_ld.n += 16
                    ch_ld.inc(e.dma_start(out=tabs[64:128, ti * 256:(ti + 1) * 256],
                                          in_=cap[:, st * STW:(st + 1) * STW]), 16)
            ctx.block.scalar(tdma)
            tabs_rdy = ch_ld.n
            CQ = (tabs[0:64, 0:256], tabs[64:128, 0:256])
            SQ = (tabs[0:64, 256:512], tabs[64:128, 256:512])
            CK = (tabs[0:64, 512:768], tabs[64:128, 512:768])
            SK = (tabs[0:64, 768:1024], tabs[64:128, 768:1024])

            def lnap(v, st=st, trg=trb.cht.n, ttg=ch_t.n):
                if trg:
                    v.wait_ge(trb.cht.h, trg)
                if ttg:
                    v.wait_ge(ch_t.h, ttg)
                for cl in range(2):
                    i = st * 2 + cl
                    yc = y[:, i * 1024:(i + 1) * 1024]
                    rg = ring[:, cl * 1024:(cl + 1) * 1024]
                    v.tensor_scalar(scr[:], yc, lnst[:, MEAN + i:MEAN + i + 1],
                                    lnst[:, RSTD + i:RSTD + i + 1], ALU.subtract, ALU.mult)
                    v.tensor_mul(scr[:], scr[:], bc[:, 0:1024])
                    ch_v.inc(v.tensor_add(rg, scr[:], bc[:, 1024:2048]))
                v.wait_ge(ch_v.h, ch_v.n)
            ctx.block.vector(lnap)
            ring_rdy = ch_v.n

            if L0A_ST_PARTS < 2:
                continue
            pairs = []
            for cl in range(2):
                for dc in range(NDC):
                    pairs.append((ring[:, cl * 1024 + dc * 128: cl * 1024 + (dc + 1) * 128],
                                  aT[:, dc * STW + cl * 128: dc * STW + (cl + 1) * 128]))
            aT_rdy = trb.emit(pairs, twait=[(ch_v, ring_rdy)], vwait=[(ch_t, ch_t.n)])

            for which, wsrc, rotdst, CT, STb in (() if L0A_ST_PARTS < 3 else (
                    ("q", wq_s, qT, CQ, SQ), ("k", wk_s, kTt, CK, SK))):
                rawdst = rotdst
                t0m = ch_t.n

                def qmm(t, wsrc=wsrc, aT_rdy=aT_rdy, gate=last_wo_ev[0], wd=wd_stage1):
                    t.wait_ge(trb.che.h, aT_rdy)
                    t.wait_ge(ch_wd.h, wd)
                    t.wait_ge(ch_v.h, gate)
                    for h in range(H):
                        pst = [psA, psB][h // 4]
                        col = (h % 4) * 256
                        for dc in range(NDC):
                            mm = t.matmul(pst[:, col:col + 256],
                                          wsrc[:, dc * 1024 + h * 128:(dc * 1024 + (h + 1) * 128)],
                                          aT[:, dc * STW: dc * STW + 256],
                                          start=(dc == 0), stop=(dc == NDC - 1))
                        ch_t.inc(mm)

                def qev(v, rawdst=rawdst, t0m=t0m, tabs_rdy=tabs_rdy):
                    for h in range(H):
                        v.wait_ge(ch_t.h, t0m + 2 * (h // 2 + 1))
                        pst = [psA, psB][h // 4]
                        col = (h % 4) * 256
                        ch_v.inc(v.tensor_copy(rawdst[:, h * STW:(h + 1) * STW], pst[:, col:col + 256]))
                    v.wait_ge(ch_v.h, ch_v.n)

                def qrot(v, rotdst=rotdst, CT=CT, STb=STb, tabs_rdy=tabs_rdy):
                    v.wait_ge(ch_ld.h, tabs_rdy)
                    for hh in range(2):
                        cols = slice(hh * 1024, (hh + 1) * 1024)
                        r4 = lambda ap: ap.rearrange("p (a b) -> p a b", a=4)
                        bt = lambda ap: ap.rearrange("p (a t) -> p a t", a=1).broadcast_to((64, 4, 256))
                        rlo = r4(rotdst[0:64, cols])
                        rhi = r4(rotdst[64:128, cols])
                        c0, c64 = bt(CT[0]), bt(CT[1])
                        s0, s64 = bt(STb[0]), bt(STb[1])
                        t1 = r4(xt1[0:64, 0:1024])
                        t2 = r4(xt2[0:64, 0:1024])
                        u0 = r4(scrb[0:64, 0:1024])
                        u64 = r4(scrb[64:128, 0:1024])
                        v.tensor_tensor(t1, rlo, c0, ALU.mult)
                        v.tensor_tensor(t2, rlo, s0, ALU.mult)
                        v.tensor_tensor(u0, rhi, s64, ALU.mult)
                        v.tensor_sub(rlo, t1, u0)
                        v.tensor_tensor(u64, rhi, c64, ALU.mult)
                        mrk = v.tensor_add(rhi, u64, t2)
                    ch_v.inc(mrk)
                    v.wait_ge(ch_v.h, ch_v.n)
                ctx.block.tensor(qmm)
                ctx.block.vector(qev)
                ctx.block.vector(qrot)
                last_wo_ev[0] = ch_v.n
            tab_gate[0] = ch_v.n
            qT_rdy = None
            qk_done_v = ch_v.n

            if L0A_ST_PARTS < 4:
                continue
            for cl in range(2):
                t0m = ch_t.n

                def vgmm(t, cl=cl, gate=ch_v.n, wd=wd_stage1):
                    t.wait_ge(ch_wd.h, wd)
                    t.wait_ge(ch_v.h, gate)
                    for dc in range(NDC):
                        lhs = aT[:, dc * STW + cl * 128: dc * STW + (cl + 1) * 128]
                        t.matmul(psA[:, 0:512], lhs, wv_s[:, dc * 1024: dc * 1024 + 512],
                                 start=(dc == 0), stop=(dc == NDC - 1))
                        t.matmul(psA[:, 512:1024], lhs, wv_s[:, dc * 1024 + 512:(dc + 1) * 1024],
                                 start=(dc == 0), stop=(dc == NDC - 1))
                        t.matmul(psB[:, 0:512], lhs, wg_s[:, dc * 1024: dc * 1024 + 512],
                                 start=(dc == 0), stop=(dc == NDC - 1))
                        mm = t.matmul(psB[:, 512:1024], lhs, wg_s[:, dc * 1024 + 512:(dc + 1) * 1024],
                                      start=(dc == 0), stop=(dc == NDC - 1))
                    ch_t.inc(mm)

                def vgev(v, cl=cl, t0m=t0m):
                    v.wait_ge(ch_t.h, t0m + 1)
                    v.tensor_copy(vtok[:, cl * 1024:(cl + 1) * 1024], psA[:, 0:1024])
                    ch_v.inc(v.tensor_copy(gtok[:, cl * 1024:(cl + 1) * 1024], psB[:, 0:1024]))
                    v.wait_ge(ch_v.h, ch_v.n)
                ctx.block.tensor(vgmm)
                ctx.block.vector(vgev)
            vg_done_v = ch_v.n

            for cl in range(2 if L0A_ST_PARTS >= 5 else 0):
                i_chunk = st * 2 + cl
                kpairs = [(kTt[:, h * STW + cl * 128: h * STW + cl * 128 + 128],
                           ktok[:, h * 128:(h + 1) * 128]) for h in range(H)]
                ktok_rdy = trb.emit(kpairs, twait=[(ch_v, vg_done_v)], vwait=[(ch_t, ch_t.n)])

                def kdv(v, rdy=ktok_rdy, cld=CONST_LD):
                    v.wait_ge(ch_ld.h, cld)
                    gb = gdec[:, :].rearrange("p h -> p h 1" if False else "p (h o) -> p h o", o=1).broadcast_to((128, 8, 128))
                    ch_v.inc(v.tensor_tensor(ktok[:].rearrange("p (h e) -> p h e", h=8),
                                             ktok[:].rearrange("p (h e) -> p h e", h=8), gb, ALU.mult))
                    v.wait_ge(ch_v.h, ch_v.n)
                ctx.block.vector(kdv)
                kd_rdy = ch_v.n

                t0m = ch_t.n

                def scmm(t, cl=cl, gate=kd_rdy):
                    t.wait_ge(ch_v.h, gate)
                    for h in range(H):
                        pst = [psC, psD][h // 4]
                        col = (h % 4) * 128
                        mm = t.matmul(pst[:, col:col + 128],
                                      kTt[:, h * STW + cl * 128: h * STW + (cl + 1) * 128],
                                      qT[:, h * STW + cl * 128: h * STW + (cl + 1) * 128],
                                      start=True, stop=True)
                        if h % 4 == 3:
                            ch_t.inc(mm)
                ctx.block.tensor(scmm)
                sc_t = ch_t.n

                def pev(v, cl=cl, t0m=t0m, cld=CONST_LD):
                    v.wait_ge(ch_ld.h, cld)
                    v.wait_ge(ch_t.h, t0m + 1)
                    v.tensor_tensor(Pt[:, 0:512], psC[:, 0:512], Dm[:, 0:512], ALU.mult)
                    v.wait_ge(ch_t.h, t0m + 2)
                    ch_v.inc(v.tensor_tensor(Pt[:, 512:1024], psD[:, 0:512], Dm[:, 512:1024], ALU.mult))
                    qsl = qT[:].rearrange("p (h t) -> p h t", h=8)[:, :, cl * 128:(cl + 1) * 128]
                    ch_v.inc(v.tensor_tensor(qxi[:].rearrange("p (h e) -> p h e", h=8), qsl,
                                             Xi[:].rearrange("p (h e) -> p h e", h=8), ALU.mult))
                    v.wait_ge(ch_v.h, ch_v.n)
                ctx.block.vector(pev)
                p_rdy = ch_v.n

                if L0A_ST_PARTS < 6:
                    continue
                t0m = ch_t.n

                def outmm(t, cl=cl, gate=p_rdy, sgate=None):
                    t.wait_ge(ch_v.h, gate)
                    for h in range(H):
                        t.matmul(psA[:, h * 128:(h + 1) * 128], Pt[:, h * 128:(h + 1) * 128],
                                 vtok[:, cl * 1024 + h * 128: cl * 1024 + (h + 1) * 128],
                                 start=True, stop=False)
                        mm = t.matmul(psA[:, h * 128:(h + 1) * 128], qxi[:, h * 128:(h + 1) * 128],
                                      Sbf[:, h * 128:(h + 1) * 128], start=False, stop=True)
                    ch_t.inc(mm)
                    for h in range(H):
                        mm = t.matmul(psB[:, h * 128:(h + 1) * 128], ktok[:, h * 128:(h + 1) * 128],
                                      vtok[:, cl * 1024 + h * 128: cl * 1024 + (h + 1) * 128],
                                      start=True, stop=True)
                    ch_t.inc(mm)
                ctx.block.tensor(outmm)
                out_t, m_t = ch_t.n - 1, ch_t.n

                def hev(v, out_t=out_t, m_t=m_t):
                    v.wait_ge(ch_t.h, out_t)
                    ch_v.inc(v.tensor_copy(heads[:], psA[:, 0:1024]))
                    gcb = gc[:, :].rearrange("p (h o) -> p h o", o=1).broadcast_to((128, 8, 128))
                    Sr = St[:].rearrange("p (h e) -> p h e", h=8)
                    v.tensor_tensor(Sr, Sr, gcb, ALU.mult)
                    v.wait_ge(ch_t.h, m_t)
                    ch_v.inc(v.tensor_add(St[:], St[:], psB[:, 0:1024]))
                    v.wait_ge(ch_v.h, ch_v.n)
                    ch_v.inc(v.tensor_copy(Sbf[:], St[:]))
                    v.wait_ge(ch_v.h, ch_v.n)
                ctx.block.vector(hev)
                heads_v = ch_v.n
                s_upd_v = ch_v.n

                if L0A_ST_PARTS < 7:
                    continue
                s0m = ch_s.n

                def gns(s_, cl=cl, heads_v=heads_v, vg=vg_done_v):
                    s_.wait_ge(ch_v.h, heads_v)
                    s_.activation(psC[:, 0:512], heads[:, 0:512], AF.Square)
                    ch_s.inc(s_.activation(psD[:, 0:512], heads[:, 512:1024], AF.Square))
                    s_.wait_ge(ch_v.h, vg)
                    ch_s.inc(s_.activation(sg[:], gtok[:, cl * 1024:(cl + 1) * 1024], AF.Silu))
                ctx.block.scalar(gns)

                def gnv(v, s0m=s0m, cl=cl):
                    hr = heads[:].rearrange("p (h e) -> p h e", h=8)
                    ch_v.inc(v.reduce_sum(st8[:, 0:8].rearrange("p (h o) -> p h o", o=1), hr, axis=AX.X))
                    v.wait_ge(ch_s.h, s0m + 1)
                    v.reduce_sum(st8[:, 8:12].rearrange("p (h o) -> p h o", o=1),
                                 psC[:, 0:512].rearrange("p (h e) -> p h e", h=4), axis=AX.X)
                    ch_v.inc(v.reduce_sum(st8[:, 12:16].rearrange("p (h o) -> p h o", o=1),
                                          psD[:, 0:512].rearrange("p (h e) -> p h e", h=4), axis=AX.X))
                    v.wait_ge(ch_v.h, ch_v.n)
                    ch_v.inc(v.tensor_scalar(st8[:, 16:24], st8[:, 0:8], 1.0 / 128, None, ALU.mult))
                    v.wait_ge(ch_v.h, ch_v.n)
                    ch_v.inc(v.tensor_mul(st8[:, 24:32], st8[:, 16:24], st8[:, 16:24]))
                    ch_v.inc(v.tensor_scalar(st8[:, 32:40], st8[:, 8:16], 1.0 / 128, EPSI,
                                             ALU.mult, ALU.add))
                    v.wait_ge(ch_v.h, ch_v.n)
                    ch_v.inc(v.tensor_sub(st8[:, 32:40], st8[:, 32:40], st8[:, 24:32]))
                    v.wait_ge(ch_v.h, ch_v.n)
                ctx.block.vector(gnv)
                gn_v = ch_v.n
                s0m2 = ch_s.n

                def gns2(s_, gn_v=gn_v):
                    s_.wait_ge(ch_v.h, gn_v)
                    ch_s.inc(s_.activation(st8[:, 40:48], st8[:, 32:40], AF.Sqrt))
                ctx.block.scalar(gns2)

                def gnv2(v, s0m2=s0m2, cl=cl):
                    v.wait_ge(ch_s.h, s0m2 + 1)
                    ch_v.inc(v.reciprocal(st8[:, 32:40], st8[:, 40:48]))
                    v.wait_ge(ch_v.h, ch_v.n)
                    hr = heads[:].rearrange("p (h e) -> p h e", h=8)
                    mb = st8[:, 16:24].rearrange("p (h o) -> p h o", o=1).broadcast_to((128, 8, 128))
                    rb = st8[:, 32:40].rearrange("p (h o) -> p h o", o=1).broadcast_to((128, 8, 128))
                    v.tensor_tensor(hr, hr, mb, ALU.subtract)
                    v.tensor_tensor(hr, hr, rb, ALU.mult)
                    v.tensor_mul(heads[:], heads[:], bc[:, 2048:3072])
                    ch_v.inc(v.tensor_add(heads[:], heads[:], bc[:, 3072:4096]))
                    v.wait_ge(ch_s.h, s0m2 + 0 + 1)
                    ch_v.inc(v.tensor_mul(sg[:], sg[:], heads[:]))
                    v.wait_ge(ch_v.h, ch_v.n)
                ctx.block.vector(gnv2)
                z_rdy = ch_v.n

                if L0A_ST_PARTS < 8:
                    continue
                zpairs = [(sg[:, dc * 128:(dc + 1) * 128],
                           zT[:, dc * STW + cl * 128: dc * STW + (cl + 1) * 128])
                          for dc in range(NDC)]
                zT_rdy = trb.emit(zpairs, twait=[(ch_v, z_rdy)], vwait=[(ch_t, ch_t.n)])

                t0m = ch_t.n

                def womm(t, cl=cl, zT_rdy=zT_rdy, hv=heads_v):
                    t.wait_ge(trb.che.h, zT_rdy)
                    t.wait_ge(ch_v.h, hv)
                    for dc in range(NDC):
                        lhs = zT[:, dc * STW + cl * 128: dc * STW + (cl + 1) * 128]
                        t.matmul(psA[:, 0:512], lhs, wo_s[:, dc * 1024: dc * 1024 + 512],
                                 start=(dc == 0), stop=(dc == NDC - 1))
                        mm = t.matmul(psA[:, 512:1024], lhs, wo_s[:, dc * 1024 + 512:(dc + 1) * 1024],
                                      start=(dc == 0), stop=(dc == NDC - 1))
                    ch_t.inc(mm)
                ctx.block.tensor(womm)
                wo_t = ch_t.n

                def woev(v, i_chunk=i_chunk, wo_t=wo_t):
                    v.wait_ge(ch_t.h, wo_t)
                    yc = y[:, i_chunk * 1024:(i_chunk + 1) * 1024]
                    ch_v.inc(v.scalar_tensor_tensor(yc, psA[:, 0:1024], 1.0, yc, ALU.mult, ALU.add))
                    v.wait_ge(ch_v.h, ch_v.n)
                ctx.block.vector(woev)
                last_wo_ev[0] = ch_v.n
            last_stage1_mm[0] = ch_t.n

        if not L0A_FFN:
            last_ffn_add = [ch_v.n]
            continue
        emit_rows([("L0_attn_ln2_scale", bc, 0), ("L0_attn_ln2_bias", bc, 1)])
        emit_stats(EPSI, vwait=(ch_v, last_wo_ev[0]))

        for i in range(2 * L0A_NST):
            def ln2ap(v, i=i, trg=trb.cht.n, ttg=ch_t.n):
                if trg:
                    v.wait_ge(trb.cht.h, trg)
                if ttg:
                    v.wait_ge(ch_t.h, ttg)
                yc = y[:, i * 1024:(i + 1) * 1024]
                v.tensor_scalar(scr[:], yc, lnst[:, MEAN + i:MEAN + i + 1],
                                lnst[:, RSTD + i:RSTD + i + 1], ALU.subtract, ALU.mult)
                v.tensor_mul(scr[:], scr[:], bc2[:, 0:1024])
                ch_v.inc(v.tensor_add(scrb[:, 0:1024], scr[:], bc2[:, 1024:2048]))
                v.wait_ge(ch_v.h, ch_v.n)
            ctx.block.vector(ln2ap)
            rr = ch_v.n
            pairs = [(scrb[:, dc * 128:(dc + 1) * 128],
                      x2T[:, dc * T + i * 128: dc * T + (i + 1) * 128]) for dc in range(NDC)]
            trb.emit(pairs, twait=[(ch_v, rr)], vwait=[(ch_t, ch_t.n)])
        x2T_rdy = trb.che.n

        ffn_gate0 = ch_t.n
        ffn_gates = []
        for fb in range(4):
            bidx = fb % 2
            ffn_gates.append(ch_t.n)
            gate_t_ffn = ffn_gates[fb - 1] if fb >= 2 else ffn_gate0
            d0 = ch_wd.n

            def fdma(g, li=li, fb=fb, bidx=bidx, gate=gate_t_ffn):
                if gate:
                    g.wait_ge(ch_t.h, gate)
                ch_wd.inc(g.dma_start(out=w1f[bidx][:, :].rearrange("p (dc f) -> p dc f", dc=8),
                                      in_=FW1[li][:, fb * 1024:(fb + 1) * 1024]
                                      .rearrange("(dc p) f -> p dc f", p=128)), 16)
                ch_wd.inc(g.dma_start(out=w2f[bidx][:, :].rearrange("p (fc m) -> p fc m", fc=8),
                                      in_=FW2[li][fb * 1024:(fb + 1) * 1024, :]
                                      .rearrange("(fc p) m -> p fc m", p=128)), 16)
            ctx.block.gpsimd(fdma)
            fdma_n = ch_wd.n
            for st in range(L0A_NST):
                prev_gelu = ch_s.n

                def w1mm(t, st=st, bidx=bidx, fdma_n=fdma_n, x2T_rdy=x2T_rdy,
                         prev_gelu=prev_gelu):
                    t.wait_ge(ch_wd.h, fdma_n)
                    t.wait_ge(trb.che.h, x2T_rdy)
                    t.wait_ge(ch_s.h, prev_gelu)
                    for fcg in range(2):
                        if fcg == 1:
                            t.wait_ge(ch_s.h, prev_gelu + 4)
                        for fc4 in range(4):
                            fc = fcg * 4 + fc4
                            pst = [psC, psD][fc4 // 2]
                            col = (fc4 % 2) * 256
                            for dc in range(NDC):
                                mm = t.matmul(pst[:, col:col + 256],
                                              w1f[bidx][:, dc * 1024 + fc * 128: dc * 1024 + (fc + 1) * 128],
                                              x2T[:, dc * T + st * STW: dc * T + (st + 1) * STW],
                                              start=(dc == 0), stop=(dc == NDC - 1))
                            ch_t.inc(mm)
                t0m = ch_t.n

                def w1ge(s_, t0m=t0m, st=st):
                    for fcg in range(2):
                        for fc4 in range(4):
                            fc = fcg * 4 + fc4
                            s_.wait_ge(ch_t.h, t0m + fcg * 4 + 2 * (fc4 // 2 + 1))
                            pst = [psC, psD][fc4 // 2]
                            col = (fc4 % 2) * 256
                            ch_s.inc(s_.activation(hfm[:, fc * STW:(fc + 1) * STW],
                                                   pst[:, col:col + 256], AF.Gelu_apprx_tanh))
                ctx.block.tensor(w1mm)
                ctx.block.scalar(w1ge)
                gelu_done = ch_s.n

                t0m = ch_t.n

                def w2mm(t, bidx=bidx, gelu_done=gelu_done, gate=last_wo_ev[0]):
                    t.wait_ge(ch_s.h, gelu_done)
                    t.wait_ge(ch_v.h, gate)
                    for tt in range(2):
                        pso = [psA, psB][tt]
                        for fc in range(NDC):
                            lhs = hfm[:, fc * STW + tt * 128: fc * STW + tt * 128 + 128]
                            t.matmul(pso[:, 0:512], lhs, w2f[bidx][:, fc * 1024: fc * 1024 + 512],
                                     start=(fc == 0), stop=(fc == NDC - 1))
                            mm = t.matmul(pso[:, 512:1024], lhs,
                                          w2f[bidx][:, fc * 1024 + 512:(fc + 1) * 1024],
                                          start=(fc == 0), stop=(fc == NDC - 1))
                        ch_t.inc(mm)
                ctx.block.tensor(w2mm)
                w2_t = ch_t.n

                def w2ev(v, st=st, w2_t=w2_t):
                    for tt in range(2):
                        v.wait_ge(ch_t.h, w2_t - 1 + tt + 0 if False else w2_t - 1 + tt)
                        i = st * 2 + tt
                        yc = y[:, i * 1024:(i + 1) * 1024]
                        pso = [psA, psB][tt]
                        ch_v.inc(v.scalar_tensor_tensor(yc, pso[:, 0:1024], 1.0, yc,
                                                        ALU.mult, ALU.add))
                    v.wait_ge(ch_v.h, ch_v.n)
                ctx.block.vector(w2ev)
                last_wo_ev[0] = ch_v.n
            last_ffn_mm[bidx] = ch_t.n
        last_ffn_add = [ch_v.n]

    if L0A_LAYERS == 0:
        last_ffn_add = [ch_v.n]

    def rest_dma(e, gate=last_ffn_add[0]):
        e.wait_ge(ch_v.h, gate)
        xr = spill.rearrange("(c p) d -> c p d", p=128)
        for i in range(NCHUNK):
            if i >= 1:
                e.wait_ge(ch_v.h, REST_BASE[0] + i)
            ch_sp.inc(e.dma_start(out=scr[:, 0:1024], in_=xr[i]), 16)
    REST_BASE = [ch_v.n]
    sp0 = ch_sp.n
    ctx.block.sync(rest_dma)

    def rest_add(v, sp0=sp0):
        for i in range(NCHUNK):
            v.wait_ge(ch_sp.h, sp0 + 16 * (i + 1))
            yc = y[:, i * 1024:(i + 1) * 1024]
            ch_v.inc(v.tensor_add(yc, yc, scr[:, 0:1024]))
        v.wait_ge(ch_v.h, ch_v.n)
    ctx.block.vector(rest_add)
    return ch_v, ch_v.n




def _l1a_consts():
    tt_ = np.arange(128)
    diff = tt_[None, :] - tt_[:, None]
    bf = ml_dtypes.bfloat16
    strict01 = (diff > 0).astype(np.float32)
    masks = np.zeros((4, 128, 512), dtype=np.float32)
    for r in range(4):
        sg = tt_[:, None] + r * 128
        tg = np.arange(512)[None, :]
        masks[r] = (tg >= sg).astype(np.float32)
    PI = [0, 0, 0, 1, 1, 2]
    PJ = [1, 2, 3, 2, 3, 3]
    return dict(l1a_delta=diff.astype(np.float32),
                l1a_strict=strict01.astype(bf),
                l1a_cmask=masks.reshape(4 * 128, 512).astype(bf),
                l1a_ip1=(tt_ + 1.0).astype(np.float32).reshape(128, 1),
                l1a_i127=(127.0 - tt_).astype(np.float32).reshape(128, 1)), PI, PJ


def emit_l1_attn(ctx, waps):
    nc = ctx.nc
    xres = ctx.tiles["xres"]
    identf = ctx.tiles["identf"]
    ident = ctx.sbuf("ident", [128, 128], BF16)
    NH, DH, PD = 16, 64, 4
    SCALE = DH ** -0.5

    consts, PI, PJ = _l1a_consts()
    for k, v in consts.items():
        ctx.const(k, v)
    CAP = ctx.const_aps

    spill = nc.dram_tensor("l1a_spill", [T, D], F32, kind="Internal").ap()
    x3dram = nc.dram_tensor("l1a_x3", [T, D], F32, kind="Internal").ap()
    wpdram = nc.dram_tensor("l1a_wp", [T, 64], BF16, kind="Internal").ap()

    ub = xres[:, :].bitcast(BF16)
    x3nT = ub[:, 0:16384]
    outT = ub[:, 16384:32768]
    shared = ctx.sbuf("l1a_shared", [128, 16640], BF16)
    k_acc = shared[:, 0:16384]
    v_ext = shared
    kvT = ctx.sbuf("l1a_kvT", [128, 16384], BF16)
    qTh = None
    khT = None
    Ptile = ctx.sbuf("l1a_P", [128, 2048], BF16)
    scr = ctx.sbuf("scr_lnsq", [128, 1024], F32)
    scrb = ctx.sbuf("l0a_scrb", [128, 1024], BF16) if "l0a_scrb" in ctx.tiles else ctx.sbuf("l1a_scrb", [128, 1024], BF16)
    ob65 = ctx.sbuf("l1a_ob", [128, 512], F32)
    lnst = ctx.sbuf("l1a_lnst", [128, 80], F32)
    SUMS, SS, MEAN, VAR, RSTD, M2 = 0, 16, 32, 48, 64, 0
    wp = ctx.sbuf("l1a_wpj", [128, 16 * 256], BF16)
    rl = ctx.sbuf("l1a_rl", [128, 16 * 96], BF16)
    jw = ctx.sbuf("l1a_jw", [128, 16 * 96], BF16)
    r36 = ctx.sbuf("l1a_r36", [128, 16 * 36], BF16)
    w36 = ctx.sbuf("l1a_w36", [128, 16 * 36], BF16)
    r36T = ctx.sbuf("l1a_r36T", [64, 2048], BF16)
    w36T = ctx.sbuf("l1a_w36T", [64, 2048], BF16)
    Dg = ctx.sbuf("l1a_Dg", [128, 16 * 128], BF16)
    delta = ctx.sbuf("l1a_delta", [128, 128], F32)
    strict = ctx.sbuf("l1a_strict", [128, 128], BF16)
    cmask = ctx.sbuf("l1a_cmask", [128, 4 * 512], BF16)
    ip1 = ctx.sbuf("l1a_ip1", [128, 1], F32)
    i127 = ctx.sbuf("l1a_i127", [128, 1], F32)
    grow = ctx.sbuf("l1a_grow", [1, 64], F32)
    gsc = ctx.sbuf("l1a_gsc", [128, 256], F32)
    Sg = ctx.sbuf("l1a_Sg", [64, 16], F32)
    SgB = ctx.sbuf("l1a_SgB", [64, 16], BF16)
    x3st = None
    dvecB = ctx.sbuf("l1a_dvecB", [128, 16], BF16)
    ones2 = ctx.sbuf("l1a_ones2", [128, 64], F32)
    x3st = None
    qwh = ctx.sbuf("l1a_qwh", [128, 512], BF16)
    negcrep = ctx.sbuf("l1a_negc", [128, 16], F32)
    ggpre = ctx.sbuf("l1a_ggpre", [128, 16 * 16], F32)
    gg = ctx.sbuf("l1a_gg", [128, 16 * 16], BF16)
    wsm = ctx.sbuf("l1a_wsm", [128, 8 * 256], BF16)
    qTh = wp[0:64, 0:2048]
    khT = wp[0:64, 2048:4096]
    Wst = [ctx.sbuf("l1a_WA", [128, 8 * 512], BF16), ctx.sbuf("l1a_WB", [128, 8 * 512], BF16)]
    qb = ctx.sbuf("l1a_qb", [128, 8], F32)
    kvb = ctx.sbuf("l1a_kvb", [128, 2048], BF16)
    obbc = ctx.sbuf("l1a_obbc", [128, 1024], BF16)
    onest = ctx.sbuf("l1a_onest", [128, 2], BF16)
    onesf = ctx.sbuf("l1a_onesf", [1, 128], F32)
    rowf = scr[0:1, 0:1024]

    psA = ctx.psum("psA", [128, 1024], F32)
    psB = ctx.psum("psB", [128, 1024], F32)
    psC = ctx.psum("psC", [128, 512], F32)
    psD = ctx.psum("psD", [128, 512], F32)
    trb = ctx._trb if hasattr(ctx, "_trb") else TrB(ctx)
    ctx._trb = trb

    ch_sp = ctx.chan("l1a_sp")
    ch_wd = ctx.chan("l1a_wd")
    ch_ld = ctx.chan("l1a_ld")
    ch_t = ctx.chan("l1a_t")
    ch_v = ctx.chan("l1a_v")
    ch_s = ctx.chan("l1a_s")

    def spill_out(e, vg=ctx.chans.get("xin") and 0 or 0):
        xr = spill.rearrange("(c p) d -> c p d", p=128)
        for i in range(NCHUNK):
            ch_sp.inc(e.dma_start(out=xr[i], in_=xres[:, i * 1024:(i + 1) * 1024]), 16)
    ctx.block.sync(spill_out)

    v0, s0 = ch_v.n, ch_s.n

    def ssec1(s_):
        for i in range(NCHUNK):
            mrk = s_.activation(scrb[:], xres[:, i * 1024:(i + 1) * 1024], AF.Square,
                                accum_out=lnst[:, SS + i:SS + i + 1])
        ch_s.inc(mrk)

    def vsec1(v):
        for i in range(NCHUNK):
            mrk = v.reduce_sum(lnst[:, SUMS + i:SUMS + i + 1],
                               xres[:, i * 1024:(i + 1) * 1024], axis=AX.X)
        ch_v.inc(mrk)
        v.wait_ge(ch_v.h, v0 + 1)
        ch_v.inc(v.tensor_scalar(lnst[:, MEAN:MEAN + 16], lnst[:, SUMS:SUMS + 16],
                                 1.0 / D, None, ALU.mult))
        v.wait_ge(ch_v.h, v0 + 2)
        ch_v.inc(v.tensor_mul(lnst[:, M2:M2 + 16], lnst[:, MEAN:MEAN + 16],
                              lnst[:, MEAN:MEAN + 16]))
        v.wait_ge(ch_s.h, s0 + 1)
        ch_v.inc(v.tensor_scalar(lnst[:, VAR:VAR + 16], lnst[:, SS:SS + 16],
                                 1.0 / D, 1e-5, ALU.mult, ALU.add))
        v.wait_ge(ch_v.h, v0 + 4)
        ch_v.inc(v.tensor_sub(lnst[:, VAR:VAR + 16], lnst[:, VAR:VAR + 16],
                              lnst[:, M2:M2 + 16]))

    def ssec2(s_):
        s_.wait_ge(ch_v.h, v0 + 5)
        ch_s.inc(s_.activation(lnst[:, VAR:VAR + 16], lnst[:, VAR:VAR + 16], AF.Sqrt))

    def vsec2(v):
        v.wait_ge(ch_s.h, s0 + 2)
        ch_v.inc(v.reciprocal(lnst[:, RSTD:RSTD + 16], lnst[:, VAR:VAR + 16]))
        v.wait_ge(ch_v.h, ch_v.n)
    ctx.block.scalar(ssec1)
    ctx.block.vector(vsec1)
    ctx.block.scalar(ssec2)
    ctx.block.vector(vsec2)

    for i in range(NCHUNK):
        def lnap(v, i=i, trg=trb.cht.n):
            if trg:
                v.wait_ge(trb.cht.h, trg)
            v.wait_ge(ch_sp.h, 16 * (i + 1))
            yc = xres[:, i * 1024:(i + 1) * 1024]
            ch_v.inc(v.tensor_scalar(scrb[:, 0:1024], yc, lnst[:, MEAN + i:MEAN + i + 1],
                                     lnst[:, RSTD + i:RSTD + i + 1], ALU.subtract, ALU.mult))
            v.wait_ge(ch_v.h, ch_v.n)
        ctx.block.vector(lnap)
        rr = ch_v.n
        pairs = [(scrb[:, dc * 128:(dc + 1) * 128],
                  x3nT[:, i * 1024 + dc * 128: i * 1024 + (dc + 1) * 128]) for dc in range(NDC)]
        trb.emit(pairs, twait=[(ch_v, rr)], vwait=[(ch_t, ch_t.n)])
    x3nT_rdy = trb.che.n

    def cdma(e):
        ch_ld.inc(e.dma_start(out=delta[:, :], in_=CAP["l1a_delta"][:, :]), 16)
        ch_ld.inc(e.dma_start(out=strict[:, :], in_=CAP["l1a_strict"][:, :]), 16)
        for r_ in range(4):
            ch_ld.inc(e.dma_start(out=cmask[:, r_ * 512:(r_ + 1) * 512],
                                  in_=CAP["l1a_cmask"][r_ * 128:(r_ + 1) * 128, :]), 16)
        ch_ld.inc(e.dma_start(out=ip1[:, :], in_=CAP["l1a_ip1"][:, :]), 16)
        ch_ld.inc(e.dma_start(out=i127[:, :], in_=CAP["l1a_i127"][:, :]), 16)
        ch_ld.inc(e.dma_start(out=grow[0:1, 0:16],
                              in_=waps["L1_attn_decay_logits"].rearrange("(a c) -> a c", a=1)), 16)
        ch_ld.inc(e.dma_start(out=grow[0:1, 16:32],
                              in_=waps["L1_attn_mem_scale"].rearrange("(a c) -> a c", a=1)), 16)
        ch_ld.inc(e.dma_start(out=grow[0:1, 32:33],
                              in_=waps["L1_attn_kv_gate"].rearrange("(a c) -> a c", a=1)), 16)
        ch_ld.inc(e.dma_start(out=qb[:, :], in_=waps["L1_attn_qkv_b"][0:1024].rearrange(
            "(hp r) -> r hp", r=128), allow_slow_non_contiguous=True), 16)
    ctx.block.scalar(cdma)

    def cdma_g(gq):
        for wi, wn in enumerate(("L1_attn_w1_write", "L1_attn_w2_write",
                                 "L1_attn_w1_read", "L1_attn_w2_read")):
            ch_ld.inc(gq.dma_start(
                out=wsm[:, :].rearrange("p (dc c) -> p dc c", dc=8)[:, :, wi * 64:(wi + 1) * 64],
                in_=waps[wn].rearrange("(dc p) c -> p dc c", p=128)), 16)
    ctx.block.gpsimd(cdma_g)
    CLD = ch_ld.n

    def mk_row_bcast(rname, roff, dst):
        r0 = ch_ld.n

        def rdma(e, rname=rname, roff=roff, tg=ch_t.n):
            if tg:
                e.wait_ge(ch_t.h, tg)
            ch_ld.inc(e.dma_start(out=rowf[0:1, :], in_=waps[rname][roff:roff + 1024]
                                  .rearrange("(a c) -> a c", a=1)), 16)
        t0 = ch_t.n

        def rmm(t, r0=r0, gate=ch_v.n):
            t.wait_ge(ch_ld.h, r0 + 16)
            t.wait_ge(ch_v.h, gate)
            ch_t.inc(t.matmul(psC[:, 0:512], onesf[0:1, 0:128], rowf[0:1, 0:512],
                              start=True, stop=True))
            ch_t.inc(t.matmul(psD[:, 0:512], onesf[0:1, 0:128], rowf[0:1, 512:1024],
                              start=True, stop=True))

        def rev(v, dst=dst, t0=t0):
            v.wait_ge(ch_t.h, t0 + 2)
            v.tensor_copy(dst[:, 0:512], psC[:, 0:512])
            ch_v.inc(v.tensor_copy(dst[:, 512:1024], psD[:, 0:512]))
            v.wait_ge(ch_v.h, ch_v.n)
        ctx.block.scalar(rdma)
        ctx.block.tensor(rmm)
        ctx.block.vector(rev)

    def ones_v(v):
        ch_v.inc(v.memset(onesf[:], 1.0))
        ch_v.inc(v.memset(onest[:, :], 1.0))
        v.wait_ge(ch_v.h, ch_v.n)
    ctx.block.vector(ones_v)
    mk_row_bcast("L1_attn_qkv_b", 1024, kvb[:, 0:1024])
    mk_row_bcast("L1_attn_qkv_b", 2048, kvb[:, 1024:2048])
    mk_row_bcast("L1_attn_out_b", 0, obbc)

    g0v, g0s = ch_v.n, ch_s.n
    t0g = ch_t.n

    def gram_t(t, gate=ch_v.n):
        t.wait_ge(ch_ld.h, CLD)
        t.wait_ge(ch_v.h, gate)
        ch_t.inc(t.matmul(psC[:, 0:64], onesf[0:1, 0:128], grow[0:1, 0:64],
                          start=True, stop=True))
    ctx.block.tensor(gram_t)

    def gram_v(v, t0g=t0g):
        v.wait_ge(ch_t.h, t0g + 1)
        ch_v.inc(v.tensor_copy(gsc[:, 64:128], psC[:, 0:64]))
        v.wait_ge(ch_v.h, ch_v.n)
    ctx.block.vector(gram_v)
    gv1 = ch_v.n

    def gram_s(s_, gv1=gv1):
        s_.wait_ge(ch_v.h, gv1)
        ch_s.inc(s_.activation(gsc[:, 112:128], gsc[:, 64:80], AF.Sigmoid))
        s_.wait_ge(ch_s.h, ch_s.n)
        ch_s.inc(s_.activation(gsc[:, 128:144], gsc[:, 112:128], AF.Ln))
    ctx.block.scalar(gram_s)
    gs1 = ch_s.n

    def gram_v2(v, gs1=gs1):
        v.wait_ge(ch_s.h, gs1)
        ch_v.inc(v.tensor_scalar(gsc[:, 144:160], gsc[:, 128:144], ip1[:, 0:1], None, ALU.mult))
        ch_v.inc(v.tensor_scalar(gsc[:, 160:176], gsc[:, 128:144], i127[:, 0:1], None, ALU.mult))
        v.wait_ge(ch_v.h, ch_v.n)
    ctx.block.vector(gram_v2)
    gv2 = ch_v.n

    def gram_s2a(s_, gv2=gv2):
        s_.wait_ge(ch_v.h, gv2)
        ch_s.inc(s_.activation(gsc[:, 144:160], gsc[:, 144:160], AF.Exp))
        ch_s.inc(s_.activation(gsc[:, 160:176], gsc[:, 160:176], AF.Exp))
        ch_s.inc(s_.activation(gsc[:, 176:192], gsc[:, 128:144], AF.Exp, scale=128.0))
    ctx.block.scalar(gram_s2a)
    for h in range(NH):
        v0h = ch_v.n

        def dg_v1(v, h=h):
            v.wait_ge(ch_s.h, ch_s.n) if False else None
            ch_v.inc(v.tensor_scalar(scr[:, 0:128], delta[:, :], gsc[:, 128 + h:129 + h],
                                     None, ALU.mult))
            v.wait_ge(ch_v.h, ch_v.n)
        ctx.block.vector(dg_v1)
        vm = ch_v.n
        s0h = ch_s.n

        def dg_s(s_, vm=vm):
            s_.wait_ge(ch_v.h, vm)
            ch_s.inc(s_.activation(scr[:, 0:128], scr[:, 0:128], AF.Exp))
        ctx.block.scalar(dg_s)
        sm = ch_s.n

        def dg_v2(v, h=h, sm=sm):
            v.wait_ge(ch_s.h, sm)
            ch_v.inc(v.tensor_mul(Dg[:, h * 128:(h + 1) * 128], scr[:, 0:128], strict[:, :]))
            v.wait_ge(ch_v.h, ch_v.n)
        ctx.block.vector(dg_v2)


    t0p = ch_t.n
    v0p = ch_v.n

    def wproj_t(t, x3r=x3nT_rdy, gate=ch_v.n, v0p=v0p):
        t.wait_ge(trb.che.h, x3r)
        t.wait_ge(ch_ld.h, CLD)
        t.wait_ge(ch_v.h, gate)
        for i in range(NCHUNK):
            if i >= 2:
                t.wait_ge(ch_v.h, v0p + i - 1)
            for dc in range(NDC):
                mm = t.matmul([psC, psD][i % 2][:, 0:256],
                              x3nT[:, i * 1024 + dc * 128: i * 1024 + (dc + 1) * 128],
                              wsm[:, dc * 256:(dc + 1) * 256],
                              start=(dc == 0), stop=(dc == NDC - 1))
            ch_t.inc(mm)

    def wproj_v(v, t0p=t0p):
        for i in range(NCHUNK):
            v.wait_ge(ch_t.h, t0p + i + 1)
            ch_v.inc(v.tensor_copy(wp[:, i * 256:(i + 1) * 256], [psC, psD][i % 2][:, 0:256]))
        v.wait_ge(ch_v.h, ch_v.n)
    ctx.block.tensor(wproj_t)
    ctx.block.vector(wproj_v)
    wp_done = ch_v.n

    sp0 = ch_sp.n

    def shift_dma(e, wp_done=wp_done, sp0=sp0):
        e.wait_ge(ch_v.h, wp_done)
        wpr = wp[:, :].rearrange("p (c g) -> p c g", c=16)
        ch_sp.inc(e.dma_start(out=wpdram.rearrange("(c p) g -> p c g", p=128),
                              in_=wpr[:, :, 0:64]), 16)
        e.wait_ge(ch_sp.h, ch_sp.n)
        ch_sp.inc(e.dma_start(out=wp[1:128, 0:64], in_=wpdram[0:127, :]), 16)
        for i in range(1, NCHUNK):
            ch_sp.inc(e.dma_start(out=wp[:, i * 256:i * 256 + 64],
                                  in_=wpdram[i * 128 - 1: (i + 1) * 128 - 1, :]), 16)
    ctx.block.sync(shift_dma)
    shift_done = ch_sp.n

    def zrow_v(v, sd=shift_done):
        v.wait_ge(ch_sp.h, sd)
        ch_v.inc(v.memset(wp[0:1, 0:64], 0.0))
        v.wait_ge(ch_v.h, ch_v.n)
    ctx.block.vector(zrow_v)
    wps_done = ch_v.n

    def ext_chunk(i):
        base = i * 256

        def ext_v(v, i=i, base=base, gate=wps_done):
            if i == 0:
                v.wait_ge(ch_v.h, gate)
            for (o1, o2, dstt) in ((0, 64, scrb), (128, 192, scr.bitcast(BF16))):
                p1 = wp[:, base + o1: base + o1 + 64].rearrange("p (h d) -> p h d", h=16)
                p2 = wp[:, base + o2: base + o2 + 64].rearrange("p (h d) -> p h d", h=16)
                Lt = dstt[:, 0:96].rearrange("p (h c) -> p h c", h=16)
                specs = [(0, 1, 0, 3), (1, 2, 3, 2), (2, 3, 5, 1)]
                for (pi, pj0, c0, n) in specs:
                    a = p1[:, :, pi:pi + 1].broadcast_to((128, 16, n))
                    b = p2[:, :, pj0:pj0 + n]
                    v.tensor_tensor(Lt[:, :, c0:c0 + n], a, b, ALU.mult)
                    a2 = p2[:, :, pi:pi + 1].broadcast_to((128, 16, n))
                    b2 = p1[:, :, pj0:pj0 + n]
                    v.tensor_tensor(dstt[:, 96:192].rearrange("p (h c) -> p h c", h=16)[:, :, c0:c0 + n],
                                    a2, b2, ALU.mult)
                ch_v.inc(v.tensor_sub(Lt[:, :, :],
                                      Lt[:, :, :],
                                      dstt[:, 96:192].rearrange("p (h c) -> p h c", h=16)))
                v.wait_ge(ch_v.h, ch_v.n)
            ch_v.inc(v.tensor_mul(scr.bitcast(BF16)[:, 192:288], scrb[:, 0:96], scrb[:, 0:96]))
            ch_v.inc(v.tensor_mul(scr.bitcast(BF16)[:, 288:384],
                                  scr.bitcast(BF16)[:, 0:96], scr.bitcast(BF16)[:, 0:96]))
            v.wait_ge(ch_v.h, ch_v.n)
            ch_v.inc(v.reduce_sum(gsc[:, 0:16].rearrange("p (h o) -> p h o", o=1),
                                  scr.bitcast(BF16)[:, 192:288].rearrange("p (h c) -> p h c", h=16),
                                  axis=AX.X))
            ch_v.inc(v.reduce_sum(gsc[:, 16:32].rearrange("p (h o) -> p h o", o=1),
                                  scr.bitcast(BF16)[:, 288:384].rearrange("p (h c) -> p h c", h=16),
                                  axis=AX.X))
            v.wait_ge(ch_v.h, ch_v.n)
        ctx.block.vector(ext_v)
        vm1 = ch_v.n
        s0e = ch_s.n

        def ext_s(s_, vm1=vm1):
            s_.wait_ge(ch_v.h, vm1)
            ch_s.inc(s_.activation(gsc[:, 0:32], gsc[:, 0:32], AF.Sqrt))
        ctx.block.scalar(ext_s)
        sm1 = ch_s.n

        def ext_v2(v, i=i, sm1=sm1):
            v.wait_ge(ch_s.h, sm1)
            ch_v.inc(v.tensor_scalar(gsc[:, 0:32], gsc[:, 0:32], 1e-12, None, ALU.max))
            v.wait_ge(ch_v.h, ch_v.n)
            ch_v.inc(v.reciprocal(gsc[:, 32:64], gsc[:, 0:32]))
            v.wait_ge(ch_v.h, ch_v.n)
            ch_v.inc(v.tensor_tensor(rl[:, i * 96:(i + 1) * 96].rearrange("p (h c) -> p h c", h=16),
                                     scr.bitcast(BF16)[:, 0:96].rearrange("p (h c) -> p h c", h=16),
                                     gsc[:, 48:64].rearrange("p (h o) -> p h o", o=1)
                                     .broadcast_to((128, 16, 6)), ALU.mult))
            ch_v.inc(v.tensor_tensor(scrb[:, 96:192].rearrange("p (h c) -> p h c", h=16),
                                     scrb[:, 0:96].rearrange("p (h c) -> p h c", h=16),
                                     gsc[:, 32:48].rearrange("p (h o) -> p h o", o=1)
                                     .broadcast_to((128, 16, 6)), ALU.mult))
            v.wait_ge(ch_v.h, ch_v.n)
            wlr = scrb[:, 96:192].rearrange("p (h c) -> p h c", h=16)
            jwr = jw[:, i * 96:(i + 1) * 96].rearrange("p (h c) -> p h c", h=16)
            for c_ in range(6):
                mrk = v.tensor_copy(jwr[:, :, c_:c_ + 1], wlr[:, :, 5 - c_:6 - c_])
            ch_v.inc(mrk)
            v.wait_ge(ch_v.h, ch_v.n)
            ch_v.inc(v.tensor_scalar(jwr[:, :, 1:2], jwr[:, :, 1:2], -1.0, None, ALU.mult))
            ch_v.inc(v.tensor_scalar(jwr[:, :, 4:5], jwr[:, :, 4:5], -1.0, None, ALU.mult))
            v.wait_ge(ch_v.h, ch_v.n)
        ctx.block.vector(ext_v2)
    for i in range(NCHUNK):
        ext_chunk(i)
    ext_done = ch_v.n

    return emit_l1_attn_part2(ctx, waps, locals())


def emit_l1_attn_part2(ctx, waps, g):
    nc = ctx.nc
    xres = ctx.tiles["xres"]
    identf = ctx.tiles["identf"]
    ident = g["ident"]
    (x3nT, outT, shared, k_acc, kvT, qTh, khT, Ptile) = (
        g["x3nT"], g["outT"], g["shared"], g["k_acc"], g["kvT"], g["qTh"], g["khT"], g["Ptile"])
    (scr, scrb, ob65, wp, rl, jw, r36, w36, r36T, w36T, Dg) = (
        g["scr"], g["scrb"], g["ob65"], g["wp"], g["rl"], g["jw"], g["r36"], g["w36"],
        g["r36T"], g["w36T"], g["Dg"])
    (gsc, Sg, ggpre, gg, Wst, qb, kvb, obbc, onest, onesf, cmask) = (
        g["gsc"], g["Sg"], g["ggpre"], g["gg"], g["Wst"], g["qb"], g["kvb"], g["obbc"],
        g["onest"], g["onesf"], g["cmask"])
    (psA, psB, psC, psD, trb) = (g["psA"], g["psB"], g["psC"], g["psD"], g["trb"])
    (ch_sp, ch_wd, ch_ld, ch_t, ch_v, ch_s) = (
        g["ch_sp"], g["ch_wd"], g["ch_ld"], g["ch_t"], g["ch_v"], g["ch_s"])
    (spill, x3dram, x3nT_rdy, ext_done, CLD) = (
        g["spill"], g["x3dram"], g["x3nT_rdy"], g["ext_done"], g["CLD"])
    SgB = g["SgB"]
    dvecB = g["dvecB"]
    ones2 = g["ones2"]
    x3st = g["Ptile"][:, :].bitcast(F32)
    qwh = g["qwh"]
    NH, DH = 16, 64
    SCALE = DH ** -0.5
    QKVW = waps["L1_attn_qkv_w"]
    GEOW = waps["L1_attn_kv_geo_w"]
    OUTW = waps["L1_attn_out_w"]

    def b36_v(v, gate=ext_done):
        v.wait_ge(ch_v.h, gate)
        v.memset(Sg[:, :], 0.0)
        ch_v.inc(v.memset(SgB[:, :], 0.0))
        ch_v.inc(v.tensor_copy(dvecB[:, :], gsc[:, 160:176]))
        ch_v.inc(v.memset(ones2[:, :], 1.0))
        v.wait_ge(ch_v.h, ch_v.n)
    ctx.block.vector(b36_v)
    b36_done = ch_v.n

    for h in range(NH):
        def bh_v(v, h=h, gate=b36_done, trg=trb.cht.n):
            if trg:
                v.wait_ge(trb.cht.h, trg)
            if h == 0:
                v.wait_ge(ch_v.h, gate)
            for (src, dst) in ((rl, r36), (jw, w36)):
                s6 = src[:, :].rearrange("p (c s) -> p c s", s=96)[:, :, h * 6:(h + 1) * 6]
                ina = s6.rearrange("p c (x o) -> p c x o", o=1).broadcast_to((128, 16, 6, 6))
                inb = s6.rearrange("p c (o y) -> p c o y", o=1).broadcast_to((128, 16, 6, 6))
                mrk = v.tensor_tensor(
                    dst[:, :].rearrange("p (c x y) -> p c x y", c=16, x=6), ina, inb, ALU.mult)
            ch_v.inc(mrk)
            v.wait_ge(ch_v.h, ch_v.n)
        ctx.block.vector(bh_v)
        bh_done = ch_v.n
        pairs = []
        for i in range(NCHUNK):
            pairs.append((r36[:, i * 36:(i + 1) * 36], r36T[0:36, i * 128:(i + 1) * 128]))
        m1 = trb.emit(pairs, twait=[(ch_v, bh_done)], vwait=[(ch_t, ch_t.n)])
        pairs = []
        for i in range(NCHUNK):
            pairs.append((w36[:, i * 36:(i + 1) * 36], w36T[0:36, i * 128:(i + 1) * 128]))
        m2 = trb.emit(pairs, vwait=[(ch_t, ch_t.n)])
        tv0, vv0 = ch_t.n, ch_v.n

        def gram_t(t, h=h, m2=m2, tv0=tv0, vv0=vv0):
            t.wait_ge(trb.che.h, m2)
            for i in range(NCHUNK):
                if i >= 1:
                    t.wait_ge(ch_v.h, vv0 + 6 * i)
                t.matmul(psC[:, 0:128], w36T[0:36, i * 128:(i + 1) * 128],
                         r36T[0:36, i * 128:(i + 1) * 128], start=True, stop=True)
                ch_t.inc(t.nop()) if False else None
                ch_t.n += 0
        del gram_t

        def gram_t(t, h=h, m2=m2, tv0=tv0, vv0=vv0):
            t.wait_ge(trb.che.h, m2)
            for i in range(NCHUNK):
                if i >= 1:
                    t.wait_ge(ch_v.h, vv0 + 6 * i)
                ch_t.inc(t.matmul(psC[:, 0:128], w36T[0:36, i * 128:(i + 1) * 128],
                                  r36T[0:36, i * 128:(i + 1) * 128], start=True, stop=True))
                t.wait_ge(ch_v.h, vv0 + 6 * i + 1)
                ch_t.inc(t.matmul(psD[:, 0:1], Ptile[:, 0:128], onest[:, 0:1],
                                  start=True, stop=True))
                ch_t.inc(t.matmul(psD[:, 1:2], r36T[0:36, i * 128:(i + 1) * 128],
                                  SgB[0:36, h:h + 1], start=True, stop=True))
                ch_t.inc(t.matmul(psD[0:36, 4:5], w36[:, i * 36:(i + 1) * 36],
                                  dvecB[:, h:h + 1], start=True, stop=True))

        def gram_v(v, h=h, tv0=tv0, vv0=vv0):
            for i in range(NCHUNK):
                v.wait_ge(ch_t.h, tv0 + 4 * i + 1)
                ch_v.inc(v.tensor_mul(Ptile[:, 0:128], psC[:, 0:128],
                                      Dg[:, h * 128:(h + 1) * 128]))
                v.wait_ge(ch_v.h, ch_v.n)
                v.wait_ge(ch_t.h, tv0 + 4 * i + 4)
                ch_v.inc(v.tensor_copy(gsc[:, 0:8], psD[:, 0:8]))
                v.wait_ge(ch_v.h, ch_v.n)
                ch_v.inc(v.scalar_tensor_tensor(ggpre[:, i * 16 + h: i * 16 + h + 1],
                                                gsc[:, 1:2], gsc[:, 144 + h:145 + h],
                                                gsc[:, 0:1], ALU.mult, ALU.add))
                ch_v.inc(v.tensor_scalar(Sg[0:36, h:h + 1], Sg[0:36, h:h + 1],
                                         gsc[0:36, 176 + h:177 + h], None, ALU.mult))
                v.wait_ge(ch_v.h, ch_v.n)
                ch_v.inc(v.tensor_add(Sg[0:36, h:h + 1], Sg[0:36, h:h + 1],
                                      psD[0:36, 4:5]))
                v.wait_ge(ch_v.h, ch_v.n)
                ch_v.inc(v.tensor_copy(SgB[0:36, h:h + 1], Sg[0:36, h:h + 1]))
                v.wait_ge(ch_v.h, ch_v.n)
        ctx.block.tensor(gram_t)
        ctx.block.vector(gram_v)
    gram_done = ch_v.n

    def gg_v(v, gate=gram_done):
        msb = gsc[:, 80:96].rearrange("p (o h) -> p o h", o=1).broadcast_to((128, 16, 16))
        ch_v.inc(v.tensor_tensor(ggpre[:, :].rearrange("p (c h) -> p c h", c=16),
                                 ggpre[:, :].rearrange("p (c h) -> p c h", c=16),
                                 msb, ALU.mult))
        v.wait_ge(ch_v.h, ch_v.n)
    ctx.block.vector(gg_v)
    ggv = ch_v.n

    def gg_s(s_, ggv=ggv):
        s_.wait_ge(ch_v.h, ggv)
        ch_s.inc(s_.activation(ggpre[:, :], ggpre[:, :], AF.Sigmoid))
    ctx.block.scalar(gg_s)
    ggs = ch_s.n

    def gg_v2(v, ggs=ggs):
        v.wait_ge(ch_s.h, ggs)
        ch_v.inc(v.tensor_scalar(gg[:, :], ggpre[:, :], gsc[:, 96:97], None, ALU.mult))
        ch_v.inc(v.memset(gsc[:, 192:224], 0.0))
        v.wait_ge(ch_v.h, ch_v.n)
    ctx.block.vector(gg_v2)
    gg_done = ch_v.n

    blocks = [("q", 0, "qkv"), ("q", 512, "qkv"), ("k", 1024, "qkv"), ("k", 1536, "qkv"),
              ("gk", 0, "geo"), ("gk", 512, "geo"),
              ("KT", 0, None),
              ("v", 2048, "qkv"), ("v", 2560, "qkv"), ("gv", 1024, "geo"), ("gv", 1536, "geo")]
    wbm = {}
    BEV_MARKS = {}
    bi = 0
    kvT_done = [0]
    for tag, cb, src in blocks:
        if tag == "KT":
            for i in range(NCHUNK):
                pairs = []
                for hh in range(NH):
                    pairs.append((k_acc[:, i * 1024 + hh * 64: i * 1024 + (hh + 1) * 64],
                                  kvT[64 * (hh % 2):64 * (hh % 2) + 64,
                                      (hh // 2) * 2048 + i * 128: (hh // 2) * 2048 + (i + 1) * 128]))
                kvT_done[0] = trb.emit(pairs, twait=[(ch_v, ch_v.n)], vwait=[(ch_t, ch_t.n)])
            for i in range(NCHUNK):
                s0k = ch_s.n

                def kmax_s(s_, i=i, gate=ch_v.n):
                    s_.wait_ge(ch_v.h, gate)
                    ch_s.inc(s_.activation(scrb[:, 0:1024], k_acc[:, i * 1024:(i + 1) * 1024],
                                           AF.Square))
                ctx.block.scalar(kmax_s)
                sm = ch_s.n

                def kmax_v(v, i=i, sm=sm):
                    v.wait_ge(ch_s.h, sm)
                    ch_v.inc(v.reduce_sum(gsc[:, 32:48].rearrange("p (h o) -> p h o", o=1),
                                          scrb[:, 0:1024].rearrange("p (h e) -> p h e", h=16),
                                          axis=AX.X))
                    v.wait_ge(ch_v.h, ch_v.n)
                    ch_v.inc(v.tensor_max(gsc[:, 208:224], gsc[:, 208:224], gsc[:, 32:48]))
                    v.wait_ge(ch_v.h, ch_v.n)
                ctx.block.vector(kmax_v)
            continue
        widx = bi % 2
        ipt = 2 if tag == "q" else 1

        def bdma(gq, cb=cb, src=src, widx=widx, gate=wbm.get(bi - 2, 0)):
            if gate:
                gq.wait_ge(ch_t.h, gate)
            dr = QKVW if src == "qkv" else GEOW
            ch_wd.inc(gq.dma_start(out=Wst[widx][:, :].rearrange("p (dc m) -> p dc m", dc=8),
                                   in_=dr[:, cb:cb + 512].rearrange("(dc p) m -> p dc m", p=128)),
                      16)
        ctx.block.gpsimd(bdma)
        wd_n = ch_wd.n
        t0b = ch_t.n
        vb0 = ch_v.n

        def bmm(t, widx=widx, wd_n=wd_n, x3r=x3nT_rdy, gate=ch_v.n, vb0=vb0, ipt=ipt):
            t.wait_ge(ch_wd.h, wd_n)
            t.wait_ge(trb.che.h, x3r)
            t.wait_ge(ch_v.h, gate)
            for i in range(NCHUNK):
                if i >= 2:
                    t.wait_ge(ch_v.h, vb0 + ipt * (i - 1))
                for dc in range(NDC):
                    mm = t.matmul([psA, psB][i % 2][:, 0:512],
                                  x3nT[:, i * 1024 + dc * 128: i * 1024 + (dc + 1) * 128],
                                  Wst[widx][:, dc * 512:(dc + 1) * 512],
                                  start=(dc == 0), stop=(dc == NDC - 1))
                ch_t.inc(mm)
        ctx.block.tensor(bmm)

        def bev(v, tag=tag, cb=cb, t0b=t0b, bi=bi):
            hw = 0 if cb in (0, 1024, 2048) else 1
            for i in range(NCHUNK):
                v.wait_ge(ch_t.h, t0b + i + 1)
                ps = [psA, psB][i % 2]
                if tag == "q":
                    v.tensor_copy(scrb[:, 0:512], ps[:, 0:512])
                    v.tensor_mul(scr[:, 0:512], scrb[:, 0:512], scrb[:, 0:512])
                    ch_v.inc(v.reduce_sum(gsc[:, 32:40].rearrange("p (h o) -> p h o", o=1),
                                          scr[:, 0:512].rearrange("p (h e) -> p h e", h=8),
                                          axis=AX.X))
                    v.wait_ge(ch_v.h, ch_v.n)
                    qcols = slice(192 + hw * 8, 192 + hw * 8 + 8)
                    ch_v.inc(v.tensor_max(gsc[:, qcols], gsc[:, qcols], gsc[:, 32:40]))
                    v.wait_ge(ch_v.h, ch_v.n)
                elif tag == "k":
                    dst = k_acc[:, i * 1024 + hw * 512: i * 1024 + hw * 512 + 512]
                    ch_v.inc(v.scalar_tensor_tensor(dst, ps[:, 0:512], 1.0,
                                                    kvb[:, hw * 512:(hw + 1) * 512],
                                                    ALU.mult, ALU.add))
                elif tag == "gk":
                    ggb = gg[:, i * 16 + hw * 8: i * 16 + (hw + 1) * 8]
                    ggx = ggb.rearrange("p (h o) -> p h o", o=1).broadcast_to((128, 8, 64))
                    v.tensor_tensor(scrb[:, 0:512].rearrange("p (h e) -> p h e", h=8),
                                    ps[:, 0:512].rearrange("p (h e) -> p h e", h=8), ggx, ALU.mult)
                    dst = k_acc[:, i * 1024 + hw * 512: i * 1024 + hw * 512 + 512]
                    ch_v.inc(v.tensor_add(dst, dst, scrb[:, 0:512]))
                elif tag == "v":
                    dst = shared[:, :].rearrange("p (c x) -> p c x", c=16)[:, i, :] \
                        .rearrange("p (h e) -> p h e", h=16)[:, hw * 8:(hw + 1) * 8, 0:64]
                    vbb = kvb[:, 1024 + hw * 512:1024 + (hw + 1) * 512] \
                        .rearrange("p (h e) -> p h e", h=8)
                    ch_v.inc(v.scalar_tensor_tensor(
                        dst, ps[:, 0:512].rearrange("p (h e) -> p h e", h=8), 1.0,
                        vbb, ALU.mult, ALU.add))
                elif tag == "gv":
                    ggb = gg[:, i * 16 + hw * 8: i * 16 + (hw + 1) * 8]
                    ggx = ggb.rearrange("p (h o) -> p h o", o=1).broadcast_to((128, 8, 64))
                    v.tensor_tensor(scrb[:, 0:512].rearrange("p (h e) -> p h e", h=8),
                                    ps[:, 0:512].rearrange("p (h e) -> p h e", h=8), ggx, ALU.mult)
                    dst = shared[:, :].rearrange("p (c x) -> p c x", c=16)[:, i, :] \
                        .rearrange("p (h e) -> p h e", h=16)[:, hw * 8:(hw + 1) * 8, 0:64]
                    ch_v.inc(v.tensor_add(dst, dst, scrb[:, 0:512]
                                          .rearrange("p (h e) -> p h e", h=8)))
            v.wait_ge(ch_v.h, ch_v.n)
        ctx.block.vector(bev)
        wbm[bi] = ch_t.n
        bi += 1
    def vones(v):
        dst = shared[:, :].rearrange("p (c h e) -> p c h e", c=16, h=16)[:, :, :, 64:65]
        ch_v.inc(v.memset(dst, 1.0))
        v.wait_ge(ch_v.h, ch_v.n)
    ctx.block.vector(vones)
    vext_done = ch_v.n

    t0c = ch_t.n

    def c_t(t, gate=vext_done):
        t.wait_ge(ch_v.h, gate)
        ch_t.inc(t.transpose(psC[0:16, 0:128], gsc[:, 192:208], identf[:]))
        ch_t.inc(t.transpose(psD[0:16, 0:128], gsc[:, 208:224], identf[:]))
        t.wait_ge(ch_v.h, CREP_M[0])
        ch_t.inc(t.transpose(psC[0:1, 0:16], gsc[0:16, 40:41], identf[0:16, 0:16]))
        t.wait_ge(ch_v.h, CREP_M[1])
        ch_t.inc(t.matmul(psD[:, 0:16], onesf[0:1, 0:128], scr[0:1, 0:16],
                          start=True, stop=True))
    CREP_M = {}

    def c_v(v, t0c=t0c):
        v.wait_ge(ch_t.h, t0c + 2)
        v.tensor_copy(gsc[0:16, 32:33].rearrange("p (a b) -> p (a b)") if False else gsc[0:16, 33:34], psC[0:16, 0:1]) if False else None
        ch_v.inc(v.reduce_max(gsc[0:16, 34:35], psC[0:16, 0:128], axis=AX.X))
        ch_v.inc(v.reduce_max(gsc[0:16, 35:36], psD[0:16, 0:128], axis=AX.X))
        v.wait_ge(ch_v.h, ch_v.n)
        ch_v.inc(v.tensor_mul(gsc[0:16, 36:37], gsc[0:16, 34:35], gsc[0:16, 35:36]))
        v.wait_ge(ch_v.h, ch_v.n)
    ctx.block.vector(c_v)
    cv1 = ch_v.n
    s0c = ch_s.n

    def c_s(s_, cv1=cv1):
        s_.wait_ge(ch_v.h, cv1)
        ch_s.inc(s_.activation(gsc[0:16, 37:38], gsc[0:16, 36:37], AF.Sqrt))
    ctx.block.scalar(c_s)
    cs1 = ch_s.n

    def c_v2(v, cs1=cs1):
        v.wait_ge(ch_s.h, cs1)
        ch_v.inc(v.tensor_scalar(gsc[0:16, 40:41], gsc[0:16, 37:38], -SCALE, None, ALU.mult))
        v.wait_ge(ch_v.h, ch_v.n)
        CREP_M[0] = ch_v.n
    ctx.block.vector(c_v2)
    CREP_M[0] = ch_v.n

    def c_v3(v, t0c=t0c):
        v.wait_ge(ch_t.h, t0c + 3)
        ch_v.inc(v.tensor_copy(scr[0:1, 0:16], psC[0:1, 0:16]))
        v.wait_ge(ch_v.h, ch_v.n)
        CREP_M[1] = ch_v.n
    ctx.block.vector(c_v3)
    CREP_M[1] = ch_v.n

    def c_v4(v, t0c=t0c):
        v.wait_ge(ch_t.h, t0c + 4)
        ch_v.inc(v.tensor_copy(gsc[:, 48:64].bitcast(F32)[:, 0:16] if False else negcrep[:, 0:16],
                               psD[:, 0:16]))
        v.wait_ge(ch_v.h, ch_v.n)
    negcrep = g["negcrep"]
    ctx.block.tensor(c_t)
    ctx.block.vector(c_v4)
    crep_done = ch_v.n

    for h in range(NH):
        hp, hl = h // 2, h % 2
        d0 = ch_wd.n

        def qdma(gq, h=h, gate=wbm.get(len(blocks) - 2, 0)):
            if h == 0 and gate:
                gq.wait_ge(ch_t.h, gate)
            if h >= 1:
                gq.wait_ge(ch_t.h, QWH_T[h - 1])
            ch_wd.inc(gq.dma_start(out=qwh[:, :].rearrange("p (dc m) -> p dc m", dc=8),
                                   in_=QKVW[:, h * 64:(h + 1) * 64]
                                   .rearrange("(dc p) m -> p dc m", p=128)), 16)
        ctx.block.gpsimd(qdma)
        qw_n = ch_wd.n
        t0q = ch_t.n

        def qmm(t, h=h, qw_n=qw_n, gate=ch_v.n, kd=kvT_done[0]):
            t.wait_ge(ch_wd.h, qw_n)
            t.wait_ge(trb.che.h, kd)
            t.wait_ge(ch_v.h, gate)
            x3r4 = x3nT.rearrange("p (i x) -> p i x", i=16)
            for tb in range(4):
                for dc in range(NDC):
                    mm = t.matmul(psC[0:64, 0:512], qwh[:, dc * 64:(dc + 1) * 64],
                                  x3r4[:, tb * 4:(tb + 1) * 4, dc * 128:(dc + 1) * 128],
                                  start=(dc == 0), stop=(dc == NDC - 1))
                ch_t.inc(mm)
                t.wait_ge(ch_s.h, QEV_S[(h, tb)])
        QWH_T = {}
        QEV_S = {}
        s0q = ch_s.n
        for tb in range(4):
            QEV_S[(h, tb)] = s0q + tb + 1

        def qev(s_, h=h, hp=hp, hl=hl, t0q=t0q, crd=crep_done):
            s_.wait_ge(ch_v.h, crd)
            for tb in range(4):
                s_.wait_ge(ch_t.h, t0q + tb + 1)
                ch_s.inc(s_.activation(qTh[0:64, tb * 512:(tb + 1) * 512], psC[0:64, 0:512],
                                       AF.Identity, bias=qb[64 * hl:64 * hl + 64, hp:hp + 1]))
        ctx.block.tensor(qmm)
        ctx.block.scalar(qev)
        QWH_T[h] = ch_t.n

        def khcp(v, h=h, hp=hp, hl=hl, gate=ch_s.n):
            v.wait_ge(ch_s.h, gate)
            ch_v.inc(v.tensor_copy(khT[0:64, :], kvT[64 * hl:64 * hl + 64,
                                                     hp * 2048:(hp + 1) * 2048]))
            v.wait_ge(ch_v.h, ch_v.n)
        ctx.block.vector(khcp)
        kh_done = ch_v.n

        outps = [psB[0:65, 0:512], psB[0:65, 512:1024], psC[0:65, 0:512], psD[0:65, 0:512]]
        t0a = ch_t.n
        SC_V = {}

        def att_t(t, h=h, t0a=t0a, kh=kh_done):
            t.wait_ge(ch_v.h, kh)
            for si in range(16):
                for tb in range(si // 4, 4):
                    gidx = (si, tb)
                    if SC_WAIT.get(gidx):
                        t.wait_ge(ch_s.h, SC_WAIT[gidx])
                    ch_t.inc(t.matmul(psA[:, (tb % 2) * 512:(tb % 2) * 512 + 512],
                                      khT[0:64, si * 128:(si + 1) * 128],
                                      qTh[0:64, tb * 512:(tb + 1) * 512],
                                      start=True, stop=True))
                    t.wait_ge(ch_v.h, PV_WAIT[gidx])
                    ch_t.inc(t.matmul(outps[tb],
                                      shared[:, :].rearrange("p (c x) -> p c x", c=16)[:, si, :]
                                      [:, h * 65:h * 65 + 65],
                                      Ptile[:, tb * 512:(tb + 1) * 512],
                                      start=(si == 0), stop=(si == (tb * 4 + 3))))
        SC_WAIT = {}
        PV_WAIT = {}
        s0a = ch_s.n
        v0a = ch_v.n
        cnt_t = 0
        cnt_s = 0
        cnt_v = 0
        for si in range(16):
            for tb in range(si // 4, 4):
                cnt_t += 1
                SC_WAIT[(si, tb)] = (s0a + cnt_s) if cnt_s else 0
                cnt_s += 1
                if tb == si // 4:
                    cnt_v += 1
                PV_WAIT[(si, tb)] = v0a + cnt_v
                cnt_t += 1
        SC_WAIT = {}
        seq = []
        for si in range(16):
            for tb in range(si // 4, 4):
                seq.append((si, tb))
        for n, gidx in enumerate(seq):
            SC_WAIT[gidx] = (s0a + n - 1) if n >= 2 else 0
        PV_WAIT = {}
        nv = 0
        for n, gidx in enumerate(seq):
            si, tb = gidx
            if tb == si // 4:
                nv += 1
                PV_WAIT[gidx] = v0a + nv
            else:
                PV_WAIT[gidx] = 0
        PV_SWAIT = {}
        for n, gidx in enumerate(seq):
            si, tb = gidx
            PV_SWAIT[gidx] = s0a + n + 1
        del att_t

        def att_t(t, h=h, t0a=t0a, kh=kh_done):
            t.wait_ge(ch_v.h, kh)
            for n, (si, tb) in enumerate(seq):
                if SC_WAIT[(si, tb)]:
                    t.wait_ge(ch_s.h, SC_WAIT[(si, tb)])
                ch_t.inc(t.matmul(psA[:, (n % 2) * 512:(n % 2) * 512 + 512],
                                  khT[0:64, si * 128:(si + 1) * 128],
                                  qTh[0:64, tb * 512:(tb + 1) * 512],
                                  start=True, stop=True))
                t.wait_ge(ch_s.h, PV_SWAIT[(si, tb)])
                if PV_WAIT[(si, tb)]:
                    t.wait_ge(ch_v.h, PV_WAIT[(si, tb)])
                ch_t.inc(t.matmul(outps[tb],
                                  shared[:, :].rearrange("p (c x) -> p c x", c=16)[:, si, :]
                                  [:, h * 65:h * 65 + 65],
                                  Ptile[:, tb * 512:(tb + 1) * 512],
                                  start=(si == 0), stop=(si == tb * 4 + 3)))

        def att_s(s_, h=h, t0a=t0a):
            for n, (si, tb) in enumerate(seq):
                s_.wait_ge(ch_t.h, t0a + 2 * n + 1)
                ch_s.inc(s_.activation(Ptile[:, tb * 512:(tb + 1) * 512],
                                       psA[:, (n % 2) * 512:(n % 2) * 512 + 512],
                                       AF.Exp, scale=SCALE,
                                       bias=negcrep[:, h:h + 1]))

        def att_v(v, h=h, s0a=s0a):
            for n, (si, tb) in enumerate(seq):
                if tb == si // 4:
                    v.wait_ge(ch_s.h, s0a + n + 1)
                    ch_v.inc(v.tensor_mul(Ptile[:, tb * 512:(tb + 1) * 512],
                                          Ptile[:, tb * 512:(tb + 1) * 512],
                                          cmask[:, (si % 4) * 512:(si % 4 + 1) * 512]))
                    v.wait_ge(ch_v.h, ch_v.n)
        ctx.block.tensor(att_t)
        ctx.block.scalar(att_s)
        ctx.block.vector(att_v)
        att_done_t = ch_t.n

        for tb in range(4):
            t0n = ch_t.n

            def nrm_v(v, h=h, tb=tb, adt=att_done_t):
                v.wait_ge(ch_t.h, adt)
                ch_v.inc(v.tensor_copy(ob65[0:65, 0:512], outps[tb]))
                v.wait_ge(ch_v.h, ch_v.n)
                ch_v.inc(v.reciprocal(ob65[64:65, 0:512], ob65[64:65, 0:512]))
                v.wait_ge(ch_v.h, ch_v.n)
            ctx.block.vector(nrm_v)
            nv1 = ch_v.n

            def nrm_t(t, nv1=nv1):
                t.wait_ge(ch_v.h, nv1)
                ch_t.inc(t.matmul(psA[0:64, 0:512], ones2[64:65, 0:64], ob65[64:65, 0:512],
                                  start=True, stop=True))
            ctx.block.tensor(nrm_t)
            nt1 = ch_t.n

            def nrm_v2(v, h=h, hp=hp, hl=hl, tb=tb, nt1=nt1):
                v.wait_ge(ch_t.h, nt1)
                ch_v.inc(v.tensor_tensor(
                    outT[64 * hl:64 * hl + 64, hp * 2048 + tb * 512: hp * 2048 + (tb + 1) * 512],
                    ob65[0:64, 0:512], psA[0:64, 0:512], ALU.mult))
                v.wait_ge(ch_v.h, ch_v.n)
            ctx.block.vector(nrm_v2)
    heads_done = ch_v.n

    d0 = ch_wd.n

    def owdma(gq, gate=ch_t.n):
        gq.wait_ge(ch_t.h, gate)
        for mb in range(2):
            ch_wd.inc(gq.dma_start(out=Wst[mb][:, :].rearrange("p (dc m) -> p dc m", dc=8),
                                   in_=OUTW[:, mb * 512:(mb + 1) * 512]
                                   .rearrange("(dc p) m -> p dc m", p=128)), 16)
    ctx.block.gpsimd(owdma)
    ow_n = ch_wd.n
    sp1 = ch_sp.n

    v0w_pre = ch_v.n

    def x2dma(e, hd=heads_done, v0w=v0w_pre):
        e.wait_ge(ch_v.h, hd)
        xr = spill.rearrange("(c p) d -> c p d", p=128)
        for i in range(NCHUNK):
            if i >= 1:
                e.wait_ge(ch_v.h, v0w + 2 * i)
            ch_sp.inc(e.dma_start(out=scr[:, 0:1024], in_=xr[i]), 16)
            e.wait_ge(ch_v.h, v0w + 2 * i + 2)
            ch_sp.inc(e.dma_start(out=x3dram.rearrange("(c p) d -> c p d", p=128)[i],
                                  in_=x3st[:, 0:1024]), 16)
    v0w = ch_v.n
    t0w = ch_t.n

    def ow_t(t, ow_n=ow_n, hd=heads_done, v0w=v0w):
        t.wait_ge(ch_wd.h, ow_n)
        t.wait_ge(ch_v.h, hd)
        for i in range(NCHUNK):
            if i >= 1:
                t.wait_ge(ch_v.h, v0w + 2 * i - 1)
            for hp in range(8):
                lhs = outT[:, hp * 2048 + i * 128: hp * 2048 + (i + 1) * 128]
                t.matmul(psA[:, 0:512], lhs, Wst[0][:, hp * 512:(hp + 1) * 512],
                         start=(hp == 0), stop=(hp == 7))
                mm = t.matmul(psA[:, 512:1024], lhs, Wst[1][:, hp * 512:(hp + 1) * 512],
                              start=(hp == 0), stop=(hp == 7))
            ch_t.inc(mm)

    def ow_v(v, t0w=t0w, sp1=sp1):
        for i in range(NCHUNK):
            v.wait_ge(ch_t.h, t0w + i + 1)
            if i >= 1:
                v.wait_ge(ch_sp.h, sp1 + 32 * i)
            ch_v.inc(v.scalar_tensor_tensor(x3st[:, 0:1024], psA[:, 0:1024], 1.0,
                                            obbc[:, 0:1024], ALU.mult, ALU.add))
            v.wait_ge(ch_v.h, ch_v.n)
            v.wait_ge(ch_sp.h, sp1 + 32 * i + 16)
            ch_v.inc(v.tensor_add(x3st[:, 0:1024], x3st[:, 0:1024], scr[:, 0:1024]))
            v.wait_ge(ch_v.h, ch_v.n)
    ctx.block.sync(x2dma)
    ctx.block.tensor(ow_t)
    ctx.block.vector(ow_v)
    x3_sp = ch_sp.n

    def back_dma(e, x3_sp=x3_sp):
        e.wait_ge(ch_sp.h, x3_sp)
        if L1A_NOBACK:
            return
        xr = x3dram.rearrange("(c p) d -> c p d", p=128)
        for i in range(NCHUNK):
            ch_sp.inc(e.dma_start(out=xres[:, i * 1024:(i + 1) * 1024], in_=xr[i]), 16)
    ctx.block.sync(back_dma)
    bk = ch_sp.n

    def fin_v(v, bk=bk):
        v.wait_ge(ch_sp.h, bk)
        ch_v.inc(v.memset(gsc[:, 0:1], 0.0))
        v.wait_ge(ch_v.h, ch_v.n)
    ctx.block.vector(fin_v)
    return ch_v, ch_v.n



EMITTERS = {}
DEBUG_DUMPS = []
MOE_GATE_ONLY = False
L0A_LAYERS = 2
L0A_NST = 8
L0A_FFN = True
L0A_ST_PARTS = 9
L1A_NOBACK = False


def build_program(blocks, weights_shapes):
    nc = bass.Bass("TRN2", target_bir_lowering=False)
    x_in = nc.dram_tensor("x_in", [T, D], F32, kind="ExternalInput").ap()
    x_out = nc.dram_tensor("x_out", [T, D], F32, kind="ExternalOutput").ap()
    waps = {wn: nc.dram_tensor(wn, list(shp), F32, kind="ExternalInput").ap()
            for wn, shp in weights_shapes.items()}

    with nc.Block() as block:
        ctx = Ctx(nc, block)
        xres = ctx.sbuf("xres", [128, NCHUNK * 1024], F32)
        ident = ctx.sbuf("ident", [128, 128], BF16)
        identf = ctx.sbuf("identf", [128, 128], F32)
        scr0 = ctx.sbuf("scr_lnsq", [128, 1024], F32)
        if len(blocks) > 1:
            ctx.arena_bytes = 137 * 1024
            ctx.arena = nc.alloc_sbuf_tensor("arena", [128, ctx.arena_bytes], mybir.dt.int8)
            ctx.arena_ptr = 0
        else:
            ctx.arena = None
        identc = ctx.const("ident", _ident_bf16)
        identcf = ctx.const("identf", lambda: np.eye(128, dtype=np.float32))
        ch_x = ctx.chan("xin")

        def load_x(e):
            ch_x.inc(e.dma_start(out=ident[:], in_=identc[:, :]), 16)
            ch_x.inc(e.dma_start(out=identf[:], in_=identcf[:, :]), 16)
            xr = x_in.rearrange("(c p) d -> c p d", p=128)
            for i in range(NCHUNK):
                ch_x.inc(e.dma_start(out=xres[:, i * 1024:(i + 1) * 1024], in_=xr[i]), 16)
        block.sync(load_x)

        def gate(e):
            e.wait_ge(ch_x.h, 16 * (NCHUNK + 2))
        block.vector(gate)
        block.scalar(gate)
        block.tensor(gate)
        block.gpsimd(gate)

        done = None
        for bidx_, b in enumerate(blocks):
            if ctx.arena is not None:
                ctx.begin_block()
            if b == "l1f":
                done = emit_l1_ffn(ctx, waps["L1_ffn_weight"])
            elif b == "l0f":
                done = emit_l0_ffn(ctx, waps)
            elif b == "l0a":
                done = emit_l0_attn(ctx, waps)
            elif b == "l1a":
                done = emit_l1_attn(ctx, waps)
            else:
                done = EMITTERS[b](ctx, waps)
            if bidx_ < len(blocks) - 1:
                def _bar(e, dc=done):
                    e.wait_ge(dc[0].h, dc[1])
                block.vector(_bar)
                block.scalar(_bar)
                block.tensor(_bar)
                block.gpsimd(_bar)
                block.sync(_bar)

        ch_out = ctx.chan("xout")
        dump_aps = {}
        for dn in DEBUG_DUMPS:
            th = ctx.tiles[dn]
            shp = list(th.shape)
            dt = th.dtype if hasattr(th, "dtype") else F32
            dump_aps[dn] = nc.dram_tensor("dump_" + dn, shp, dt, kind="ExternalOutput").ap()

        def store_x(e):
            e.wait_ge(done[0].h, done[1])
            xr = x_out.rearrange("(c p) d -> c p d", p=128)
            for i in range(NCHUNK):
                ch_out.inc(e.dma_start(out=xr[i], in_=xres[:, i * 1024:(i + 1) * 1024]), 16)
            for dn, ap in dump_aps.items():
                ch_out.inc(e.dma_start(out=ap[:, :], in_=ctx.tiles[dn][:, :]), 16)
            e.wait_ge(ch_out.h, 16 * (NCHUNK + len(dump_aps)))
        block.sync(store_x)
    return nc, ctx


BLOCK_WEIGHTS = {
    "l0a": ["L0_attn_Wq", "L0_attn_Wk", "L0_attn_Wv", "L0_attn_Wg", "L0_attn_Wo",
            "L0_attn_gn_scale", "L0_attn_gn_bias", "L0_attn_ln1_scale", "L0_attn_ln1_bias",
            "L0_attn_ln2_scale", "L0_attn_ln2_bias", "L0_attn_ffn_w1", "L0_attn_ffn_w2"],
    "l0f": ["L0_ffn_Wg", "L0_ffn_bg", "L0_ffn_W1", "L0_ffn_b1", "L0_ffn_W2", "L0_ffn_b2"],
    "l1a": ["L1_attn_qkv_w", "L1_attn_qkv_b", "L1_attn_w1_write", "L1_attn_w2_write",
            "L1_attn_w1_read", "L1_attn_w2_read", "L1_attn_kv_geo_w", "L1_attn_out_w",
            "L1_attn_out_b", "L1_attn_kv_gate", "L1_attn_mem_scale", "L1_attn_decay_logits"],
    "l1f": ["L1_ffn_weight"],
}


def run_blocks(x_full, inputs, blocks):
    wnames = []
    for b in blocks:
        wnames += BLOCK_WEIGHTS[b]
    shapes = {}
    for w in wnames:
        shp = np.asarray(inputs[w]).shape
        if w == "L0_ffn_bg":
            shp = (1, 8)
        if shp == ():
            shp = (1,)
        shapes[w] = shp
    nc, ctx = build_program(blocks, shapes)
    in_maps = []
    for c in range(8):
        m = {"x_in": np.ascontiguousarray(np.asarray(x_full[c], dtype=np.float32))}
        perm = np.concatenate([np.arange(0, 128, 2), np.arange(1, 128, 2)])
        for w in wnames:
            a = np.ascontiguousarray(np.asarray(inputs[w], dtype=np.float32))
            if a.ndim == 0:
                a = a.reshape(1)
            if w == "L0_ffn_bg":
                a = a.reshape(1, 8)
            if w in ("L0_attn_Wq", "L0_attn_Wk"):
                a = np.ascontiguousarray(a[..., perm])
            m[w] = a
        for cn, arr in ctx.consts_np.items():
            m["c_" + cn] = arr
        in_maps.append(m)
    res = run_bass_kernel_spmd(nc, in_maps, core_ids=list(range(8)))
    if DEBUG_DUMPS:
        global LAST_DUMPS
        LAST_DUMPS = {dn: res.results[0]["dump_" + dn] for dn in DEBUG_DUMPS}
    return np.stack([res.results[c]["x_out"] for c in range(8)], axis=0)


def kernel(**inputs):
    x = np.asarray(inputs["L0_attn_x"], dtype=np.float32)
    return run_blocks(x, inputs, ["l0a", "l0f", "l1a", "l1f"])
